# version 10
# baseline (speedup 1.0000x reference)
import contextlib
import numpy as np
import concourse.bass as bass
import concourse.mybir as mybir
import concourse.bass_isa as bass_isa
from concourse.bass_utils import run_bass_kernel_spmd

F32 = mybir.dt.float32
BF16 = mybir.dt.bfloat16
ALU = mybir.AluOpType
AF = mybir.ActivationFunctionType
AXX = mybir.AxisListType.X
RED = bass_isa.ReduceOp

NCORES = 8
L, D, H, DK, I3 = 4096, 1024, 16, 64, 3072
CH, SLAB = 128, 512
NCH, NSLAB, NDC, NIC = L // CH, L // SLAB, D // 128, I3 // 128
CMAGIC = 12582912.0
P0_TILES = NCH
SKIP_TABLES = False
SKIP_WQKV = False
SKIP_DQ = False
P1_CHUNKS = NCH
P1_CUT = 99
XQ_MODE = 0
SKIP_XQ = False
XQ_BARRIER = False
XQ_OPS = 3
SKIP_RET = False
SKIP_KLN = False
SKIP_TR = False
LN_EPS = 1e-05
QMAX = 32767.0


class Ctr:
    __slots__ = ("h", "c", "name")

    def __init__(self, h, name):
        self.h, self.c, self.name = h, 0, name

    def bump(self, n=1):
        self.c += n
        return self.c


class Buf:
    __slots__ = ("w", "r")

    def __init__(self):
        self.w, self.r = {}, {}

    def note_w(self, ctr, tok):
        self.w = {ctr: tok}
        self.r = {}

    def note_w2(self, ctr, tok):
        if self.w.get(ctr, 0) < tok:
            self.w[ctr] = tok

    def note_r(self, ctr, tok):
        if self.r.get(ctr, 0) < tok:
            self.r[ctr] = tok


class Prog:
    def __init__(self, nc):
        self.nc = nc
        self.ops = {e: [] for e in ("sy", "sc", "ve", "te", "gp")}
        self.es = contextlib.ExitStack()
        self._mk = lambda n: Ctr(self.es.enter_context(nc.semaphore(n)), n)
        self.ctr = {"sc": self._mk("c_sc"), "ve": self._mk("c_ve"),
                    "te": self._mk("c_te"), "gp": self._mk("c_gp"),
                    "cc": self._mk("c_cc")}
        self.dstr = {}
        self.plan = []
        self.waited = {e: {} for e in self.ops}

    def dma_ctr(self, name):
        if name not in self.dstr:
            self.dstr[name] = self._mk("dm_" + name)
        return self.dstr[name]

    def E(self, eng, build, rd=(), wr=(), wr2=(), waits=(), dma=None,
          cc=False, noinc=False):
        if cc:
            ctr, n = self.ctr["cc"], 1
        elif dma is not None:
            ctr, n = self.dma_ctr(dma), 16
        else:
            ctr, n = self.ctr[eng], 1
        need = {}
        if dma is not None and ctr.c:
            need[ctr] = ctr.c

        def add(c, v):
            if v and need.get(c, 0) < v:
                need[c] = v
        for (c, v) in waits:
            add(c, v)
        for b in rd:
            for c, v in b.w.items():
                add(c, v)
        for b in wr:
            for src in (b.w, b.r):
                for c, v in src.items():
                    add(c, v)
        wm = self.waited[eng]
        final = []
        for c, v in need.items():
            if wm.get(c, 0) >= v:
                continue
            final.append((c.h, v))
            wm[c] = v
        tok = ctr.c if noinc else ctr.bump(n)

        pad = (eng == "te" and not noinc and not cc)

        def f(g, _w=tuple(final), _b=build, _h=ctr.h, _n=n, _ni=noinc,
              _pad=pad):
            for (h, v) in _w:
                g.wait_ge(h, v)
            ins = _b(g)
            if not _ni:
                ins.then_inc(_h, _n)
            if _pad:
                g.nop()
                g.nop()
                g.nop()
                g.nop()
        self.ops[eng].append(f)
        self.plan.append(
            (eng, ctr, (None if noinc else tok), tuple(need.items()),
             tuple(id(b) for b in rd), tuple(id(b) for b in wr)
             + tuple(id(b) for b in wr2)))
        for b in rd:
            b.note_r(ctr, tok)
        for b in wr:
            b.note_w(ctr, tok)
        for b in wr2:
            b.note_w2(ctr, tok)
        return tok

    def W(self, eng, waits):
        wm = self.waited[eng]
        final = []
        final2 = []
        for c, v in waits:
            if v and wm.get(c, 0) < v:
                final.append((c.h, v))
                final2.append((c, v))
                wm[c] = v
        if final:
            def f(g, _w=tuple(final)):
                for (h, v) in _w:
                    g.wait_ge(h, v)
            self.ops[eng].append(f)
            self.plan.append((eng, None, None, tuple((c, v) for c, v in final2),
                              (), ()))

    def barrier(self):
        snap = [(c, c.c) for c in self.ctr.values()]
        snap += [(c, c.c) for c in self.dstr.values()]
        for e in self.ops:
            self.W(e, snap)

    def wait_all_dma(self, eng):
        self.W(eng, [(c, c.c) for c in self.dstr.values()])


def build_program(debug=None):
    nc = bass.Bass("TRN2", target_bir_lowering=False)
    P = Prog(nc)
    E, W = P.E, P.W

    dT = lambda n, s, k: nc.dram_tensor(n, s, F32, kind=k)
    x_nat = dT("x_nat", [L, D], "ExternalInput").ap()
    xT = dT("xT", [D, L], "ExternalInput").ap()
    wqkvT = dT("wqkvT", [D, 3 * D], "ExternalInput").ap()
    woT = dT("woT", [D, D], "ExternalInput").ap()
    w_up = dT("w_up", [I3, D], "ExternalInput").ap()
    w_gate = dT("w_gate", [I3, D], "ExternalInput").ap()
    w_down = dT("w_down", [D, I3], "ExternalInput").ap()
    lgp8 = dT("lgp8", [128, 8], "ExternalInput").ap()
    lg16 = dT("lg16", [128, 16], "ExternalInput").ap()
    emat = dT("emat", [128, 128], "ExternalInput").ap()
    utri = dT("utri", [128, 128], "ExternalInput").ap()
    rampq = dT("rampq", [128, 128], "ExternalInput").ap()
    rampk = dT("rampk", [128, 64], "ExternalInput").ap()
    ident = dT("ident", [128, 128], "ExternalInput").ap()
    outT = dT("outT", [D, L], "ExternalOutput").ap()

    def scratch(name, shape, dbgkey):
        kind = "ExternalOutput" if debug == dbgkey else "Internal"
        return nc.dram_tensor(name, shape, BF16, kind=kind)

    y_sp = scratch("y_sp", [8, 128, L], "p1")
    up_sp = scratch("up_sp", [NIC, 128, L], "p2b")
    gate_sp = scratch("gate_sp", [NIC, 128, L], "none")
    g2_sp = scratch("g2_sp", [NIC, 128, L], "p2d")
    o_sp = scratch("o_sp", [NIC, 128, L], "p2e")
    wupdq_sp = nc.dram_tensor(
        "wupdq_sp", [NIC, 128, NDC * 128], BF16,
        kind="ExternalOutput" if debug == "dq" else "Internal")
    wgatedq_sp = nc.dram_tensor("wgatedq_sp", [NIC, 128, NDC * 128], BF16)
    wdndq_sp = nc.dram_tensor("wdndq_sp", [NDC, 128, NIC * 128], BF16)
    pmax_sp = nc.dram_tensor("pmax_sp", [4, 128], F32)
    cc_in = nc.dram_tensor("cc_in", [1, 16], F32)
    cc_out = nc.dram_tensor("cc_out", [1, 16], F32)
    b_ccin, b_ccout = Buf(), Buf()
    b_ysp, b_upsp, b_gatesp, b_g2sp, b_osp = (Buf() for _ in range(5))
    dbg_xn = None
    if debug in ("p0", "p0nocc", "ar1"):
        dbg_xn = nc.dram_tensor("dbg_xn", [128, NDC * L], BF16,
                                kind="ExternalOutput")

    es_g, es_x, es_a = (contextlib.ExitStack() for _ in range(3))
    SBg = lambda n, s, d: es_g.enter_context(nc.sbuf_tensor(n, s, d))
    SBa = lambda n, s, d: es_a.enter_context(nc.sbuf_tensor(n, s, d))
    PSa = lambda n, s, d: es_a.enter_context(nc.psum_tensor(n, s, d))

    stat = SBg("stat", [128, 16], F32)
    col = lambda i: stat[:, i:i + 1]
    b_stat = Buf()
    kst = SBg("kst", [128, 80], F32)
    b_kst = Buf()
    ccsb = SBg("ccsb", [1, 16], F32)
    b_ccsb = Buf()
    pmrow = SBg("pmrow", [1, 128], F32)
    b_pmrow = Buf()
    b_pmax = Buf()
    scl = SBg("scl", [128, 16], F32)
    cmag = SBg("cmag", [128, 1], F32)
    b_cmag = Buf()
    utri_sb = SBg("utri_sb", [128, 128], F32)
    ident_sb = SBg("ident_sb", [128, 128], F32)
    sc_ = lambda i: scl[:, i:i + 1]
    b_scl = Buf()
    b_mx = Buf()

    xnT = es_x.enter_context(nc.sbuf_tensor("xnT", [128, NDC * L], BF16, side="right"))
    b_xn = [Buf() for _ in range(NCH)]

    wqkv_sb = SBa("wqkv_sb", [128, NDC * 3 * D], BF16)
    b_wqkv = Buf()
    gam = SBa("gam", [128, H * 128], BF16)
    cq = SBa("cq", [128, 1024], BF16)
    ck = SBa("ck", [128, 1024], BF16)
    gcp = SBa("gcp", [128, 8], F32)
    b_tab = Buf()
    lg16_sb = SBa("lg16_sb", [128, 16], F32)
    lgp8_sb = SBa("lgp8_sb", [128, 8], F32)
    emat_sb = SBa("emat_sb", [128, 128], F32)

    rampq_sb = SBa("rampq_sb", [128, 128], F32)
    rampk_sb = SBa("rampk_sb", [128, 64], F32)

    b_const = Buf()
    xin = [SBa(f"xin{i}", [128, 1024], F32) for i in range(2)]
    b_xin = [Buf(), Buf()]
    scr1 = SBa("scr1", [128, 1024], F32)
    b_scr1 = Buf()
    q_tok = SBa("q_tok", [128, 1024], F32)
    b_qtok = Buf()
    kn_f32 = SBa("kn_f32", [128, 1024], F32)
    b_kn = Buf()
    ksq_f32 = SBa("ksq_f32", [128, 1024], F32)
    b_ksq = Buf()
    k_raw = SBa("k_raw", [128, 1024], F32)
    b_kraw = Buf()
    q_ch = [SBa(f"q_ch{i}", [128, 1024], BF16) for i in range(2)]
    b_q = [Buf(), Buf()]
    qt_ch = [SBa(f"qt_ch{i}", [128, 1024], BF16) for i in range(2)]
    b_qt = [Buf(), Buf()]
    knT_ch = [SBa(f"knT_ch{i}", [128, 1024], BF16) for i in range(2)]
    b_knT = [Buf(), Buf()]
    kh_ch = [SBa(f"kh_ch{i}", [128, 1024], BF16) for i in range(2)]
    b_kh = [Buf(), Buf()]
    v_ch = [SBa(f"v_ch{i}", [128, 1024], BF16) for i in range(2)]
    b_v = [Buf(), Buf()]
    a_scr = [SBa(f"a_scr{i}", [128, 256], BF16) for i in range(2)]
    b_a = [Buf(), Buf()]
    st_bf = SBa("st_bf", [128, 512], BF16)
    b_stb = Buf()
    y_stage = [SBa(f"y_stage{i}", [128, 4096], BF16) for i in range(2)]
    b_yst = [Buf(), Buf()]

    dq_in = [xin[0], xin[1], SBa("dq_in2", [128, 1024], F32)]
    b_dqin = [b_xin[0], b_xin[1], Buf()]
    dq_s = [SBa(f"dq_s{i}", [128, 96], F32) for i in range(2)]
    b_dqs = [Buf(), Buf()]
    dq_u = [SBa("dq_u0", [128, 1024], F32), scr1]
    b_dqu = [Buf(), b_scr1]
    dq_nat = [SBa(f"dq_nat{i}", [128, 1024], BF16) for i in range(2)]
    b_dqn = [Buf(), Buf()]
    dq_tr = [SBa(f"dq_tr{i}", [128, 1024], BF16) for i in range(2)]
    b_dqt = [Buf(), Buf()]

    p_q = PSa("p_q", [128, 1024], F32)
    b_pq = Buf()
    p_k = PSa("p_k", [128, 1024], F32)
    b_pk = Buf()
    p_v = PSa("p_v", [128, 1024], F32)
    b_pv = Buf()
    p_r = [PSa(f"p_r{i}", [128, 512], F32) for i in range(2)]
    b_pr = [Buf(), Buf()]

    for (dst, src) in [(lgp8_sb, lgp8), (lg16_sb, lg16), (emat_sb, emat),
                       (utri_sb, utri), (rampq_sb, rampq), (rampk_sb, rampk),
                       (ident_sb, ident)]:
        E("sy", lambda g, d=dst, s=src: g.dma_start(out=d[:], in_=s[:]),
          wr2=[b_const], dma="cst")
    if not SKIP_WQKV:
        for dc in range(NDC):
            E("gp", lambda g, dc=dc: g.dma_start(
                out=wqkv_sb[:, dc * 3072:(dc + 1) * 3072],
                in_=wqkvT[dc * 128:(dc + 1) * 128, :]),
              wr=([b_wqkv] if dc == 0 else []),
              wr2=([b_wqkv] if dc > 0 else []), dma=f"wq{dc % 2}")
    E("gp", lambda g: g.memset(ccsb[:], 0.0), wr=[b_ccsb])
    E("gp", lambda g: g.memset(cmag[:], CMAGIC), wr=[b_cmag])
    E("gp", lambda g: g.dma_start(out=cc_in.ap(), in_=ccsb[:]),
      rd=[b_ccsb], wr=[b_ccin], dma="ccd")
    for fn in (AF.Sigmoid, AF.Ln):
        E("sc", lambda g, fn=fn: g.activation(lg16_sb[:], lg16_sb[:], fn),
          rd=[b_const], wr=[b_const])
        E("sc", lambda g, fn=fn: g.activation(lgp8_sb[:], lgp8_sb[:], fn),
          rd=[b_const], wr=[b_const])
    E("sc", lambda g: g.activation(gcp[:], lgp8_sb[:], AF.Exp, scale=128.0),
      rd=[b_const], wr2=[b_tab])
    for h in range(H if not SKIP_TABLES else 0):
        E("ve", lambda g, h=h: g.tensor_scalar(
            scr1[:, 0:128], emat_sb[:], lg16_sb[:, h:h + 1],
            None, ALU.mult), rd=[b_const], wr=[b_scr1])
        E("sc", lambda g, h=h: g.activation(
            gam[:, h * 128:(h + 1) * 128], scr1[:, 0:128],
            AF.Exp), rd=[b_scr1], wr2=[b_tab])
        E("ve", lambda g, h=h: g.tensor_tensor(
            gam[:, h * 128:(h + 1) * 128], gam[:, h * 128:(h + 1) * 128],
            utri_sb[:], ALU.mult), rd=[b_tab, b_const], wr2=[b_tab])
        E("ve", lambda g, h=h: g.tensor_scalar(
            scr1[:, 128:192], rampk_sb[:], lg16_sb[:, h:h + 1],
            None, ALU.mult), rd=[b_const], wr=[b_scr1])
        E("sc", lambda g, h=h: g.activation(
            ck[:, h * 64:(h + 1) * 64], scr1[:, 128:192], AF.Exp),
          rd=[b_scr1], wr2=[b_tab])
    for oc in range(8 if not SKIP_TABLES else 0):
        E("ve", lambda g, oc=oc: g.tensor_scalar(
            scr1[:, 0:128], rampq_sb[:], lgp8_sb[:, oc:oc + 1],
            None, ALU.mult), rd=[b_const], wr=[b_scr1])
        E("sc", lambda g, oc=oc: g.activation(
            cq[:, oc * 128:(oc + 1) * 128], scr1[:, 0:128],
            AF.Exp), rd=[b_scr1], wr2=[b_tab])
    E("gp", lambda g: g.memset(st_bf[:], 0.0), wr=[b_stb])

    tab_waits = [(P.ctr["sc"], P.ctr["sc"].c), (P.ctr["ve"], P.ctr["ve"].c)]

    for i in range(P0_TILES):
        par = i % 2
        t0 = i * CH
        E("sy", lambda g, i=i, par=par: g.dma_start(
            out=xin[par][:], in_=x_nat[i * CH:(i + 1) * CH, :]),
          wr=[b_xin[par]], dma=f"xin{par}")
        E("sc", lambda g, par=par: g.activation(scr1[:], xin[par][:],
                                                AF.Square),
          rd=[b_xin[par]], wr=[b_scr1])
        E("ve", lambda g, par=par: g.tensor_reduce(
            col(0), xin[par][:], axis=AXX, op=ALU.add),
          rd=[b_xin[par]], wr=[b_stat])
        E("ve", lambda g: g.tensor_reduce(col(1), scr1[:], axis=AXX,
                                          op=ALU.add),
          rd=[b_scr1], wr=[b_stat])
        E("ve", lambda g: g.tensor_scalar(col(2), col(0), 1.0 / 1024.0, None,
                                          ALU.mult), rd=[b_stat], wr=[b_stat])
        E("ve", lambda g: g.tensor_tensor(col(3), col(2), col(2), ALU.mult),
          rd=[b_stat], wr=[b_stat])
        E("ve", lambda g: g.scalar_tensor_tensor(
            col(4), col(1), 1.0 / 1024.0, col(3), ALU.mult, ALU.subtract),
          rd=[b_stat], wr=[b_stat])
        E("ve", lambda g: g.tensor_scalar(col(4), col(4), LN_EPS, None,
                                          ALU.add), rd=[b_stat], wr=[b_stat])
        E("sc", lambda g: g.activation(col(5), col(4), AF.Ln),
          rd=[b_stat], wr=[b_stat])
        E("sc", lambda g: g.activation(col(6), col(5), AF.Exp, scale=-0.5),
          rd=[b_stat], wr=[b_stat])
        E("ve", lambda g, par=par: g.tensor_scalar(
            scr1[:], xin[par][:], col(2), col(6), ALU.subtract, ALU.mult),
          rd=[b_xin[par], b_stat], wr=[b_scr1])
        E("ve", lambda g: g.tensor_reduce(
            col(7), scr1[:], axis=AXX, op=ALU.max, apply_absolute_value=True),
          rd=[b_scr1], wr=[b_stat])
        E("ve", lambda g, first=(i == 0): g.tensor_tensor(
            sc_(14), col(7) if first else sc_(14), col(7), ALU.max),
          rd=[b_stat], wr=[b_mx])
        for dc in range(NDC):
            E("te", lambda g, dc=dc: g.transpose(
                p_q[:, dc * 128:(dc + 1) * 128],
                scr1[:, dc * 128:(dc + 1) * 128], ident_sb[:]),
              rd=[b_scr1, b_const], wr=([b_pq] if dc == 0 else []),
              wr2=([b_pq] if dc > 0 else []))
        E("sc", lambda g, t0=t0: g.copy(
            bass.AP(xnT, t0, [[NDC * L, 128], [L, NDC], [1, CH]]),
            p_q[:].rearrange("p (dc t) -> p dc t", t=CH)),
          rd=[b_pq], wr2=[b_xn[i]])

    def allreduce_max(cols_in, in_bufs, psum, b_psum):
        for j, cin in enumerate(cols_in):
            E("gp", lambda g, j=j, cin=cin: g.dma_start(
                out=pmax_sp.ap()[j, :], in_=cin),
              rd=in_bufs, wr=[b_pmax], dma="ccd")
            E("gp", lambda g, j=j: g.dma_start(
                out=pmrow[:], in_=pmax_sp.ap()[j:j + 1, :]),
              rd=[b_pmax], wr=[b_pmrow], dma="ccd")
            E("ve", lambda g, j=j: g.tensor_reduce(
                ccsb[0:1, j:j + 1], pmrow[:], axis=AXX, op=ALU.max),
              rd=[b_pmrow], wr=([b_ccsb] if j == 0 else []),
              wr2=([b_ccsb] if j > 0 else []))
            E("gp", lambda g, j=j: g.dma_start(
                out=cc_in.ap()[0:1, j:j + 1], in_=ccsb[0:1, j:j + 1]),
              rd=[b_ccsb], wr=([b_ccin] if j == 0 else []),
              wr2=([b_ccin] if j > 0 else []), dma="ccd")
        E("gp", lambda g: g.collective_compute(
            "AllReduce", ALU.max, replica_groups=[list(range(NCORES))],
            ins=[cc_in.ap().opt()], outs=[cc_out.ap().opt()]),
          rd=[b_ccin], wr=[b_ccout], cc=True)
        E("gp", lambda g: g.dma_start(out=ccsb[:], in_=cc_out.ap()),
          rd=[b_ccout], wr=[b_ccsb], dma="ccd")

    def bcast_scale(j_src, c_s, c_sinv, psum, b_psum):
        E("te", lambda g, j=j_src, psum=psum: g.matmul(
            psum[:, 0:1], utri_sb[0:1, :], ccsb[0:1, j:j + 1],
            start=True, stop=True),
          rd=[b_ccsb, b_const], wr=[b_psum])
        E("ve", lambda g, c_s=c_s, psum=psum: g.tensor_scalar(
            sc_(c_s), psum[:, 0:1], 1.0 / QMAX, 1e-12, ALU.mult, ALU.max),
          rd=[b_psum], wr2=[b_scl])
        E("ve", lambda g, c_s=c_s, c_i=c_sinv: g.reciprocal(sc_(c_i),
                                                            sc_(c_s)),
          rd=[b_scl], wr2=[b_scl])

    def ar1_issue():
        j = 0
        E("gp", lambda g: g.dma_start(out=pmax_sp.ap()[j, :], in_=sc_(14)),
          rd=[b_mx], wr=[b_pmax], dma="ccd")
        E("gp", lambda g: g.dma_start(out=pmrow[:], in_=pmax_sp.ap()[j:j + 1, :]),
          rd=[b_pmax], wr=[b_pmrow], dma="ccd")
        E("ve", lambda g: g.tensor_reduce(
            ccsb[0:1, j:j + 1], pmrow[:], axis=AXX, op=ALU.max),
          rd=[b_pmrow], wr=[b_ccsb])
        E("gp", lambda g: g.dma_start(
            out=cc_in.ap()[0:1, j:j + 1], in_=ccsb[0:1, j:j + 1]),
          rd=[b_ccsb], wr=[b_ccin], dma="ccd")
        E("gp", lambda g: g.collective_compute(
            "AllReduce", ALU.max, replica_groups=[list(range(NCORES))],
            ins=[cc_in.ap().opt()], outs=[cc_out.ap().opt()]),
          rd=[b_ccin], wr=[b_ccout], cc=True)

    def ar1_consume():
        E("gp", lambda g: g.dma_start(
            out=sc_(0), in_=bass.AP(cc_out, 0, [[0, 128], [1, 1]])),
          rd=[b_ccout], wr2=[b_scl], dma="ccd")
        E("ve", lambda g: g.tensor_scalar(
            sc_(0), sc_(0), 1.0 / QMAX, 1e-12, ALU.mult, ALU.max),
          rd=[b_scl], wr2=[b_scl])
        E("ve", lambda g: g.reciprocal(sc_(1), sc_(0)),
          rd=[b_scl], wr2=[b_scl])

    if debug in ("p0", "p0nocc", "ar1"):
        if debug != "p0nocc":
            allreduce_max([sc_(14)], [b_mx], p_r[0][:, 256:384], b_pr[0])
            bcast_scale(0, 0, 1, p_r[0][:, 256:384], b_pr[0])

    if debug == "ar1":
        E("gp", lambda g: g.dma_start(out=dbg_xn.ap()[:, 0:16], in_=scl[:]),
          rd=[b_scl], dma="dbg")
        P.wait_all_dma("gp")
        P.wait_all_dma("sy")
        return nc, P, locals()

    if debug in ("p0", "p0nocc"):
        E("sy", lambda g: g.dma_start(out=dbg_xn.ap(), in_=xnT[:]),
          rd=list(b_xn), dma="dbg")
        P.wait_all_dma("sy")
        return nc, P, locals()

    dq_done = {}
    dq_cnt = [0]

    def dq_load(t):
        mat, src_ap, dd, oc = dq_plan[t]
        E("sy", lambda g, t=t, src_ap=src_ap: g.dma_start(
            out=dq_in[t % 3][:], in_=src_ap), wr=[b_dqin[t % 3]], dma="dqi")

    def dq_ab(t):
        i3, i2 = t % 3, t % 2
        E("ve", lambda g, i3=i3, i2=i2: g.tensor_reduce(
            dq_s[i2][:, 0:32],
            dq_in[i3][:].rearrange("p (b k) -> p b k", k=32),
            axis=AXX, op=ALU.max, apply_absolute_value=True),
          rd=[b_dqin[i3]], wr=[b_dqs[i2]])
        E("ve", lambda g, i2=i2: g.tensor_scalar(
            dq_s[i2][:, 32:64], dq_s[i2][:, 0:32], 1.0 / 7.0, 1e-12,
            ALU.mult, ALU.max), rd=[b_dqs[i2]], wr2=[b_dqs[i2]])
        E("ve", lambda g, i2=i2: g.reciprocal(
            dq_s[i2][:, 64:96], dq_s[i2][:, 32:64]),
          rd=[b_dqs[i2]], wr2=[b_dqs[i2]])
        E("gp", lambda g, i3=i3, i2=i2: g.tensor_tensor(
            dq_u[i2][:], dq_in[i3][:],
            bass.AP(dq_s[i2], 64, [[96, 128], [1, 32], [0, 32]]), ALU.mult),
          rd=[b_dqin[i3], b_dqs[i2]], wr=[b_dqu[i2]])

    def dq_cd(t):
        mat, src_ap, dd, oc = dq_plan[t]
        i2 = t % 2
        E("ve", lambda g, i2=i2: g.tensor_scalar(
            dq_u[i2][:], dq_u[i2][:], CMAGIC, -CMAGIC, ALU.add, ALU.add),
          rd=[b_dqu[i2]], wr=[b_dqu[i2]])
        E("gp", lambda g, i2=i2: g.tensor_tensor(
            dq_nat[i2][:], dq_u[i2][:],
            bass.AP(dq_s[i2], 32, [[96, 128], [1, 32], [0, 32]]), ALU.mult),
          rd=[b_dqu[i2], b_dqs[i2]], wr=[b_dqn[i2]])
        for dc in range(NDC):
            E("sy", lambda g, dc=dc, i2=i2: g.dma_start(
                out=dq_tr[i2][:, dc * 128:(dc + 1) * 128],
                in_=dq_nat[i2][:, dc * 128:(dc + 1) * 128], transpose=True),
              rd=[b_dqn[i2]], wr=([b_dqt[i2]] if dc == 0 else []),
              wr2=([b_dqt[i2]] if dc > 0 else []), dma=f"dqt{i2}_{dc % 4}")
        tout = E("gp", lambda g, i2=i2, oc=oc, dst=dd: g.dma_start(
            out=dst.ap()[oc, :, :], in_=dq_tr[i2][:]),
          rd=[b_dqt[i2]], dma="dqo")
        dq_done[(mat, oc)] = (P.dma_ctr("dqo"), tout)

    dq_plan = []
    for oc in range(NIC):
        dq_plan.append(("up", w_up[oc * 128:(oc + 1) * 128, :], wupdq_sp, oc))
        dq_plan.append(("gate", w_gate[oc * 128:(oc + 1) * 128, :],
                        wgatedq_sp, oc))
    NDQ = len(dq_plan)
    dq_ld = [0]
    dq_abi = [0]
    dq_cdi = [0]

    def dq_target(c):
        return max(0, min(NDQ, (c + 1) * NDQ // NCH))

    def emit_dq_stages(c):
        if SKIP_DQ:
            return
        while dq_cdi[0] < dq_target(c - 2):
            dq_cd(dq_cdi[0]); dq_cdi[0] += 1
        while dq_abi[0] < dq_target(c - 1):
            dq_ab(dq_abi[0]); dq_abi[0] += 1
        while dq_ld[0] < dq_target(c):
            dq_load(dq_ld[0]); dq_ld[0] += 1

    def emit_proj(c, which):
        t0 = c * CH
        ps, b_ps, off = which
        for half in range(2):
            for dc in range(NDC):
                E("te", lambda g, ps=ps, half=half, dc=dc, t0=t0, off=off:
                  g.matmul(
                      ps[:, half * 512:(half + 1) * 512],
                      xnT[:, dc * L + t0: dc * L + t0 + CH],
                      wqkv_sb[:, dc * 3072 + off + half * 512:
                              dc * 3072 + off + (half + 1) * 512],
                      start=(dc == 0), stop=(dc == NDC - 1)),
                  rd=[b_xn[c], b_wqkv],
                  wr=([b_ps] if (half == 0 and dc == 0) else []),
                  wr2=([b_ps] if (half == 1 and dc == NDC - 1) else []),
                  noinc=(dc != NDC - 1))

    def emit_kln_apply(c):
        par = c % 2
        E("sc", lambda g: g.copy(q_tok[:], p_q[:]), rd=[b_pq], wr=[b_qtok])
        E("sc", lambda g, par=par: g.copy(v_ch[par][:], p_v[:]),
          rd=[b_pv], wr=[b_v[par]])
        E("ve", lambda g: g.tensor_tensor(
            kn_f32[:], k_raw[:],
            bass.AP(kst, 32, [[80, 128], [1, 16], [0, 64]]), ALU.subtract),
          rd=[b_kraw, b_kst], wr=[b_kn])
        E("ve", lambda g: g.tensor_tensor(
            kn_f32[:], kn_f32[:],
            bass.AP(kst, 48, [[80, 128], [1, 16], [0, 64]]), ALU.mult),
          rd=[b_kst], wr2=[b_kn])
        E("ve", lambda g, par=par: g.tensor_tensor(
            kh_ch[par][:], kn_f32[:], ck[:], ALU.mult),
          rd=[b_kn], wr=[b_kh[par]], waits=tab_waits)

    def emit_kln_pre(c):
        tks_ = E("ve", lambda g: g.tensor_reduce(
            kst[:, 0:16], p_k[:].rearrange("p (h e) -> p h e", e=DK),
            axis=AXX, op=ALU.add), rd=[b_pk], wr=[b_kst])
        E("sc", lambda g: g.copy(k_raw[:], p_k[:]), rd=[b_pk], wr=[b_kraw],
          waits=[(P.ctr["ve"], tks_)])
        E("sc", lambda g: g.activation(ksq_f32[:], k_raw[:], AF.Square),
          rd=[b_kraw], wr=[b_ksq])
        E("ve", lambda g: g.tensor_reduce(
            kst[:, 16:32], ksq_f32[:].rearrange("p (h e) -> p h e", e=DK),
            axis=AXX, op=ALU.add), rd=[b_ksq], wr2=[b_kst])
        E("ve", lambda g: g.tensor_scalar(
            kst[:, 32:48], kst[:, 0:16], 1.0 / DK, None, ALU.mult),
          rd=[b_kst], wr2=[b_kst])
        E("ve", lambda g: g.tensor_tensor(
            kst[:, 64:80], kst[:, 32:48], kst[:, 32:48], ALU.mult),
          rd=[b_kst], wr2=[b_kst])
        E("ve", lambda g: g.scalar_tensor_tensor(
            kst[:, 48:64], kst[:, 16:32], 1.0 / DK, kst[:, 64:80],
            ALU.mult, ALU.subtract), rd=[b_kst], wr2=[b_kst])
        E("ve", lambda g: g.tensor_scalar(
            kst[:, 48:64], kst[:, 48:64], LN_EPS, None, ALU.add),
          rd=[b_kst], wr2=[b_kst])
        E("sc", lambda g: g.activation(kst[:, 48:64], kst[:, 48:64], AF.Ln),
          rd=[b_kst], wr2=[b_kst])
        E("sc", lambda g: g.activation(
            kst[:, 48:64], kst[:, 48:64], AF.Exp, scale=-0.5),
          rd=[b_kst], wr2=[b_kst])

    def emit_transposes(c):
        par = c % 2
        for j in range(NDC):
            E("te", lambda g, j=j: g.transpose(
                p_v[:, j * 128:(j + 1) * 128], q_tok[:, j * 128:(j + 1) * 128],
                ident_sb[:]), rd=[b_qtok, b_const],
              wr=([b_pv] if j == 0 else []), wr2=([b_pv] if j > 0 else []))
        tqt_ = E("ve", lambda g, par=par: g.tensor_tensor(
            qt_ch[par][:], p_v[:], cq[:], ALU.mult),
          rd=[b_pv], wr=[b_qt[par]], waits=tab_waits)
        E("sc", lambda g, par=par: g.copy(q_ch[par][:], p_v[:]),
          rd=[b_pv], wr=[b_q[par]], waits=[(P.ctr["ve"], tqt_)])
        for j in range(NDC):
            E("te", lambda g, j=j: g.transpose(
                p_v[:, j * 128:(j + 1) * 128], kn_f32[:, j * 128:(j + 1) * 128],
                ident_sb[:]), rd=[b_kn, b_const],
              wr=([b_pv] if j == 0 else []), wr2=([b_pv] if j > 0 else []))
        E("sc", lambda g, par=par: g.copy(knT_ch[par][:], p_v[:]),
          rd=[b_pv], wr=[b_knT[par]])

    def emit_retention(c):
        par = c % 2
        gpar = (c // 4) % 2
        for p in range(8):
            rpar = p % 2
            for half in range(2):
                E("te", lambda g, p=p, half=half, par=par, rpar=rpar: g.matmul(
                    p_r[rpar][:, half * 128:(half + 1) * 128],
                    knT_ch[par][64 * half:64 * (half + 1),
                                p * 128:(p + 1) * 128],
                    q_ch[par][64 * half:64 * (half + 1),
                              p * 128:(p + 1) * 128],
                    start=True, stop=True, tile_position=(64 * half, 0)),
                  rd=[b_knT[par], b_q[par]],
                  wr=([b_pr[rpar]] if half == 0 else []),
                  wr2=([b_pr[rpar]] if half == 1 else []))
            E("ve", lambda g, p=p, rpar=rpar: g.tensor_tensor(
                a_scr[rpar][:], p_r[rpar][:, 0:256],
                gam[:, p * 256:(p + 1) * 256], ALU.mult),
              rd=[b_pr[rpar]], wr=[b_a[rpar]], waits=tab_waits)
            for half in range(2):
                h = 2 * p + half
                E("te", lambda g, h=h, half=half, par=par, rpar=rpar: g.matmul(
                    p_r[rpar][64 * half:64 * (half + 1), 256:384],
                    v_ch[par][:, h * 64:(h + 1) * 64],
                    a_scr[rpar][:, half * 128:(half + 1) * 128],
                    start=True, stop=False, tile_position=(0, 64 * half)),
                  rd=[b_v[par], b_a[rpar]], wr2=[b_pr[rpar]], noinc=True)
                E("te", lambda g, p=p, half=half, par=par, rpar=rpar: g.matmul(
                    p_r[rpar][64 * half:64 * (half + 1), 256:384],
                    st_bf[64 * half:64 * (half + 1), p * 64:(p + 1) * 64],
                    qt_ch[par][64 * half:64 * (half + 1),
                               p * 128:(p + 1) * 128],
                    start=False, stop=True,
                    tile_position=(64 * half, 64 * half)),
                  rd=[b_stb, b_qt[par]], wr2=[b_pr[rpar]])
            for half in range(2):
                h = 2 * p + half
                E("te", lambda g, h=h, half=half, par=par, rpar=rpar: g.matmul(
                    p_r[rpar][64 * half:64 * (half + 1), 384:448],
                    kh_ch[par][:, h * 64:(h + 1) * 64],
                    v_ch[par][:, h * 64:(h + 1) * 64],
                    start=True, stop=True, tile_position=(0, 64 * half)),
                  rd=[b_kh[par], b_v[par]], wr2=[b_pr[rpar]])
            tye_ = E("sc", lambda g, p=p, rpar=rpar, c=c, gpar=gpar: g.copy(
                y_stage[gpar][:, p * 512 + (c % 4) * 128:
                              p * 512 + (c % 4) * 128 + 128],
                p_r[rpar][:, 256:384]),
              rd=[b_pr[rpar]],
              wr=([b_yst[gpar]] if (p == 0 and c % 4 == 0) else []),
              wr2=([b_yst[gpar]] if not (p == 0 and c % 4 == 0) else []))
            E("ve", lambda g, p=p, rpar=rpar: g.scalar_tensor_tensor(
                st_bf[:, p * 64:(p + 1) * 64], st_bf[:, p * 64:(p + 1) * 64],
                gcp[:, p:p + 1], p_r[rpar][:, 384:448], ALU.mult, ALU.add),
              rd=[b_stb, b_pr[rpar]], wr=[b_stb],
              waits=tab_waits + [(P.ctr["sc"], tye_)])

    def emit_tail(c):
        par = c % 2
        gpar = (c // 4) % 2
        if c % 4 == 3:
            gidx = c // 4
            for pr in range(8):
                E("gp", lambda g, gidx=gidx, gpar=gpar, pr=pr: g.dma_start(
                    out=y_sp.ap()[pr, :, gidx * 4 * CH:(gidx + 1) * 4 * CH],
                    in_=y_stage[gpar][:, pr * 512:(pr + 1) * 512]),
                  rd=[b_yst[gpar]], wr2=[b_ysp],
                  dma=f"ysp{(gpar, pr % 2)}")
        emit_dq_stages(c)
        if c == 2 and debug not in ("p0", "p0nocc", "ar1"):
            ar1_issue()

    QKV = [(p_q, b_pq, 0), (p_k, b_pk, 1024), (p_v, b_pv, 2048)]
    emit_proj(0, QKV[0])
    emit_proj(0, QKV[1])
    emit_kln_pre(0)
    for c in range(P1_CHUNKS):
        emit_proj(c, QKV[2])
        emit_kln_apply(c)
        if c + 1 < P1_CHUNKS:
            emit_proj(c + 1, QKV[0])
            emit_proj(c + 1, QKV[1])
        emit_transposes(c)
        if c + 1 < P1_CHUNKS:
            emit_kln_pre(c + 1)
        emit_retention(c)
        emit_tail(c)
    for c in range(P1_CHUNKS, P1_CHUNKS + 3):
        emit_dq_stages(c)
    if debug not in ("p0", "p0nocc", "ar1"):
        ar1_consume()

    if XQ_BARRIER and not SKIP_XQ:
        P.barrier()
    for dc in range(NDC if not SKIP_XQ else 0):
        for half in range(4):
            base = dc * L + half * 1024
            grp = list(b_xn)
            E("ve", lambda g, base=base: g.tensor_scalar(
                ksq_f32[:], xnT[:, base:base + 1024], sc_(1), CMAGIC,
                ALU.mult, ALU.add),
              rd=[b_scl] + grp, wr=[b_ksq])
            E("ve", lambda g, base=base: g.tensor_scalar(
                xnT[:, base:base + 1024], ksq_f32[:], CMAGIC, sc_(0),
                ALU.subtract, ALU.mult),
              rd=[b_ksq, b_scl],
              wr=(grp if (dc == 0 and half == 0) else []),
              wr2=(grp if not (dc == 0 and half == 0) else []))

    if debug == "p1":
        P.wait_all_dma("sy")
        return nc, P, locals()

    P.barrier()
    es_a.close()
    es_b = contextlib.ExitStack()
    SBb = lambda n, s, d: es_b.enter_context(nc.sbuf_tensor(n, s, d))
    PSb = lambda n, s, d: es_b.enter_context(nc.psum_tensor(n, s, d))

    wst = [SBb(f"wst{i}", [128, 1024], BF16) for i in range(4)]
    b_wst = [Buf() for _ in range(4)]
    stg = [SBb(f"stg{i}", [128, 512], BF16) for i in range(4)]
    b_stg = [Buf() for _ in range(4)]
    dqdw_in = [SBb(f"dqdw_in{i}", [128, 3072], F32) for i in range(2)]
    b_dqdwin = [Buf(), Buf()]
    dqd_s = SBb("dqd_s", [128, 288], F32)
    b_dqds = Buf()
    dqd_u = SBb("dqd_u", [128, 3072], F32)
    b_dqdu = Buf()
    dqd_nat = SBb("dqd_nat", [128, 3072], BF16)
    b_dqdn = Buf()
    dqd_tr = [SBb(f"dqd_tr{i}", [128, 3072], BF16) for i in range(2)]
    b_dqdt = [Buf(), Buf()]
    p_f = [PSb(f"p_f{i}", [128, 512], F32) for i in range(4)]
    b_pf = [Buf() for _ in range(4)]

    dqd_done = {}
    dqd_cnt = [0]

    def emit_dqd(dc):
        par = dqd_cnt[0] % 2
        tpar = dqd_cnt[0] % 2
        dqd_cnt[0] += 1
        E("sy", lambda g, dc=dc, par=par: g.dma_start(
            out=dqdw_in[par][:], in_=w_down[dc * 128:(dc + 1) * 128, :]),
          wr=[b_dqdwin[par]], dma="dqdi")
        E("ve", lambda g, par=par: g.tensor_reduce(
            dqd_s[:, 0:96], dqdw_in[par][:].rearrange("p (b k) -> p b k",
                                                      k=32),
            axis=AXX, op=ALU.max, apply_absolute_value=True),
          rd=[b_dqdwin[par]], wr=[b_dqds])
        E("ve", lambda g: g.tensor_scalar(
            dqd_s[:, 96:192], dqd_s[:, 0:96], 1.0 / 7.0, 1e-12, ALU.mult,
            ALU.max), rd=[b_dqds], wr2=[b_dqds])
        E("ve", lambda g: g.reciprocal(dqd_s[:, 192:288], dqd_s[:, 96:192]),
          rd=[b_dqds], wr2=[b_dqds])
        E("gp", lambda g, par=par: g.tensor_tensor(
            dqd_u[:], dqdw_in[par][:],
            bass.AP(dqd_s, 192, [[288, 128], [1, 96], [0, 32]]), ALU.mult),
          rd=[b_dqdwin[par], b_dqds], wr=[b_dqdu])
        E("ve", lambda g: g.tensor_scalar(
            dqd_u[:], dqd_u[:], CMAGIC, -CMAGIC, ALU.add, ALU.add),
          rd=[b_dqdu], wr=[b_dqdu])
        E("ve", lambda g: g.tensor_scalar(
            dqd_u[:], dqd_u[:], 7.0, -8.0, ALU.min, ALU.max),
          rd=[b_dqdu], wr=[b_dqdu])
        E("gp", lambda g: g.tensor_tensor(
            dqd_nat[:], dqd_u[:],
            bass.AP(dqd_s, 96, [[288, 128], [1, 96], [0, 32]]), ALU.mult),
          rd=[b_dqdu, b_dqds], wr=[b_dqdn])
        for ic in range(NIC):
            E("sy", lambda g, ic=ic, tpar=tpar: g.dma_start(
                out=dqd_tr[tpar][:, ic * 128:(ic + 1) * 128],
                in_=dqd_nat[:, ic * 128:(ic + 1) * 128], transpose=True),
              rd=[b_dqdn], wr=([b_dqdt[tpar]] if ic == 0 else []),
              wr2=([b_dqdt[tpar]] if ic > 0 else []), dma=f"dqdt{tpar}")
        tout = E("gp", lambda g, tpar=tpar, dc=dc: g.dma_start(
            out=wdndq_sp.ap()[dc, :, :], in_=dqd_tr[tpar][:]),
          rd=[b_dqdt[tpar]], dma="dqdo")
        dqd_done[dc] = (P.dma_ctr("dqdo"), tout)

    dqd_emitted = [0]
    mats2b = [("up", wupdq_sp, up_sp, b_upsp, 10),
              ("gate", wgatedq_sp, gate_sp, b_gatesp, 11)]
    ldplan = [(m[0], m[1], oc) for m in mats2b for oc in range(NIC)]

    def load_wst(k):
        mname, wsp, oc = ldplan[k]
        E("sy", lambda g, oc=oc, wsp=wsp, k=k: g.dma_start(
            out=wst[k % 4][:], in_=wsp.ap()[oc, :, :]),
          wr=[b_wst[k % 4]], waits=[dq_done[(mname, oc)]], dma=f"wst{k % 4}")

    load_wst(0)
    load_wst(1)
    kk = [2]
    for (mname, wsp, osp, b_osp_, mxcol) in mats2b:
        for oc in range(NIC):
            wb = b_wst[oc % 4]
            if kk[0] < len(ldplan):
                load_wst(kk[0])
                kk[0] += 1
            for slab in range(NSLAB):
                pf = (oc * NSLAB + slab) % 4
                for dc in range(NDC):
                    E("te", lambda g, oc=oc, slab=slab, dc=dc, pf=pf: g.matmul(
                        p_f[pf][:], wst[oc % 4][:, dc * 128:(dc + 1) * 128],
                        xnT[:, dc * L + slab * SLAB:
                            dc * L + (slab + 1) * SLAB],
                        start=(dc == 0), stop=(dc == NDC - 1)),
                      rd=[wb] + (list(b_xn) if (oc == 0 and slab == 0)
                                 else []),
                      wr=([b_pf[pf]] if dc == 0 else []),
                      wr2=([b_pf[pf]] if dc == NDC - 1 else []),
                      noinc=(dc != NDC - 1))
                E("sc", lambda g, pf=pf: g.copy(stg[pf][:], p_f[pf][:]),
                  rd=[b_pf[pf]], wr=[b_stg[pf]])
                E("ve", lambda g, pf=pf: g.tensor_reduce(
                    col(7), stg[pf][:], axis=AXX, op=ALU.max,
                    apply_absolute_value=True),
                  rd=[b_stg[pf]], wr=[b_stat])
                E("ve", lambda g, first=(oc == 0 and slab == 0), mxcol=mxcol:
                  g.tensor_tensor(
                      sc_(mxcol), col(7) if first else sc_(mxcol), col(7),
                      ALU.max), rd=[b_stat], wr2=[b_mx])
                E("gp", lambda g, oc=oc, slab=slab, pf=pf, osp=osp:
                  g.dma_start(
                      out=osp.ap()[oc, :, slab * SLAB:(slab + 1) * SLAB],
                      in_=stg[pf][:]),
                  rd=[b_stg[pf]], wr2=[b_osp_], dma=f"spl{pf}")
            if dqd_emitted[0] < NDC and oc % 3 == 2:
                emit_dqd(dqd_emitted[0])
                dqd_emitted[0] += 1
    while dqd_emitted[0] < NDC:
        emit_dqd(dqd_emitted[0])
        dqd_emitted[0] += 1

    if debug == "p2b":
        P.wait_all_dma("sy")
        return nc, P, locals()

    b_mx.note_w(P.ctr["ve"], P.ctr["ve"].c)
    allreduce_max([sc_(10), sc_(11)], [b_mx], p_f[0][:], b_pf[0])
    bcast_scale(0, 2, 3, p_f[0][:], b_pf[0])
    bcast_scale(1, 4, 5, p_f[0][:], b_pf[0])

    P.barrier()
    es_x.close()
    es_b.close()
    es_c = contextlib.ExitStack()
    es_cio = contextlib.ExitStack()
    SBc = lambda n, s, d: es_c.enter_context(nc.sbuf_tensor(n, s, d))
    SBio = lambda n, s, d: es_cio.enter_context(nc.sbuf_tensor(n, s, d))

    attn_sb = SBc("attn_sb", [128, NDC * L], BF16)
    b_attn = Buf()
    woT_sb = SBc("woT_sb", [128, NDC * D], BF16)
    b_wo = Buf()
    y_sl = [SBc(f"y_sl{i}", [128, 4096], BF16) for i in range(2)]
    b_ysl = [Buf(), Buf()]
    p_w = [es_c.enter_context(nc.psum_tensor(f"p_w{i}", [128, 512], F32))
           for i in range(2)]
    b_pw = [Buf(), Buf()]
    io_a = [SBio(f"io_a{i}", [128, 4096], BF16) for i in range(2)]
    b_ioa = [Buf(), Buf()]
    io_b = [SBio(f"io_b{i}", [128, 4096], BF16) for i in range(2)]
    b_iob = [Buf(), Buf()]
    io_c = [SBio(f"io_c{i}", [128, 4096], BF16) for i in range(2)]
    b_ioc = [Buf(), Buf()]
    fs1 = [SBio(f"fs1_{i}", [128, 2048], F32) for i in range(2)]
    b_fs1 = [Buf(), Buf()]
    fs2 = [SBio(f"fs2_{i}", [128, 2048], F32) for i in range(2)]
    b_fs2 = [Buf(), Buf()]
    fs3 = [SBio(f"fs3_{i}", [128, 2048], F32) for i in range(2)]
    b_fs3 = [Buf(), Buf()]

    for dc in range(NDC):
        E("gp", lambda g, dc=dc: g.dma_start(
            out=woT_sb[:, dc * D:(dc + 1) * D],
            in_=woT[dc * 128:(dc + 1) * 128, :]),
          wr=([b_wo] if dc == 0 else []), wr2=([b_wo] if dc > 0 else []),
          dma=f"wo{dc % 2}")

    wo_units = []
    for slab in range(NSLAB):
        wo_units.append(("load", slab))
        for oc in range(NDC):
            wo_units.append(("mm", slab, oc))
    wo_i = [0]

    def emit_wo_unit():
        if wo_i[0] >= len(wo_units):
            return
        u = wo_units[wo_i[0]]
        wo_i[0] += 1
        if u[0] == "load":
            slab = u[1]
            ypar = slab % 2
            for pr in range(8):
                E("sy", lambda g, slab=slab, ypar=ypar, pr=pr: g.dma_start(
                    out=y_sl[ypar][:, pr * 512:(pr + 1) * 512],
                    in_=y_sp.ap()[pr, :, slab * SLAB:(slab + 1) * SLAB]),
                  rd=[b_ysp], wr=([b_ysl[ypar]] if pr == 0 else []),
                  wr2=([b_ysl[ypar]] if pr > 0 else []),
                  dma=f"ysl{(ypar, pr % 2)}")
        else:
            slab, oc = u[1], u[2]
            ypar = slab % 2
            wpar = (slab * NDC + oc) % 2
            for dc in range(NDC):
                E("te", lambda g, oc=oc, dc=dc, ypar=ypar, wpar=wpar: g.matmul(
                    p_w[wpar][:],
                    woT_sb[:, dc * D + oc * 128: dc * D + (oc + 1) * 128],
                    y_sl[ypar][:, dc * 512:(dc + 1) * 512],
                    start=(dc == 0), stop=(dc == NDC - 1)),
                  rd=[b_wo, b_ysl[ypar]],
                  wr=([b_pw[wpar]] if dc == 0 else []),
                  wr2=([b_pw[wpar]] if dc == NDC - 1 else []),
                  noinc=(dc != NDC - 1))
            E("sc", lambda g, oc=oc, slab=slab, wpar=wpar: g.copy(
                attn_sb[:, oc * L + slab * SLAB: oc * L + (slab + 1) * SLAB],
                p_w[wpar][:]), rd=[b_pw[wpar]], wr2=[b_attn])

    for ic in range(NIC):
        par = ic % 2
        E("sy", lambda g, ic=ic, par=par: g.dma_start(
            out=io_a[par][:], in_=gate_sp.ap()[ic, :, :]),
          rd=[b_gatesp], wr=[b_ioa[par]], dma=f"ioa{par}")
        for hh in range(2):
            h0 = hh * 2048
            E("sc", lambda g, par=par, h0=h0: g.activation(
                fs1[par][:], io_a[par][:, h0:h0 + 2048], AF.Identity,
                bias=cmag[:], scale=sc_(5)),
              rd=[b_ioa[par], b_scl, b_cmag], wr=[b_fs1[par]])
            E("ve", lambda g, par=par: g.tensor_scalar(
                fs2[par][:], fs1[par][:], CMAGIC, sc_(4), ALU.subtract,
                ALU.mult),
              rd=[b_fs1[par], b_scl], wr=[b_fs2[par]])
            E("sc", lambda g, par=par: g.activation(fs1[par][:], fs2[par][:],
                                                    AF.Sigmoid),
              rd=[b_fs2[par]], wr=[b_fs1[par]])
            E("sc", lambda g, par=par: g.activation(
                fs3[par][:], fs1[par][:], AF.Identity, bias=cmag[:],
                scale=65536.0),
              rd=[b_fs1[par], b_cmag], wr=[b_fs3[par]])
            E("gp", lambda g, par=par: g.tensor_scalar(
                fs1[par][:], fs3[par][:], -CMAGIC, 65535.0, ALU.add, ALU.min),
              rd=[b_fs3[par]], wr=[b_fs1[par]])
            E("ve", lambda g, par=par: g.tensor_scalar(
                fs3[par][:], fs1[par][:], 0.0, 1.0 / 65536.0, ALU.max,
                ALU.mult),
              rd=[b_fs1[par]], wr=[b_fs3[par]])
            E("ve", lambda g, par=par, h0=h0: g.tensor_tensor(
                io_c[par][:, h0:h0 + 2048], fs2[par][:], fs3[par][:],
                ALU.mult),
              rd=[b_fs2[par], b_fs3[par]],
              wr=([b_ioc[par]] if hh == 0 else []),
              wr2=([b_ioc[par]] if hh else []))
        E("ve", lambda g, par=par: g.tensor_reduce(
            col(7), io_c[par][:], axis=AXX, op=ALU.max,
            apply_absolute_value=True), rd=[b_ioc[par]], wr=[b_stat])
        E("ve", lambda g, first=(ic == 0): g.tensor_tensor(
            sc_(12), col(7) if first else sc_(12), col(7), ALU.max),
          rd=[b_stat], wr2=[b_mx])
        E("gp", lambda g, ic=ic, par=par: g.dma_start(
            out=g2_sp.ap()[ic, :, :], in_=io_c[par][:]),
          rd=[b_ioc[par]], wr2=[b_g2sp], dma=f"ioc{par}")
        for _ in range(3):
            emit_wo_unit()
    while wo_i[0] < len(wo_units):
        emit_wo_unit()

    if debug == "p2d":
        P.wait_all_dma("sy")
        return nc, P, locals()

    b_mx.note_w(P.ctr["ve"], P.ctr["ve"].c)
    allreduce_max([sc_(12)], [b_mx], p_w[0][:], b_pw[0])
    bcast_scale(0, 6, 7, p_w[0][:], b_pw[0])

    for ic in range(NIC):
        par = ic % 2
        E("sy", lambda g, ic=ic, par=par: g.dma_start(
            out=io_a[par][:], in_=up_sp.ap()[ic, :, :]),
          rd=[b_upsp], wr=[b_ioa[par]], dma=f"ioa{par}")
        E("sy", lambda g, ic=ic, par=par: g.dma_start(
            out=io_b[par][:], in_=g2_sp.ap()[ic, :, :]),
          rd=[b_g2sp], wr=[b_iob[par]], dma=f"iob{par}")
        for hh in range(2):
            h0 = hh * 2048
            E("sc", lambda g, par=par, h0=h0: g.activation(
                fs1[par][:], io_a[par][:, h0:h0 + 2048], AF.Identity,
                bias=cmag[:], scale=sc_(3)),
              rd=[b_ioa[par], b_scl, b_cmag], wr=[b_fs1[par]])
            E("ve", lambda g, par=par: g.tensor_scalar(
                fs2[par][:], fs1[par][:], CMAGIC, sc_(2), ALU.subtract,
                ALU.mult),
              rd=[b_fs1[par], b_scl], wr=[b_fs2[par]])
            E("sc", lambda g, par=par, h0=h0: g.activation(
                fs3[par][:], io_b[par][:, h0:h0 + 2048], AF.Identity,
                bias=cmag[:], scale=sc_(7)),
              rd=[b_iob[par], b_scl, b_cmag], wr=[b_fs3[par]])
            E("gp", lambda g, par=par: g.tensor_scalar(
                fs1[par][:], fs3[par][:], CMAGIC, sc_(6), ALU.subtract,
                ALU.mult),
              rd=[b_fs3[par], b_scl], wr=[b_fs1[par]])
            E("ve", lambda g, par=par, h0=h0: g.tensor_tensor(
                io_c[par][:, h0:h0 + 2048], fs1[par][:], fs2[par][:],
                ALU.mult),
              rd=[b_fs1[par], b_fs2[par]],
              wr=([b_ioc[par]] if hh == 0 else []),
              wr2=([b_ioc[par]] if hh else []))
        E("ve", lambda g, par=par: g.tensor_reduce(
            col(7), io_c[par][:], axis=AXX, op=ALU.max,
            apply_absolute_value=True), rd=[b_ioc[par]], wr=[b_stat])
        E("ve", lambda g, first=(ic == 0): g.tensor_tensor(
            sc_(13), col(7) if first else sc_(13), col(7), ALU.max),
          rd=[b_stat], wr2=[b_mx])
        E("gp", lambda g, ic=ic, par=par: g.dma_start(
            out=o_sp.ap()[ic, :, :], in_=io_c[par][:]),
          rd=[b_ioc[par]], wr2=[b_osp], dma=f"ioc{par}")

    if debug == "p2e":
        P.wait_all_dma("sy")
        return nc, P, locals()

    b_mx.note_w(P.ctr["ve"], P.ctr["ve"].c)
    allreduce_max([sc_(13)], [b_mx], p_w[0][:], b_pw[0])
    bcast_scale(0, 8, 9, p_w[0][:], b_pw[0])

    P.barrier()
    es_cio.close()
    es_d = contextlib.ExitStack()
    SBd = lambda n, s, d: es_d.enter_context(nc.sbuf_tensor(n, s, d))
    oq_sl = [SBd(f"oq_sl{i}", [128, NIC * 512], BF16) for i in range(2)]
    b_oq = [Buf(), Buf()]
    wdn = [SBd(f"wdn{i}", [128, NIC * 128], BF16) for i in range(2)]
    b_wdn = [Buf(), Buf()]
    xTt = [SBd(f"xTt{i}", [128, 512], F32) for i in range(2)]
    b_xT = [Buf(), Buf()]
    outt = [SBd(f"outt{i}", [128, 512], F32) for i in range(2)]
    b_out = [Buf(), Buf()]
    ofs = [SBd(f"ofs{i}", [128, 2048], F32) for i in range(2)]
    b_ofs = [Buf(), Buf()]
    ofs2 = [SBd(f"ofs2{i}", [128, 2048], F32) for i in range(2)]
    b_ofs2 = [Buf(), Buf()]
    ep1 = SBd("ep1", [128, 512], F32)
    b_ep1 = Buf()
    p_d = [es_d.enter_context(nc.psum_tensor(f"p_d{i}", [128, 512], F32))
           for i in range(4)]
    b_pd = [Buf() for _ in range(4)]

    wdn_wait = (P.dma_ctr("dqdo"), max(t for (_c, t) in dqd_done.values()))
    last_out = 0
    for slab in range(NSLAB):
        spar = slab % 2
        for ic in range(NIC):
            E("sy", lambda g, ic=ic, slab=slab, spar=spar: g.dma_start(
                out=oq_sl[spar][:, ic * 512:(ic + 1) * 512],
                in_=o_sp.ap()[ic, :, slab * SLAB:(slab + 1) * SLAB]),
              rd=[b_osp], wr=([b_oq[spar]] if ic == 0 else []),
              wr2=([b_oq[spar]] if ic > 0 else []), dma=f"oq{ic % 4}")
        for third in range(6):
            a0, a1 = third * 2048, (third + 1) * 2048
            tp = third % 2
            E("sc", lambda g, spar=spar, a0=a0, a1=a1, tp=tp: g.activation(
                ofs[tp][:], oq_sl[spar][:, a0:a1], AF.Identity, bias=cmag[:],
                scale=sc_(9)), rd=[b_oq[spar], b_scl, b_cmag], wr=[b_ofs[tp]])
            E("ve", lambda g, tp=tp: g.tensor_scalar(
                ofs2[tp][:], ofs[tp][:], -CMAGIC, QMAX, ALU.add, ALU.min),
              rd=[b_ofs[tp]], wr=[b_ofs2[tp]])
            E("ve", lambda g, spar=spar, a0=a0, a1=a1, tp=tp: g.tensor_scalar(
                oq_sl[spar][:, a0:a1], ofs2[tp][:], -QMAX - 1.0, sc_(8),
                ALU.max, ALU.mult),
              rd=[b_ofs2[tp], b_scl], wr2=[b_oq[spar]])
        if slab == 0:
            E("sy", lambda g: g.dma_start(
                out=wdn[0][:], in_=wdndq_sp.ap()[0, :, :]),
              wr=[b_wdn[0]], waits=[wdn_wait], dma="wdn0")
        for oc in range(NDC):
            pd = oc % 4
            wpar2 = (slab * NDC + oc) % 2
            nslab, noc = (slab, oc + 1) if oc + 1 < NDC else (slab + 1, 0)
            if nslab < NSLAB:
                E("sy", lambda g, noc=noc, w2=1 - wpar2: g.dma_start(
                    out=wdn[w2][:], in_=wdndq_sp.ap()[noc, :, :]),
                  wr=[b_wdn[1 - wpar2]], waits=[wdn_wait],
                  dma=f"wdn{1 - wpar2}")
            for ic in range(NIC):
                E("te", lambda g, ic=ic, spar=spar, pd=pd, wpar2=wpar2:
                  g.matmul(
                      p_d[pd][:], wdn[wpar2][:, ic * 128:(ic + 1) * 128],
                      oq_sl[spar][:, ic * 512:(ic + 1) * 512],
                      start=(ic == 0), stop=(ic == NIC - 1)),
                  rd=[b_wdn[wpar2], b_oq[spar]],
                  wr=([b_pd[pd]] if ic == 0 else []),
                  wr2=([b_pd[pd]] if ic == NIC - 1 else []),
                  noinc=(ic != NIC - 1))
            opar = oc % 2
            E("sy", lambda g, oc=oc, slab=slab, opar=opar: g.dma_start(
                out=xTt[opar][:],
                in_=xT[oc * 128:(oc + 1) * 128,
                       slab * SLAB:(slab + 1) * SLAB]),
              wr=[b_xT[opar]], dma=f"xt{opar}")
            E("ve", lambda g, oc=oc, slab=slab, pd=pd: g.tensor_tensor(
                ep1[:], p_d[pd][:],
                attn_sb[:, oc * L + slab * SLAB: oc * L + (slab + 1) * SLAB],
                ALU.add), rd=[b_pd[pd], b_attn], wr=[b_ep1])
            E("ve", lambda g, opar=opar: g.tensor_tensor(
                outt[opar][:], ep1[:], xTt[opar][:], ALU.add),
              rd=[b_ep1, b_xT[opar]], wr=[b_out[opar]])
            last_out = E("gp", lambda g, oc=oc, slab=slab, opar=opar:
                         g.dma_start(
                             out=outT[oc * 128:(oc + 1) * 128,
                                      slab * SLAB:(slab + 1) * SLAB],
                             in_=outt[opar][:]),
                         rd=[b_out[opar]], dma=f"out{opar}")

    P.wait_all_dma("sy")
    return nc, P, locals()


def finalize(nc, P):
    with nc.Block() as blk:
        @blk.sync
        def _(g):
            for f in P.ops["sy"]:
                f(g)

        @blk.scalar
        def _(g):
            for f in P.ops["sc"]:
                f(g)

        @blk.vector
        def _(g):
            for f in P.ops["ve"]:
                f(g)

        @blk.tensor
        def _(g):
            for f in P.ops["te"]:
                f(g)

        @blk.gpsimd
        def _(g):
            for f in P.ops["gp"]:
                f(g)
    return nc


def host_consts():
    idx = np.arange(128, dtype=np.float32)
    emat = idx[None, :] - idx[:, None]
    emat = np.where(emat >= 0, emat, 1e9).astype(np.float32)
    utri = (idx[None, :] >= idx[:, None]).astype(np.float32)
    rampq = np.ascontiguousarray(
        np.broadcast_to(idx[None, :] + 1.0, (128, 128))).astype(np.float32)
    rampk = np.ascontiguousarray(
        np.broadcast_to((127.0 - idx)[:, None], (128, 64))).astype(np.float32)
    return dict(emat=np.ascontiguousarray(emat),
                utri=np.ascontiguousarray(utri), rampq=rampq, rampk=rampk,
                ident=np.eye(128, dtype=np.float32))


def host_inputs(L0_attn_x, L0_attn_W_qkv, L0_attn_W_o, L0_attn_log_gamma,
                L0_ffn_w_gate, L0_ffn_w_up, L0_ffn_w_down):
    x = np.asarray(L0_attn_x, dtype=np.float32)
    wqkvT = np.ascontiguousarray(np.asarray(L0_attn_W_qkv, np.float32).T)
    woT = np.ascontiguousarray(np.asarray(L0_attn_W_o, np.float32).T)
    lg = np.asarray(L0_attn_log_gamma, np.float32)
    lgp8 = np.empty((128, 8), np.float32)
    for j in range(8):
        lgp8[:64, j] = lg[2 * j]
        lgp8[64:, j] = lg[2 * j + 1]
    lg16 = np.ascontiguousarray(
        np.broadcast_to(lg[None, :], (128, 16))).astype(np.float32)
    shared = dict(wqkvT=wqkvT, woT=woT,
                  w_up=np.ascontiguousarray(np.asarray(L0_ffn_w_up,
                                                       np.float32)),
                  w_gate=np.ascontiguousarray(np.asarray(L0_ffn_w_gate,
                                                         np.float32)),
                  w_down=np.ascontiguousarray(np.asarray(L0_ffn_w_down,
                                                         np.float32)),
                  lgp8=lgp8, lg16=lg16, **host_consts())
    in_maps = []
    for c in range(NCORES):
        xc = np.ascontiguousarray(x[c])
        in_maps.append(dict(x_nat=xc, xT=np.ascontiguousarray(xc.T), **shared))
    return in_maps


def kernel(**inputs):
    nc, P, _ = build_program()
    finalize(nc, P)
    in_maps = host_inputs(**inputs)
    res = run_bass_kernel_spmd(nc, in_maps, core_ids=list(range(NCORES)))
    out = np.empty((NCORES, L, D), dtype=np.float32)
    for c in range(NCORES):
        out[c] = np.asarray(res.results[c]["outT"]).T
    return out


# version 11
# speedup vs baseline: 1.4688x; 1.4688x over previous
import contextlib
import numpy as np
import concourse.bass as bass
import concourse.mybir as mybir
import concourse.bass_isa as bass_isa
from concourse.bass_utils import run_bass_kernel_spmd

F32 = mybir.dt.float32
BF16 = mybir.dt.bfloat16
ALU = mybir.AluOpType
AF = mybir.ActivationFunctionType
AXX = mybir.AxisListType.X
RED = bass_isa.ReduceOp

NCORES = 8
L, D, H, DK, I3 = 4096, 1024, 16, 64, 3072
CH, SLAB = 128, 512
NCH, NSLAB, NDC, NIC = L // CH, L // SLAB, D // 128, I3 // 128
CMAGIC = 12582912.0
P0_TILES = NCH
SKIP_TABLES = False
SKIP_WQKV = False
SKIP_DQ = False
P1_CHUNKS = NCH
P1_CUT = 99
XQ_MODE = 0
SKIP_XQ = False
XQ_BARRIER = False
XQ_OPS = 3
SKIP_RET = False
SKIP_KLN = False
SKIP_TR = False
LN_EPS = 1e-05
QMAX = 32767.0


class Ctr:
    __slots__ = ("h", "c", "name")

    def __init__(self, h, name):
        self.h, self.c, self.name = h, 0, name

    def bump(self, n=1):
        self.c += n
        return self.c


class Buf:
    __slots__ = ("w", "r")

    def __init__(self):
        self.w, self.r = {}, {}

    def note_w(self, ctr, tok):
        self.w = {ctr: tok}
        self.r = {}

    def note_w2(self, ctr, tok):
        if self.w.get(ctr, 0) < tok:
            self.w[ctr] = tok

    def note_r(self, ctr, tok):
        if self.r.get(ctr, 0) < tok:
            self.r[ctr] = tok


class Prog:
    def __init__(self, nc):
        self.nc = nc
        self.ops = {e: [] for e in ("sy", "sc", "ve", "te", "gp")}
        self.es = contextlib.ExitStack()
        self._mk = lambda n: Ctr(self.es.enter_context(nc.semaphore(n)), n)
        self.ctr = {"sc": self._mk("c_sc"), "ve": self._mk("c_ve"),
                    "te": self._mk("c_te"), "gp": self._mk("c_gp"),
                    "cc": self._mk("c_cc")}
        self.dstr = {}
        self.plan = []
        self.waited = {e: {} for e in self.ops}

    def dma_ctr(self, name):
        if name not in self.dstr:
            self.dstr[name] = self._mk("dm_" + name)
        return self.dstr[name]

    def E(self, eng, build, rd=(), wr=(), wr2=(), waits=(), dma=None,
          cc=False, noinc=False):
        if cc:
            ctr, n = self.ctr["cc"], 1
        elif dma is not None:
            ctr, n = self.dma_ctr(dma), 16
        else:
            ctr, n = self.ctr[eng], 1
        need = {}
        if dma is not None and ctr.c:
            need[ctr] = ctr.c

        def add(c, v):
            if v and need.get(c, 0) < v:
                need[c] = v
        for (c, v) in waits:
            add(c, v)
        for b in rd:
            for c, v in b.w.items():
                add(c, v)
        for b in wr:
            for src in (b.w, b.r):
                for c, v in src.items():
                    add(c, v)
        wm = self.waited[eng]
        final = []
        for c, v in need.items():
            if wm.get(c, 0) >= v:
                continue
            final.append((c.h, v))
            wm[c] = v
        tok = ctr.c if noinc else ctr.bump(n)

        pad = (eng == "te" and not noinc and not cc)

        def f(g, _w=tuple(final), _b=build, _h=ctr.h, _n=n, _ni=noinc,
              _pad=pad):
            for (h, v) in _w:
                g.wait_ge(h, v)
            ins = _b(g)
            if not _ni:
                ins.then_inc(_h, _n)
            if _pad:
                g.nop()
                g.nop()
                g.nop()
                g.nop()
        self.ops[eng].append(f)
        self.plan.append(
            (eng, ctr, (None if noinc else tok), tuple(need.items()),
             tuple(id(b) for b in rd), tuple(id(b) for b in wr)
             + tuple(id(b) for b in wr2)))
        for b in rd:
            b.note_r(ctr, tok)
        for b in wr:
            b.note_w(ctr, tok)
        for b in wr2:
            b.note_w2(ctr, tok)
        return tok

    def W(self, eng, waits):
        wm = self.waited[eng]
        final = []
        final2 = []
        for c, v in waits:
            if v and wm.get(c, 0) < v:
                final.append((c.h, v))
                final2.append((c, v))
                wm[c] = v
        if final:
            def f(g, _w=tuple(final)):
                for (h, v) in _w:
                    g.wait_ge(h, v)
            self.ops[eng].append(f)
            self.plan.append((eng, None, None, tuple((c, v) for c, v in final2),
                              (), ()))

    def barrier(self):
        snap = [(c, c.c) for c in self.ctr.values()]
        snap += [(c, c.c) for c in self.dstr.values()]
        for e in self.ops:
            self.W(e, snap)

    def wait_all_dma(self, eng):
        self.W(eng, [(c, c.c) for c in self.dstr.values()])


def build_program(debug=None):
    nc = bass.Bass("TRN2", target_bir_lowering=False)
    P = Prog(nc)
    E, W = P.E, P.W

    dT = lambda n, s, k: nc.dram_tensor(n, s, F32, kind=k)
    x_nat = dT("x_nat", [L, D], "ExternalInput").ap()
    xT = dT("xT", [D, L], "ExternalInput").ap()
    wqkvT = dT("wqkvT", [D, 3 * D], "ExternalInput").ap()
    woT = dT("woT", [D, D], "ExternalInput").ap()
    w_up = dT("w_up", [I3, D], "ExternalInput").ap()
    w_gate = dT("w_gate", [I3, D], "ExternalInput").ap()
    w_down = dT("w_down", [D, I3], "ExternalInput").ap()
    lgp8 = dT("lgp8", [128, 8], "ExternalInput").ap()
    lg16 = dT("lg16", [128, 16], "ExternalInput").ap()
    emat = dT("emat", [128, 128], "ExternalInput").ap()
    utri = dT("utri", [128, 128], "ExternalInput").ap()
    rampq = dT("rampq", [128, 128], "ExternalInput").ap()
    rampk = dT("rampk", [128, 64], "ExternalInput").ap()
    ident = dT("ident", [128, 128], "ExternalInput").ap()
    outT = dT("outT", [D, L], "ExternalOutput").ap()

    def scratch(name, shape, dbgkey):
        kind = "ExternalOutput" if debug == dbgkey else "Internal"
        return nc.dram_tensor(name, shape, BF16, kind=kind)

    y_sp = scratch("y_sp", [8, 128, L], "p1")
    up_sp = scratch("up_sp", [NIC, 128, L], "p2b")
    gate_sp = scratch("gate_sp", [NIC, 128, L], "none")
    g2_sp = scratch("g2_sp", [NIC, 128, L], "p2d")
    o_sp = scratch("o_sp", [NIC, 128, L], "p2e")
    wupdq_sp = nc.dram_tensor(
        "wupdq_sp", [NIC, 128, NDC * 128], BF16,
        kind="ExternalOutput" if debug == "dq" else "Internal")
    wgatedq_sp = nc.dram_tensor("wgatedq_sp", [NIC, 128, NDC * 128], BF16)
    wdndq_sp = nc.dram_tensor("wdndq_sp", [NDC, 128, NIC * 128], BF16)
    pmax_sp = nc.dram_tensor("pmax_sp", [4, 128], F32)
    cc_in = nc.dram_tensor("cc_in", [1, 16], F32)
    cc_out = nc.dram_tensor("cc_out", [1, 16], F32)
    b_ccin, b_ccout = Buf(), Buf()
    b_ysp, b_upsp, b_gatesp, b_g2sp, b_osp = (Buf() for _ in range(5))
    dbg_xn = None
    if debug in ("p0", "p0nocc", "ar1"):
        dbg_xn = nc.dram_tensor("dbg_xn", [128, NDC * L], BF16,
                                kind="ExternalOutput")

    es_g, es_x, es_a = (contextlib.ExitStack() for _ in range(3))
    SBg = lambda n, s, d: es_g.enter_context(nc.sbuf_tensor(n, s, d))
    SBa = lambda n, s, d: es_a.enter_context(nc.sbuf_tensor(n, s, d))
    PSa = lambda n, s, d: es_a.enter_context(nc.psum_tensor(n, s, d))

    stat = SBg("stat", [128, 16], F32)
    col = lambda i: stat[:, i:i + 1]
    b_stat = Buf()
    kst = SBg("kst", [128, 80], F32)
    b_kst = Buf()
    ccsb = SBg("ccsb", [1, 16], F32)
    b_ccsb = Buf()
    pmrow = SBg("pmrow", [1, 128], F32)
    b_pmrow = Buf()
    b_pmax = Buf()
    scl = SBg("scl", [128, 16], F32)
    cmag = SBg("cmag", [128, 1], F32)
    b_cmag = Buf()
    utri_sb = SBg("utri_sb", [128, 128], F32)
    ident_sb = SBg("ident_sb", [128, 128], F32)
    sc_ = lambda i: scl[:, i:i + 1]
    b_scl = Buf()
    b_mx = Buf()

    xnT = es_x.enter_context(nc.sbuf_tensor("xnT", [128, NDC * L], BF16, side="right"))
    b_xn = [Buf() for _ in range(NCH)]

    wqkv_sb = SBa("wqkv_sb", [128, NDC * 3 * D], BF16)
    b_wqkv = Buf()
    gam = SBa("gam", [128, H * 128], BF16)
    cq = SBa("cq", [128, 1024], BF16)
    ck = SBa("ck", [128, 1024], BF16)
    gcp = SBa("gcp", [128, 8], F32)
    b_tab = Buf()
    lg16_sb = SBa("lg16_sb", [128, 16], F32)
    lgp8_sb = SBa("lgp8_sb", [128, 8], F32)
    emat_sb = SBa("emat_sb", [128, 128], F32)

    rampq_sb = SBa("rampq_sb", [128, 128], F32)
    rampk_sb = SBa("rampk_sb", [128, 64], F32)

    b_const = Buf()
    xin = [SBa(f"xin{i}", [128, 1024], F32) for i in range(2)]
    b_xin = [Buf(), Buf()]
    scr1 = SBa("scr1", [128, 1024], F32)
    b_scr1 = Buf()
    q_tok = SBa("q_tok", [128, 1024], F32)
    b_qtok = Buf()
    kn_f32 = SBa("kn_f32", [128, 1024], F32)
    b_kn = Buf()
    ksq_f32 = SBa("ksq_f32", [128, 1024], F32)
    b_ksq = Buf()
    k_raw = SBa("k_raw", [128, 1024], F32)
    b_kraw = Buf()
    q_ch = [SBa(f"q_ch{i}", [128, 1024], BF16) for i in range(2)]
    b_q = [Buf(), Buf()]
    qt_ch = [SBa(f"qt_ch{i}", [128, 1024], BF16) for i in range(2)]
    b_qt = [Buf(), Buf()]
    knT_ch = [SBa(f"knT_ch{i}", [128, 1024], BF16) for i in range(2)]
    b_knT = [Buf(), Buf()]
    kh_ch = [SBa(f"kh_ch{i}", [128, 1024], BF16) for i in range(2)]
    b_kh = [Buf(), Buf()]
    v_ch = [SBa(f"v_ch{i}", [128, 1024], BF16) for i in range(2)]
    b_v = [Buf(), Buf()]
    a_scr = [SBa(f"a_scr{i}", [128, 256], BF16) for i in range(2)]
    b_a = [Buf(), Buf()]
    st_bf = SBa("st_bf", [128, 512], BF16)
    b_stb = Buf()
    y_stage = [SBa(f"y_stage{i}", [128, 4096], BF16) for i in range(2)]
    b_yst = [Buf(), Buf()]

    dq_in = [xin[0], xin[1], SBa("dq_in2", [128, 1024], F32)]
    b_dqin = [b_xin[0], b_xin[1], Buf()]
    dq_s = [SBa(f"dq_s{i}", [128, 96], F32) for i in range(2)]
    b_dqs = [Buf(), Buf()]
    dq_u = [SBa("dq_u0", [128, 1024], F32), scr1]
    b_dqu = [Buf(), b_scr1]
    dq_nat = [SBa(f"dq_nat{i}", [128, 1024], BF16) for i in range(2)]
    b_dqn = [Buf(), Buf()]
    dq_tr = [SBa(f"dq_tr{i}", [128, 1024], BF16) for i in range(2)]
    b_dqt = [Buf(), Buf()]

    p_q = PSa("p_q", [128, 1024], F32)
    b_pq = Buf()
    p_k = PSa("p_k", [128, 1024], F32)
    b_pk = Buf()
    p_v = PSa("p_v", [128, 1024], F32)
    b_pv = Buf()
    p_r = [PSa(f"p_r{i}", [128, 512], F32) for i in range(2)]
    b_pr = [Buf(), Buf()]

    for (dst, src) in [(lgp8_sb, lgp8), (lg16_sb, lg16), (emat_sb, emat),
                       (utri_sb, utri), (rampq_sb, rampq), (rampk_sb, rampk),
                       (ident_sb, ident)]:
        E("sy", lambda g, d=dst, s=src: g.dma_start(out=d[:], in_=s[:]),
          wr2=[b_const], dma="cst")
    if not SKIP_WQKV:
        for dc in range(NDC):
            E("gp", lambda g, dc=dc: g.dma_start(
                out=wqkv_sb[:, dc * 3072:(dc + 1) * 3072],
                in_=wqkvT[dc * 128:(dc + 1) * 128, :]),
              wr=([b_wqkv] if dc == 0 else []),
              wr2=([b_wqkv] if dc > 0 else []), dma=f"wq{dc % 2}")
    E("gp", lambda g: g.memset(ccsb[:], 0.0), wr=[b_ccsb])
    E("gp", lambda g: g.memset(cmag[:], CMAGIC), wr=[b_cmag])
    E("gp", lambda g: g.dma_start(out=cc_in.ap(), in_=ccsb[:]),
      rd=[b_ccsb], wr=[b_ccin], dma="ccd")
    for fn in (AF.Sigmoid, AF.Ln):
        E("sc", lambda g, fn=fn: g.activation(lg16_sb[:], lg16_sb[:], fn),
          rd=[b_const], wr=[b_const])
        E("sc", lambda g, fn=fn: g.activation(lgp8_sb[:], lgp8_sb[:], fn),
          rd=[b_const], wr=[b_const])
    E("sc", lambda g: g.activation(gcp[:], lgp8_sb[:], AF.Exp, scale=128.0),
      rd=[b_const], wr2=[b_tab])
    for h in range(H if not SKIP_TABLES else 0):
        E("ve", lambda g, h=h: g.tensor_scalar(
            scr1[:, 0:128], emat_sb[:], lg16_sb[:, h:h + 1],
            None, ALU.mult), rd=[b_const], wr=[b_scr1])
        E("sc", lambda g, h=h: g.activation(
            gam[:, h * 128:(h + 1) * 128], scr1[:, 0:128],
            AF.Exp), rd=[b_scr1], wr2=[b_tab])
        E("ve", lambda g, h=h: g.tensor_tensor(
            gam[:, h * 128:(h + 1) * 128], gam[:, h * 128:(h + 1) * 128],
            utri_sb[:], ALU.mult), rd=[b_tab, b_const], wr2=[b_tab])
        E("ve", lambda g, h=h: g.tensor_scalar(
            scr1[:, 128:192], rampk_sb[:], lg16_sb[:, h:h + 1],
            None, ALU.mult), rd=[b_const], wr=[b_scr1])
        E("sc", lambda g, h=h: g.activation(
            ck[:, h * 64:(h + 1) * 64], scr1[:, 128:192], AF.Exp),
          rd=[b_scr1], wr2=[b_tab])
    for oc in range(8 if not SKIP_TABLES else 0):
        E("ve", lambda g, oc=oc: g.tensor_scalar(
            scr1[:, 0:128], rampq_sb[:], lgp8_sb[:, oc:oc + 1],
            None, ALU.mult), rd=[b_const], wr=[b_scr1])
        E("sc", lambda g, oc=oc: g.activation(
            cq[:, oc * 128:(oc + 1) * 128], scr1[:, 0:128],
            AF.Exp), rd=[b_scr1], wr2=[b_tab])
    E("gp", lambda g: g.memset(st_bf[:], 0.0), wr=[b_stb])

    tab_waits = [(P.ctr["sc"], P.ctr["sc"].c), (P.ctr["ve"], P.ctr["ve"].c)]

    for i in range(P0_TILES):
        par = i % 2
        t0 = i * CH
        E("sy", lambda g, i=i, par=par: g.dma_start(
            out=xin[par][:], in_=x_nat[i * CH:(i + 1) * CH, :]),
          wr=[b_xin[par]], dma=f"xin{par}")
        E("sc", lambda g, par=par: g.activation(scr1[:], xin[par][:],
                                                AF.Square),
          rd=[b_xin[par]], wr=[b_scr1])
        E("ve", lambda g, par=par: g.tensor_reduce(
            col(0), xin[par][:], axis=AXX, op=ALU.add),
          rd=[b_xin[par]], wr=[b_stat])
        E("ve", lambda g: g.tensor_reduce(col(1), scr1[:], axis=AXX,
                                          op=ALU.add),
          rd=[b_scr1], wr=[b_stat])
        E("ve", lambda g: g.tensor_scalar(col(2), col(0), 1.0 / 1024.0, None,
                                          ALU.mult), rd=[b_stat], wr=[b_stat])
        E("ve", lambda g: g.tensor_tensor(col(3), col(2), col(2), ALU.mult),
          rd=[b_stat], wr=[b_stat])
        E("ve", lambda g: g.scalar_tensor_tensor(
            col(4), col(1), 1.0 / 1024.0, col(3), ALU.mult, ALU.subtract),
          rd=[b_stat], wr=[b_stat])
        E("ve", lambda g: g.tensor_scalar(col(4), col(4), LN_EPS, None,
                                          ALU.add), rd=[b_stat], wr=[b_stat])
        E("sc", lambda g: g.activation(col(5), col(4), AF.Ln),
          rd=[b_stat], wr=[b_stat])
        E("sc", lambda g: g.activation(col(6), col(5), AF.Exp, scale=-0.5),
          rd=[b_stat], wr=[b_stat])
        E("ve", lambda g, par=par: g.tensor_scalar(
            scr1[:], xin[par][:], col(2), col(6), ALU.subtract, ALU.mult),
          rd=[b_xin[par], b_stat], wr=[b_scr1])
        E("ve", lambda g: g.tensor_reduce(
            col(7), scr1[:], axis=AXX, op=ALU.max, apply_absolute_value=True),
          rd=[b_scr1], wr=[b_stat])
        E("ve", lambda g, first=(i == 0): g.tensor_tensor(
            sc_(14), col(7) if first else sc_(14), col(7), ALU.max),
          rd=[b_stat], wr=[b_mx])
        for dc in range(NDC):
            E("te", lambda g, dc=dc: g.transpose(
                p_q[:, dc * 128:(dc + 1) * 128],
                scr1[:, dc * 128:(dc + 1) * 128], ident_sb[:]),
              rd=[b_scr1, b_const], wr=([b_pq] if dc == 0 else []),
              wr2=([b_pq] if dc > 0 else []))
        E("sc", lambda g, t0=t0: g.copy(
            bass.AP(xnT, t0, [[NDC * L, 128], [L, NDC], [1, CH]]),
            p_q[:].rearrange("p (dc t) -> p dc t", t=CH)),
          rd=[b_pq], wr2=[b_xn[i]])

    def allreduce_max(cols_in, in_bufs, psum, b_psum):
        for j, cin in enumerate(cols_in):
            E("gp", lambda g, j=j, cin=cin: g.dma_start(
                out=pmax_sp.ap()[j, :], in_=cin),
              rd=in_bufs, wr=[b_pmax], dma="ccd")
            E("gp", lambda g, j=j: g.dma_start(
                out=pmrow[:], in_=pmax_sp.ap()[j:j + 1, :]),
              rd=[b_pmax], wr=[b_pmrow], dma="ccd")
            E("ve", lambda g, j=j: g.tensor_reduce(
                ccsb[0:1, j:j + 1], pmrow[:], axis=AXX, op=ALU.max),
              rd=[b_pmrow], wr=([b_ccsb] if j == 0 else []),
              wr2=([b_ccsb] if j > 0 else []))
            E("gp", lambda g, j=j: g.dma_start(
                out=cc_in.ap()[0:1, j:j + 1], in_=ccsb[0:1, j:j + 1]),
              rd=[b_ccsb], wr=([b_ccin] if j == 0 else []),
              wr2=([b_ccin] if j > 0 else []), dma="ccd")
        E("gp", lambda g: g.collective_compute(
            "AllReduce", ALU.max, replica_groups=[list(range(NCORES))],
            ins=[cc_in.ap().opt()], outs=[cc_out.ap().opt()]),
          rd=[b_ccin], wr=[b_ccout], cc=True)
        E("gp", lambda g: g.dma_start(out=ccsb[:], in_=cc_out.ap()),
          rd=[b_ccout], wr=[b_ccsb], dma="ccd")

    def bcast_scale(j_src, c_s, c_sinv, psum, b_psum):
        E("te", lambda g, j=j_src, psum=psum: g.matmul(
            psum[:, 0:1], utri_sb[0:1, :], ccsb[0:1, j:j + 1],
            start=True, stop=True),
          rd=[b_ccsb, b_const], wr=[b_psum])
        E("ve", lambda g, c_s=c_s, psum=psum: g.tensor_scalar(
            sc_(c_s), psum[:, 0:1], 1.0 / QMAX, 1e-12, ALU.mult, ALU.max),
          rd=[b_psum], wr2=[b_scl])
        E("ve", lambda g, c_s=c_s, c_i=c_sinv: g.reciprocal(sc_(c_i),
                                                            sc_(c_s)),
          rd=[b_scl], wr2=[b_scl])

    def ar1_issue():
        j = 0
        E("gp", lambda g: g.dma_start(out=pmax_sp.ap()[j, :], in_=sc_(14)),
          rd=[b_mx], wr=[b_pmax], dma="ccd")
        E("gp", lambda g: g.dma_start(out=pmrow[:], in_=pmax_sp.ap()[j:j + 1, :]),
          rd=[b_pmax], wr=[b_pmrow], dma="ccd")
        E("ve", lambda g: g.tensor_reduce(
            ccsb[0:1, j:j + 1], pmrow[:], axis=AXX, op=ALU.max),
          rd=[b_pmrow], wr=[b_ccsb])
        E("gp", lambda g: g.dma_start(
            out=cc_in.ap()[0:1, j:j + 1], in_=ccsb[0:1, j:j + 1]),
          rd=[b_ccsb], wr=[b_ccin], dma="ccd")
        E("gp", lambda g: g.collective_compute(
            "AllReduce", ALU.max, replica_groups=[list(range(NCORES))],
            ins=[cc_in.ap().opt()], outs=[cc_out.ap().opt()]),
          rd=[b_ccin], wr=[b_ccout], cc=True)

    def ar1_consume():
        E("gp", lambda g: g.dma_start(
            out=sc_(0), in_=bass.AP(cc_out, 0, [[0, 128], [1, 1]])),
          rd=[b_ccout], wr2=[b_scl], dma="ccd")
        E("ve", lambda g: g.tensor_scalar(
            sc_(0), sc_(0), 1.0 / QMAX, 1e-12, ALU.mult, ALU.max),
          rd=[b_scl], wr2=[b_scl])
        E("ve", lambda g: g.reciprocal(sc_(1), sc_(0)),
          rd=[b_scl], wr2=[b_scl])

    if debug in ("p0", "p0nocc", "ar1"):
        if debug != "p0nocc":
            allreduce_max([sc_(14)], [b_mx], p_r[0][:, 256:384], b_pr[0])
            bcast_scale(0, 0, 1, p_r[0][:, 256:384], b_pr[0])

    if debug == "ar1":
        E("gp", lambda g: g.dma_start(out=dbg_xn.ap()[:, 0:16], in_=scl[:]),
          rd=[b_scl], dma="dbg")
        P.wait_all_dma("gp")
        P.wait_all_dma("sy")
        return nc, P, locals()

    if debug in ("p0", "p0nocc"):
        E("sy", lambda g: g.dma_start(out=dbg_xn.ap(), in_=xnT[:]),
          rd=list(b_xn), dma="dbg")
        P.wait_all_dma("sy")
        return nc, P, locals()

    dq_done = {}
    dq_cnt = [0]

    def dq_load(t):
        mat, src_ap, dd, oc = dq_plan[t]
        E("sy", lambda g, t=t, src_ap=src_ap: g.dma_start(
            out=dq_in[t % 3][:], in_=src_ap), wr=[b_dqin[t % 3]], dma="dqi")

    def dq_ab(t):
        i3, i2 = t % 3, t % 2
        E("ve", lambda g, i3=i3, i2=i2: g.tensor_reduce(
            dq_s[i2][:, 0:32],
            dq_in[i3][:].rearrange("p (b k) -> p b k", k=32),
            axis=AXX, op=ALU.max, apply_absolute_value=True),
          rd=[b_dqin[i3]], wr=[b_dqs[i2]])
        E("ve", lambda g, i2=i2: g.tensor_scalar(
            dq_s[i2][:, 32:64], dq_s[i2][:, 0:32], 1.0 / 7.0, 1e-12,
            ALU.mult, ALU.max), rd=[b_dqs[i2]], wr2=[b_dqs[i2]])
        E("ve", lambda g, i2=i2: g.reciprocal(
            dq_s[i2][:, 64:96], dq_s[i2][:, 32:64]),
          rd=[b_dqs[i2]], wr2=[b_dqs[i2]])
        E("gp", lambda g, i3=i3, i2=i2: g.tensor_tensor(
            dq_u[i2][:], dq_in[i3][:],
            bass.AP(dq_s[i2], 64, [[96, 128], [1, 32], [0, 32]]), ALU.mult),
          rd=[b_dqin[i3], b_dqs[i2]], wr=[b_dqu[i2]])

    def dq_cd(t):
        mat, src_ap, dd, oc = dq_plan[t]
        i2 = t % 2
        E("ve", lambda g, i2=i2: g.tensor_scalar(
            dq_u[i2][:], dq_u[i2][:], CMAGIC, -CMAGIC, ALU.add, ALU.add),
          rd=[b_dqu[i2]], wr=[b_dqu[i2]])
        E("gp", lambda g, i2=i2: g.tensor_tensor(
            dq_nat[i2][:], dq_u[i2][:],
            bass.AP(dq_s[i2], 32, [[96, 128], [1, 32], [0, 32]]), ALU.mult),
          rd=[b_dqu[i2], b_dqs[i2]], wr=[b_dqn[i2]])
        for dc in range(NDC):
            E("sy", lambda g, dc=dc, i2=i2: g.dma_start(
                out=dq_tr[i2][:, dc * 128:(dc + 1) * 128],
                in_=dq_nat[i2][:, dc * 128:(dc + 1) * 128], transpose=True),
              rd=[b_dqn[i2]], wr=([b_dqt[i2]] if dc == 0 else []),
              wr2=([b_dqt[i2]] if dc > 0 else []), dma=f"dqt{i2}_{dc % 4}")
        tout = E("gp", lambda g, i2=i2, oc=oc, dst=dd: g.dma_start(
            out=dst.ap()[oc, :, :], in_=dq_tr[i2][:]),
          rd=[b_dqt[i2]], dma="dqo")
        dq_done[(mat, oc)] = (P.dma_ctr("dqo"), tout)

    dq_plan = []
    for oc in range(NIC):
        dq_plan.append(("up", w_up[oc * 128:(oc + 1) * 128, :], wupdq_sp, oc))
        dq_plan.append(("gate", w_gate[oc * 128:(oc + 1) * 128, :],
                        wgatedq_sp, oc))
    NDQ = len(dq_plan)
    dq_ld = [0]
    dq_abi = [0]
    dq_cdi = [0]

    def dq_target(c):
        return max(0, min(NDQ, (c + 1) * NDQ // NCH))

    def emit_dq_stages(c):
        if SKIP_DQ:
            return
        while dq_cdi[0] < dq_target(c - 2):
            dq_cd(dq_cdi[0]); dq_cdi[0] += 1
        while dq_abi[0] < dq_target(c - 1):
            dq_ab(dq_abi[0]); dq_abi[0] += 1
        while dq_ld[0] < dq_target(c):
            dq_load(dq_ld[0]); dq_ld[0] += 1

    def emit_proj(c, which):
        t0 = c * CH
        ps, b_ps, off = which
        for half in range(2):
            for dc in range(NDC):
                E("te", lambda g, ps=ps, half=half, dc=dc, t0=t0, off=off:
                  g.matmul(
                      ps[:, half * 512:(half + 1) * 512],
                      xnT[:, dc * L + t0: dc * L + t0 + CH],
                      wqkv_sb[:, dc * 3072 + off + half * 512:
                              dc * 3072 + off + (half + 1) * 512],
                      start=(dc == 0), stop=(dc == NDC - 1)),
                  rd=[b_xn[c], b_wqkv],
                  wr=([b_ps] if (half == 0 and dc == 0) else []),
                  wr2=([b_ps] if (half == 1 and dc == NDC - 1) else []),
                  noinc=(dc != NDC - 1))

    def emit_kln_apply(c):
        par = c % 2
        E("sc", lambda g: g.copy(q_tok[:], p_q[:]), rd=[b_pq], wr=[b_qtok])
        E("sc", lambda g, par=par: g.copy(v_ch[par][:], p_v[:]),
          rd=[b_pv], wr=[b_v[par]])
        E("ve", lambda g: g.tensor_tensor(
            kn_f32[:], k_raw[:],
            bass.AP(kst, 32, [[80, 128], [1, 16], [0, 64]]), ALU.subtract),
          rd=[b_kraw, b_kst], wr=[b_kn])
        E("ve", lambda g: g.tensor_tensor(
            kn_f32[:], kn_f32[:],
            bass.AP(kst, 48, [[80, 128], [1, 16], [0, 64]]), ALU.mult),
          rd=[b_kst], wr2=[b_kn])
        E("ve", lambda g, par=par: g.tensor_tensor(
            kh_ch[par][:], kn_f32[:], ck[:], ALU.mult),
          rd=[b_kn], wr=[b_kh[par]], waits=tab_waits)

    def emit_kln_pre(c):
        tks_ = E("ve", lambda g: g.tensor_reduce(
            kst[:, 0:16], p_k[:].rearrange("p (h e) -> p h e", e=DK),
            axis=AXX, op=ALU.add), rd=[b_pk], wr=[b_kst])
        E("sc", lambda g: g.copy(k_raw[:], p_k[:]), rd=[b_pk], wr=[b_kraw],
          waits=[(P.ctr["ve"], tks_)])
        E("sc", lambda g: g.activation(ksq_f32[:], k_raw[:], AF.Square),
          rd=[b_kraw], wr=[b_ksq])
        E("ve", lambda g: g.tensor_reduce(
            kst[:, 16:32], ksq_f32[:].rearrange("p (h e) -> p h e", e=DK),
            axis=AXX, op=ALU.add), rd=[b_ksq], wr2=[b_kst])
        E("ve", lambda g: g.tensor_scalar(
            kst[:, 32:48], kst[:, 0:16], 1.0 / DK, None, ALU.mult),
          rd=[b_kst], wr2=[b_kst])
        E("ve", lambda g: g.tensor_tensor(
            kst[:, 64:80], kst[:, 32:48], kst[:, 32:48], ALU.mult),
          rd=[b_kst], wr2=[b_kst])
        E("ve", lambda g: g.scalar_tensor_tensor(
            kst[:, 48:64], kst[:, 16:32], 1.0 / DK, kst[:, 64:80],
            ALU.mult, ALU.subtract), rd=[b_kst], wr2=[b_kst])
        E("ve", lambda g: g.tensor_scalar(
            kst[:, 48:64], kst[:, 48:64], LN_EPS, None, ALU.add),
          rd=[b_kst], wr2=[b_kst])
        E("sc", lambda g: g.activation(kst[:, 48:64], kst[:, 48:64], AF.Ln),
          rd=[b_kst], wr2=[b_kst])
        E("sc", lambda g: g.activation(
            kst[:, 48:64], kst[:, 48:64], AF.Exp, scale=-0.5),
          rd=[b_kst], wr2=[b_kst])

    def emit_transposes(c):
        par = c % 2
        for j in range(NDC):
            E("te", lambda g, j=j: g.transpose(
                p_v[:, j * 128:(j + 1) * 128], q_tok[:, j * 128:(j + 1) * 128],
                ident_sb[:]), rd=[b_qtok, b_const],
              wr=([b_pv] if j == 0 else []), wr2=([b_pv] if j > 0 else []))
        tqt_ = E("ve", lambda g, par=par: g.tensor_tensor(
            qt_ch[par][:], p_v[:], cq[:], ALU.mult),
          rd=[b_pv], wr=[b_qt[par]], waits=tab_waits)
        E("sc", lambda g, par=par: g.copy(q_ch[par][:], p_v[:]),
          rd=[b_pv], wr=[b_q[par]], waits=[(P.ctr["ve"], tqt_)])
        for j in range(NDC):
            E("te", lambda g, j=j: g.transpose(
                p_v[:, j * 128:(j + 1) * 128], kn_f32[:, j * 128:(j + 1) * 128],
                ident_sb[:]), rd=[b_kn, b_const],
              wr=([b_pv] if j == 0 else []), wr2=([b_pv] if j > 0 else []))
        E("sc", lambda g, par=par: g.copy(knT_ch[par][:], p_v[:]),
          rd=[b_pv], wr=[b_knT[par]])

    def emit_retention(c):
        par = c % 2
        gpar = (c // 4) % 2
        for p in range(8):
            rpar = p % 2
            for half in range(2):
                E("te", lambda g, p=p, half=half, par=par, rpar=rpar: g.matmul(
                    p_r[rpar][:, half * 128:(half + 1) * 128],
                    knT_ch[par][64 * half:64 * (half + 1),
                                p * 128:(p + 1) * 128],
                    q_ch[par][64 * half:64 * (half + 1),
                              p * 128:(p + 1) * 128],
                    start=True, stop=True, tile_position=(64 * half, 0)),
                  rd=[b_knT[par], b_q[par]],
                  wr=([b_pr[rpar]] if half == 0 else []),
                  wr2=([b_pr[rpar]] if half == 1 else []))
            E("ve", lambda g, p=p, rpar=rpar: g.tensor_tensor(
                a_scr[rpar][:], p_r[rpar][:, 0:256],
                gam[:, p * 256:(p + 1) * 256], ALU.mult),
              rd=[b_pr[rpar]], wr=[b_a[rpar]], waits=tab_waits)
            for half in range(2):
                h = 2 * p + half
                E("te", lambda g, h=h, half=half, par=par, rpar=rpar: g.matmul(
                    p_r[rpar][64 * half:64 * (half + 1), 256:384],
                    v_ch[par][:, h * 64:(h + 1) * 64],
                    a_scr[rpar][:, half * 128:(half + 1) * 128],
                    start=True, stop=False, tile_position=(0, 64 * half)),
                  rd=[b_v[par], b_a[rpar]], wr2=[b_pr[rpar]], noinc=True)
                E("te", lambda g, p=p, half=half, par=par, rpar=rpar: g.matmul(
                    p_r[rpar][64 * half:64 * (half + 1), 256:384],
                    st_bf[64 * half:64 * (half + 1), p * 64:(p + 1) * 64],
                    qt_ch[par][64 * half:64 * (half + 1),
                               p * 128:(p + 1) * 128],
                    start=False, stop=True,
                    tile_position=(64 * half, 64 * half)),
                  rd=[b_stb, b_qt[par]], wr2=[b_pr[rpar]])
            for half in range(2):
                h = 2 * p + half
                E("te", lambda g, h=h, half=half, par=par, rpar=rpar: g.matmul(
                    p_r[rpar][64 * half:64 * (half + 1), 384:448],
                    kh_ch[par][:, h * 64:(h + 1) * 64],
                    v_ch[par][:, h * 64:(h + 1) * 64],
                    start=True, stop=True, tile_position=(0, 64 * half)),
                  rd=[b_kh[par], b_v[par]], wr2=[b_pr[rpar]])
            tye_ = E("sc", lambda g, p=p, rpar=rpar, c=c, gpar=gpar: g.copy(
                y_stage[gpar][:, p * 512 + (c % 4) * 128:
                              p * 512 + (c % 4) * 128 + 128],
                p_r[rpar][:, 256:384]),
              rd=[b_pr[rpar]],
              wr=([b_yst[gpar]] if (p == 0 and c % 4 == 0) else []),
              wr2=([b_yst[gpar]] if not (p == 0 and c % 4 == 0) else []))
            E("ve", lambda g, p=p, rpar=rpar: g.scalar_tensor_tensor(
                st_bf[:, p * 64:(p + 1) * 64], st_bf[:, p * 64:(p + 1) * 64],
                gcp[:, p:p + 1], p_r[rpar][:, 384:448], ALU.mult, ALU.add),
              rd=[b_stb, b_pr[rpar]], wr=[b_stb],
              waits=tab_waits + [(P.ctr["sc"], tye_)])

    def emit_tail(c):
        par = c % 2
        gpar = (c // 4) % 2
        if c % 4 == 3:
            gidx = c // 4
            for pr in range(8):
                E("gp", lambda g, gidx=gidx, gpar=gpar, pr=pr: g.dma_start(
                    out=y_sp.ap()[pr, :, gidx * 4 * CH:(gidx + 1) * 4 * CH],
                    in_=y_stage[gpar][:, pr * 512:(pr + 1) * 512]),
                  rd=[b_yst[gpar]], wr2=[b_ysp],
                  dma=f"ysp{(gpar, pr % 2)}")
        emit_dq_stages(c)
        if c == 2 and debug not in ("p0", "p0nocc", "ar1"):
            ar1_issue()

    QKV = [(p_q, b_pq, 0), (p_k, b_pk, 1024), (p_v, b_pv, 2048)]
    emit_proj(0, QKV[0])
    emit_proj(0, QKV[1])
    emit_kln_pre(0)
    for c in range(P1_CHUNKS):
        emit_proj(c, QKV[2])
        emit_kln_apply(c)
        if c + 1 < P1_CHUNKS:
            emit_proj(c + 1, QKV[0])
            emit_proj(c + 1, QKV[1])
        emit_transposes(c)
        if c + 1 < P1_CHUNKS:
            emit_kln_pre(c + 1)
        emit_retention(c)
        emit_tail(c)
    for c in range(P1_CHUNKS, P1_CHUNKS + 3):
        emit_dq_stages(c)
    if debug not in ("p0", "p0nocc", "ar1"):
        ar1_consume()

    if XQ_BARRIER and not SKIP_XQ:
        P.barrier()
    for dc in range(NDC if not SKIP_XQ else 0):
        for half in range(4):
            base = dc * L + half * 1024
            grp = list(b_xn)
            E("ve", lambda g, base=base: g.tensor_scalar(
                ksq_f32[:], xnT[:, base:base + 1024], sc_(1), CMAGIC,
                ALU.mult, ALU.add),
              rd=[b_scl] + grp, wr=[b_ksq])
            E("ve", lambda g, base=base: g.tensor_scalar(
                xnT[:, base:base + 1024], ksq_f32[:], CMAGIC, sc_(0),
                ALU.subtract, ALU.mult),
              rd=[b_ksq, b_scl],
              wr=(grp if (dc == 0 and half == 0) else []),
              wr2=(grp if not (dc == 0 and half == 0) else []))

    if debug == "p1":
        P.wait_all_dma("sy")
        return nc, P, locals()

    P.barrier()
    es_a.close()
    es_b = contextlib.ExitStack()
    SBb = lambda n, s, d: es_b.enter_context(nc.sbuf_tensor(n, s, d))
    PSb = lambda n, s, d: es_b.enter_context(nc.psum_tensor(n, s, d))

    wst = [SBb(f"wst{i}", [128, 1024], BF16) for i in range(4)]
    b_wst = [Buf() for _ in range(4)]
    stg = [SBb(f"stg{i}", [128, 512], BF16) for i in range(4)]
    b_stg = [Buf() for _ in range(4)]
    dqdw_in = [SBb(f"dqdw_in{i}", [128, 3072], F32) for i in range(2)]
    b_dqdwin = [Buf(), Buf()]
    dqd_s = SBb("dqd_s", [128, 288], F32)
    b_dqds = Buf()
    dqd_u = SBb("dqd_u", [128, 3072], F32)
    b_dqdu = Buf()
    dqd_nat = SBb("dqd_nat", [128, 3072], BF16)
    b_dqdn = Buf()
    dqd_tr = [SBb(f"dqd_tr{i}", [128, 3072], BF16) for i in range(2)]
    b_dqdt = [Buf(), Buf()]
    p_f = [PSb(f"p_f{i}", [128, 512], F32) for i in range(4)]
    b_pf = [Buf() for _ in range(4)]

    dqd_done = {}
    dqd_cnt = [0]

    def emit_dqd(dc):
        par = dqd_cnt[0] % 2
        tpar = dqd_cnt[0] % 2
        dqd_cnt[0] += 1
        E("sy", lambda g, dc=dc, par=par: g.dma_start(
            out=dqdw_in[par][:], in_=w_down[dc * 128:(dc + 1) * 128, :]),
          wr=[b_dqdwin[par]], dma="dqdi")
        E("ve", lambda g, par=par: g.tensor_reduce(
            dqd_s[:, 0:96], dqdw_in[par][:].rearrange("p (b k) -> p b k",
                                                      k=32),
            axis=AXX, op=ALU.max, apply_absolute_value=True),
          rd=[b_dqdwin[par]], wr=[b_dqds])
        E("ve", lambda g: g.tensor_scalar(
            dqd_s[:, 96:192], dqd_s[:, 0:96], 1.0 / 7.0, 1e-12, ALU.mult,
            ALU.max), rd=[b_dqds], wr2=[b_dqds])
        E("ve", lambda g: g.reciprocal(dqd_s[:, 192:288], dqd_s[:, 96:192]),
          rd=[b_dqds], wr2=[b_dqds])
        E("gp", lambda g, par=par: g.tensor_tensor(
            dqd_u[:], dqdw_in[par][:],
            bass.AP(dqd_s, 192, [[288, 128], [1, 96], [0, 32]]), ALU.mult),
          rd=[b_dqdwin[par], b_dqds], wr=[b_dqdu])
        E("ve", lambda g: g.tensor_scalar(
            dqd_u[:], dqd_u[:], CMAGIC, -CMAGIC, ALU.add, ALU.add),
          rd=[b_dqdu], wr=[b_dqdu])
        E("ve", lambda g: g.tensor_scalar(
            dqd_u[:], dqd_u[:], 7.0, -8.0, ALU.min, ALU.max),
          rd=[b_dqdu], wr=[b_dqdu])
        E("gp", lambda g: g.tensor_tensor(
            dqd_nat[:], dqd_u[:],
            bass.AP(dqd_s, 96, [[288, 128], [1, 96], [0, 32]]), ALU.mult),
          rd=[b_dqdu, b_dqds], wr=[b_dqdn])
        for ic in range(NIC):
            E("sy", lambda g, ic=ic, tpar=tpar: g.dma_start(
                out=dqd_tr[tpar][:, ic * 128:(ic + 1) * 128],
                in_=dqd_nat[:, ic * 128:(ic + 1) * 128], transpose=True),
              rd=[b_dqdn], wr=([b_dqdt[tpar]] if ic == 0 else []),
              wr2=([b_dqdt[tpar]] if ic > 0 else []), dma=f"dqdt{tpar}")
        tout = E("gp", lambda g, tpar=tpar, dc=dc: g.dma_start(
            out=wdndq_sp.ap()[dc, :, :], in_=dqd_tr[tpar][:]),
          rd=[b_dqdt[tpar]], dma="dqdo")
        dqd_done[dc] = (P.dma_ctr("dqdo"), tout)

    dqd_emitted = [0]
    mats2b = [("up", wupdq_sp, up_sp, b_upsp, 10),
              ("gate", wgatedq_sp, gate_sp, b_gatesp, 11)]
    ldplan = [(m[0], m[1], oc) for m in mats2b for oc in range(NIC)]

    def load_wst(k):
        mname, wsp, oc = ldplan[k]
        E("sy", lambda g, oc=oc, wsp=wsp, k=k: g.dma_start(
            out=wst[k % 4][:], in_=wsp.ap()[oc, :, :]),
          wr=[b_wst[k % 4]], waits=[dq_done[(mname, oc)]], dma=f"wst{k % 4}")

    load_wst(0)
    load_wst(1)
    kk = [2]
    for (mname, wsp, osp, b_osp_, mxcol) in mats2b:
        for oc in range(NIC):
            wb = b_wst[oc % 4]
            if kk[0] < len(ldplan):
                load_wst(kk[0])
                kk[0] += 1
            for slab in range(NSLAB):
                pf = (oc * NSLAB + slab) % 4
                for dc in range(NDC):
                    E("te", lambda g, oc=oc, slab=slab, dc=dc, pf=pf: g.matmul(
                        p_f[pf][:], wst[oc % 4][:, dc * 128:(dc + 1) * 128],
                        xnT[:, dc * L + slab * SLAB:
                            dc * L + (slab + 1) * SLAB],
                        start=(dc == 0), stop=(dc == NDC - 1)),
                      rd=[wb] + (list(b_xn) if (oc == 0 and slab == 0)
                                 else []),
                      wr=([b_pf[pf]] if dc == 0 else []),
                      wr2=([b_pf[pf]] if dc == NDC - 1 else []),
                      noinc=(dc != NDC - 1))
                E("sc", lambda g, pf=pf: g.copy(stg[pf][:], p_f[pf][:]),
                  rd=[b_pf[pf]], wr=[b_stg[pf]])
                E("ve", lambda g, pf=pf: g.tensor_reduce(
                    col(7), stg[pf][:], axis=AXX, op=ALU.max,
                    apply_absolute_value=True),
                  rd=[b_stg[pf]], wr=[b_stat])
                E("ve", lambda g, first=(oc == 0 and slab == 0), mxcol=mxcol:
                  g.tensor_tensor(
                      sc_(mxcol), col(7) if first else sc_(mxcol), col(7),
                      ALU.max), rd=[b_stat], wr2=[b_mx])
                E("gp", lambda g, oc=oc, slab=slab, pf=pf, osp=osp:
                  g.dma_start(
                      out=osp.ap()[oc, :, slab * SLAB:(slab + 1) * SLAB],
                      in_=stg[pf][:]),
                  rd=[b_stg[pf]], wr2=[b_osp_], dma=f"spl{pf}")
            if dqd_emitted[0] < NDC and oc % 3 == 2:
                emit_dqd(dqd_emitted[0])
                dqd_emitted[0] += 1
    while dqd_emitted[0] < NDC:
        emit_dqd(dqd_emitted[0])
        dqd_emitted[0] += 1

    if debug == "p2b":
        P.wait_all_dma("sy")
        return nc, P, locals()

    b_mx.note_w(P.ctr["ve"], P.ctr["ve"].c)
    allreduce_max([sc_(10), sc_(11)], [b_mx], p_f[0][:], b_pf[0])
    bcast_scale(0, 2, 3, p_f[0][:], b_pf[0])
    bcast_scale(1, 4, 5, p_f[0][:], b_pf[0])

    P.barrier()
    es_x.close()
    es_b.close()
    es_c = contextlib.ExitStack()
    es_cio = contextlib.ExitStack()
    SBc = lambda n, s, d: es_c.enter_context(nc.sbuf_tensor(n, s, d))
    SBio = lambda n, s, d: es_cio.enter_context(nc.sbuf_tensor(n, s, d))

    attn_sb = SBc("attn_sb", [128, NDC * L], BF16)
    b_attn = Buf()
    woT_sb = SBc("woT_sb", [128, NDC * D], BF16)
    b_wo = Buf()
    y_sl = [SBc(f"y_sl{i}", [128, 4096], BF16) for i in range(2)]
    b_ysl = [Buf(), Buf()]
    p_w = [es_c.enter_context(nc.psum_tensor(f"p_w{i}", [128, 512], F32))
           for i in range(2)]
    b_pw = [Buf(), Buf()]
    io_a = [SBio(f"io_a{i}", [128, 4096], BF16) for i in range(2)]
    b_ioa = [Buf(), Buf()]
    io_b = [SBio(f"io_b{i}", [128, 4096], BF16) for i in range(2)]
    b_iob = [Buf(), Buf()]
    io_c = [SBio(f"io_c{i}", [128, 4096], BF16) for i in range(2)]
    b_ioc = [Buf(), Buf()]
    fs1 = [SBio(f"fs1_{i}", [128, 2048], F32) for i in range(2)]
    b_fs1 = [Buf(), Buf()]
    fs2 = [SBio(f"fs2_{i}", [128, 2048], F32) for i in range(2)]
    b_fs2 = [Buf(), Buf()]
    fs3 = [SBio(f"fs3_{i}", [128, 2048], F32) for i in range(2)]
    b_fs3 = [Buf(), Buf()]

    for dc in range(NDC):
        E("gp", lambda g, dc=dc: g.dma_start(
            out=woT_sb[:, dc * D:(dc + 1) * D],
            in_=woT[dc * 128:(dc + 1) * 128, :]),
          wr=([b_wo] if dc == 0 else []), wr2=([b_wo] if dc > 0 else []),
          dma=f"wo{dc % 2}")

    wo_units = []
    for slab in range(NSLAB):
        wo_units.append(("load", slab))
        for oc in range(NDC):
            wo_units.append(("mm", slab, oc))
    wo_i = [0]

    def emit_wo_unit():
        if wo_i[0] >= len(wo_units):
            return
        u = wo_units[wo_i[0]]
        wo_i[0] += 1
        if u[0] == "load":
            slab = u[1]
            ypar = slab % 2
            for pr in range(8):
                E("sy", lambda g, slab=slab, ypar=ypar, pr=pr: g.dma_start(
                    out=y_sl[ypar][:, pr * 512:(pr + 1) * 512],
                    in_=y_sp.ap()[pr, :, slab * SLAB:(slab + 1) * SLAB]),
                  rd=[b_ysp], wr=([b_ysl[ypar]] if pr == 0 else []),
                  wr2=([b_ysl[ypar]] if pr > 0 else []),
                  dma=f"ysl{(ypar, pr % 2)}")
        else:
            slab, oc = u[1], u[2]
            ypar = slab % 2
            wpar = (slab * NDC + oc) % 2
            for dc in range(NDC):
                E("te", lambda g, oc=oc, dc=dc, ypar=ypar, wpar=wpar: g.matmul(
                    p_w[wpar][:],
                    woT_sb[:, dc * D + oc * 128: dc * D + (oc + 1) * 128],
                    y_sl[ypar][:, dc * 512:(dc + 1) * 512],
                    start=(dc == 0), stop=(dc == NDC - 1)),
                  rd=[b_wo, b_ysl[ypar]],
                  wr=([b_pw[wpar]] if dc == 0 else []),
                  wr2=([b_pw[wpar]] if dc == NDC - 1 else []),
                  noinc=(dc != NDC - 1))
            E("sc", lambda g, oc=oc, slab=slab, wpar=wpar: g.copy(
                attn_sb[:, oc * L + slab * SLAB: oc * L + (slab + 1) * SLAB],
                p_w[wpar][:]), rd=[b_pw[wpar]], wr2=[b_attn])

    for ic in range(NIC):
        par = ic % 2
        E("sy", lambda g, ic=ic, par=par: g.dma_start(
            out=io_a[par][:], in_=gate_sp.ap()[ic, :, :]),
          rd=[b_gatesp], wr=[b_ioa[par]], dma=f"ioa{par}")
        for hh in range(2):
            h0 = hh * 2048
            E("sc", lambda g, par=par, h0=h0: g.activation(
                fs1[par][:], io_a[par][:, h0:h0 + 2048], AF.Identity,
                bias=cmag[:], scale=sc_(5)),
              rd=[b_ioa[par], b_scl, b_cmag], wr=[b_fs1[par]])
            E("ve", lambda g, par=par: g.tensor_scalar(
                fs2[par][:], fs1[par][:], CMAGIC, sc_(4), ALU.subtract,
                ALU.mult),
              rd=[b_fs1[par], b_scl], wr=[b_fs2[par]])
            E("sc", lambda g, par=par: g.activation(fs1[par][:], fs2[par][:],
                                                    AF.Sigmoid),
              rd=[b_fs2[par]], wr=[b_fs1[par]])
            E("sc", lambda g, par=par: g.activation(
                fs3[par][:], fs1[par][:], AF.Identity, bias=cmag[:],
                scale=65536.0),
              rd=[b_fs1[par], b_cmag], wr=[b_fs3[par]])
            E("ve", lambda g, par=par: g.tensor_scalar(
                fs1[par][:], fs3[par][:], -CMAGIC, 65535.0, ALU.add, ALU.min),
              rd=[b_fs3[par]], wr=[b_fs1[par]])
            E("ve", lambda g, par=par: g.tensor_scalar(
                fs3[par][:], fs1[par][:], 0.0, 1.0 / 65536.0, ALU.max,
                ALU.mult),
              rd=[b_fs1[par]], wr=[b_fs3[par]])
            E("ve", lambda g, par=par, h0=h0: g.tensor_tensor(
                io_c[par][:, h0:h0 + 2048], fs2[par][:], fs3[par][:],
                ALU.mult),
              rd=[b_fs2[par], b_fs3[par]],
              wr=([b_ioc[par]] if hh == 0 else []),
              wr2=([b_ioc[par]] if hh else []))
        E("ve", lambda g, par=par: g.tensor_reduce(
            col(7), io_c[par][:], axis=AXX, op=ALU.max,
            apply_absolute_value=True), rd=[b_ioc[par]], wr=[b_stat])
        E("ve", lambda g, first=(ic == 0): g.tensor_tensor(
            sc_(12), col(7) if first else sc_(12), col(7), ALU.max),
          rd=[b_stat], wr2=[b_mx])
        E("gp", lambda g, ic=ic, par=par: g.dma_start(
            out=g2_sp.ap()[ic, :, :], in_=io_c[par][:]),
          rd=[b_ioc[par]], wr2=[b_g2sp], dma=f"ioc{par}")
        for _ in range(3):
            emit_wo_unit()
    while wo_i[0] < len(wo_units):
        emit_wo_unit()

    if debug == "p2d":
        P.wait_all_dma("sy")
        return nc, P, locals()

    b_mx.note_w(P.ctr["ve"], P.ctr["ve"].c)
    allreduce_max([sc_(12)], [b_mx], p_w[0][:], b_pw[0])
    bcast_scale(0, 6, 7, p_w[0][:], b_pw[0])

    for ic in range(NIC):
        par = ic % 2
        E("sy", lambda g, ic=ic, par=par: g.dma_start(
            out=io_a[par][:], in_=up_sp.ap()[ic, :, :]),
          rd=[b_upsp], wr=[b_ioa[par]], dma=f"ioa{par}")
        E("sy", lambda g, ic=ic, par=par: g.dma_start(
            out=io_b[par][:], in_=g2_sp.ap()[ic, :, :]),
          rd=[b_g2sp], wr=[b_iob[par]], dma=f"iob{par}")
        for hh in range(2):
            h0 = hh * 2048
            E("sc", lambda g, par=par, h0=h0: g.activation(
                fs1[par][:], io_a[par][:, h0:h0 + 2048], AF.Identity,
                bias=cmag[:], scale=sc_(3)),
              rd=[b_ioa[par], b_scl, b_cmag], wr=[b_fs1[par]])
            E("ve", lambda g, par=par: g.tensor_scalar(
                fs2[par][:], fs1[par][:], CMAGIC, sc_(2), ALU.subtract,
                ALU.mult),
              rd=[b_fs1[par], b_scl], wr=[b_fs2[par]])
            E("sc", lambda g, par=par, h0=h0: g.activation(
                fs3[par][:], io_b[par][:, h0:h0 + 2048], AF.Identity,
                bias=cmag[:], scale=sc_(7)),
              rd=[b_iob[par], b_scl, b_cmag], wr=[b_fs3[par]])
            E("ve", lambda g, par=par: g.tensor_scalar(
                fs1[par][:], fs3[par][:], CMAGIC, sc_(6), ALU.subtract,
                ALU.mult),
              rd=[b_fs3[par], b_scl], wr=[b_fs1[par]])
            E("ve", lambda g, par=par, h0=h0: g.tensor_tensor(
                io_c[par][:, h0:h0 + 2048], fs1[par][:], fs2[par][:],
                ALU.mult),
              rd=[b_fs1[par], b_fs2[par]],
              wr=([b_ioc[par]] if hh == 0 else []),
              wr2=([b_ioc[par]] if hh else []))
        E("ve", lambda g, par=par: g.tensor_reduce(
            col(7), io_c[par][:], axis=AXX, op=ALU.max,
            apply_absolute_value=True), rd=[b_ioc[par]], wr=[b_stat])
        E("ve", lambda g, first=(ic == 0): g.tensor_tensor(
            sc_(13), col(7) if first else sc_(13), col(7), ALU.max),
          rd=[b_stat], wr2=[b_mx])
        E("gp", lambda g, ic=ic, par=par: g.dma_start(
            out=o_sp.ap()[ic, :, :], in_=io_c[par][:]),
          rd=[b_ioc[par]], wr2=[b_osp], dma=f"ioc{par}")

    if debug == "p2e":
        P.wait_all_dma("sy")
        return nc, P, locals()

    b_mx.note_w(P.ctr["ve"], P.ctr["ve"].c)
    allreduce_max([sc_(13)], [b_mx], p_w[0][:], b_pw[0])
    bcast_scale(0, 8, 9, p_w[0][:], b_pw[0])

    P.barrier()
    es_cio.close()
    es_d = contextlib.ExitStack()
    SBd = lambda n, s, d: es_d.enter_context(nc.sbuf_tensor(n, s, d))
    oq_sl = [SBd(f"oq_sl{i}", [128, NIC * 512], BF16) for i in range(2)]
    b_oq = [Buf(), Buf()]
    wdn = [SBd(f"wdn{i}", [128, NIC * 128], BF16) for i in range(2)]
    b_wdn = [Buf(), Buf()]
    xTt = [SBd(f"xTt{i}", [128, 512], F32) for i in range(2)]
    b_xT = [Buf(), Buf()]
    outt = [SBd(f"outt{i}", [128, 512], F32) for i in range(2)]
    b_out = [Buf(), Buf()]
    ofs = [SBd(f"ofs{i}", [128, 2048], F32) for i in range(2)]
    b_ofs = [Buf(), Buf()]
    ofs2 = [SBd(f"ofs2{i}", [128, 2048], F32) for i in range(2)]
    b_ofs2 = [Buf(), Buf()]
    ep1 = SBd("ep1", [128, 512], F32)
    b_ep1 = Buf()
    p_d = [es_d.enter_context(nc.psum_tensor(f"p_d{i}", [128, 512], F32))
           for i in range(4)]
    b_pd = [Buf() for _ in range(4)]

    wdn_wait = (P.dma_ctr("dqdo"), max(t for (_c, t) in dqd_done.values()))
    last_out = 0
    for slab in range(NSLAB):
        spar = slab % 2
        for ic in range(NIC):
            E("sy", lambda g, ic=ic, slab=slab, spar=spar: g.dma_start(
                out=oq_sl[spar][:, ic * 512:(ic + 1) * 512],
                in_=o_sp.ap()[ic, :, slab * SLAB:(slab + 1) * SLAB]),
              rd=[b_osp], wr=([b_oq[spar]] if ic == 0 else []),
              wr2=([b_oq[spar]] if ic > 0 else []), dma=f"oq{ic % 4}")
        for third in range(6):
            a0, a1 = third * 2048, (third + 1) * 2048
            tp = third % 2
            E("sc", lambda g, spar=spar, a0=a0, a1=a1, tp=tp: g.activation(
                ofs[tp][:], oq_sl[spar][:, a0:a1], AF.Identity, bias=cmag[:],
                scale=sc_(9)), rd=[b_oq[spar], b_scl, b_cmag], wr=[b_ofs[tp]])
            E("ve", lambda g, tp=tp: g.tensor_scalar(
                ofs2[tp][:], ofs[tp][:], -CMAGIC, QMAX, ALU.add, ALU.min),
              rd=[b_ofs[tp]], wr=[b_ofs2[tp]])
            E("ve", lambda g, spar=spar, a0=a0, a1=a1, tp=tp: g.tensor_scalar(
                oq_sl[spar][:, a0:a1], ofs2[tp][:], -QMAX - 1.0, sc_(8),
                ALU.max, ALU.mult),
              rd=[b_ofs2[tp], b_scl], wr2=[b_oq[spar]])
        if slab == 0:
            E("sy", lambda g: g.dma_start(
                out=wdn[0][:], in_=wdndq_sp.ap()[0, :, :]),
              wr=[b_wdn[0]], waits=[wdn_wait], dma="wdn0")
        for oc in range(NDC):
            pd = oc % 4
            wpar2 = (slab * NDC + oc) % 2
            nslab, noc = (slab, oc + 1) if oc + 1 < NDC else (slab + 1, 0)
            if nslab < NSLAB:
                E("sy", lambda g, noc=noc, w2=1 - wpar2: g.dma_start(
                    out=wdn[w2][:], in_=wdndq_sp.ap()[noc, :, :]),
                  wr=[b_wdn[1 - wpar2]], waits=[wdn_wait],
                  dma=f"wdn{1 - wpar2}")
            for ic in range(NIC):
                E("te", lambda g, ic=ic, spar=spar, pd=pd, wpar2=wpar2:
                  g.matmul(
                      p_d[pd][:], wdn[wpar2][:, ic * 128:(ic + 1) * 128],
                      oq_sl[spar][:, ic * 512:(ic + 1) * 512],
                      start=(ic == 0), stop=(ic == NIC - 1)),
                  rd=[b_wdn[wpar2], b_oq[spar]],
                  wr=([b_pd[pd]] if ic == 0 else []),
                  wr2=([b_pd[pd]] if ic == NIC - 1 else []),
                  noinc=(ic != NIC - 1))
            opar = oc % 2
            E("sy", lambda g, oc=oc, slab=slab, opar=opar: g.dma_start(
                out=xTt[opar][:],
                in_=xT[oc * 128:(oc + 1) * 128,
                       slab * SLAB:(slab + 1) * SLAB]),
              wr=[b_xT[opar]], dma=f"xt{opar}")
            E("ve", lambda g, oc=oc, slab=slab, pd=pd: g.tensor_tensor(
                ep1[:], p_d[pd][:],
                attn_sb[:, oc * L + slab * SLAB: oc * L + (slab + 1) * SLAB],
                ALU.add), rd=[b_pd[pd], b_attn], wr=[b_ep1])
            E("ve", lambda g, opar=opar: g.tensor_tensor(
                outt[opar][:], ep1[:], xTt[opar][:], ALU.add),
              rd=[b_ep1, b_xT[opar]], wr=[b_out[opar]])
            last_out = E("gp", lambda g, oc=oc, slab=slab, opar=opar:
                         g.dma_start(
                             out=outT[oc * 128:(oc + 1) * 128,
                                      slab * SLAB:(slab + 1) * SLAB],
                             in_=outt[opar][:]),
                         rd=[b_out[opar]], dma=f"out{opar}")

    P.wait_all_dma("sy")
    return nc, P, locals()


def finalize(nc, P):
    with nc.Block() as blk:
        @blk.sync
        def _(g):
            for f in P.ops["sy"]:
                f(g)

        @blk.scalar
        def _(g):
            for f in P.ops["sc"]:
                f(g)

        @blk.vector
        def _(g):
            for f in P.ops["ve"]:
                f(g)

        @blk.tensor
        def _(g):
            for f in P.ops["te"]:
                f(g)

        @blk.gpsimd
        def _(g):
            for f in P.ops["gp"]:
                f(g)
    return nc


def host_consts():
    idx = np.arange(128, dtype=np.float32)
    emat = idx[None, :] - idx[:, None]
    emat = np.where(emat >= 0, emat, 1e9).astype(np.float32)
    utri = (idx[None, :] >= idx[:, None]).astype(np.float32)
    rampq = np.ascontiguousarray(
        np.broadcast_to(idx[None, :] + 1.0, (128, 128))).astype(np.float32)
    rampk = np.ascontiguousarray(
        np.broadcast_to((127.0 - idx)[:, None], (128, 64))).astype(np.float32)
    return dict(emat=np.ascontiguousarray(emat),
                utri=np.ascontiguousarray(utri), rampq=rampq, rampk=rampk,
                ident=np.eye(128, dtype=np.float32))


def host_inputs(L0_attn_x, L0_attn_W_qkv, L0_attn_W_o, L0_attn_log_gamma,
                L0_ffn_w_gate, L0_ffn_w_up, L0_ffn_w_down):
    x = np.asarray(L0_attn_x, dtype=np.float32)
    wqkvT = np.ascontiguousarray(np.asarray(L0_attn_W_qkv, np.float32).T)
    woT = np.ascontiguousarray(np.asarray(L0_attn_W_o, np.float32).T)
    lg = np.asarray(L0_attn_log_gamma, np.float32)
    lgp8 = np.empty((128, 8), np.float32)
    for j in range(8):
        lgp8[:64, j] = lg[2 * j]
        lgp8[64:, j] = lg[2 * j + 1]
    lg16 = np.ascontiguousarray(
        np.broadcast_to(lg[None, :], (128, 16))).astype(np.float32)
    shared = dict(wqkvT=wqkvT, woT=woT,
                  w_up=np.ascontiguousarray(np.asarray(L0_ffn_w_up,
                                                       np.float32)),
                  w_gate=np.ascontiguousarray(np.asarray(L0_ffn_w_gate,
                                                         np.float32)),
                  w_down=np.ascontiguousarray(np.asarray(L0_ffn_w_down,
                                                         np.float32)),
                  lgp8=lgp8, lg16=lg16, **host_consts())
    in_maps = []
    for c in range(NCORES):
        xc = np.ascontiguousarray(x[c])
        in_maps.append(dict(x_nat=xc, xT=np.ascontiguousarray(xc.T), **shared))
    return in_maps


def kernel(**inputs):
    nc, P, _ = build_program()
    finalize(nc, P)
    in_maps = host_inputs(**inputs)
    res = run_bass_kernel_spmd(nc, in_maps, core_ids=list(range(NCORES)))
    out = np.empty((NCORES, L, D), dtype=np.float32)
    for c in range(NCORES):
        out[c] = np.asarray(res.results[c]["outT"]).T
    return out


# version 12
# speedup vs baseline: 1.4821x; 1.0090x over previous
import contextlib
import numpy as np
import concourse.bass as bass
import concourse.mybir as mybir
import concourse.bass_isa as bass_isa
from concourse.bass_utils import run_bass_kernel_spmd

F32 = mybir.dt.float32
BF16 = mybir.dt.bfloat16
ALU = mybir.AluOpType
AF = mybir.ActivationFunctionType
AXX = mybir.AxisListType.X
RED = bass_isa.ReduceOp

NCORES = 8
L, D, H, DK, I3 = 4096, 1024, 16, 64, 3072
CH, SLAB = 128, 512
NCH, NSLAB, NDC, NIC = L // CH, L // SLAB, D // 128, I3 // 128
CMAGIC = 12582912.0
P0_TILES = NCH
SKIP_TABLES = False
SKIP_WQKV = False
SKIP_DQ = False
P1_CHUNKS = NCH
P1_CUT = 99
XQ_MODE = 0
SKIP_XQ = False
XQ_BARRIER = False
XQ_OPS = 3
SKIP_RET = False
SKIP_KLN = False
SKIP_TR = False
LN_EPS = 1e-05
QMAX = 32767.0


class Ctr:
    __slots__ = ("h", "c", "name")

    def __init__(self, h, name):
        self.h, self.c, self.name = h, 0, name

    def bump(self, n=1):
        self.c += n
        return self.c


class Buf:
    __slots__ = ("w", "r")

    def __init__(self):
        self.w, self.r = {}, {}

    def note_w(self, ctr, tok):
        self.w = {ctr: tok}
        self.r = {}

    def note_w2(self, ctr, tok):
        if self.w.get(ctr, 0) < tok:
            self.w[ctr] = tok

    def note_r(self, ctr, tok):
        if self.r.get(ctr, 0) < tok:
            self.r[ctr] = tok


class Prog:
    def __init__(self, nc):
        self.nc = nc
        self.ops = {e: [] for e in ("sy", "sc", "ve", "te", "gp")}
        self.es = contextlib.ExitStack()
        self._mk = lambda n: Ctr(self.es.enter_context(nc.semaphore(n)), n)
        self.ctr = {"sc": self._mk("c_sc"), "ve": self._mk("c_ve"),
                    "te": self._mk("c_te"), "gp": self._mk("c_gp"),
                    "cc": self._mk("c_cc")}
        self.dstr = {}
        self.plan = []
        self.waited = {e: {} for e in self.ops}

    def dma_ctr(self, name):
        if name not in self.dstr:
            self.dstr[name] = self._mk("dm_" + name)
        return self.dstr[name]

    def E(self, eng, build, rd=(), wr=(), wr2=(), waits=(), dma=None,
          cc=False, noinc=False):
        if cc:
            ctr, n = self.ctr["cc"], 1
        elif dma is not None:
            ctr, n = self.dma_ctr(dma), 16
        else:
            ctr, n = self.ctr[eng], 1
        need = {}
        if dma is not None and ctr.c:
            need[ctr] = ctr.c

        def add(c, v):
            if v and need.get(c, 0) < v:
                need[c] = v
        for (c, v) in waits:
            add(c, v)
        for b in rd:
            for c, v in b.w.items():
                add(c, v)
        for b in wr:
            for src in (b.w, b.r):
                for c, v in src.items():
                    add(c, v)
        wm = self.waited[eng]
        final = []
        for c, v in need.items():
            if wm.get(c, 0) >= v:
                continue
            final.append((c.h, v))
            wm[c] = v
        tok = ctr.c if noinc else ctr.bump(n)

        pad = (eng == "te" and not noinc and not cc)

        def f(g, _w=tuple(final), _b=build, _h=ctr.h, _n=n, _ni=noinc,
              _pad=pad):
            for (h, v) in _w:
                g.wait_ge(h, v)
            ins = _b(g)
            if not _ni:
                ins.then_inc(_h, _n)
            if _pad:
                g.nop()
                g.nop()
                g.nop()
                g.nop()
        self.ops[eng].append(f)
        self.plan.append(
            (eng, ctr, (None if noinc else tok), tuple(need.items()),
             tuple(id(b) for b in rd), tuple(id(b) for b in wr)
             + tuple(id(b) for b in wr2)))
        for b in rd:
            b.note_r(ctr, tok)
        for b in wr:
            b.note_w(ctr, tok)
        for b in wr2:
            b.note_w2(ctr, tok)
        return tok

    def W(self, eng, waits):
        wm = self.waited[eng]
        final = []
        final2 = []
        for c, v in waits:
            if v and wm.get(c, 0) < v:
                final.append((c.h, v))
                final2.append((c, v))
                wm[c] = v
        if final:
            def f(g, _w=tuple(final)):
                for (h, v) in _w:
                    g.wait_ge(h, v)
            self.ops[eng].append(f)
            self.plan.append((eng, None, None, tuple((c, v) for c, v in final2),
                              (), ()))

    def barrier(self):
        snap = [(c, c.c) for c in self.ctr.values()]
        snap += [(c, c.c) for c in self.dstr.values()]
        for e in self.ops:
            self.W(e, snap)

    def wait_all_dma(self, eng):
        self.W(eng, [(c, c.c) for c in self.dstr.values()])


def build_program(debug=None):
    nc = bass.Bass("TRN2", target_bir_lowering=False)
    P = Prog(nc)
    E, W = P.E, P.W

    dT = lambda n, s, k: nc.dram_tensor(n, s, F32, kind=k)
    x_nat = dT("x_nat", [L, D], "ExternalInput").ap()
    xT = dT("xT", [D, L], "ExternalInput").ap()
    wqkvT = dT("wqkvT", [D, 3 * D], "ExternalInput").ap()
    woT = dT("woT", [D, D], "ExternalInput").ap()
    w_up = dT("w_up", [I3, D], "ExternalInput").ap()
    w_gate = dT("w_gate", [I3, D], "ExternalInput").ap()
    w_down = dT("w_down", [D, I3], "ExternalInput").ap()
    lgp8 = dT("lgp8", [128, 8], "ExternalInput").ap()
    lg16 = dT("lg16", [128, 16], "ExternalInput").ap()
    emat = dT("emat", [128, 128], "ExternalInput").ap()
    utri = dT("utri", [128, 128], "ExternalInput").ap()
    rampq = dT("rampq", [128, 128], "ExternalInput").ap()
    rampk = dT("rampk", [128, 64], "ExternalInput").ap()
    ident = dT("ident", [128, 128], "ExternalInput").ap()
    outT = dT("outT", [D, L], "ExternalOutput").ap()

    def scratch(name, shape, dbgkey):
        kind = "ExternalOutput" if debug == dbgkey else "Internal"
        return nc.dram_tensor(name, shape, BF16, kind=kind)

    y_sp = scratch("y_sp", [8, 128, L], "p1")
    up_sp = scratch("up_sp", [NIC, 128, L], "p2b")
    gate_sp = scratch("gate_sp", [NIC, 128, L], "none")
    g2_sp = scratch("g2_sp", [NIC, 128, L], "p2d")
    o_sp = scratch("o_sp", [NIC, 128, L], "p2e")
    wupdq_sp = nc.dram_tensor(
        "wupdq_sp", [NIC, 128, NDC * 128], BF16,
        kind="ExternalOutput" if debug == "dq" else "Internal")
    wgatedq_sp = nc.dram_tensor("wgatedq_sp", [NIC, 128, NDC * 128], BF16)
    wdndq_sp = nc.dram_tensor("wdndq_sp", [NDC, 128, NIC * 128], BF16)
    pmax_sp = nc.dram_tensor("pmax_sp", [4, 128], F32)
    cc_in = nc.dram_tensor("cc_in", [1, 16], F32)
    cc_out = nc.dram_tensor("cc_out", [1, 16], F32)
    b_ccin, b_ccout = Buf(), Buf()
    b_ysp, b_upsp, b_gatesp, b_g2sp, b_osp = (Buf() for _ in range(5))
    dbg_xn = None
    if debug in ("p0", "p0nocc", "ar1"):
        dbg_xn = nc.dram_tensor("dbg_xn", [128, NDC * L], BF16,
                                kind="ExternalOutput")

    es_g, es_x, es_a = (contextlib.ExitStack() for _ in range(3))
    SBg = lambda n, s, d: es_g.enter_context(nc.sbuf_tensor(n, s, d))
    SBa = lambda n, s, d: es_a.enter_context(nc.sbuf_tensor(n, s, d))
    PSa = lambda n, s, d: es_a.enter_context(nc.psum_tensor(n, s, d))

    stat = SBg("stat", [128, 16], F32)
    col = lambda i: stat[:, i:i + 1]
    b_stat = Buf()
    kst = SBg("kst", [128, 80], F32)
    b_kst = Buf()
    ccsb = SBg("ccsb", [1, 16], F32)
    b_ccsb = Buf()
    pmrow = SBg("pmrow", [1, 128], F32)
    b_pmrow = Buf()
    b_pmax = Buf()
    scl = SBg("scl", [128, 16], F32)
    cmag = SBg("cmag", [128, 1], F32)
    b_cmag = Buf()
    utri_sb = SBg("utri_sb", [128, 128], F32)
    ident_sb = SBg("ident_sb", [128, 128], F32)
    sc_ = lambda i: scl[:, i:i + 1]
    b_scl = Buf()
    b_mx = Buf()

    xnT = es_x.enter_context(nc.sbuf_tensor("xnT", [128, NDC * L], BF16, side="right"))
    b_xn = [Buf() for _ in range(NCH)]

    wqkv_sb = SBa("wqkv_sb", [128, NDC * 3 * D], BF16)
    b_wqkv = Buf()
    gam = SBa("gam", [128, H * 128], BF16)
    cq = SBa("cq", [128, 1024], BF16)
    ck = SBa("ck", [128, 1024], BF16)
    gcp = SBa("gcp", [128, 8], F32)
    b_tab = Buf()
    lg16_sb = SBa("lg16_sb", [128, 16], F32)
    lgp8_sb = SBa("lgp8_sb", [128, 8], F32)
    emat_sb = SBa("emat_sb", [128, 128], F32)

    rampq_sb = SBa("rampq_sb", [128, 128], F32)
    rampk_sb = SBa("rampk_sb", [128, 64], F32)

    b_const = Buf()
    xin = [SBa(f"xin{i}", [128, 1024], F32) for i in range(2)]
    b_xin = [Buf(), Buf()]
    scr1 = SBa("scr1", [128, 1024], F32)
    b_scr1 = Buf()
    q_tok = SBa("q_tok", [128, 1024], F32)
    b_qtok = Buf()
    kn_f32 = SBa("kn_f32", [128, 1024], F32)
    b_kn = Buf()
    ksq_f32 = SBa("ksq_f32", [128, 1024], F32)
    b_ksq = Buf()
    k_raw = SBa("k_raw", [128, 1024], F32)
    b_kraw = Buf()
    q_ch = [SBa(f"q_ch{i}", [128, 1024], BF16) for i in range(2)]
    b_q = [Buf(), Buf()]
    qt_ch = [SBa(f"qt_ch{i}", [128, 1024], BF16) for i in range(2)]
    b_qt = [Buf(), Buf()]
    knT_ch = [SBa(f"knT_ch{i}", [128, 1024], BF16) for i in range(2)]
    b_knT = [Buf(), Buf()]
    kh_ch = [SBa(f"kh_ch{i}", [128, 1024], BF16) for i in range(2)]
    b_kh = [Buf(), Buf()]
    v_ch = [SBa(f"v_ch{i}", [128, 1024], BF16) for i in range(2)]
    b_v = [Buf(), Buf()]
    a_scr = [SBa(f"a_scr{i}", [128, 256], BF16) for i in range(2)]
    b_a = [Buf(), Buf()]
    st_bf = SBa("st_bf", [128, 512], BF16)
    b_stb = Buf()
    y_stage = [SBa(f"y_stage{i}", [128, 4096], BF16) for i in range(2)]
    b_yst = [Buf(), Buf()]

    dq_in = [xin[0], xin[1], SBa("dq_in2", [128, 1024], F32)]
    b_dqin = [b_xin[0], b_xin[1], Buf()]
    dq_s = [SBa(f"dq_s{i}", [128, 96], F32) for i in range(2)]
    b_dqs = [Buf(), Buf()]
    dq_u = [SBa("dq_u0", [128, 1024], F32), scr1]
    b_dqu = [Buf(), b_scr1]
    dq_nat = [SBa(f"dq_nat{i}", [128, 1024], BF16) for i in range(2)]
    b_dqn = [Buf(), Buf()]
    dq_tr = [SBa(f"dq_tr{i}", [128, 1024], BF16) for i in range(2)]
    b_dqt = [Buf(), Buf()]

    p_q = PSa("p_q", [128, 1024], F32)
    b_pq = Buf()
    p_k = PSa("p_k", [128, 1024], F32)
    b_pk = Buf()
    p_v = PSa("p_v", [128, 1024], F32)
    b_pv = Buf()
    p_r = [PSa(f"p_r{i}", [128, 512], F32) for i in range(2)]
    b_pr = [Buf(), Buf()]

    for (dst, src) in [(lgp8_sb, lgp8), (lg16_sb, lg16), (emat_sb, emat),
                       (utri_sb, utri), (rampq_sb, rampq), (rampk_sb, rampk),
                       (ident_sb, ident)]:
        E("sy", lambda g, d=dst, s=src: g.dma_start(out=d[:], in_=s[:]),
          wr2=[b_const], dma="cst")
    if not SKIP_WQKV:
        for dc in range(NDC):
            E("gp", lambda g, dc=dc: g.dma_start(
                out=wqkv_sb[:, dc * 3072:(dc + 1) * 3072],
                in_=wqkvT[dc * 128:(dc + 1) * 128, :]),
              wr=([b_wqkv] if dc == 0 else []),
              wr2=([b_wqkv] if dc > 0 else []), dma=f"wq{dc % 2}")
    E("gp", lambda g: g.memset(ccsb[:], 0.0), wr=[b_ccsb])
    E("gp", lambda g: g.memset(cmag[:], CMAGIC), wr=[b_cmag])
    E("gp", lambda g: g.dma_start(out=cc_in.ap(), in_=ccsb[:]),
      rd=[b_ccsb], wr=[b_ccin], dma="ccd")
    for fn in (AF.Sigmoid, AF.Ln):
        E("sc", lambda g, fn=fn: g.activation(lg16_sb[:], lg16_sb[:], fn),
          rd=[b_const], wr=[b_const])
        E("sc", lambda g, fn=fn: g.activation(lgp8_sb[:], lgp8_sb[:], fn),
          rd=[b_const], wr=[b_const])
    E("sc", lambda g: g.activation(gcp[:], lgp8_sb[:], AF.Exp, scale=128.0),
      rd=[b_const], wr2=[b_tab])
    for h in range(H if not SKIP_TABLES else 0):
        E("ve", lambda g, h=h: g.tensor_scalar(
            scr1[:, 0:128], emat_sb[:], lg16_sb[:, h:h + 1],
            None, ALU.mult), rd=[b_const], wr=[b_scr1])
        E("sc", lambda g, h=h: g.activation(
            gam[:, h * 128:(h + 1) * 128], scr1[:, 0:128],
            AF.Exp), rd=[b_scr1], wr2=[b_tab])
        E("ve", lambda g, h=h: g.tensor_tensor(
            gam[:, h * 128:(h + 1) * 128], gam[:, h * 128:(h + 1) * 128],
            utri_sb[:], ALU.mult), rd=[b_tab, b_const], wr2=[b_tab])
        E("ve", lambda g, h=h: g.tensor_scalar(
            scr1[:, 128:192], rampk_sb[:], lg16_sb[:, h:h + 1],
            None, ALU.mult), rd=[b_const], wr=[b_scr1])
        E("sc", lambda g, h=h: g.activation(
            ck[:, h * 64:(h + 1) * 64], scr1[:, 128:192], AF.Exp),
          rd=[b_scr1], wr2=[b_tab])
    for oc in range(8 if not SKIP_TABLES else 0):
        E("ve", lambda g, oc=oc: g.tensor_scalar(
            scr1[:, 0:128], rampq_sb[:], lgp8_sb[:, oc:oc + 1],
            None, ALU.mult), rd=[b_const], wr=[b_scr1])
        E("sc", lambda g, oc=oc: g.activation(
            cq[:, oc * 128:(oc + 1) * 128], scr1[:, 0:128],
            AF.Exp), rd=[b_scr1], wr2=[b_tab])
    E("gp", lambda g: g.memset(st_bf[:], 0.0), wr=[b_stb])

    tab_waits = [(P.ctr["sc"], P.ctr["sc"].c), (P.ctr["ve"], P.ctr["ve"].c)]

    for i in range(P0_TILES):
        par = i % 2
        t0 = i * CH
        E("sy", lambda g, i=i, par=par: g.dma_start(
            out=xin[par][:], in_=x_nat[i * CH:(i + 1) * CH, :]),
          wr=[b_xin[par]], dma=f"xin{par}")
        E("sc", lambda g, par=par: g.activation(scr1[:], xin[par][:],
                                                AF.Square),
          rd=[b_xin[par]], wr=[b_scr1])
        E("ve", lambda g, par=par: g.tensor_reduce(
            col(0), xin[par][:], axis=AXX, op=ALU.add),
          rd=[b_xin[par]], wr=[b_stat])
        E("ve", lambda g: g.tensor_reduce(col(1), scr1[:], axis=AXX,
                                          op=ALU.add),
          rd=[b_scr1], wr=[b_stat])
        E("ve", lambda g: g.tensor_scalar(col(2), col(0), 1.0 / 1024.0, None,
                                          ALU.mult), rd=[b_stat], wr=[b_stat])
        E("ve", lambda g: g.tensor_tensor(col(3), col(2), col(2), ALU.mult),
          rd=[b_stat], wr=[b_stat])
        E("ve", lambda g: g.scalar_tensor_tensor(
            col(4), col(1), 1.0 / 1024.0, col(3), ALU.mult, ALU.subtract),
          rd=[b_stat], wr=[b_stat])
        E("ve", lambda g: g.tensor_scalar(col(4), col(4), LN_EPS, None,
                                          ALU.add), rd=[b_stat], wr=[b_stat])
        E("sc", lambda g: g.activation(col(5), col(4), AF.Ln),
          rd=[b_stat], wr=[b_stat])
        E("sc", lambda g: g.activation(col(6), col(5), AF.Exp, scale=-0.5),
          rd=[b_stat], wr=[b_stat])
        E("ve", lambda g, par=par: g.tensor_scalar(
            scr1[:], xin[par][:], col(2), col(6), ALU.subtract, ALU.mult),
          rd=[b_xin[par], b_stat], wr=[b_scr1])
        E("ve", lambda g: g.tensor_reduce(
            col(7), scr1[:], axis=AXX, op=ALU.max, apply_absolute_value=True),
          rd=[b_scr1], wr=[b_stat])
        E("ve", lambda g, first=(i == 0): g.tensor_tensor(
            sc_(14), col(7) if first else sc_(14), col(7), ALU.max),
          rd=[b_stat], wr=[b_mx])
        for dc in range(NDC):
            E("te", lambda g, dc=dc: g.transpose(
                p_q[:, dc * 128:(dc + 1) * 128],
                scr1[:, dc * 128:(dc + 1) * 128], ident_sb[:]),
              rd=[b_scr1, b_const], wr=([b_pq] if dc == 0 else []),
              wr2=([b_pq] if dc > 0 else []))
        E("sc", lambda g, t0=t0: g.copy(
            bass.AP(xnT, t0, [[NDC * L, 128], [L, NDC], [1, CH]]),
            p_q[:].rearrange("p (dc t) -> p dc t", t=CH)),
          rd=[b_pq], wr2=[b_xn[i]])

    def allreduce_max(cols_in, in_bufs, psum, b_psum):
        for j, cin in enumerate(cols_in):
            E("gp", lambda g, j=j, cin=cin: g.dma_start(
                out=pmax_sp.ap()[j, :], in_=cin),
              rd=in_bufs, wr=[b_pmax], dma="ccd")
            E("gp", lambda g, j=j: g.dma_start(
                out=pmrow[:], in_=pmax_sp.ap()[j:j + 1, :]),
              rd=[b_pmax], wr=[b_pmrow], dma="ccd")
            E("ve", lambda g, j=j: g.tensor_reduce(
                ccsb[0:1, j:j + 1], pmrow[:], axis=AXX, op=ALU.max),
              rd=[b_pmrow], wr=([b_ccsb] if j == 0 else []),
              wr2=([b_ccsb] if j > 0 else []))
            E("gp", lambda g, j=j: g.dma_start(
                out=cc_in.ap()[0:1, j:j + 1], in_=ccsb[0:1, j:j + 1]),
              rd=[b_ccsb], wr=([b_ccin] if j == 0 else []),
              wr2=([b_ccin] if j > 0 else []), dma="ccd")
        E("gp", lambda g: g.collective_compute(
            "AllReduce", ALU.max, replica_groups=[list(range(NCORES))],
            ins=[cc_in.ap().opt()], outs=[cc_out.ap().opt()]),
          rd=[b_ccin], wr=[b_ccout], cc=True)
        E("gp", lambda g: g.dma_start(out=ccsb[:], in_=cc_out.ap()),
          rd=[b_ccout], wr=[b_ccsb], dma="ccd")

    def bcast_scale(j_src, c_s, c_sinv, psum, b_psum):
        E("te", lambda g, j=j_src, psum=psum: g.matmul(
            psum[:, 0:1], utri_sb[0:1, :], ccsb[0:1, j:j + 1],
            start=True, stop=True),
          rd=[b_ccsb, b_const], wr=[b_psum])
        E("ve", lambda g, c_s=c_s, psum=psum: g.tensor_scalar(
            sc_(c_s), psum[:, 0:1], 1.0 / QMAX, 1e-12, ALU.mult, ALU.max),
          rd=[b_psum], wr2=[b_scl])
        E("ve", lambda g, c_s=c_s, c_i=c_sinv: g.reciprocal(sc_(c_i),
                                                            sc_(c_s)),
          rd=[b_scl], wr2=[b_scl])

    def ar1_issue():
        j = 0
        E("gp", lambda g: g.dma_start(out=pmax_sp.ap()[j, :], in_=sc_(14)),
          rd=[b_mx], wr=[b_pmax], dma="ccd")
        E("gp", lambda g: g.dma_start(out=pmrow[:], in_=pmax_sp.ap()[j:j + 1, :]),
          rd=[b_pmax], wr=[b_pmrow], dma="ccd")
        E("ve", lambda g: g.tensor_reduce(
            ccsb[0:1, j:j + 1], pmrow[:], axis=AXX, op=ALU.max),
          rd=[b_pmrow], wr=[b_ccsb])
        E("gp", lambda g: g.dma_start(
            out=cc_in.ap()[0:1, j:j + 1], in_=ccsb[0:1, j:j + 1]),
          rd=[b_ccsb], wr=[b_ccin], dma="ccd")
        E("gp", lambda g: g.collective_compute(
            "AllReduce", ALU.max, replica_groups=[list(range(NCORES))],
            ins=[cc_in.ap().opt()], outs=[cc_out.ap().opt()]),
          rd=[b_ccin], wr=[b_ccout], cc=True)

    def ar1_consume():
        E("gp", lambda g: g.dma_start(
            out=sc_(0), in_=bass.AP(cc_out, 0, [[0, 128], [1, 1]])),
          rd=[b_ccout], wr2=[b_scl], dma="ccd")
        E("ve", lambda g: g.tensor_scalar(
            sc_(0), sc_(0), 1.0 / QMAX, 1e-12, ALU.mult, ALU.max),
          rd=[b_scl], wr2=[b_scl])
        E("ve", lambda g: g.reciprocal(sc_(1), sc_(0)),
          rd=[b_scl], wr2=[b_scl])

    if debug in ("p0", "p0nocc", "ar1"):
        if debug != "p0nocc":
            allreduce_max([sc_(14)], [b_mx], p_r[0][:, 256:384], b_pr[0])
            bcast_scale(0, 0, 1, p_r[0][:, 256:384], b_pr[0])

    if debug == "ar1":
        E("gp", lambda g: g.dma_start(out=dbg_xn.ap()[:, 0:16], in_=scl[:]),
          rd=[b_scl], dma="dbg")
        P.wait_all_dma("gp")
        P.wait_all_dma("sy")
        return nc, P, locals()

    if debug in ("p0", "p0nocc"):
        E("sy", lambda g: g.dma_start(out=dbg_xn.ap(), in_=xnT[:]),
          rd=list(b_xn), dma="dbg")
        P.wait_all_dma("sy")
        return nc, P, locals()

    dq_done = {}
    dq_cnt = [0]

    def dq_load(t):
        mat, src_ap, dd, oc = dq_plan[t]
        E("sy", lambda g, t=t, src_ap=src_ap: g.dma_start(
            out=dq_in[t % 3][:], in_=src_ap), wr=[b_dqin[t % 3]], dma="dqi")

    def dq_ab(t):
        i3, i2 = t % 3, t % 2
        E("ve", lambda g, i3=i3, i2=i2: g.tensor_reduce(
            dq_s[i2][:, 0:32],
            dq_in[i3][:].rearrange("p (b k) -> p b k", k=32),
            axis=AXX, op=ALU.max, apply_absolute_value=True),
          rd=[b_dqin[i3]], wr=[b_dqs[i2]])
        E("ve", lambda g, i2=i2: g.tensor_scalar(
            dq_s[i2][:, 32:64], dq_s[i2][:, 0:32], 1.0 / 7.0, 1e-12,
            ALU.mult, ALU.max), rd=[b_dqs[i2]], wr2=[b_dqs[i2]])
        E("ve", lambda g, i2=i2: g.reciprocal(
            dq_s[i2][:, 64:96], dq_s[i2][:, 32:64]),
          rd=[b_dqs[i2]], wr2=[b_dqs[i2]])
        E("gp", lambda g, i3=i3, i2=i2: g.tensor_tensor(
            dq_u[i2][:], dq_in[i3][:],
            bass.AP(dq_s[i2], 64, [[96, 128], [1, 32], [0, 32]]), ALU.mult),
          rd=[b_dqin[i3], b_dqs[i2]], wr=[b_dqu[i2]])

    def dq_cd(t):
        mat, src_ap, dd, oc = dq_plan[t]
        i2 = t % 2
        E("ve", lambda g, i2=i2: g.tensor_scalar(
            dq_u[i2][:], dq_u[i2][:], CMAGIC, -CMAGIC, ALU.add, ALU.add),
          rd=[b_dqu[i2]], wr=[b_dqu[i2]])
        E("gp", lambda g, i2=i2: g.tensor_tensor(
            dq_nat[i2][:], dq_u[i2][:],
            bass.AP(dq_s[i2], 32, [[96, 128], [1, 32], [0, 32]]), ALU.mult),
          rd=[b_dqu[i2], b_dqs[i2]], wr=[b_dqn[i2]])
        for dc in range(NDC):
            E("sy", lambda g, dc=dc, i2=i2: g.dma_start(
                out=dq_tr[i2][:, dc * 128:(dc + 1) * 128],
                in_=dq_nat[i2][:, dc * 128:(dc + 1) * 128], transpose=True),
              rd=[b_dqn[i2]], wr=([b_dqt[i2]] if dc == 0 else []),
              wr2=([b_dqt[i2]] if dc > 0 else []), dma=f"dqt{i2}_{dc % 4}")
        tout = E("gp", lambda g, i2=i2, oc=oc, dst=dd: g.dma_start(
            out=dst.ap()[oc, :, :], in_=dq_tr[i2][:]),
          rd=[b_dqt[i2]], dma="dqo")
        dq_done[(mat, oc)] = (P.dma_ctr("dqo"), tout)

    dq_plan = []
    for oc in range(NIC):
        dq_plan.append(("up", w_up[oc * 128:(oc + 1) * 128, :], wupdq_sp, oc))
        dq_plan.append(("gate", w_gate[oc * 128:(oc + 1) * 128, :],
                        wgatedq_sp, oc))
    NDQ = len(dq_plan)
    dq_ld = [0]
    dq_abi = [0]
    dq_cdi = [0]

    def dq_target(c):
        return max(0, min(NDQ, (c + 1) * NDQ // NCH))

    def emit_dq_stages(c):
        if SKIP_DQ:
            return
        while dq_cdi[0] < dq_target(c - 2):
            dq_cd(dq_cdi[0]); dq_cdi[0] += 1
        while dq_abi[0] < dq_target(c - 1):
            dq_ab(dq_abi[0]); dq_abi[0] += 1
        while dq_ld[0] < dq_target(c):
            dq_load(dq_ld[0]); dq_ld[0] += 1

    def emit_proj(c, which):
        t0 = c * CH
        ps, b_ps, off = which
        for half in range(2):
            for dc in range(NDC):
                E("te", lambda g, ps=ps, half=half, dc=dc, t0=t0, off=off:
                  g.matmul(
                      ps[:, half * 512:(half + 1) * 512],
                      xnT[:, dc * L + t0: dc * L + t0 + CH],
                      wqkv_sb[:, dc * 3072 + off + half * 512:
                              dc * 3072 + off + (half + 1) * 512],
                      start=(dc == 0), stop=(dc == NDC - 1)),
                  rd=[b_xn[c], b_wqkv],
                  wr=([b_ps] if (half == 0 and dc == 0) else []),
                  wr2=([b_ps] if (half == 1 and dc == NDC - 1) else []),
                  noinc=(dc != NDC - 1))

    def emit_kln_apply(c):
        par = c % 2
        E("sc", lambda g: g.copy(q_tok[:], p_q[:]), rd=[b_pq], wr=[b_qtok])
        E("sc", lambda g, par=par: g.copy(v_ch[par][:], p_v[:]),
          rd=[b_pv], wr=[b_v[par]])
        E("ve", lambda g: g.tensor_tensor(
            kn_f32[:], k_raw[:],
            bass.AP(kst, 32, [[80, 128], [1, 16], [0, 64]]), ALU.subtract),
          rd=[b_kraw, b_kst], wr=[b_kn])
        E("ve", lambda g: g.tensor_tensor(
            kn_f32[:], kn_f32[:],
            bass.AP(kst, 48, [[80, 128], [1, 16], [0, 64]]), ALU.mult),
          rd=[b_kst], wr2=[b_kn])
        E("ve", lambda g, par=par: g.tensor_tensor(
            kh_ch[par][:], kn_f32[:], ck[:], ALU.mult),
          rd=[b_kn], wr=[b_kh[par]], waits=tab_waits)

    def emit_kln_pre(c):
        tks_ = E("ve", lambda g: g.tensor_reduce(
            kst[:, 0:16], p_k[:].rearrange("p (h e) -> p h e", e=DK),
            axis=AXX, op=ALU.add), rd=[b_pk], wr=[b_kst])
        E("sc", lambda g: g.copy(k_raw[:], p_k[:]), rd=[b_pk], wr=[b_kraw],
          waits=[(P.ctr["ve"], tks_)])
        E("sc", lambda g: g.activation(ksq_f32[:], k_raw[:], AF.Square),
          rd=[b_kraw], wr=[b_ksq])
        E("ve", lambda g: g.tensor_reduce(
            kst[:, 16:32], ksq_f32[:].rearrange("p (h e) -> p h e", e=DK),
            axis=AXX, op=ALU.add), rd=[b_ksq], wr2=[b_kst])
        E("ve", lambda g: g.tensor_scalar(
            kst[:, 32:48], kst[:, 0:16], 1.0 / DK, None, ALU.mult),
          rd=[b_kst], wr2=[b_kst])
        E("ve", lambda g: g.tensor_tensor(
            kst[:, 64:80], kst[:, 32:48], kst[:, 32:48], ALU.mult),
          rd=[b_kst], wr2=[b_kst])
        E("ve", lambda g: g.scalar_tensor_tensor(
            kst[:, 48:64], kst[:, 16:32], 1.0 / DK, kst[:, 64:80],
            ALU.mult, ALU.subtract), rd=[b_kst], wr2=[b_kst])
        E("ve", lambda g: g.tensor_scalar(
            kst[:, 48:64], kst[:, 48:64], LN_EPS, None, ALU.add),
          rd=[b_kst], wr2=[b_kst])
        E("sc", lambda g: g.activation(kst[:, 48:64], kst[:, 48:64], AF.Ln),
          rd=[b_kst], wr2=[b_kst])
        E("sc", lambda g: g.activation(
            kst[:, 48:64], kst[:, 48:64], AF.Exp, scale=-0.5),
          rd=[b_kst], wr2=[b_kst])

    def emit_transposes(c):
        par = c % 2
        for j in range(NDC):
            E("te", lambda g, j=j: g.transpose(
                p_v[:, j * 128:(j + 1) * 128], q_tok[:, j * 128:(j + 1) * 128],
                ident_sb[:]), rd=[b_qtok, b_const],
              wr=([b_pv] if j == 0 else []), wr2=([b_pv] if j > 0 else []))
        tqt_ = E("ve", lambda g, par=par: g.tensor_tensor(
            qt_ch[par][:], p_v[:], cq[:], ALU.mult),
          rd=[b_pv], wr=[b_qt[par]], waits=tab_waits)
        E("sc", lambda g, par=par: g.copy(q_ch[par][:], p_v[:]),
          rd=[b_pv], wr=[b_q[par]], waits=[(P.ctr["ve"], tqt_)])
        for j in range(NDC):
            E("te", lambda g, j=j: g.transpose(
                p_v[:, j * 128:(j + 1) * 128], kn_f32[:, j * 128:(j + 1) * 128],
                ident_sb[:]), rd=[b_kn, b_const],
              wr=([b_pv] if j == 0 else []), wr2=([b_pv] if j > 0 else []))
        E("sc", lambda g, par=par: g.copy(knT_ch[par][:], p_v[:]),
          rd=[b_pv], wr=[b_knT[par]])

    def emit_retention(c):
        par = c % 2
        gpar = (c // 4) % 2
        for p in range(8):
            rpar = p % 2
            for half in range(2):
                E("te", lambda g, p=p, half=half, par=par, rpar=rpar: g.matmul(
                    p_r[rpar][:, half * 128:(half + 1) * 128],
                    knT_ch[par][64 * half:64 * (half + 1),
                                p * 128:(p + 1) * 128],
                    q_ch[par][64 * half:64 * (half + 1),
                              p * 128:(p + 1) * 128],
                    start=True, stop=True, tile_position=(64 * half, 0)),
                  rd=[b_knT[par], b_q[par]],
                  wr=([b_pr[rpar]] if half == 0 else []),
                  wr2=([b_pr[rpar]] if half == 1 else []))
            E("ve", lambda g, p=p, rpar=rpar: g.tensor_tensor(
                a_scr[rpar][:], p_r[rpar][:, 0:256],
                gam[:, p * 256:(p + 1) * 256], ALU.mult),
              rd=[b_pr[rpar]], wr=[b_a[rpar]], waits=tab_waits)
            for half in range(2):
                h = 2 * p + half
                E("te", lambda g, h=h, half=half, par=par, rpar=rpar: g.matmul(
                    p_r[rpar][64 * half:64 * (half + 1), 256:384],
                    v_ch[par][:, h * 64:(h + 1) * 64],
                    a_scr[rpar][:, half * 128:(half + 1) * 128],
                    start=True, stop=False, tile_position=(0, 64 * half)),
                  rd=[b_v[par], b_a[rpar]], wr2=[b_pr[rpar]], noinc=True)
                E("te", lambda g, p=p, half=half, par=par, rpar=rpar: g.matmul(
                    p_r[rpar][64 * half:64 * (half + 1), 256:384],
                    st_bf[64 * half:64 * (half + 1), p * 64:(p + 1) * 64],
                    qt_ch[par][64 * half:64 * (half + 1),
                               p * 128:(p + 1) * 128],
                    start=False, stop=True,
                    tile_position=(64 * half, 64 * half)),
                  rd=[b_stb, b_qt[par]], wr2=[b_pr[rpar]])
            for half in range(2):
                h = 2 * p + half
                E("te", lambda g, h=h, half=half, par=par, rpar=rpar: g.matmul(
                    p_r[rpar][64 * half:64 * (half + 1), 384:448],
                    kh_ch[par][:, h * 64:(h + 1) * 64],
                    v_ch[par][:, h * 64:(h + 1) * 64],
                    start=True, stop=True, tile_position=(0, 64 * half)),
                  rd=[b_kh[par], b_v[par]], wr2=[b_pr[rpar]])
            tye_ = E("sc", lambda g, p=p, rpar=rpar, c=c, gpar=gpar: g.copy(
                y_stage[gpar][:, p * 512 + (c % 4) * 128:
                              p * 512 + (c % 4) * 128 + 128],
                p_r[rpar][:, 256:384]),
              rd=[b_pr[rpar]],
              wr=([b_yst[gpar]] if (p == 0 and c % 4 == 0) else []),
              wr2=([b_yst[gpar]] if not (p == 0 and c % 4 == 0) else []))
            E("ve", lambda g, p=p, rpar=rpar: g.scalar_tensor_tensor(
                st_bf[:, p * 64:(p + 1) * 64], st_bf[:, p * 64:(p + 1) * 64],
                gcp[:, p:p + 1], p_r[rpar][:, 384:448], ALU.mult, ALU.add),
              rd=[b_stb, b_pr[rpar]], wr=[b_stb],
              waits=tab_waits + [(P.ctr["sc"], tye_)])

    def emit_tail(c):
        par = c % 2
        gpar = (c // 4) % 2
        if c % 4 == 3:
            gidx = c // 4
            for pr in range(8):
                E("gp", lambda g, gidx=gidx, gpar=gpar, pr=pr: g.dma_start(
                    out=y_sp.ap()[pr, :, gidx * 4 * CH:(gidx + 1) * 4 * CH],
                    in_=y_stage[gpar][:, pr * 512:(pr + 1) * 512]),
                  rd=[b_yst[gpar]], wr2=[b_ysp],
                  dma=f"ysp{(gpar, pr % 2)}")
        emit_dq_stages(c)
        if c == 2 and debug not in ("p0", "p0nocc", "ar1"):
            ar1_issue()

    QKV = [(p_q, b_pq, 0), (p_k, b_pk, 1024), (p_v, b_pv, 2048)]
    emit_proj(0, QKV[0])
    emit_proj(0, QKV[1])
    emit_kln_pre(0)
    for c in range(P1_CHUNKS):
        emit_proj(c, QKV[2])
        emit_kln_apply(c)
        if c + 1 < P1_CHUNKS:
            emit_proj(c + 1, QKV[0])
            emit_proj(c + 1, QKV[1])
        emit_transposes(c)
        if c + 1 < P1_CHUNKS:
            emit_kln_pre(c + 1)
        emit_retention(c)
        emit_tail(c)
    for c in range(P1_CHUNKS, P1_CHUNKS + 3):
        emit_dq_stages(c)
    if debug not in ("p0", "p0nocc", "ar1"):
        ar1_consume()

    if XQ_BARRIER and not SKIP_XQ:
        P.barrier()
    for dc in range(NDC if not SKIP_XQ else 0):
        for half in range(4):
            base = dc * L + half * 1024
            grp = list(b_xn)
            E("ve", lambda g, base=base: g.tensor_scalar(
                ksq_f32[:], xnT[:, base:base + 1024], sc_(1), CMAGIC,
                ALU.mult, ALU.add),
              rd=[b_scl] + grp, wr=[b_ksq])
            E("ve", lambda g, base=base: g.tensor_scalar(
                xnT[:, base:base + 1024], ksq_f32[:], CMAGIC, sc_(0),
                ALU.subtract, ALU.mult),
              rd=[b_ksq, b_scl],
              wr=(grp if (dc == 0 and half == 0) else []),
              wr2=(grp if not (dc == 0 and half == 0) else []))

    if debug == "p1":
        P.wait_all_dma("sy")
        return nc, P, locals()

    P.barrier()
    es_a.close()
    es_b = contextlib.ExitStack()
    SBb = lambda n, s, d: es_b.enter_context(nc.sbuf_tensor(n, s, d))
    PSb = lambda n, s, d: es_b.enter_context(nc.psum_tensor(n, s, d))

    wst = [SBb(f"wst{i}", [128, 1024], BF16) for i in range(4)]
    b_wst = [Buf() for _ in range(4)]
    stg = [SBb(f"stg{i}", [128, 512], BF16) for i in range(4)]
    b_stg = [Buf() for _ in range(4)]
    dqdw_in = [SBb(f"dqdw_in{i}", [128, 3072], F32) for i in range(2)]
    b_dqdwin = [Buf(), Buf()]
    dqd_s = SBb("dqd_s", [128, 288], F32)
    b_dqds = Buf()
    dqd_u = SBb("dqd_u", [128, 3072], F32)
    b_dqdu = Buf()
    dqd_nat = SBb("dqd_nat", [128, 3072], BF16)
    b_dqdn = Buf()
    dqd_tr = [SBb(f"dqd_tr{i}", [128, 3072], BF16) for i in range(2)]
    b_dqdt = [Buf(), Buf()]
    p_f = [PSb(f"p_f{i}", [128, 512], F32) for i in range(4)]
    b_pf = [Buf() for _ in range(4)]

    dqd_done = {}

    def dqd_ab(t):
        par = t % 2
        E("sy", lambda g, t=t, par=par: g.dma_start(
            out=dqdw_in[par][:], in_=w_down[t * 128:(t + 1) * 128, :]),
          wr=[b_dqdwin[par]], dma="dqdi")
        E("ve", lambda g, par=par: g.tensor_reduce(
            dqd_s[:, 0:96], dqdw_in[par][:].rearrange("p (b k) -> p b k",
                                                      k=32),
            axis=AXX, op=ALU.max, apply_absolute_value=True),
          rd=[b_dqdwin[par]], wr=[b_dqds])
        E("ve", lambda g: g.tensor_scalar(
            dqd_s[:, 96:192], dqd_s[:, 0:96], 1.0 / 7.0, 1e-12, ALU.mult,
            ALU.max), rd=[b_dqds], wr2=[b_dqds])
        E("ve", lambda g: g.reciprocal(dqd_s[:, 192:288], dqd_s[:, 96:192]),
          rd=[b_dqds], wr2=[b_dqds])
        E("gp", lambda g, par=par: g.tensor_tensor(
            dqd_u[:], dqdw_in[par][:],
            bass.AP(dqd_s, 192, [[288, 128], [1, 96], [0, 32]]), ALU.mult),
          rd=[b_dqdwin[par], b_dqds], wr=[b_dqdu])

    def dqd_cd(t):
        par = t % 2
        E("ve", lambda g: g.tensor_scalar(
            dqd_u[:], dqd_u[:], CMAGIC, -CMAGIC, ALU.add, ALU.add),
          rd=[b_dqdu], wr=[b_dqdu])
        E("gp", lambda g: g.tensor_tensor(
            dqd_nat[:], dqd_u[:],
            bass.AP(dqd_s, 96, [[288, 128], [1, 96], [0, 32]]), ALU.mult),
          rd=[b_dqdu, b_dqds], wr=[b_dqdn])
        for ic in range(NIC):
            E("sy", lambda g, ic=ic, par=par: g.dma_start(
                out=dqd_tr[par][:, ic * 128:(ic + 1) * 128],
                in_=dqd_nat[:, ic * 128:(ic + 1) * 128], transpose=True),
              rd=[b_dqdn], wr=([b_dqdt[par]] if ic == 0 else []),
              wr2=([b_dqdt[par]] if ic > 0 else []), dma=f"dqdt{par}_{ic % 4}")
        tout = E("gp", lambda g, par=par, t=t: g.dma_start(
            out=wdndq_sp.ap()[t, :, :], in_=dqd_tr[par][:]),
          rd=[b_dqdt[par]], dma="dqdo")
        dqd_done[t] = (P.dma_ctr("dqdo"), tout)

    dqd_emitted = [0]
    mats2b = [("up", wupdq_sp, up_sp, b_upsp, 10),
              ("gate", wgatedq_sp, gate_sp, b_gatesp, 11)]
    ldplan = [(m[0], m[1], oc) for m in mats2b for oc in range(NIC)]

    def load_wst(k):
        mname, wsp, oc = ldplan[k]
        E("sy", lambda g, oc=oc, wsp=wsp, k=k: g.dma_start(
            out=wst[k % 4][:], in_=wsp.ap()[oc, :, :]),
          wr=[b_wst[k % 4]], waits=[dq_done[(mname, oc)]], dma=f"wst{k % 4}")

    load_wst(0)
    load_wst(1)
    kk = [2]
    for (mname, wsp, osp, b_osp_, mxcol) in mats2b:
        for oc in range(NIC):
            wb = b_wst[oc % 4]
            if kk[0] < len(ldplan):
                load_wst(kk[0])
                kk[0] += 1
            for slab in range(NSLAB):
                pf = (oc * NSLAB + slab) % 4
                for dc in range(NDC):
                    E("te", lambda g, oc=oc, slab=slab, dc=dc, pf=pf: g.matmul(
                        p_f[pf][:], wst[oc % 4][:, dc * 128:(dc + 1) * 128],
                        xnT[:, dc * L + slab * SLAB:
                            dc * L + (slab + 1) * SLAB],
                        start=(dc == 0), stop=(dc == NDC - 1)),
                      rd=[wb] + (list(b_xn) if (oc == 0 and slab == 0)
                                 else []),
                      wr=([b_pf[pf]] if dc == 0 else []),
                      wr2=([b_pf[pf]] if dc == NDC - 1 else []),
                      noinc=(dc != NDC - 1))
                E("sc", lambda g, pf=pf: g.copy(stg[pf][:], p_f[pf][:]),
                  rd=[b_pf[pf]], wr=[b_stg[pf]])
                E("ve", lambda g, pf=pf: g.tensor_reduce(
                    col(7), stg[pf][:], axis=AXX, op=ALU.max,
                    apply_absolute_value=True),
                  rd=[b_stg[pf]], wr=[b_stat])
                E("ve", lambda g, first=(oc == 0 and slab == 0), mxcol=mxcol:
                  g.tensor_tensor(
                      sc_(mxcol), col(7) if first else sc_(mxcol), col(7),
                      ALU.max), rd=[b_stat], wr2=[b_mx])
                E("gp", lambda g, oc=oc, slab=slab, pf=pf, osp=osp:
                  g.dma_start(
                      out=osp.ap()[oc, :, slab * SLAB:(slab + 1) * SLAB],
                      in_=stg[pf][:]),
                  rd=[b_stg[pf]], wr2=[b_osp_], dma=f"spl{pf}")
            if oc % 3 == 2:
                if 0 < dqd_emitted[0] <= NDC:
                    dqd_cd(dqd_emitted[0] - 1)
                if dqd_emitted[0] < NDC:
                    dqd_ab(dqd_emitted[0])
                dqd_emitted[0] += 1
    while dqd_emitted[0] <= NDC:
        if 0 < dqd_emitted[0] <= NDC:
            dqd_cd(dqd_emitted[0] - 1)
        if dqd_emitted[0] < NDC:
            dqd_ab(dqd_emitted[0])
        dqd_emitted[0] += 1

    if debug == "p2b":
        P.wait_all_dma("sy")
        return nc, P, locals()

    b_mx.note_w(P.ctr["ve"], P.ctr["ve"].c)
    allreduce_max([sc_(10), sc_(11)], [b_mx], p_f[0][:], b_pf[0])
    bcast_scale(0, 2, 3, p_f[0][:], b_pf[0])
    bcast_scale(1, 4, 5, p_f[0][:], b_pf[0])

    P.barrier()
    es_x.close()
    es_b.close()
    es_c = contextlib.ExitStack()
    es_cio = contextlib.ExitStack()
    SBc = lambda n, s, d: es_c.enter_context(nc.sbuf_tensor(n, s, d))
    SBio = lambda n, s, d: es_cio.enter_context(nc.sbuf_tensor(n, s, d))

    attn_sb = SBc("attn_sb", [128, NDC * L], BF16)
    b_attn = Buf()
    woT_sb = SBc("woT_sb", [128, NDC * D], BF16)
    b_wo = Buf()
    y_sl = [SBc(f"y_sl{i}", [128, 4096], BF16) for i in range(2)]
    b_ysl = [Buf(), Buf()]
    p_w = [es_c.enter_context(nc.psum_tensor(f"p_w{i}", [128, 512], F32))
           for i in range(2)]
    b_pw = [Buf(), Buf()]
    io_a = [SBio(f"io_a{i}", [128, 4096], BF16) for i in range(2)]
    b_ioa = [Buf(), Buf()]
    io_b = [SBio(f"io_b{i}", [128, 4096], BF16) for i in range(2)]
    b_iob = [Buf(), Buf()]
    io_c = [SBio(f"io_c{i}", [128, 4096], BF16) for i in range(2)]
    b_ioc = [Buf(), Buf()]
    fs1 = [SBio(f"fs1_{i}", [128, 2048], F32) for i in range(2)]
    b_fs1 = [Buf(), Buf()]
    fs2 = [SBio(f"fs2_{i}", [128, 2048], F32) for i in range(2)]
    b_fs2 = [Buf(), Buf()]
    fs3 = [SBio(f"fs3_{i}", [128, 2048], F32) for i in range(2)]
    b_fs3 = [Buf(), Buf()]

    for dc in range(NDC):
        E("gp", lambda g, dc=dc: g.dma_start(
            out=woT_sb[:, dc * D:(dc + 1) * D],
            in_=woT[dc * 128:(dc + 1) * 128, :]),
          wr=([b_wo] if dc == 0 else []), wr2=([b_wo] if dc > 0 else []),
          dma=f"wo{dc % 2}")

    wo_units = []
    for slab in range(NSLAB):
        wo_units.append(("load", slab))
        for oc in range(NDC):
            wo_units.append(("mm", slab, oc))
    wo_i = [0]

    def emit_wo_unit():
        if wo_i[0] >= len(wo_units):
            return
        u = wo_units[wo_i[0]]
        wo_i[0] += 1
        if u[0] == "load":
            slab = u[1]
            ypar = slab % 2
            for pr in range(8):
                E("sy", lambda g, slab=slab, ypar=ypar, pr=pr: g.dma_start(
                    out=y_sl[ypar][:, pr * 512:(pr + 1) * 512],
                    in_=y_sp.ap()[pr, :, slab * SLAB:(slab + 1) * SLAB]),
                  rd=[b_ysp], wr=([b_ysl[ypar]] if pr == 0 else []),
                  wr2=([b_ysl[ypar]] if pr > 0 else []),
                  dma=f"ysl{(ypar, pr % 2)}")
        else:
            slab, oc = u[1], u[2]
            ypar = slab % 2
            wpar = (slab * NDC + oc) % 2
            for dc in range(NDC):
                E("te", lambda g, oc=oc, dc=dc, ypar=ypar, wpar=wpar: g.matmul(
                    p_w[wpar][:],
                    woT_sb[:, dc * D + oc * 128: dc * D + (oc + 1) * 128],
                    y_sl[ypar][:, dc * 512:(dc + 1) * 512],
                    start=(dc == 0), stop=(dc == NDC - 1)),
                  rd=[b_wo, b_ysl[ypar]],
                  wr=([b_pw[wpar]] if dc == 0 else []),
                  wr2=([b_pw[wpar]] if dc == NDC - 1 else []),
                  noinc=(dc != NDC - 1))
            E("sc", lambda g, oc=oc, slab=slab, wpar=wpar: g.copy(
                attn_sb[:, oc * L + slab * SLAB: oc * L + (slab + 1) * SLAB],
                p_w[wpar][:]), rd=[b_pw[wpar]], wr2=[b_attn])

    for ic in range(NIC):
        par = ic % 2
        E("sy", lambda g, ic=ic, par=par: g.dma_start(
            out=io_a[par][:], in_=gate_sp.ap()[ic, :, :]),
          rd=[b_gatesp], wr=[b_ioa[par]], dma=f"ioa{par}")
        for hh in range(2):
            h0 = hh * 2048
            E("sc", lambda g, par=par, h0=h0: g.activation(
                fs1[par][:], io_a[par][:, h0:h0 + 2048], AF.Identity,
                bias=cmag[:], scale=sc_(5)),
              rd=[b_ioa[par], b_scl, b_cmag], wr=[b_fs1[par]])
            E("ve", lambda g, par=par: g.tensor_scalar(
                fs2[par][:], fs1[par][:], CMAGIC, sc_(4), ALU.subtract,
                ALU.mult),
              rd=[b_fs1[par], b_scl], wr=[b_fs2[par]])
            E("sc", lambda g, par=par: g.activation(fs1[par][:], fs2[par][:],
                                                    AF.Sigmoid),
              rd=[b_fs2[par]], wr=[b_fs1[par]])
            E("sc", lambda g, par=par: g.activation(
                fs3[par][:], fs1[par][:], AF.Identity, bias=cmag[:],
                scale=65536.0),
              rd=[b_fs1[par], b_cmag], wr=[b_fs3[par]])
            E("ve", lambda g, par=par: g.tensor_scalar(
                fs1[par][:], fs3[par][:], -CMAGIC, 65535.0, ALU.add, ALU.min),
              rd=[b_fs3[par]], wr=[b_fs1[par]])
            E("ve", lambda g, par=par: g.tensor_scalar(
                fs3[par][:], fs1[par][:], 0.0, 1.0 / 65536.0, ALU.max,
                ALU.mult),
              rd=[b_fs1[par]], wr=[b_fs3[par]])
            E("gp", lambda g, par=par, h0=h0: g.tensor_tensor(
                io_c[par][:, h0:h0 + 2048], fs2[par][:], fs3[par][:],
                ALU.mult),
              rd=[b_fs2[par], b_fs3[par]],
              wr=([b_ioc[par]] if hh == 0 else []),
              wr2=([b_ioc[par]] if hh else []))
        E("ve", lambda g, par=par: g.tensor_reduce(
            col(7), io_c[par][:], axis=AXX, op=ALU.max,
            apply_absolute_value=True), rd=[b_ioc[par]], wr=[b_stat])
        E("ve", lambda g, first=(ic == 0): g.tensor_tensor(
            sc_(12), col(7) if first else sc_(12), col(7), ALU.max),
          rd=[b_stat], wr2=[b_mx])
        E("gp", lambda g, ic=ic, par=par: g.dma_start(
            out=g2_sp.ap()[ic, :, :], in_=io_c[par][:]),
          rd=[b_ioc[par]], wr2=[b_g2sp], dma=f"ioc{par}")
        for _ in range(3):
            emit_wo_unit()
    while wo_i[0] < len(wo_units):
        emit_wo_unit()

    if debug == "p2d":
        P.wait_all_dma("sy")
        return nc, P, locals()

    b_mx.note_w(P.ctr["ve"], P.ctr["ve"].c)
    allreduce_max([sc_(12)], [b_mx], p_w[0][:], b_pw[0])
    bcast_scale(0, 6, 7, p_w[0][:], b_pw[0])

    for ic in range(NIC):
        par = ic % 2
        E("sy", lambda g, ic=ic, par=par: g.dma_start(
            out=io_a[par][:], in_=up_sp.ap()[ic, :, :]),
          rd=[b_upsp], wr=[b_ioa[par]], dma=f"ioa{par}")
        E("sy", lambda g, ic=ic, par=par: g.dma_start(
            out=io_b[par][:], in_=g2_sp.ap()[ic, :, :]),
          rd=[b_g2sp], wr=[b_iob[par]], dma=f"iob{par}")
        for hh in range(2):
            h0 = hh * 2048
            E("sc", lambda g, par=par, h0=h0: g.activation(
                fs1[par][:], io_a[par][:, h0:h0 + 2048], AF.Identity,
                bias=cmag[:], scale=sc_(3)),
              rd=[b_ioa[par], b_scl, b_cmag], wr=[b_fs1[par]])
            E("ve", lambda g, par=par: g.tensor_scalar(
                fs2[par][:], fs1[par][:], CMAGIC, sc_(2), ALU.subtract,
                ALU.mult),
              rd=[b_fs1[par], b_scl], wr=[b_fs2[par]])
            E("sc", lambda g, par=par, h0=h0: g.activation(
                fs3[par][:], io_b[par][:, h0:h0 + 2048], AF.Identity,
                bias=cmag[:], scale=sc_(7)),
              rd=[b_iob[par], b_scl, b_cmag], wr=[b_fs3[par]])
            E("ve", lambda g, par=par: g.tensor_scalar(
                fs1[par][:], fs3[par][:], CMAGIC, sc_(6), ALU.subtract,
                ALU.mult),
              rd=[b_fs3[par], b_scl], wr=[b_fs1[par]])
            E("gp", lambda g, par=par, h0=h0: g.tensor_tensor(
                io_c[par][:, h0:h0 + 2048], fs1[par][:], fs2[par][:],
                ALU.mult),
              rd=[b_fs1[par], b_fs2[par]],
              wr=([b_ioc[par]] if hh == 0 else []),
              wr2=([b_ioc[par]] if hh else []))
        E("ve", lambda g, par=par: g.tensor_reduce(
            col(7), io_c[par][:], axis=AXX, op=ALU.max,
            apply_absolute_value=True), rd=[b_ioc[par]], wr=[b_stat])
        E("ve", lambda g, first=(ic == 0): g.tensor_tensor(
            sc_(13), col(7) if first else sc_(13), col(7), ALU.max),
          rd=[b_stat], wr2=[b_mx])
        E("gp", lambda g, ic=ic, par=par: g.dma_start(
            out=o_sp.ap()[ic, :, :], in_=io_c[par][:]),
          rd=[b_ioc[par]], wr2=[b_osp], dma=f"ioc{par}")

    if debug == "p2e":
        P.wait_all_dma("sy")
        return nc, P, locals()

    b_mx.note_w(P.ctr["ve"], P.ctr["ve"].c)
    allreduce_max([sc_(13)], [b_mx], p_w[0][:], b_pw[0])
    bcast_scale(0, 8, 9, p_w[0][:], b_pw[0])

    P.barrier()
    es_cio.close()
    es_d = contextlib.ExitStack()
    SBd = lambda n, s, d: es_d.enter_context(nc.sbuf_tensor(n, s, d))
    oq_sl = [SBd(f"oq_sl{i}", [128, NIC * 512], BF16) for i in range(2)]
    b_oq = [Buf(), Buf()]
    wdn = [SBd(f"wdn{i}", [128, NIC * 128], BF16) for i in range(2)]
    b_wdn = [Buf(), Buf()]
    xTt = [SBd(f"xTt{i}", [128, 512], F32) for i in range(2)]
    b_xT = [Buf(), Buf()]
    outt = [SBd(f"outt{i}", [128, 512], F32) for i in range(2)]
    b_out = [Buf(), Buf()]
    ofs = [SBd(f"ofs{i}", [128, 2048], F32) for i in range(2)]
    b_ofs = [Buf(), Buf()]
    ofs2 = [SBd(f"ofs2{i}", [128, 2048], F32) for i in range(2)]
    b_ofs2 = [Buf(), Buf()]
    ep1 = SBd("ep1", [128, 512], F32)
    b_ep1 = Buf()
    p_d = [es_d.enter_context(nc.psum_tensor(f"p_d{i}", [128, 512], F32))
           for i in range(4)]
    b_pd = [Buf() for _ in range(4)]

    wdn_wait = (P.dma_ctr("dqdo"), max(t for (_c, t) in dqd_done.values()))
    last_out = 0
    for slab in range(NSLAB):
        spar = slab % 2
        for ic in range(NIC):
            E("sy", lambda g, ic=ic, slab=slab, spar=spar: g.dma_start(
                out=oq_sl[spar][:, ic * 512:(ic + 1) * 512],
                in_=o_sp.ap()[ic, :, slab * SLAB:(slab + 1) * SLAB]),
              rd=[b_osp], wr=([b_oq[spar]] if ic == 0 else []),
              wr2=([b_oq[spar]] if ic > 0 else []), dma=f"oq{ic % 4}")
        for third in range(6):
            a0, a1 = third * 2048, (third + 1) * 2048
            tp = third % 2
            E("sc", lambda g, spar=spar, a0=a0, a1=a1, tp=tp: g.activation(
                ofs[tp][:], oq_sl[spar][:, a0:a1], AF.Identity, bias=cmag[:],
                scale=sc_(9)), rd=[b_oq[spar], b_scl, b_cmag], wr=[b_ofs[tp]])
            E("ve", lambda g, tp=tp: g.tensor_scalar(
                ofs2[tp][:], ofs[tp][:], -CMAGIC, QMAX, ALU.add, ALU.min),
              rd=[b_ofs[tp]], wr=[b_ofs2[tp]])
            E("ve", lambda g, spar=spar, a0=a0, a1=a1, tp=tp: g.tensor_scalar(
                oq_sl[spar][:, a0:a1], ofs2[tp][:], -QMAX - 1.0, sc_(8),
                ALU.max, ALU.mult),
              rd=[b_ofs2[tp], b_scl], wr2=[b_oq[spar]])
        if slab == 0:
            E("sy", lambda g: g.dma_start(
                out=wdn[0][:], in_=wdndq_sp.ap()[0, :, :]),
              wr=[b_wdn[0]], waits=[wdn_wait], dma="wdn0")
        for oc in range(NDC):
            pd = oc % 4
            wpar2 = (slab * NDC + oc) % 2
            nslab, noc = (slab, oc + 1) if oc + 1 < NDC else (slab + 1, 0)
            if nslab < NSLAB:
                E("sy", lambda g, noc=noc, w2=1 - wpar2: g.dma_start(
                    out=wdn[w2][:], in_=wdndq_sp.ap()[noc, :, :]),
                  wr=[b_wdn[1 - wpar2]], waits=[wdn_wait],
                  dma=f"wdn{1 - wpar2}")
            for ic in range(NIC):
                E("te", lambda g, ic=ic, spar=spar, pd=pd, wpar2=wpar2:
                  g.matmul(
                      p_d[pd][:], wdn[wpar2][:, ic * 128:(ic + 1) * 128],
                      oq_sl[spar][:, ic * 512:(ic + 1) * 512],
                      start=(ic == 0), stop=(ic == NIC - 1)),
                  rd=[b_wdn[wpar2], b_oq[spar]],
                  wr=([b_pd[pd]] if ic == 0 else []),
                  wr2=([b_pd[pd]] if ic == NIC - 1 else []),
                  noinc=(ic != NIC - 1))
            opar = oc % 2
            E("sy", lambda g, oc=oc, slab=slab, opar=opar: g.dma_start(
                out=xTt[opar][:],
                in_=xT[oc * 128:(oc + 1) * 128,
                       slab * SLAB:(slab + 1) * SLAB]),
              wr=[b_xT[opar]], dma=f"xt{opar}")
            E("ve", lambda g, oc=oc, slab=slab, pd=pd: g.tensor_tensor(
                ep1[:], p_d[pd][:],
                attn_sb[:, oc * L + slab * SLAB: oc * L + (slab + 1) * SLAB],
                ALU.add), rd=[b_pd[pd], b_attn], wr=[b_ep1])
            E("ve", lambda g, opar=opar: g.tensor_tensor(
                outt[opar][:], ep1[:], xTt[opar][:], ALU.add),
              rd=[b_ep1, b_xT[opar]], wr=[b_out[opar]])
            last_out = E("gp", lambda g, oc=oc, slab=slab, opar=opar:
                         g.dma_start(
                             out=outT[oc * 128:(oc + 1) * 128,
                                      slab * SLAB:(slab + 1) * SLAB],
                             in_=outt[opar][:]),
                         rd=[b_out[opar]], dma=f"out{opar}")

    P.wait_all_dma("sy")
    return nc, P, locals()


def finalize(nc, P):
    with nc.Block() as blk:
        @blk.sync
        def _(g):
            for f in P.ops["sy"]:
                f(g)

        @blk.scalar
        def _(g):
            for f in P.ops["sc"]:
                f(g)

        @blk.vector
        def _(g):
            for f in P.ops["ve"]:
                f(g)

        @blk.tensor
        def _(g):
            for f in P.ops["te"]:
                f(g)

        @blk.gpsimd
        def _(g):
            for f in P.ops["gp"]:
                f(g)
    return nc


def host_consts():
    idx = np.arange(128, dtype=np.float32)
    emat = idx[None, :] - idx[:, None]
    emat = np.where(emat >= 0, emat, 1e9).astype(np.float32)
    utri = (idx[None, :] >= idx[:, None]).astype(np.float32)
    rampq = np.ascontiguousarray(
        np.broadcast_to(idx[None, :] + 1.0, (128, 128))).astype(np.float32)
    rampk = np.ascontiguousarray(
        np.broadcast_to((127.0 - idx)[:, None], (128, 64))).astype(np.float32)
    return dict(emat=np.ascontiguousarray(emat),
                utri=np.ascontiguousarray(utri), rampq=rampq, rampk=rampk,
                ident=np.eye(128, dtype=np.float32))


def host_inputs(L0_attn_x, L0_attn_W_qkv, L0_attn_W_o, L0_attn_log_gamma,
                L0_ffn_w_gate, L0_ffn_w_up, L0_ffn_w_down):
    x = np.asarray(L0_attn_x, dtype=np.float32)
    wqkvT = np.ascontiguousarray(np.asarray(L0_attn_W_qkv, np.float32).T)
    woT = np.ascontiguousarray(np.asarray(L0_attn_W_o, np.float32).T)
    lg = np.asarray(L0_attn_log_gamma, np.float32)
    lgp8 = np.empty((128, 8), np.float32)
    for j in range(8):
        lgp8[:64, j] = lg[2 * j]
        lgp8[64:, j] = lg[2 * j + 1]
    lg16 = np.ascontiguousarray(
        np.broadcast_to(lg[None, :], (128, 16))).astype(np.float32)
    shared = dict(wqkvT=wqkvT, woT=woT,
                  w_up=np.ascontiguousarray(np.asarray(L0_ffn_w_up,
                                                       np.float32)),
                  w_gate=np.ascontiguousarray(np.asarray(L0_ffn_w_gate,
                                                         np.float32)),
                  w_down=np.ascontiguousarray(np.asarray(L0_ffn_w_down,
                                                         np.float32)),
                  lgp8=lgp8, lg16=lg16, **host_consts())
    in_maps = []
    for c in range(NCORES):
        xc = np.ascontiguousarray(x[c])
        in_maps.append(dict(x_nat=xc, xT=np.ascontiguousarray(xc.T), **shared))
    return in_maps


def kernel(**inputs):
    nc, P, _ = build_program()
    finalize(nc, P)
    in_maps = host_inputs(**inputs)
    res = run_bass_kernel_spmd(nc, in_maps, core_ids=list(range(NCORES)))
    out = np.empty((NCORES, L, D), dtype=np.float32)
    for c in range(NCORES):
        out[c] = np.asarray(res.results[c]["outT"]).T
    return out


# version 13
# speedup vs baseline: 1.5376x; 1.0375x over previous
import contextlib
import numpy as np
import concourse.bass as bass
import concourse.mybir as mybir
import concourse.bass_isa as bass_isa
from concourse.bass_utils import run_bass_kernel_spmd

F32 = mybir.dt.float32
BF16 = mybir.dt.bfloat16
ALU = mybir.AluOpType
AF = mybir.ActivationFunctionType
AXX = mybir.AxisListType.X
RED = bass_isa.ReduceOp

NCORES = 8
L, D, H, DK, I3 = 4096, 1024, 16, 64, 3072
CH, SLAB = 128, 512
NCH, NSLAB, NDC, NIC = L // CH, L // SLAB, D // 128, I3 // 128
CMAGIC = 12582912.0
P0_TILES = NCH
SKIP_TABLES = False
SKIP_WQKV = False
SKIP_DQ = False
P1_CHUNKS = NCH
P1_CUT = 99
XQ_MODE = 0
SKIP_XQ = False
XQ_BARRIER = False
XQ_OPS = 3
SKIP_RET = False
SKIP_KLN = False
SKIP_TR = False
LN_EPS = 1e-05
QMAX = 32767.0


class Ctr:
    __slots__ = ("h", "c", "name")

    def __init__(self, h, name):
        self.h, self.c, self.name = h, 0, name

    def bump(self, n=1):
        self.c += n
        return self.c


class Buf:
    __slots__ = ("w", "r")

    def __init__(self):
        self.w, self.r = {}, {}

    def note_w(self, ctr, tok):
        self.w = {ctr: tok}
        self.r = {}

    def note_w2(self, ctr, tok):
        if self.w.get(ctr, 0) < tok:
            self.w[ctr] = tok

    def note_r(self, ctr, tok):
        if self.r.get(ctr, 0) < tok:
            self.r[ctr] = tok


class Prog:
    def __init__(self, nc):
        self.nc = nc
        self.ops = {e: [] for e in ("sy", "sc", "ve", "te", "gp")}
        self.es = contextlib.ExitStack()
        self._mk = lambda n: Ctr(self.es.enter_context(nc.semaphore(n)), n)
        self.ctr = {"sc": self._mk("c_sc"), "ve": self._mk("c_ve"),
                    "te": self._mk("c_te"), "gp": self._mk("c_gp"),
                    "cc": self._mk("c_cc")}
        self.dstr = {}
        self.plan = []
        self.waited = {e: {} for e in self.ops}

    def dma_ctr(self, name):
        if name not in self.dstr:
            self.dstr[name] = self._mk("dm_" + name)
        return self.dstr[name]

    def E(self, eng, build, rd=(), wr=(), wr2=(), waits=(), dma=None,
          cc=False, noinc=False):
        if cc:
            ctr, n = self.ctr["cc"], 1
        elif dma is not None:
            ctr, n = self.dma_ctr(dma), 16
        else:
            ctr, n = self.ctr[eng], 1
        need = {}
        if dma is not None and ctr.c:
            need[ctr] = ctr.c

        def add(c, v):
            if v and need.get(c, 0) < v:
                need[c] = v
        for (c, v) in waits:
            add(c, v)
        for b in rd:
            for c, v in b.w.items():
                add(c, v)
        for b in wr:
            for src in (b.w, b.r):
                for c, v in src.items():
                    add(c, v)
        wm = self.waited[eng]
        final = []
        for c, v in need.items():
            if wm.get(c, 0) >= v:
                continue
            final.append((c.h, v))
            wm[c] = v
        tok = ctr.c if noinc else ctr.bump(n)

        pad = (eng == "te" and not noinc and not cc)

        def f(g, _w=tuple(final), _b=build, _h=ctr.h, _n=n, _ni=noinc,
              _pad=pad):
            for (h, v) in _w:
                g.wait_ge(h, v)
            ins = _b(g)
            if not _ni:
                ins.then_inc(_h, _n)
            if _pad:
                g.nop()
                g.nop()
                g.nop()
                g.nop()
        self.ops[eng].append(f)
        self.plan.append(
            (eng, ctr, (None if noinc else tok), tuple(need.items()),
             tuple(id(b) for b in rd), tuple(id(b) for b in wr)
             + tuple(id(b) for b in wr2)))
        for b in rd:
            b.note_r(ctr, tok)
        for b in wr:
            b.note_w(ctr, tok)
        for b in wr2:
            b.note_w2(ctr, tok)
        return tok

    def W(self, eng, waits):
        wm = self.waited[eng]
        final = []
        final2 = []
        for c, v in waits:
            if v and wm.get(c, 0) < v:
                final.append((c.h, v))
                final2.append((c, v))
                wm[c] = v
        if final:
            def f(g, _w=tuple(final)):
                for (h, v) in _w:
                    g.wait_ge(h, v)
            self.ops[eng].append(f)
            self.plan.append((eng, None, None, tuple((c, v) for c, v in final2),
                              (), ()))

    def barrier(self):
        snap = [(c, c.c) for c in self.ctr.values()]
        snap += [(c, c.c) for c in self.dstr.values()]
        for e in self.ops:
            self.W(e, snap)

    def wait_all_dma(self, eng):
        self.W(eng, [(c, c.c) for c in self.dstr.values()])


def build_program(debug=None):
    nc = bass.Bass("TRN2", target_bir_lowering=False)
    P = Prog(nc)
    E, W = P.E, P.W

    dT = lambda n, s, k: nc.dram_tensor(n, s, F32, kind=k)
    x_nat = dT("x_nat", [L, D], "ExternalInput").ap()
    xT = dT("xT", [D, L], "ExternalInput").ap()
    wqkvT = dT("wqkvT", [D, 3 * D], "ExternalInput").ap()
    woT = dT("woT", [D, D], "ExternalInput").ap()
    w_up = dT("w_up", [I3, D], "ExternalInput").ap()
    w_gate = dT("w_gate", [I3, D], "ExternalInput").ap()
    w_down = dT("w_down", [D, I3], "ExternalInput").ap()
    lgp8 = dT("lgp8", [128, 8], "ExternalInput").ap()
    lg16 = dT("lg16", [128, 16], "ExternalInput").ap()
    emat = dT("emat", [128, 128], "ExternalInput").ap()
    utri = dT("utri", [128, 128], "ExternalInput").ap()
    rampq = dT("rampq", [128, 128], "ExternalInput").ap()
    rampk = dT("rampk", [128, 64], "ExternalInput").ap()
    ident = dT("ident", [128, 128], "ExternalInput").ap()
    outT = dT("outT", [D, L], "ExternalOutput").ap()

    def scratch(name, shape, dbgkey):
        kind = "ExternalOutput" if debug == dbgkey else "Internal"
        return nc.dram_tensor(name, shape, BF16, kind=kind)

    y_sp = scratch("y_sp", [8, 128, L], "p1")
    up_sp = scratch("up_sp", [NIC, 128, L], "p2b")
    gate_sp = scratch("gate_sp", [NIC, 128, L], "none")
    g2_sp = scratch("g2_sp", [NIC, 128, L], "p2d")
    o_sp = scratch("o_sp", [NIC, 128, L], "p2e")
    wupdq_sp = nc.dram_tensor(
        "wupdq_sp", [NIC, 128, NDC * 128], BF16,
        kind="ExternalOutput" if debug == "dq" else "Internal")
    wgatedq_sp = nc.dram_tensor("wgatedq_sp", [NIC, 128, NDC * 128], BF16)
    wdndq_sp = nc.dram_tensor("wdndq_sp", [NDC, 128, NIC * 128], BF16)
    pmax_sp = nc.dram_tensor("pmax_sp", [4, 128], F32)
    cc_in = nc.dram_tensor("cc_in", [1, 16], F32)
    cc_out = nc.dram_tensor("cc_out", [1, 16], F32)
    b_ccin, b_ccout = Buf(), Buf()
    b_ysp, b_upsp, b_gatesp, b_g2sp, b_osp = (Buf() for _ in range(5))
    dbg_xn = None
    if debug in ("p0", "p0nocc", "ar1"):
        dbg_xn = nc.dram_tensor("dbg_xn", [128, NDC * L], BF16,
                                kind="ExternalOutput")

    es_g, es_x, es_a = (contextlib.ExitStack() for _ in range(3))
    SBg = lambda n, s, d: es_g.enter_context(nc.sbuf_tensor(n, s, d))
    SBa = lambda n, s, d: es_a.enter_context(nc.sbuf_tensor(n, s, d))
    PSa = lambda n, s, d: es_a.enter_context(nc.psum_tensor(n, s, d))

    stat = SBg("stat", [128, 16], F32)
    col = lambda i: stat[:, i:i + 1]
    b_stat = Buf()
    kst = SBg("kst", [128, 80], F32)
    b_kst = Buf()
    ccsb = SBg("ccsb", [1, 16], F32)
    b_ccsb = Buf()
    pmrow = SBg("pmrow", [1, 128], F32)
    b_pmrow = Buf()
    b_pmax = Buf()
    scl = SBg("scl", [128, 16], F32)
    cmag = SBg("cmag", [128, 1], F32)
    b_cmag = Buf()
    utri_sb = SBg("utri_sb", [128, 128], F32)
    ident_sb = SBg("ident_sb", [128, 128], F32)
    sc_ = lambda i: scl[:, i:i + 1]
    b_scl = Buf()
    b_mx = Buf()

    xnT = es_x.enter_context(nc.sbuf_tensor("xnT", [128, NDC * L], BF16, side="right"))
    b_xn = [Buf() for _ in range(NCH)]

    wqkv_sb = SBa("wqkv_sb", [128, NDC * 3 * D], BF16)
    b_wqkv = Buf()
    gam = SBa("gam", [128, H * 128], BF16)
    cq = SBa("cq", [128, 1024], BF16)
    ck = SBa("ck", [128, 1024], BF16)
    gcp = SBa("gcp", [128, 8], F32)
    b_tab = Buf()
    lg16_sb = SBa("lg16_sb", [128, 16], F32)
    lgp8_sb = SBa("lgp8_sb", [128, 8], F32)
    emat_sb = SBa("emat_sb", [128, 128], F32)

    rampq_sb = SBa("rampq_sb", [128, 128], F32)
    rampk_sb = SBa("rampk_sb", [128, 64], F32)

    b_const = Buf()
    xin = [SBa(f"xin{i}", [128, 1024], F32) for i in range(2)]
    b_xin = [Buf(), Buf()]
    scr1 = SBa("scr1", [128, 1024], F32)
    b_scr1 = Buf()
    q_tok = SBa("q_tok", [128, 1024], F32)
    b_qtok = Buf()
    kn_f32 = SBa("kn_f32", [128, 1024], F32)
    b_kn = Buf()
    ksq_f32 = SBa("ksq_f32", [128, 1024], F32)
    b_ksq = Buf()
    k_raw = SBa("k_raw", [128, 1024], F32)
    b_kraw = Buf()
    q_ch = [SBa(f"q_ch{i}", [128, 1024], BF16) for i in range(2)]
    b_q = [Buf(), Buf()]
    qt_ch = [SBa(f"qt_ch{i}", [128, 1024], BF16) for i in range(2)]
    b_qt = [Buf(), Buf()]
    knT_ch = [SBa(f"knT_ch{i}", [128, 1024], BF16) for i in range(2)]
    b_knT = [Buf(), Buf()]
    kh_ch = [SBa(f"kh_ch{i}", [128, 1024], BF16) for i in range(2)]
    b_kh = [Buf(), Buf()]
    v_ch = [SBa(f"v_ch{i}", [128, 1024], BF16) for i in range(2)]
    b_v = [Buf(), Buf()]
    a_scr = [SBa(f"a_scr{i}", [128, 256], BF16) for i in range(2)]
    b_a = [Buf(), Buf()]
    st_bf = SBa("st_bf", [128, 512], BF16)
    b_stb = Buf()
    y_stage = [SBa(f"y_stage{i}", [128, 4096], BF16) for i in range(2)]
    b_yst = [Buf(), Buf()]

    dq_in = [xin[0], xin[1], SBa("dq_in2", [128, 1024], F32)]
    b_dqin = [b_xin[0], b_xin[1], Buf()]
    dq_s = [SBa(f"dq_s{i}", [128, 96], F32) for i in range(2)]
    b_dqs = [Buf(), Buf()]
    dq_u = [SBa("dq_u0", [128, 1024], F32), scr1]
    b_dqu = [Buf(), b_scr1]
    dq_nat = [SBa(f"dq_nat{i}", [128, 1024], BF16) for i in range(2)]
    b_dqn = [Buf(), Buf()]
    dq_tr = [SBa(f"dq_tr{i}", [128, 1024], BF16) for i in range(2)]
    b_dqt = [Buf(), Buf()]

    p_q = PSa("p_q", [128, 1024], F32)
    b_pq = Buf()
    p_k = PSa("p_k", [128, 1024], F32)
    b_pk = Buf()
    p_v = PSa("p_v", [128, 1024], F32)
    b_pv = Buf()
    p_r = [PSa(f"p_r{i}", [128, 512], F32) for i in range(2)]
    b_pr = [Buf(), Buf()]

    for (dst, src) in [(lgp8_sb, lgp8), (lg16_sb, lg16), (emat_sb, emat),
                       (utri_sb, utri), (rampq_sb, rampq), (rampk_sb, rampk),
                       (ident_sb, ident)]:
        E("sy", lambda g, d=dst, s=src: g.dma_start(out=d[:], in_=s[:]),
          wr2=[b_const], dma="cst")
    if not SKIP_WQKV:
        for dc in range(NDC):
            E("gp", lambda g, dc=dc: g.dma_start(
                out=wqkv_sb[:, dc * 3072:(dc + 1) * 3072],
                in_=wqkvT[dc * 128:(dc + 1) * 128, :]),
              wr=([b_wqkv] if dc == 0 else []),
              wr2=([b_wqkv] if dc > 0 else []), dma=f"wq{dc % 2}")
    E("gp", lambda g: g.memset(ccsb[:], 0.0), wr=[b_ccsb])
    E("gp", lambda g: g.memset(cmag[:], CMAGIC), wr=[b_cmag])
    E("gp", lambda g: g.dma_start(out=cc_in.ap(), in_=ccsb[:]),
      rd=[b_ccsb], wr=[b_ccin], dma="ccd")
    for fn in (AF.Sigmoid, AF.Ln):
        E("sc", lambda g, fn=fn: g.activation(lg16_sb[:], lg16_sb[:], fn),
          rd=[b_const], wr=[b_const])
        E("sc", lambda g, fn=fn: g.activation(lgp8_sb[:], lgp8_sb[:], fn),
          rd=[b_const], wr=[b_const])
    E("sc", lambda g: g.activation(gcp[:], lgp8_sb[:], AF.Exp, scale=128.0),
      rd=[b_const], wr2=[b_tab])
    for h in range(H if not SKIP_TABLES else 0):
        E("ve", lambda g, h=h: g.tensor_scalar(
            scr1[:, 0:128], emat_sb[:], lg16_sb[:, h:h + 1],
            None, ALU.mult), rd=[b_const], wr=[b_scr1])
        E("sc", lambda g, h=h: g.activation(
            gam[:, h * 128:(h + 1) * 128], scr1[:, 0:128],
            AF.Exp), rd=[b_scr1], wr2=[b_tab])
        E("ve", lambda g, h=h: g.tensor_tensor(
            gam[:, h * 128:(h + 1) * 128], gam[:, h * 128:(h + 1) * 128],
            utri_sb[:], ALU.mult), rd=[b_tab, b_const], wr2=[b_tab])
        E("ve", lambda g, h=h: g.tensor_scalar(
            scr1[:, 128:192], rampk_sb[:], lg16_sb[:, h:h + 1],
            None, ALU.mult), rd=[b_const], wr=[b_scr1])
        E("sc", lambda g, h=h: g.activation(
            ck[:, h * 64:(h + 1) * 64], scr1[:, 128:192], AF.Exp),
          rd=[b_scr1], wr2=[b_tab])
    for oc in range(8 if not SKIP_TABLES else 0):
        E("ve", lambda g, oc=oc: g.tensor_scalar(
            scr1[:, 0:128], rampq_sb[:], lgp8_sb[:, oc:oc + 1],
            None, ALU.mult), rd=[b_const], wr=[b_scr1])
        E("sc", lambda g, oc=oc: g.activation(
            cq[:, oc * 128:(oc + 1) * 128], scr1[:, 0:128],
            AF.Exp), rd=[b_scr1], wr2=[b_tab])
    E("gp", lambda g: g.memset(st_bf[:], 0.0), wr=[b_stb])

    tab_waits = [(P.ctr["sc"], P.ctr["sc"].c), (P.ctr["ve"], P.ctr["ve"].c)]

    for i in range(P0_TILES):
        par = i % 2
        t0 = i * CH
        E("sy", lambda g, i=i, par=par: g.dma_start(
            out=xin[par][:], in_=x_nat[i * CH:(i + 1) * CH, :]),
          wr=[b_xin[par]], dma=f"xin{par}")
        E("sc", lambda g, par=par: g.activation(scr1[:], xin[par][:],
                                                AF.Square),
          rd=[b_xin[par]], wr=[b_scr1])
        E("ve", lambda g, par=par: g.tensor_reduce(
            col(0), xin[par][:], axis=AXX, op=ALU.add),
          rd=[b_xin[par]], wr=[b_stat])
        E("ve", lambda g: g.tensor_reduce(col(1), scr1[:], axis=AXX,
                                          op=ALU.add),
          rd=[b_scr1], wr=[b_stat])
        E("ve", lambda g: g.tensor_scalar(col(2), col(0), 1.0 / 1024.0, None,
                                          ALU.mult), rd=[b_stat], wr=[b_stat])
        E("ve", lambda g: g.tensor_tensor(col(3), col(2), col(2), ALU.mult),
          rd=[b_stat], wr=[b_stat])
        E("ve", lambda g: g.scalar_tensor_tensor(
            col(4), col(1), 1.0 / 1024.0, col(3), ALU.mult, ALU.subtract),
          rd=[b_stat], wr=[b_stat])
        E("ve", lambda g: g.tensor_scalar(col(4), col(4), LN_EPS, None,
                                          ALU.add), rd=[b_stat], wr=[b_stat])
        E("sc", lambda g: g.activation(col(5), col(4), AF.Ln),
          rd=[b_stat], wr=[b_stat])
        E("sc", lambda g: g.activation(col(6), col(5), AF.Exp, scale=-0.5),
          rd=[b_stat], wr=[b_stat])
        E("ve", lambda g, par=par: g.tensor_scalar(
            scr1[:], xin[par][:], col(2), col(6), ALU.subtract, ALU.mult),
          rd=[b_xin[par], b_stat], wr=[b_scr1])
        E("ve", lambda g: g.tensor_reduce(
            col(7), scr1[:], axis=AXX, op=ALU.max, apply_absolute_value=True),
          rd=[b_scr1], wr=[b_stat])
        E("ve", lambda g, first=(i == 0): g.tensor_tensor(
            sc_(14), col(7) if first else sc_(14), col(7), ALU.max),
          rd=[b_stat], wr=[b_mx])
        for dc in range(NDC):
            E("te", lambda g, dc=dc: g.transpose(
                p_q[:, dc * 128:(dc + 1) * 128],
                scr1[:, dc * 128:(dc + 1) * 128], ident_sb[:]),
              rd=[b_scr1, b_const], wr=([b_pq] if dc == 0 else []),
              wr2=([b_pq] if dc > 0 else []))
        E("sc", lambda g, t0=t0: g.copy(
            bass.AP(xnT, t0, [[NDC * L, 128], [L, NDC], [1, CH]]),
            p_q[:].rearrange("p (dc t) -> p dc t", t=CH)),
          rd=[b_pq], wr2=[b_xn[i]])

    def allreduce_max(cols_in, in_bufs, psum, b_psum):
        for j, cin in enumerate(cols_in):
            E("gp", lambda g, j=j, cin=cin: g.dma_start(
                out=pmax_sp.ap()[j, :], in_=cin),
              rd=in_bufs, wr=[b_pmax], dma="ccd")
            E("gp", lambda g, j=j: g.dma_start(
                out=pmrow[:], in_=pmax_sp.ap()[j:j + 1, :]),
              rd=[b_pmax], wr=[b_pmrow], dma="ccd")
            E("ve", lambda g, j=j: g.tensor_reduce(
                ccsb[0:1, j:j + 1], pmrow[:], axis=AXX, op=ALU.max),
              rd=[b_pmrow], wr=([b_ccsb] if j == 0 else []),
              wr2=([b_ccsb] if j > 0 else []))
            E("gp", lambda g, j=j: g.dma_start(
                out=cc_in.ap()[0:1, j:j + 1], in_=ccsb[0:1, j:j + 1]),
              rd=[b_ccsb], wr=([b_ccin] if j == 0 else []),
              wr2=([b_ccin] if j > 0 else []), dma="ccd")
        E("gp", lambda g: g.collective_compute(
            "AllReduce", ALU.max, replica_groups=[list(range(NCORES))],
            ins=[cc_in.ap().opt()], outs=[cc_out.ap().opt()]),
          rd=[b_ccin], wr=[b_ccout], cc=True)
        E("gp", lambda g: g.dma_start(out=ccsb[:], in_=cc_out.ap()),
          rd=[b_ccout], wr=[b_ccsb], dma="ccd")

    def bcast_scale(j_src, c_s, c_sinv, psum, b_psum):
        E("te", lambda g, j=j_src, psum=psum: g.matmul(
            psum[:, 0:1], utri_sb[0:1, :], ccsb[0:1, j:j + 1],
            start=True, stop=True),
          rd=[b_ccsb, b_const], wr=[b_psum])
        E("ve", lambda g, c_s=c_s, psum=psum: g.tensor_scalar(
            sc_(c_s), psum[:, 0:1], 1.0 / QMAX, 1e-12, ALU.mult, ALU.max),
          rd=[b_psum], wr2=[b_scl])
        E("ve", lambda g, c_s=c_s, c_i=c_sinv: g.reciprocal(sc_(c_i),
                                                            sc_(c_s)),
          rd=[b_scl], wr2=[b_scl])

    def ar1_issue():
        j = 0
        E("gp", lambda g: g.dma_start(out=pmax_sp.ap()[j, :], in_=sc_(14)),
          rd=[b_mx], wr=[b_pmax], dma="ccd")
        E("gp", lambda g: g.dma_start(out=pmrow[:], in_=pmax_sp.ap()[j:j + 1, :]),
          rd=[b_pmax], wr=[b_pmrow], dma="ccd")
        E("ve", lambda g: g.tensor_reduce(
            ccsb[0:1, j:j + 1], pmrow[:], axis=AXX, op=ALU.max),
          rd=[b_pmrow], wr=[b_ccsb])
        E("gp", lambda g: g.dma_start(
            out=cc_in.ap()[0:1, j:j + 1], in_=ccsb[0:1, j:j + 1]),
          rd=[b_ccsb], wr=[b_ccin], dma="ccd")
        E("gp", lambda g: g.collective_compute(
            "AllReduce", ALU.max, replica_groups=[list(range(NCORES))],
            ins=[cc_in.ap().opt()], outs=[cc_out.ap().opt()]),
          rd=[b_ccin], wr=[b_ccout], cc=True)

    def ar1_consume():
        E("gp", lambda g: g.dma_start(
            out=sc_(0), in_=bass.AP(cc_out, 0, [[0, 128], [1, 1]])),
          rd=[b_ccout], wr2=[b_scl], dma="ccd")
        E("ve", lambda g: g.tensor_scalar(
            sc_(0), sc_(0), 1.0 / QMAX, 1e-12, ALU.mult, ALU.max),
          rd=[b_scl], wr2=[b_scl])
        E("ve", lambda g: g.reciprocal(sc_(1), sc_(0)),
          rd=[b_scl], wr2=[b_scl])

    if debug in ("p0", "p0nocc", "ar1"):
        if debug != "p0nocc":
            allreduce_max([sc_(14)], [b_mx], p_r[0][:, 256:384], b_pr[0])
            bcast_scale(0, 0, 1, p_r[0][:, 256:384], b_pr[0])

    if debug == "ar1":
        E("gp", lambda g: g.dma_start(out=dbg_xn.ap()[:, 0:16], in_=scl[:]),
          rd=[b_scl], dma="dbg")
        P.wait_all_dma("gp")
        P.wait_all_dma("sy")
        return nc, P, locals()

    if debug in ("p0", "p0nocc"):
        E("sy", lambda g: g.dma_start(out=dbg_xn.ap(), in_=xnT[:]),
          rd=list(b_xn), dma="dbg")
        P.wait_all_dma("sy")
        return nc, P, locals()

    dq_done = {}
    dq_cnt = [0]

    def dq_load(t):
        mat, src_ap, dd, oc = dq_plan[t]
        E("sy", lambda g, t=t, src_ap=src_ap: g.dma_start(
            out=dq_in[t % 3][:], in_=src_ap), wr=[b_dqin[t % 3]], dma="dqi")

    def dq_ab(t):
        i3, i2 = t % 3, t % 2
        E("ve", lambda g, i3=i3, i2=i2: g.tensor_reduce(
            dq_s[i2][:, 0:32],
            dq_in[i3][:].rearrange("p (b k) -> p b k", k=32),
            axis=AXX, op=ALU.max, apply_absolute_value=True),
          rd=[b_dqin[i3]], wr=[b_dqs[i2]])
        E("ve", lambda g, i2=i2: g.tensor_scalar(
            dq_s[i2][:, 32:64], dq_s[i2][:, 0:32], 1.0 / 7.0, 1e-12,
            ALU.mult, ALU.max), rd=[b_dqs[i2]], wr2=[b_dqs[i2]])
        E("ve", lambda g, i2=i2: g.reciprocal(
            dq_s[i2][:, 64:96], dq_s[i2][:, 32:64]),
          rd=[b_dqs[i2]], wr2=[b_dqs[i2]])
        E("gp", lambda g, i3=i3, i2=i2: g.tensor_tensor(
            dq_u[i2][:], dq_in[i3][:],
            bass.AP(dq_s[i2], 64, [[96, 128], [1, 32], [0, 32]]), ALU.mult),
          rd=[b_dqin[i3], b_dqs[i2]], wr=[b_dqu[i2]])

    def dq_cd(t):
        mat, src_ap, dd, oc = dq_plan[t]
        i2 = t % 2
        E("ve", lambda g, i2=i2: g.tensor_scalar(
            dq_u[i2][:], dq_u[i2][:], CMAGIC, -CMAGIC, ALU.add, ALU.add),
          rd=[b_dqu[i2]], wr=[b_dqu[i2]])
        E("gp", lambda g, i2=i2: g.tensor_tensor(
            dq_nat[i2][:], dq_u[i2][:],
            bass.AP(dq_s[i2], 32, [[96, 128], [1, 32], [0, 32]]), ALU.mult),
          rd=[b_dqu[i2], b_dqs[i2]], wr=[b_dqn[i2]])
        for dc in range(NDC):
            E("sy", lambda g, dc=dc, i2=i2: g.dma_start(
                out=dq_tr[i2][:, dc * 128:(dc + 1) * 128],
                in_=dq_nat[i2][:, dc * 128:(dc + 1) * 128], transpose=True),
              rd=[b_dqn[i2]], wr=([b_dqt[i2]] if dc == 0 else []),
              wr2=([b_dqt[i2]] if dc > 0 else []), dma=f"dqt{i2}_{dc % 4}")
        tout = E("gp", lambda g, i2=i2, oc=oc, dst=dd: g.dma_start(
            out=dst.ap()[oc, :, :], in_=dq_tr[i2][:]),
          rd=[b_dqt[i2]], dma="dqo")
        dq_done[(mat, oc)] = (P.dma_ctr("dqo"), tout)

    dq_plan = []
    for oc in range(NIC):
        dq_plan.append(("up", w_up[oc * 128:(oc + 1) * 128, :], wupdq_sp, oc))
        dq_plan.append(("gate", w_gate[oc * 128:(oc + 1) * 128, :],
                        wgatedq_sp, oc))
    NDQ = len(dq_plan)
    dq_ld = [0]
    dq_abi = [0]
    dq_cdi = [0]

    def dq_target(c):
        return max(0, min(NDQ, (c + 1) * NDQ // NCH))

    def emit_dq_stages(c):
        if SKIP_DQ:
            return
        while dq_cdi[0] < dq_target(c - 2):
            dq_cd(dq_cdi[0]); dq_cdi[0] += 1
        while dq_abi[0] < dq_target(c - 1):
            dq_ab(dq_abi[0]); dq_abi[0] += 1
        while dq_ld[0] < dq_target(c):
            dq_load(dq_ld[0]); dq_ld[0] += 1

    def emit_proj(c, which):
        t0 = c * CH
        ps, b_ps, off = which
        for half in range(2):
            for dc in range(NDC):
                E("te", lambda g, ps=ps, half=half, dc=dc, t0=t0, off=off:
                  g.matmul(
                      ps[:, half * 512:(half + 1) * 512],
                      xnT[:, dc * L + t0: dc * L + t0 + CH],
                      wqkv_sb[:, dc * 3072 + off + half * 512:
                              dc * 3072 + off + (half + 1) * 512],
                      start=(dc == 0), stop=(dc == NDC - 1)),
                  rd=[b_xn[c], b_wqkv],
                  wr=([b_ps] if (half == 0 and dc == 0) else []),
                  wr2=([b_ps] if (half == 1 and dc == NDC - 1) else []),
                  noinc=(dc != NDC - 1))

    def emit_kln_apply(c):
        par = c % 2
        E("sc", lambda g: g.copy(q_tok[:], p_q[:]), rd=[b_pq], wr=[b_qtok])
        E("sc", lambda g, par=par: g.copy(v_ch[par][:], p_v[:]),
          rd=[b_pv], wr=[b_v[par]])
        E("ve", lambda g: g.tensor_tensor(
            kn_f32[:], k_raw[:],
            bass.AP(kst, 32, [[80, 128], [1, 16], [0, 64]]), ALU.subtract),
          rd=[b_kraw, b_kst], wr=[b_kn])
        E("ve", lambda g: g.tensor_tensor(
            kn_f32[:], kn_f32[:],
            bass.AP(kst, 48, [[80, 128], [1, 16], [0, 64]]), ALU.mult),
          rd=[b_kst], wr2=[b_kn])
        E("ve", lambda g, par=par: g.tensor_tensor(
            kh_ch[par][:], kn_f32[:], ck[:], ALU.mult),
          rd=[b_kn], wr=[b_kh[par]], waits=tab_waits)

    def emit_kln_pre(c):
        tks_ = E("ve", lambda g: g.tensor_reduce(
            kst[:, 0:16], p_k[:].rearrange("p (h e) -> p h e", e=DK),
            axis=AXX, op=ALU.add), rd=[b_pk], wr=[b_kst])
        E("sc", lambda g: g.copy(k_raw[:], p_k[:]), rd=[b_pk], wr=[b_kraw],
          waits=[(P.ctr["ve"], tks_)])
        E("sc", lambda g: g.activation(ksq_f32[:], k_raw[:], AF.Square),
          rd=[b_kraw], wr=[b_ksq])
        E("ve", lambda g: g.tensor_reduce(
            kst[:, 16:32], ksq_f32[:].rearrange("p (h e) -> p h e", e=DK),
            axis=AXX, op=ALU.add), rd=[b_ksq], wr2=[b_kst])
        E("ve", lambda g: g.tensor_scalar(
            kst[:, 32:48], kst[:, 0:16], 1.0 / DK, None, ALU.mult),
          rd=[b_kst], wr2=[b_kst])
        E("ve", lambda g: g.tensor_tensor(
            kst[:, 64:80], kst[:, 32:48], kst[:, 32:48], ALU.mult),
          rd=[b_kst], wr2=[b_kst])
        E("ve", lambda g: g.scalar_tensor_tensor(
            kst[:, 48:64], kst[:, 16:32], 1.0 / DK, kst[:, 64:80],
            ALU.mult, ALU.subtract), rd=[b_kst], wr2=[b_kst])
        E("ve", lambda g: g.tensor_scalar(
            kst[:, 48:64], kst[:, 48:64], LN_EPS, None, ALU.add),
          rd=[b_kst], wr2=[b_kst])
        E("sc", lambda g: g.activation(kst[:, 48:64], kst[:, 48:64], AF.Ln),
          rd=[b_kst], wr2=[b_kst])
        E("sc", lambda g: g.activation(
            kst[:, 48:64], kst[:, 48:64], AF.Exp, scale=-0.5),
          rd=[b_kst], wr2=[b_kst])

    def emit_transposes(c):
        par = c % 2
        for j in range(NDC):
            E("te", lambda g, j=j: g.transpose(
                p_v[:, j * 128:(j + 1) * 128], q_tok[:, j * 128:(j + 1) * 128],
                ident_sb[:]), rd=[b_qtok, b_const],
              wr=([b_pv] if j == 0 else []), wr2=([b_pv] if j > 0 else []))
        tqt_ = E("ve", lambda g, par=par: g.tensor_tensor(
            qt_ch[par][:], p_v[:], cq[:], ALU.mult),
          rd=[b_pv], wr=[b_qt[par]], waits=tab_waits)
        E("sc", lambda g, par=par: g.copy(q_ch[par][:], p_v[:]),
          rd=[b_pv], wr=[b_q[par]], waits=[(P.ctr["ve"], tqt_)])
        for j in range(NDC):
            E("te", lambda g, j=j: g.transpose(
                p_v[:, j * 128:(j + 1) * 128], kn_f32[:, j * 128:(j + 1) * 128],
                ident_sb[:]), rd=[b_kn, b_const],
              wr=([b_pv] if j == 0 else []), wr2=([b_pv] if j > 0 else []))
        E("sc", lambda g, par=par: g.copy(knT_ch[par][:], p_v[:]),
          rd=[b_pv], wr=[b_knT[par]])

    def emit_retention(c):
        par = c % 2
        gpar = (c // 4) % 2
        for p in range(8):
            rpar = p % 2
            for half in range(2):
                E("te", lambda g, p=p, half=half, par=par, rpar=rpar: g.matmul(
                    p_r[rpar][:, half * 128:(half + 1) * 128],
                    knT_ch[par][64 * half:64 * (half + 1),
                                p * 128:(p + 1) * 128],
                    q_ch[par][64 * half:64 * (half + 1),
                              p * 128:(p + 1) * 128],
                    start=True, stop=True, tile_position=(64 * half, 0)),
                  rd=[b_knT[par], b_q[par]],
                  wr=([b_pr[rpar]] if half == 0 else []),
                  wr2=([b_pr[rpar]] if half == 1 else []))
            E("ve", lambda g, p=p, rpar=rpar: g.tensor_tensor(
                a_scr[rpar][:], p_r[rpar][:, 0:256],
                gam[:, p * 256:(p + 1) * 256], ALU.mult),
              rd=[b_pr[rpar]], wr=[b_a[rpar]], waits=tab_waits)
            for half in range(2):
                h = 2 * p + half
                E("te", lambda g, h=h, half=half, par=par, rpar=rpar: g.matmul(
                    p_r[rpar][64 * half:64 * (half + 1), 256:384],
                    v_ch[par][:, h * 64:(h + 1) * 64],
                    a_scr[rpar][:, half * 128:(half + 1) * 128],
                    start=True, stop=False, tile_position=(0, 64 * half)),
                  rd=[b_v[par], b_a[rpar]], wr2=[b_pr[rpar]], noinc=True)
                E("te", lambda g, p=p, half=half, par=par, rpar=rpar: g.matmul(
                    p_r[rpar][64 * half:64 * (half + 1), 256:384],
                    st_bf[64 * half:64 * (half + 1), p * 64:(p + 1) * 64],
                    qt_ch[par][64 * half:64 * (half + 1),
                               p * 128:(p + 1) * 128],
                    start=False, stop=True,
                    tile_position=(64 * half, 64 * half)),
                  rd=[b_stb, b_qt[par]], wr2=[b_pr[rpar]])
            for half in range(2):
                h = 2 * p + half
                E("te", lambda g, h=h, half=half, par=par, rpar=rpar: g.matmul(
                    p_r[rpar][64 * half:64 * (half + 1), 384:448],
                    kh_ch[par][:, h * 64:(h + 1) * 64],
                    v_ch[par][:, h * 64:(h + 1) * 64],
                    start=True, stop=True, tile_position=(0, 64 * half)),
                  rd=[b_kh[par], b_v[par]], wr2=[b_pr[rpar]])
            tye_ = E("sc", lambda g, p=p, rpar=rpar, c=c, gpar=gpar: g.copy(
                y_stage[gpar][:, p * 512 + (c % 4) * 128:
                              p * 512 + (c % 4) * 128 + 128],
                p_r[rpar][:, 256:384]),
              rd=[b_pr[rpar]],
              wr=([b_yst[gpar]] if (p == 0 and c % 4 == 0) else []),
              wr2=([b_yst[gpar]] if not (p == 0 and c % 4 == 0) else []))
            E("ve", lambda g, p=p, rpar=rpar: g.scalar_tensor_tensor(
                st_bf[:, p * 64:(p + 1) * 64], st_bf[:, p * 64:(p + 1) * 64],
                gcp[:, p:p + 1], p_r[rpar][:, 384:448], ALU.mult, ALU.add),
              rd=[b_stb, b_pr[rpar]], wr=[b_stb],
              waits=tab_waits + [(P.ctr["sc"], tye_)])

    def emit_tail(c):
        par = c % 2
        gpar = (c // 4) % 2
        if c % 4 == 3:
            gidx = c // 4
            for pr in range(8):
                E("gp", lambda g, gidx=gidx, gpar=gpar, pr=pr: g.dma_start(
                    out=y_sp.ap()[pr, :, gidx * 4 * CH:(gidx + 1) * 4 * CH],
                    in_=y_stage[gpar][:, pr * 512:(pr + 1) * 512]),
                  rd=[b_yst[gpar]], wr2=[b_ysp],
                  dma=f"ysp{(gpar, pr % 2)}")
        emit_dq_stages(c)
        if c == 2 and debug not in ("p0", "p0nocc", "ar1"):
            ar1_issue()

    QKV = [(p_q, b_pq, 0), (p_k, b_pk, 1024), (p_v, b_pv, 2048)]
    emit_proj(0, QKV[0])
    emit_proj(0, QKV[1])
    emit_kln_pre(0)
    for c in range(P1_CHUNKS):
        emit_proj(c, QKV[2])
        emit_kln_apply(c)
        if c + 1 < P1_CHUNKS:
            emit_proj(c + 1, QKV[0])
            emit_proj(c + 1, QKV[1])
        emit_transposes(c)
        if c + 1 < P1_CHUNKS:
            emit_kln_pre(c + 1)
        emit_retention(c)
        emit_tail(c)
    for c in range(P1_CHUNKS, P1_CHUNKS + 3):
        emit_dq_stages(c)
    if debug not in ("p0", "p0nocc", "ar1"):
        ar1_consume()

    if XQ_BARRIER and not SKIP_XQ:
        P.barrier()
    for dc in range(NDC if not SKIP_XQ else 0):
        for half in range(4):
            base = dc * L + half * 1024
            grp = list(b_xn)
            E("ve", lambda g, base=base: g.tensor_scalar(
                ksq_f32[:], xnT[:, base:base + 1024], sc_(1), CMAGIC,
                ALU.mult, ALU.add),
              rd=[b_scl] + grp, wr=[b_ksq])
            E("ve", lambda g, base=base: g.tensor_scalar(
                xnT[:, base:base + 1024], ksq_f32[:], CMAGIC, sc_(0),
                ALU.subtract, ALU.mult),
              rd=[b_ksq, b_scl],
              wr=(grp if (dc == 0 and half == 0) else []),
              wr2=(grp if not (dc == 0 and half == 0) else []))

    if debug == "p1":
        P.wait_all_dma("sy")
        return nc, P, locals()

    P.barrier()
    es_a.close()
    es_b = contextlib.ExitStack()
    SBb = lambda n, s, d: es_b.enter_context(nc.sbuf_tensor(n, s, d))
    PSb = lambda n, s, d: es_b.enter_context(nc.psum_tensor(n, s, d))

    wst = [SBb(f"wst{i}", [128, 1024], BF16) for i in range(4)]
    b_wst = [Buf() for _ in range(4)]
    stg = [SBb(f"stg{i}", [128, 512], BF16) for i in range(4)]
    b_stg = [Buf() for _ in range(4)]
    dqdw_in = [SBb(f"dqdw_in{i}", [128, 3072], F32) for i in range(2)]
    b_dqdwin = [Buf(), Buf()]
    dqd_s = SBb("dqd_s", [128, 288], F32)
    b_dqds = Buf()
    dqd_u = SBb("dqd_u", [128, 3072], F32)
    b_dqdu = Buf()
    dqd_nat = SBb("dqd_nat", [128, 3072], BF16)
    b_dqdn = Buf()
    dqd_tr = [SBb(f"dqd_tr{i}", [128, 3072], BF16) for i in range(2)]
    b_dqdt = [Buf(), Buf()]
    p_f = [PSb(f"p_f{i}", [128, 512], F32) for i in range(4)]
    b_pf = [Buf() for _ in range(4)]

    dqd_done = {}

    def dqd_ab(t):
        par = t % 2
        E("sy", lambda g, t=t, par=par: g.dma_start(
            out=dqdw_in[par][:], in_=w_down[t * 128:(t + 1) * 128, :]),
          wr=[b_dqdwin[par]], dma="dqdi")
        E("ve", lambda g, par=par: g.tensor_reduce(
            dqd_s[:, 0:96], dqdw_in[par][:].rearrange("p (b k) -> p b k",
                                                      k=32),
            axis=AXX, op=ALU.max, apply_absolute_value=True),
          rd=[b_dqdwin[par]], wr=[b_dqds])
        E("ve", lambda g: g.tensor_scalar(
            dqd_s[:, 96:192], dqd_s[:, 0:96], 1.0 / 7.0, 1e-12, ALU.mult,
            ALU.max), rd=[b_dqds], wr2=[b_dqds])
        E("ve", lambda g: g.reciprocal(dqd_s[:, 192:288], dqd_s[:, 96:192]),
          rd=[b_dqds], wr2=[b_dqds])
        E("gp", lambda g, par=par: g.tensor_tensor(
            dqd_u[:], dqdw_in[par][:],
            bass.AP(dqd_s, 192, [[288, 128], [1, 96], [0, 32]]), ALU.mult),
          rd=[b_dqdwin[par], b_dqds], wr=[b_dqdu])

    def dqd_cd(t):
        par = t % 2
        E("ve", lambda g: g.tensor_scalar(
            dqd_u[:], dqd_u[:], CMAGIC, -CMAGIC, ALU.add, ALU.add),
          rd=[b_dqdu], wr=[b_dqdu])
        E("gp", lambda g: g.tensor_tensor(
            dqd_nat[:], dqd_u[:],
            bass.AP(dqd_s, 96, [[288, 128], [1, 96], [0, 32]]), ALU.mult),
          rd=[b_dqdu, b_dqds], wr=[b_dqdn])
        for ic in range(NIC):
            E("sy", lambda g, ic=ic, par=par: g.dma_start(
                out=dqd_tr[par][:, ic * 128:(ic + 1) * 128],
                in_=dqd_nat[:, ic * 128:(ic + 1) * 128], transpose=True),
              rd=[b_dqdn], wr=([b_dqdt[par]] if ic == 0 else []),
              wr2=([b_dqdt[par]] if ic > 0 else []), dma=f"dqdt{par}_{ic % 4}")
        tout = E("gp", lambda g, par=par, t=t: g.dma_start(
            out=wdndq_sp.ap()[t, :, :], in_=dqd_tr[par][:]),
          rd=[b_dqdt[par]], dma="dqdo")
        dqd_done[t] = (P.dma_ctr("dqdo"), tout)

    dqd_emitted = [0]
    mats2b = [("up", wupdq_sp, up_sp, b_upsp, 10),
              ("gate", wgatedq_sp, gate_sp, b_gatesp, 11)]
    ldplan = [(m[0], m[1], oc) for m in mats2b for oc in range(NIC)]

    def load_wst(k):
        mname, wsp, oc = ldplan[k]
        E("sy", lambda g, oc=oc, wsp=wsp, k=k: g.dma_start(
            out=wst[k % 4][:], in_=wsp.ap()[oc, :, :]),
          wr=[b_wst[k % 4]], waits=[dq_done[(mname, oc)]], dma=f"wst{k % 4}")

    load_wst(0)
    load_wst(1)
    kk = [2]
    for (mname, wsp, osp, b_osp_, mxcol) in mats2b:
        for oc in range(NIC):
            wb = b_wst[oc % 4]
            if kk[0] < len(ldplan):
                load_wst(kk[0])
                kk[0] += 1
            for slab in range(NSLAB):
                pf = (oc * NSLAB + slab) % 4
                for dc in range(NDC):
                    E("te", lambda g, oc=oc, slab=slab, dc=dc, pf=pf: g.matmul(
                        p_f[pf][:], wst[oc % 4][:, dc * 128:(dc + 1) * 128],
                        xnT[:, dc * L + slab * SLAB:
                            dc * L + (slab + 1) * SLAB],
                        start=(dc == 0), stop=(dc == NDC - 1)),
                      rd=[wb] + (list(b_xn) if (oc == 0 and slab == 0)
                                 else []),
                      wr=([b_pf[pf]] if dc == 0 else []),
                      wr2=([b_pf[pf]] if dc == NDC - 1 else []),
                      noinc=(dc != NDC - 1))
                E("sc", lambda g, pf=pf: g.copy(stg[pf][:], p_f[pf][:]),
                  rd=[b_pf[pf]], wr=[b_stg[pf]])
                E("ve", lambda g, pf=pf: g.tensor_reduce(
                    col(7), stg[pf][:], axis=AXX, op=ALU.max,
                    apply_absolute_value=True),
                  rd=[b_stg[pf]], wr=[b_stat])
                E("ve", lambda g, first=(oc == 0 and slab == 0), mxcol=mxcol:
                  g.tensor_tensor(
                      sc_(mxcol), col(7) if first else sc_(mxcol), col(7),
                      ALU.max), rd=[b_stat], wr2=[b_mx])
                E("gp", lambda g, oc=oc, slab=slab, pf=pf, osp=osp:
                  g.dma_start(
                      out=osp.ap()[oc, :, slab * SLAB:(slab + 1) * SLAB],
                      in_=stg[pf][:]),
                  rd=[b_stg[pf]], wr2=[b_osp_], dma=f"spl{pf}")
            if oc % 3 == 2:
                if 0 < dqd_emitted[0] <= NDC:
                    dqd_cd(dqd_emitted[0] - 1)
                if dqd_emitted[0] < NDC:
                    dqd_ab(dqd_emitted[0])
                dqd_emitted[0] += 1
    while dqd_emitted[0] <= NDC:
        if 0 < dqd_emitted[0] <= NDC:
            dqd_cd(dqd_emitted[0] - 1)
        if dqd_emitted[0] < NDC:
            dqd_ab(dqd_emitted[0])
        dqd_emitted[0] += 1

    if debug == "p2b":
        P.wait_all_dma("sy")
        return nc, P, locals()

    b_mx.note_w(P.ctr["ve"], P.ctr["ve"].c)
    allreduce_max([sc_(10), sc_(11)], [b_mx], p_f[0][:], b_pf[0])
    bcast_scale(0, 2, 3, p_f[0][:], b_pf[0])
    bcast_scale(1, 4, 5, p_f[0][:], b_pf[0])

    P.barrier()
    es_x.close()
    es_b.close()
    es_c = contextlib.ExitStack()
    es_cio = contextlib.ExitStack()
    SBc = lambda n, s, d: es_c.enter_context(nc.sbuf_tensor(n, s, d))
    SBio = lambda n, s, d: es_cio.enter_context(nc.sbuf_tensor(n, s, d))

    attn_sb = SBc("attn_sb", [128, NDC * L], BF16)
    b_attn = Buf()
    woT_sb = SBc("woT_sb", [128, NDC * D], BF16)
    b_wo = Buf()
    y_sl = [SBc(f"y_sl{i}", [128, 4096], BF16) for i in range(2)]
    b_ysl = [Buf(), Buf()]
    p_w = [es_c.enter_context(nc.psum_tensor(f"p_w{i}", [128, 512], F32))
           for i in range(2)]
    b_pw = [Buf(), Buf()]
    io_a = [SBio(f"io_a{i}", [128, 4096], BF16) for i in range(2)]
    b_ioa = [Buf(), Buf()]
    io_b = [SBio(f"io_b{i}", [128, 4096], BF16) for i in range(2)]
    b_iob = [Buf(), Buf()]
    io_c = [SBio(f"io_c{i}", [128, 4096], BF16) for i in range(2)]
    b_ioc = [Buf(), Buf()]
    fs1 = [SBio(f"fs1_{i}", [128, 2048], F32) for i in range(2)]
    b_fs1 = [Buf(), Buf()]
    fs2 = [SBio(f"fs2_{i}", [128, 2048], F32) for i in range(2)]
    b_fs2 = [Buf(), Buf()]
    fs3 = [SBio(f"fs3_{i}", [128, 2048], F32) for i in range(2)]
    b_fs3 = [Buf(), Buf()]

    for dc in range(NDC):
        E("gp", lambda g, dc=dc: g.dma_start(
            out=woT_sb[:, dc * D:(dc + 1) * D],
            in_=woT[dc * 128:(dc + 1) * 128, :]),
          wr=([b_wo] if dc == 0 else []), wr2=([b_wo] if dc > 0 else []),
          dma=f"wo{dc % 2}")

    wo_units = []
    for slab in range(NSLAB):
        wo_units.append(("load", slab))
        for oc in range(NDC):
            wo_units.append(("mm", slab, oc))
    wo_i = [0]

    def emit_wo_unit():
        if wo_i[0] >= len(wo_units):
            return
        u = wo_units[wo_i[0]]
        wo_i[0] += 1
        if u[0] == "load":
            slab = u[1]
            ypar = slab % 2
            for pr in range(8):
                E("sy", lambda g, slab=slab, ypar=ypar, pr=pr: g.dma_start(
                    out=y_sl[ypar][:, pr * 512:(pr + 1) * 512],
                    in_=y_sp.ap()[pr, :, slab * SLAB:(slab + 1) * SLAB]),
                  rd=[b_ysp], wr=([b_ysl[ypar]] if pr == 0 else []),
                  wr2=([b_ysl[ypar]] if pr > 0 else []),
                  dma=f"ysl{(ypar, pr % 2)}")
        else:
            slab, oc = u[1], u[2]
            ypar = slab % 2
            wpar = (slab * NDC + oc) % 2
            for dc in range(NDC):
                E("te", lambda g, oc=oc, dc=dc, ypar=ypar, wpar=wpar: g.matmul(
                    p_w[wpar][:],
                    woT_sb[:, dc * D + oc * 128: dc * D + (oc + 1) * 128],
                    y_sl[ypar][:, dc * 512:(dc + 1) * 512],
                    start=(dc == 0), stop=(dc == NDC - 1)),
                  rd=[b_wo, b_ysl[ypar]],
                  wr=([b_pw[wpar]] if dc == 0 else []),
                  wr2=([b_pw[wpar]] if dc == NDC - 1 else []),
                  noinc=(dc != NDC - 1))
            E("sc", lambda g, oc=oc, slab=slab, wpar=wpar: g.copy(
                attn_sb[:, oc * L + slab * SLAB: oc * L + (slab + 1) * SLAB],
                p_w[wpar][:]), rd=[b_pw[wpar]], wr2=[b_attn])

    for ic in range(NIC):
        par = ic % 2
        E("sy", lambda g, ic=ic, par=par: g.dma_start(
            out=io_a[par][:], in_=gate_sp.ap()[ic, :, :]),
          rd=[b_gatesp], wr=[b_ioa[par]], dma=f"ioa{par}")
        for hh in range(2):
            h0 = hh * 2048
            E("sc", lambda g, par=par, h0=h0: g.activation(
                fs1[par][:], io_a[par][:, h0:h0 + 2048], AF.Identity,
                bias=cmag[:], scale=sc_(5)),
              rd=[b_ioa[par], b_scl, b_cmag], wr=[b_fs1[par]])
            E("ve", lambda g, par=par: g.tensor_scalar(
                fs2[par][:], fs1[par][:], CMAGIC, sc_(4), ALU.subtract,
                ALU.mult),
              rd=[b_fs1[par], b_scl], wr=[b_fs2[par]])
            E("sc", lambda g, par=par: g.activation(fs1[par][:], fs2[par][:],
                                                    AF.Sigmoid),
              rd=[b_fs2[par]], wr=[b_fs1[par]])
            E("sc", lambda g, par=par: g.activation(
                fs3[par][:], fs1[par][:], AF.Identity, bias=cmag[:],
                scale=65536.0),
              rd=[b_fs1[par], b_cmag], wr=[b_fs3[par]])
            E("ve", lambda g, par=par: g.tensor_scalar(
                fs1[par][:], fs3[par][:], -CMAGIC, 65535.0, ALU.add, ALU.min),
              rd=[b_fs3[par]], wr=[b_fs1[par]])
            E("ve", lambda g, par=par: g.tensor_scalar(
                fs3[par][:], fs1[par][:], 0.0, 1.0 / 65536.0, ALU.max,
                ALU.mult),
              rd=[b_fs1[par]], wr=[b_fs3[par]])
            E("ve", lambda g, par=par, h0=h0: g.tensor_tensor(
                io_c[par][:, h0:h0 + 2048], fs2[par][:], fs3[par][:],
                ALU.mult),
              rd=[b_fs2[par], b_fs3[par]],
              wr=([b_ioc[par]] if hh == 0 else []),
              wr2=([b_ioc[par]] if hh else []))
        E("ve", lambda g, par=par: g.tensor_reduce(
            col(7), io_c[par][:], axis=AXX, op=ALU.max,
            apply_absolute_value=True), rd=[b_ioc[par]], wr=[b_stat])
        E("ve", lambda g, first=(ic == 0): g.tensor_tensor(
            sc_(12), col(7) if first else sc_(12), col(7), ALU.max),
          rd=[b_stat], wr2=[b_mx])
        E("gp", lambda g, ic=ic, par=par: g.dma_start(
            out=g2_sp.ap()[ic, :, :], in_=io_c[par][:]),
          rd=[b_ioc[par]], wr2=[b_g2sp], dma=f"ioc{par}")
        for _ in range(3):
            emit_wo_unit()
    while wo_i[0] < len(wo_units):
        emit_wo_unit()

    if debug == "p2d":
        P.wait_all_dma("sy")
        return nc, P, locals()

    b_mx.note_w(P.ctr["ve"], P.ctr["ve"].c)
    allreduce_max([sc_(12)], [b_mx], p_w[0][:], b_pw[0])
    bcast_scale(0, 6, 7, p_w[0][:], b_pw[0])

    for ic in range(NIC):
        par = ic % 2
        E("sy", lambda g, ic=ic, par=par: g.dma_start(
            out=io_a[par][:], in_=up_sp.ap()[ic, :, :]),
          rd=[b_upsp], wr=[b_ioa[par]], dma=f"ioa{par}")
        E("sy", lambda g, ic=ic, par=par: g.dma_start(
            out=io_b[par][:], in_=g2_sp.ap()[ic, :, :]),
          rd=[b_g2sp], wr=[b_iob[par]], dma=f"iob{par}")
        for hh in range(2):
            h0 = hh * 2048
            E("sc", lambda g, par=par, h0=h0: g.activation(
                fs1[par][:], io_a[par][:, h0:h0 + 2048], AF.Identity,
                bias=cmag[:], scale=sc_(3)),
              rd=[b_ioa[par], b_scl, b_cmag], wr=[b_fs1[par]])
            E("ve", lambda g, par=par: g.tensor_scalar(
                fs2[par][:], fs1[par][:], CMAGIC, sc_(2), ALU.subtract,
                ALU.mult),
              rd=[b_fs1[par], b_scl], wr=[b_fs2[par]])
            E("sc", lambda g, par=par, h0=h0: g.activation(
                fs3[par][:], io_b[par][:, h0:h0 + 2048], AF.Identity,
                bias=cmag[:], scale=sc_(7)),
              rd=[b_iob[par], b_scl, b_cmag], wr=[b_fs3[par]])
            E("ve", lambda g, par=par: g.tensor_scalar(
                fs1[par][:], fs3[par][:], CMAGIC, sc_(6), ALU.subtract,
                ALU.mult),
              rd=[b_fs3[par], b_scl], wr=[b_fs1[par]])
            E("ve", lambda g, par=par, h0=h0: g.tensor_tensor(
                io_c[par][:, h0:h0 + 2048], fs1[par][:], fs2[par][:],
                ALU.mult),
              rd=[b_fs1[par], b_fs2[par]],
              wr=([b_ioc[par]] if hh == 0 else []),
              wr2=([b_ioc[par]] if hh else []))
        E("ve", lambda g, par=par: g.tensor_reduce(
            col(7), io_c[par][:], axis=AXX, op=ALU.max,
            apply_absolute_value=True), rd=[b_ioc[par]], wr=[b_stat])
        E("ve", lambda g, first=(ic == 0): g.tensor_tensor(
            sc_(13), col(7) if first else sc_(13), col(7), ALU.max),
          rd=[b_stat], wr2=[b_mx])
        E("gp", lambda g, ic=ic, par=par: g.dma_start(
            out=o_sp.ap()[ic, :, :], in_=io_c[par][:]),
          rd=[b_ioc[par]], wr2=[b_osp], dma=f"ioc{par}")

    if debug == "p2e":
        P.wait_all_dma("sy")
        return nc, P, locals()

    b_mx.note_w(P.ctr["ve"], P.ctr["ve"].c)
    allreduce_max([sc_(13)], [b_mx], p_w[0][:], b_pw[0])
    bcast_scale(0, 8, 9, p_w[0][:], b_pw[0])

    P.barrier()
    es_cio.close()
    es_d = contextlib.ExitStack()
    SBd = lambda n, s, d: es_d.enter_context(nc.sbuf_tensor(n, s, d))
    oq_sl = [SBd(f"oq_sl{i}", [128, NIC * 512], BF16) for i in range(2)]
    b_oq = [Buf(), Buf()]
    wdn = [SBd(f"wdn{i}", [128, NIC * 128], BF16) for i in range(2)]
    b_wdn = [Buf(), Buf()]
    xTt = [SBd(f"xTt{i}", [128, 512], F32) for i in range(2)]
    b_xT = [Buf(), Buf()]
    outt = [SBd(f"outt{i}", [128, 512], F32) for i in range(2)]
    b_out = [Buf(), Buf()]
    ofs = [SBd(f"ofs{i}", [128, 2048], F32) for i in range(2)]
    b_ofs = [Buf(), Buf()]
    ofs2 = [SBd(f"ofs2{i}", [128, 2048], F32) for i in range(2)]
    b_ofs2 = [Buf(), Buf()]
    ep1 = SBd("ep1", [128, 512], F32)
    b_ep1 = Buf()
    p_d = [es_d.enter_context(nc.psum_tensor(f"p_d{i}", [128, 512], F32))
           for i in range(4)]
    b_pd = [Buf() for _ in range(4)]

    wdn_wait = (P.dma_ctr("dqdo"), max(t for (_c, t) in dqd_done.values()))
    last_out = 0
    for slab in range(NSLAB):
        spar = slab % 2
        for ic in range(NIC):
            E("sy", lambda g, ic=ic, slab=slab, spar=spar: g.dma_start(
                out=oq_sl[spar][:, ic * 512:(ic + 1) * 512],
                in_=o_sp.ap()[ic, :, slab * SLAB:(slab + 1) * SLAB]),
              rd=[b_osp], wr=([b_oq[spar]] if ic == 0 else []),
              wr2=([b_oq[spar]] if ic > 0 else []), dma=f"oq{ic % 4}")
        for third in range(6):
            a0, a1 = third * 2048, (third + 1) * 2048
            tp = third % 2
            E("sc", lambda g, spar=spar, a0=a0, a1=a1, tp=tp: g.activation(
                ofs[tp][:], oq_sl[spar][:, a0:a1], AF.Identity, bias=cmag[:],
                scale=sc_(9)), rd=[b_oq[spar], b_scl, b_cmag], wr=[b_ofs[tp]])
            E("ve", lambda g, tp=tp: g.tensor_scalar(
                ofs2[tp][:], ofs[tp][:], -CMAGIC, QMAX, ALU.add, ALU.min),
              rd=[b_ofs[tp]], wr=[b_ofs2[tp]])
            E("ve", lambda g, spar=spar, a0=a0, a1=a1, tp=tp: g.tensor_scalar(
                oq_sl[spar][:, a0:a1], ofs2[tp][:], -QMAX - 1.0, sc_(8),
                ALU.max, ALU.mult),
              rd=[b_ofs2[tp], b_scl], wr2=[b_oq[spar]])
        if slab == 0:
            E("sy", lambda g: g.dma_start(
                out=wdn[0][:], in_=wdndq_sp.ap()[0, :, :]),
              wr=[b_wdn[0]], waits=[wdn_wait], dma="wdn0")
        for oc in range(NDC):
            pd = oc % 4
            wpar2 = (slab * NDC + oc) % 2
            nslab, noc = (slab, oc + 1) if oc + 1 < NDC else (slab + 1, 0)
            if nslab < NSLAB:
                E("sy", lambda g, noc=noc, w2=1 - wpar2: g.dma_start(
                    out=wdn[w2][:], in_=wdndq_sp.ap()[noc, :, :]),
                  wr=[b_wdn[1 - wpar2]], waits=[wdn_wait],
                  dma=f"wdn{1 - wpar2}")
            for ic in range(NIC):
                E("te", lambda g, ic=ic, spar=spar, pd=pd, wpar2=wpar2:
                  g.matmul(
                      p_d[pd][:], wdn[wpar2][:, ic * 128:(ic + 1) * 128],
                      oq_sl[spar][:, ic * 512:(ic + 1) * 512],
                      start=(ic == 0), stop=(ic == NIC - 1)),
                  rd=[b_wdn[wpar2], b_oq[spar]],
                  wr=([b_pd[pd]] if ic == 0 else []),
                  wr2=([b_pd[pd]] if ic == NIC - 1 else []),
                  noinc=(ic != NIC - 1))
            opar = oc % 2
            E("sy", lambda g, oc=oc, slab=slab, opar=opar: g.dma_start(
                out=xTt[opar][:],
                in_=xT[oc * 128:(oc + 1) * 128,
                       slab * SLAB:(slab + 1) * SLAB]),
              wr=[b_xT[opar]], dma=f"xt{opar}")
            E("ve", lambda g, oc=oc, slab=slab, pd=pd: g.tensor_tensor(
                ep1[:], p_d[pd][:],
                attn_sb[:, oc * L + slab * SLAB: oc * L + (slab + 1) * SLAB],
                ALU.add), rd=[b_pd[pd], b_attn], wr=[b_ep1])
            E("ve", lambda g, opar=opar: g.tensor_tensor(
                outt[opar][:], ep1[:], xTt[opar][:], ALU.add),
              rd=[b_ep1, b_xT[opar]], wr=[b_out[opar]])
            last_out = E("gp", lambda g, oc=oc, slab=slab, opar=opar:
                         g.dma_start(
                             out=outT[oc * 128:(oc + 1) * 128,
                                      slab * SLAB:(slab + 1) * SLAB],
                             in_=outt[opar][:]),
                         rd=[b_out[opar]], dma=f"out{opar}")

    P.wait_all_dma("sy")
    return nc, P, locals()


def finalize(nc, P):
    with nc.Block() as blk:
        @blk.sync
        def _(g):
            for f in P.ops["sy"]:
                f(g)

        @blk.scalar
        def _(g):
            for f in P.ops["sc"]:
                f(g)

        @blk.vector
        def _(g):
            for f in P.ops["ve"]:
                f(g)

        @blk.tensor
        def _(g):
            for f in P.ops["te"]:
                f(g)

        @blk.gpsimd
        def _(g):
            for f in P.ops["gp"]:
                f(g)
    return nc


def host_consts():
    idx = np.arange(128, dtype=np.float32)
    emat = idx[None, :] - idx[:, None]
    emat = np.where(emat >= 0, emat, 1e9).astype(np.float32)
    utri = (idx[None, :] >= idx[:, None]).astype(np.float32)
    rampq = np.ascontiguousarray(
        np.broadcast_to(idx[None, :] + 1.0, (128, 128))).astype(np.float32)
    rampk = np.ascontiguousarray(
        np.broadcast_to((127.0 - idx)[:, None], (128, 64))).astype(np.float32)
    return dict(emat=np.ascontiguousarray(emat),
                utri=np.ascontiguousarray(utri), rampq=rampq, rampk=rampk,
                ident=np.eye(128, dtype=np.float32))


def host_inputs(L0_attn_x, L0_attn_W_qkv, L0_attn_W_o, L0_attn_log_gamma,
                L0_ffn_w_gate, L0_ffn_w_up, L0_ffn_w_down):
    x = np.asarray(L0_attn_x, dtype=np.float32)
    wqkvT = np.ascontiguousarray(np.asarray(L0_attn_W_qkv, np.float32).T)
    woT = np.ascontiguousarray(np.asarray(L0_attn_W_o, np.float32).T)
    lg = np.asarray(L0_attn_log_gamma, np.float32)
    lgp8 = np.empty((128, 8), np.float32)
    for j in range(8):
        lgp8[:64, j] = lg[2 * j]
        lgp8[64:, j] = lg[2 * j + 1]
    lg16 = np.ascontiguousarray(
        np.broadcast_to(lg[None, :], (128, 16))).astype(np.float32)
    shared = dict(wqkvT=wqkvT, woT=woT,
                  w_up=np.ascontiguousarray(np.asarray(L0_ffn_w_up,
                                                       np.float32)),
                  w_gate=np.ascontiguousarray(np.asarray(L0_ffn_w_gate,
                                                         np.float32)),
                  w_down=np.ascontiguousarray(np.asarray(L0_ffn_w_down,
                                                         np.float32)),
                  lgp8=lgp8, lg16=lg16, **host_consts())
    in_maps = []
    for c in range(NCORES):
        xc = np.ascontiguousarray(x[c])
        in_maps.append(dict(x_nat=xc, xT=np.ascontiguousarray(xc.T), **shared))
    return in_maps


def kernel(**inputs):
    nc, P, _ = build_program()
    finalize(nc, P)
    in_maps = host_inputs(**inputs)
    res = run_bass_kernel_spmd(nc, in_maps, core_ids=list(range(NCORES)))
    out = np.empty((NCORES, L, D), dtype=np.float32)
    for c in range(NCORES):
        out[c] = np.asarray(res.results[c]["outT"]).T
    return out


# version 14
# speedup vs baseline: 1.5970x; 1.0386x over previous
import contextlib
import numpy as np
import concourse.bass as bass
import concourse.mybir as mybir
import concourse.bass_isa as bass_isa
from concourse.bass_utils import run_bass_kernel_spmd

F32 = mybir.dt.float32
BF16 = mybir.dt.bfloat16
ALU = mybir.AluOpType
AF = mybir.ActivationFunctionType
AXX = mybir.AxisListType.X
RED = bass_isa.ReduceOp

NCORES = 8
L, D, H, DK, I3 = 4096, 1024, 16, 64, 3072
CH, SLAB = 128, 512
NCH, NSLAB, NDC, NIC = L // CH, L // SLAB, D // 128, I3 // 128
CMAGIC = 12582912.0
P0_TILES = NCH
SKIP_TABLES = False
SKIP_WQKV = False
SKIP_DQ = False
P1_CHUNKS = NCH
P1_CUT = 99
XQ_MODE = 0
SKIP_XQ = False
XQ_BARRIER = False
XQ_OPS = 3
SKIP_RET = False
SKIP_KLN = False
SKIP_TR = False
LN_EPS = 1e-05
QMAX = 32767.0


class Ctr:
    __slots__ = ("h", "c", "name")

    def __init__(self, h, name):
        self.h, self.c, self.name = h, 0, name

    def bump(self, n=1):
        self.c += n
        return self.c


class Buf:
    __slots__ = ("w", "r")

    def __init__(self):
        self.w, self.r = {}, {}

    def note_w(self, ctr, tok):
        self.w = {ctr: tok}
        self.r = {}

    def note_w2(self, ctr, tok):
        if self.w.get(ctr, 0) < tok:
            self.w[ctr] = tok

    def note_r(self, ctr, tok):
        if self.r.get(ctr, 0) < tok:
            self.r[ctr] = tok


class Prog:
    def __init__(self, nc):
        self.nc = nc
        self.ops = {e: [] for e in ("sy", "sc", "ve", "te", "gp")}
        self.es = contextlib.ExitStack()
        self._mk = lambda n: Ctr(self.es.enter_context(nc.semaphore(n)), n)
        self.ctr = {"sc": self._mk("c_sc"), "ve": self._mk("c_ve"),
                    "te": self._mk("c_te"), "gp": self._mk("c_gp"),
                    "cc": self._mk("c_cc")}
        self.dstr = {}
        self.plan = []
        self.waited = {e: {} for e in self.ops}

    def dma_ctr(self, name):
        if name not in self.dstr:
            self.dstr[name] = self._mk("dm_" + name)
        return self.dstr[name]

    def E(self, eng, build, rd=(), wr=(), wr2=(), waits=(), dma=None,
          cc=False, noinc=False):
        if cc:
            ctr, n = self.ctr["cc"], 1
        elif dma is not None:
            ctr, n = self.dma_ctr(dma), 16
        else:
            ctr, n = self.ctr[eng], 1
        need = {}
        if dma is not None and ctr.c:
            need[ctr] = ctr.c

        def add(c, v):
            if v and need.get(c, 0) < v:
                need[c] = v
        for (c, v) in waits:
            add(c, v)
        for b in rd:
            for c, v in b.w.items():
                add(c, v)
        for b in wr:
            for src in (b.w, b.r):
                for c, v in src.items():
                    add(c, v)
        wm = self.waited[eng]
        final = []
        for c, v in need.items():
            if wm.get(c, 0) >= v:
                continue
            final.append((c.h, v))
            wm[c] = v
        tok = ctr.c if noinc else ctr.bump(n)

        pad = (eng == "te" and not noinc and not cc)

        def f(g, _w=tuple(final), _b=build, _h=ctr.h, _n=n, _ni=noinc,
              _pad=pad):
            for (h, v) in _w:
                g.wait_ge(h, v)
            ins = _b(g)
            if not _ni:
                ins.then_inc(_h, _n)
            if _pad:
                g.nop()
                g.nop()
                g.nop()
                g.nop()
        self.ops[eng].append(f)
        self.plan.append(
            (eng, ctr, (None if noinc else tok), tuple(need.items()),
             tuple(id(b) for b in rd), tuple(id(b) for b in wr)
             + tuple(id(b) for b in wr2)))
        for b in rd:
            b.note_r(ctr, tok)
        for b in wr:
            b.note_w(ctr, tok)
        for b in wr2:
            b.note_w2(ctr, tok)
        return tok

    def W(self, eng, waits):
        wm = self.waited[eng]
        final = []
        final2 = []
        for c, v in waits:
            if v and wm.get(c, 0) < v:
                final.append((c.h, v))
                final2.append((c, v))
                wm[c] = v
        if final:
            def f(g, _w=tuple(final)):
                for (h, v) in _w:
                    g.wait_ge(h, v)
            self.ops[eng].append(f)
            self.plan.append((eng, None, None, tuple((c, v) for c, v in final2),
                              (), ()))

    def barrier(self):
        snap = [(c, c.c) for c in self.ctr.values()]
        snap += [(c, c.c) for c in self.dstr.values()]
        for e in self.ops:
            self.W(e, snap)

    def wait_all_dma(self, eng):
        self.W(eng, [(c, c.c) for c in self.dstr.values()])


def build_program(debug=None):
    nc = bass.Bass("TRN2", target_bir_lowering=False)
    P = Prog(nc)
    E, W = P.E, P.W

    dT = lambda n, s, k: nc.dram_tensor(n, s, F32, kind=k)
    x_nat = dT("x_nat", [L, D], "ExternalInput").ap()
    xT = dT("xT", [D, L], "ExternalInput").ap()
    wqkvT = dT("wqkvT", [D, 3 * D], "ExternalInput").ap()
    woT = dT("woT", [D, D], "ExternalInput").ap()
    w_up = dT("w_up", [I3, D], "ExternalInput").ap()
    w_gate = dT("w_gate", [I3, D], "ExternalInput").ap()
    w_down = dT("w_down", [D, I3], "ExternalInput").ap()
    lgp8 = dT("lgp8", [128, 8], "ExternalInput").ap()
    lg16 = dT("lg16", [128, 16], "ExternalInput").ap()
    emat = dT("emat", [128, 128], "ExternalInput").ap()
    utri = dT("utri", [128, 128], "ExternalInput").ap()
    rampq = dT("rampq", [128, 128], "ExternalInput").ap()
    rampk = dT("rampk", [128, 64], "ExternalInput").ap()
    ident = dT("ident", [128, 128], "ExternalInput").ap()
    outT = dT("outT", [D, L], "ExternalOutput").ap()

    def scratch(name, shape, dbgkey):
        kind = "ExternalOutput" if debug == dbgkey else "Internal"
        return nc.dram_tensor(name, shape, BF16, kind=kind)

    y_sp = scratch("y_sp", [8, 128, L], "p1")
    up_sp = scratch("up_sp", [NIC, 128, L], "p2b")
    gate_sp = scratch("gate_sp", [NIC, 128, L], "none")
    g2_sp = scratch("g2_sp", [NIC, 128, L], "p2d")
    o_sp = scratch("o_sp", [NIC, 128, L], "p2e")
    wupdq_sp = nc.dram_tensor(
        "wupdq_sp", [NIC, 128, NDC * 128], BF16,
        kind="ExternalOutput" if debug == "dq" else "Internal")
    wgatedq_sp = nc.dram_tensor("wgatedq_sp", [NIC, 128, NDC * 128], BF16)
    wdndq_sp = nc.dram_tensor("wdndq_sp", [NDC, 128, NIC * 128], BF16)
    pmax_sp = nc.dram_tensor("pmax_sp", [4, 128], F32)
    cc_in = nc.dram_tensor("cc_in", [1, 16], F32)
    cc_out = nc.dram_tensor("cc_out", [1, 16], F32)
    b_ccin, b_ccout = Buf(), Buf()
    b_ysp, b_upsp, b_gatesp, b_g2sp, b_osp = (Buf() for _ in range(5))
    dbg_xn = None
    if debug in ("p0", "p0nocc", "ar1"):
        dbg_xn = nc.dram_tensor("dbg_xn", [128, NDC * L], BF16,
                                kind="ExternalOutput")

    es_g, es_x, es_a = (contextlib.ExitStack() for _ in range(3))
    SBg = lambda n, s, d: es_g.enter_context(nc.sbuf_tensor(n, s, d))
    SBa = lambda n, s, d: es_a.enter_context(nc.sbuf_tensor(n, s, d))
    PSa = lambda n, s, d: es_a.enter_context(nc.psum_tensor(n, s, d))

    stat = SBg("stat", [128, 16], F32)
    col = lambda i: stat[:, i:i + 1]
    b_stat = Buf()
    kst = SBg("kst", [128, 80], F32)
    b_kst = Buf()
    ccsb = SBg("ccsb", [1, 16], F32)
    b_ccsb = Buf()
    pmrow = SBg("pmrow", [1, 128], F32)
    b_pmrow = Buf()
    b_pmax = Buf()
    scl = SBg("scl", [128, 16], F32)
    cmag = SBg("cmag", [128, 1], F32)
    b_cmag = Buf()
    utri_sb = SBg("utri_sb", [128, 128], F32)
    ident_sb = SBg("ident_sb", [128, 128], F32)
    sc_ = lambda i: scl[:, i:i + 1]
    b_scl = Buf()
    b_mx = Buf()

    xnT = es_x.enter_context(nc.sbuf_tensor("xnT", [128, NDC * L], BF16, side="right"))
    b_xn = [Buf() for _ in range(NCH)]

    wqkv_sb = SBa("wqkv_sb", [128, NDC * 3 * D], BF16)
    b_wqkv = Buf()
    gam = SBa("gam", [128, H * 128], BF16)
    cq = SBa("cq", [128, 1024], BF16)
    ck = SBa("ck", [128, 1024], BF16)
    gcp = SBa("gcp", [128, 8], F32)
    b_tab = Buf()
    lg16_sb = SBa("lg16_sb", [128, 16], F32)
    lgp8_sb = SBa("lgp8_sb", [128, 8], F32)
    emat_sb = SBa("emat_sb", [128, 128], F32)

    rampq_sb = SBa("rampq_sb", [128, 128], F32)
    rampk_sb = SBa("rampk_sb", [128, 64], F32)

    b_const = Buf()
    xin = [SBa(f"xin{i}", [128, 1024], F32) for i in range(2)]
    b_xin = [Buf(), Buf()]
    scr1 = SBa("scr1", [128, 1024], F32)
    b_scr1 = Buf()
    q_tok = SBa("q_tok", [128, 1024], F32)
    b_qtok = Buf()
    kn_f32 = SBa("kn_f32", [128, 1024], F32)
    b_kn = Buf()
    ksq_f32 = SBa("ksq_f32", [128, 1024], F32)
    b_ksq = Buf()
    k_raw = SBa("k_raw", [128, 1024], F32)
    b_kraw = Buf()
    q_ch = [SBa(f"q_ch{i}", [128, 1024], BF16) for i in range(2)]
    b_q = [Buf(), Buf()]
    qt_ch = [SBa(f"qt_ch{i}", [128, 1024], BF16) for i in range(2)]
    b_qt = [Buf(), Buf()]
    knT_ch = [SBa(f"knT_ch{i}", [128, 1024], BF16) for i in range(2)]
    b_knT = [Buf(), Buf()]
    kh_ch = [SBa(f"kh_ch{i}", [128, 1024], BF16) for i in range(2)]
    b_kh = [Buf(), Buf()]
    v_ch = [SBa(f"v_ch{i}", [128, 1024], BF16) for i in range(2)]
    b_v = [Buf(), Buf()]
    a_scr = [SBa(f"a_scr{i}", [128, 256], BF16) for i in range(2)]
    b_a = [Buf(), Buf()]
    st_bf = SBa("st_bf", [128, 512], BF16)
    b_stb = Buf()
    y_stage = [SBa(f"y_stage{i}", [128, 4096], BF16) for i in range(2)]
    b_yst = [Buf(), Buf()]

    dq_in = [xin[0], xin[1], SBa("dq_in2", [128, 1024], F32)]
    b_dqin = [b_xin[0], b_xin[1], Buf()]
    dq_s = [SBa(f"dq_s{i}", [128, 96], F32) for i in range(2)]
    b_dqs = [Buf(), Buf()]
    dq_u = [SBa("dq_u0", [128, 1024], F32), scr1]
    b_dqu = [Buf(), b_scr1]
    dq_nat = [SBa(f"dq_nat{i}", [128, 1024], BF16) for i in range(2)]
    b_dqn = [Buf(), Buf()]
    dq_tr = [SBa(f"dq_tr{i}", [128, 1024], BF16) for i in range(2)]
    b_dqt = [Buf(), Buf()]

    p_q = PSa("p_q", [128, 1024], F32)
    b_pq = Buf()
    p_k = PSa("p_k", [128, 1024], F32)
    b_pk = Buf()
    p_v = PSa("p_v", [128, 1024], F32)
    b_pv = Buf()
    p_r = [PSa(f"p_r{i}", [128, 512], F32) for i in range(2)]
    b_pr = [Buf(), Buf()]

    for (dst, src) in [(lgp8_sb, lgp8), (lg16_sb, lg16), (emat_sb, emat),
                       (utri_sb, utri), (rampq_sb, rampq), (rampk_sb, rampk),
                       (ident_sb, ident)]:
        E("sy", lambda g, d=dst, s=src: g.dma_start(out=d[:], in_=s[:]),
          wr2=[b_const], dma="cst")
    if not SKIP_WQKV:
        for dc in range(NDC):
            E("gp", lambda g, dc=dc: g.dma_start(
                out=wqkv_sb[:, dc * 3072:(dc + 1) * 3072],
                in_=wqkvT[dc * 128:(dc + 1) * 128, :]),
              wr=([b_wqkv] if dc == 0 else []),
              wr2=([b_wqkv] if dc > 0 else []), dma=f"wq{dc % 2}")
    E("gp", lambda g: g.memset(ccsb[:], 0.0), wr=[b_ccsb])
    E("gp", lambda g: g.memset(cmag[:], CMAGIC), wr=[b_cmag])
    E("gp", lambda g: g.dma_start(out=cc_in.ap(), in_=ccsb[:]),
      rd=[b_ccsb], wr=[b_ccin], dma="ccd")
    for fn in (AF.Sigmoid, AF.Ln):
        E("sc", lambda g, fn=fn: g.activation(lg16_sb[:], lg16_sb[:], fn),
          rd=[b_const], wr=[b_const])
        E("sc", lambda g, fn=fn: g.activation(lgp8_sb[:], lgp8_sb[:], fn),
          rd=[b_const], wr=[b_const])
    E("sc", lambda g: g.activation(gcp[:], lgp8_sb[:], AF.Exp, scale=128.0),
      rd=[b_const], wr2=[b_tab])
    for h in range(H if not SKIP_TABLES else 0):
        E("ve", lambda g, h=h: g.tensor_scalar(
            scr1[:, 0:128], emat_sb[:], lg16_sb[:, h:h + 1],
            None, ALU.mult), rd=[b_const], wr=[b_scr1])
        E("sc", lambda g, h=h: g.activation(
            gam[:, h * 128:(h + 1) * 128], scr1[:, 0:128],
            AF.Exp), rd=[b_scr1], wr2=[b_tab])
        E("ve", lambda g, h=h: g.tensor_tensor(
            gam[:, h * 128:(h + 1) * 128], gam[:, h * 128:(h + 1) * 128],
            utri_sb[:], ALU.mult), rd=[b_tab, b_const], wr2=[b_tab])
        E("ve", lambda g, h=h: g.tensor_scalar(
            scr1[:, 128:192], rampk_sb[:], lg16_sb[:, h:h + 1],
            None, ALU.mult), rd=[b_const], wr=[b_scr1])
        E("sc", lambda g, h=h: g.activation(
            ck[:, h * 64:(h + 1) * 64], scr1[:, 128:192], AF.Exp),
          rd=[b_scr1], wr2=[b_tab])
    for oc in range(8 if not SKIP_TABLES else 0):
        E("ve", lambda g, oc=oc: g.tensor_scalar(
            scr1[:, 0:128], rampq_sb[:], lgp8_sb[:, oc:oc + 1],
            None, ALU.mult), rd=[b_const], wr=[b_scr1])
        E("sc", lambda g, oc=oc: g.activation(
            cq[:, oc * 128:(oc + 1) * 128], scr1[:, 0:128],
            AF.Exp), rd=[b_scr1], wr2=[b_tab])
    E("gp", lambda g: g.memset(st_bf[:], 0.0), wr=[b_stb])

    tab_waits = [(P.ctr["sc"], P.ctr["sc"].c), (P.ctr["ve"], P.ctr["ve"].c)]

    for i in range(P0_TILES):
        par = i % 2
        t0 = i * CH
        E("sy", lambda g, i=i, par=par: g.dma_start(
            out=xin[par][:], in_=x_nat[i * CH:(i + 1) * CH, :]),
          wr=[b_xin[par]], dma=f"xin{par}")
        E("sc", lambda g, par=par: g.activation(scr1[:], xin[par][:],
                                                AF.Square),
          rd=[b_xin[par]], wr=[b_scr1])
        E("ve", lambda g, par=par: g.tensor_reduce(
            col(0), xin[par][:], axis=AXX, op=ALU.add),
          rd=[b_xin[par]], wr=[b_stat])
        E("ve", lambda g: g.tensor_reduce(col(1), scr1[:], axis=AXX,
                                          op=ALU.add),
          rd=[b_scr1], wr=[b_stat])
        E("ve", lambda g: g.tensor_scalar(col(2), col(0), 1.0 / 1024.0, None,
                                          ALU.mult), rd=[b_stat], wr=[b_stat])
        E("ve", lambda g: g.tensor_tensor(col(3), col(2), col(2), ALU.mult),
          rd=[b_stat], wr=[b_stat])
        E("ve", lambda g: g.scalar_tensor_tensor(
            col(4), col(1), 1.0 / 1024.0, col(3), ALU.mult, ALU.subtract),
          rd=[b_stat], wr=[b_stat])
        E("ve", lambda g: g.tensor_scalar(col(4), col(4), LN_EPS, None,
                                          ALU.add), rd=[b_stat], wr=[b_stat])
        E("sc", lambda g: g.activation(col(5), col(4), AF.Ln),
          rd=[b_stat], wr=[b_stat])
        E("sc", lambda g: g.activation(col(6), col(5), AF.Exp, scale=-0.5),
          rd=[b_stat], wr=[b_stat])
        E("ve", lambda g, par=par: g.tensor_scalar(
            scr1[:], xin[par][:], col(2), col(6), ALU.subtract, ALU.mult),
          rd=[b_xin[par], b_stat], wr=[b_scr1])
        E("ve", lambda g: g.tensor_reduce(
            col(7), scr1[:], axis=AXX, op=ALU.max, apply_absolute_value=True),
          rd=[b_scr1], wr=[b_stat])
        E("ve", lambda g, first=(i == 0): g.tensor_tensor(
            sc_(14), col(7) if first else sc_(14), col(7), ALU.max),
          rd=[b_stat], wr=[b_mx])
        for dc in range(NDC):
            E("te", lambda g, dc=dc: g.transpose(
                p_q[:, dc * 128:(dc + 1) * 128],
                scr1[:, dc * 128:(dc + 1) * 128], ident_sb[:]),
              rd=[b_scr1, b_const], wr=([b_pq] if dc == 0 else []),
              wr2=([b_pq] if dc > 0 else []))
        E("sc", lambda g, t0=t0: g.copy(
            bass.AP(xnT, t0, [[NDC * L, 128], [L, NDC], [1, CH]]),
            p_q[:].rearrange("p (dc t) -> p dc t", t=CH)),
          rd=[b_pq], wr2=[b_xn[i]])

    def allreduce_max(cols_in, in_bufs, psum, b_psum):
        for j, cin in enumerate(cols_in):
            E("gp", lambda g, j=j, cin=cin: g.dma_start(
                out=pmax_sp.ap()[j, :], in_=cin),
              rd=in_bufs, wr=[b_pmax], dma="ccd")
            E("gp", lambda g, j=j: g.dma_start(
                out=pmrow[:], in_=pmax_sp.ap()[j:j + 1, :]),
              rd=[b_pmax], wr=[b_pmrow], dma="ccd")
            E("ve", lambda g, j=j: g.tensor_reduce(
                ccsb[0:1, j:j + 1], pmrow[:], axis=AXX, op=ALU.max),
              rd=[b_pmrow], wr=([b_ccsb] if j == 0 else []),
              wr2=([b_ccsb] if j > 0 else []))
            E("gp", lambda g, j=j: g.dma_start(
                out=cc_in.ap()[0:1, j:j + 1], in_=ccsb[0:1, j:j + 1]),
              rd=[b_ccsb], wr=([b_ccin] if j == 0 else []),
              wr2=([b_ccin] if j > 0 else []), dma="ccd")
        E("gp", lambda g: g.collective_compute(
            "AllReduce", ALU.max, replica_groups=[list(range(NCORES))],
            ins=[cc_in.ap().opt()], outs=[cc_out.ap().opt()]),
          rd=[b_ccin], wr=[b_ccout], cc=True)
        E("gp", lambda g: g.dma_start(out=ccsb[:], in_=cc_out.ap()),
          rd=[b_ccout], wr=[b_ccsb], dma="ccd")

    def bcast_scale(j_src, c_s, c_sinv, psum, b_psum):
        E("te", lambda g, j=j_src, psum=psum: g.matmul(
            psum[:, 0:1], utri_sb[0:1, :], ccsb[0:1, j:j + 1],
            start=True, stop=True),
          rd=[b_ccsb, b_const], wr=[b_psum])
        E("ve", lambda g, c_s=c_s, psum=psum: g.tensor_scalar(
            sc_(c_s), psum[:, 0:1], 1.0 / QMAX, 1e-12, ALU.mult, ALU.max),
          rd=[b_psum], wr2=[b_scl])
        E("ve", lambda g, c_s=c_s, c_i=c_sinv: g.reciprocal(sc_(c_i),
                                                            sc_(c_s)),
          rd=[b_scl], wr2=[b_scl])

    def ar1_issue():
        j = 0
        E("gp", lambda g: g.dma_start(out=pmax_sp.ap()[j, :], in_=sc_(14)),
          rd=[b_mx], wr=[b_pmax], dma="ccd")
        E("gp", lambda g: g.dma_start(out=pmrow[:], in_=pmax_sp.ap()[j:j + 1, :]),
          rd=[b_pmax], wr=[b_pmrow], dma="ccd")
        E("ve", lambda g: g.tensor_reduce(
            ccsb[0:1, j:j + 1], pmrow[:], axis=AXX, op=ALU.max),
          rd=[b_pmrow], wr=[b_ccsb])
        E("gp", lambda g: g.dma_start(
            out=cc_in.ap()[0:1, j:j + 1], in_=ccsb[0:1, j:j + 1]),
          rd=[b_ccsb], wr=[b_ccin], dma="ccd")
        E("gp", lambda g: g.collective_compute(
            "AllReduce", ALU.max, replica_groups=[list(range(NCORES))],
            ins=[cc_in.ap().opt()], outs=[cc_out.ap().opt()]),
          rd=[b_ccin], wr=[b_ccout], cc=True)

    def ar1_consume():
        E("gp", lambda g: g.dma_start(
            out=sc_(0), in_=bass.AP(cc_out, 0, [[0, 128], [1, 1]])),
          rd=[b_ccout], wr2=[b_scl], dma="ccd")
        E("ve", lambda g: g.tensor_scalar(
            sc_(0), sc_(0), 1.0 / QMAX, 1e-12, ALU.mult, ALU.max),
          rd=[b_scl], wr2=[b_scl])
        E("ve", lambda g: g.reciprocal(sc_(1), sc_(0)),
          rd=[b_scl], wr2=[b_scl])

    if debug in ("p0", "p0nocc", "ar1"):
        if debug != "p0nocc":
            allreduce_max([sc_(14)], [b_mx], p_r[0][:, 256:384], b_pr[0])
            bcast_scale(0, 0, 1, p_r[0][:, 256:384], b_pr[0])

    if debug == "ar1":
        E("gp", lambda g: g.dma_start(out=dbg_xn.ap()[:, 0:16], in_=scl[:]),
          rd=[b_scl], dma="dbg")
        P.wait_all_dma("gp")
        P.wait_all_dma("sy")
        return nc, P, locals()

    if debug in ("p0", "p0nocc"):
        E("sy", lambda g: g.dma_start(out=dbg_xn.ap(), in_=xnT[:]),
          rd=list(b_xn), dma="dbg")
        P.wait_all_dma("sy")
        return nc, P, locals()

    dq_done = {}
    dq_cnt = [0]

    def dq_load(t):
        mat, src_ap, dd, oc = dq_plan[t]
        E("sy", lambda g, t=t, src_ap=src_ap: g.dma_start(
            out=dq_in[t % 3][:], in_=src_ap), wr=[b_dqin[t % 3]], dma="dqi")

    def dq_ab(t):
        i3, i2 = t % 3, t % 2
        E("ve", lambda g, i3=i3, i2=i2: g.tensor_reduce(
            dq_s[i2][:, 0:32],
            dq_in[i3][:].rearrange("p (b k) -> p b k", k=32),
            axis=AXX, op=ALU.max, apply_absolute_value=True),
          rd=[b_dqin[i3]], wr=[b_dqs[i2]])
        E("ve", lambda g, i2=i2: g.tensor_scalar(
            dq_s[i2][:, 32:64], dq_s[i2][:, 0:32], 1.0 / 7.0, 1e-12,
            ALU.mult, ALU.max), rd=[b_dqs[i2]], wr2=[b_dqs[i2]])
        E("ve", lambda g, i2=i2: g.reciprocal(
            dq_s[i2][:, 64:96], dq_s[i2][:, 32:64]),
          rd=[b_dqs[i2]], wr2=[b_dqs[i2]])
        E("gp", lambda g, i3=i3, i2=i2: g.tensor_tensor(
            dq_u[i2][:], dq_in[i3][:],
            bass.AP(dq_s[i2], 64, [[96, 128], [1, 32], [0, 32]]), ALU.mult),
          rd=[b_dqin[i3], b_dqs[i2]], wr=[b_dqu[i2]])

    def dq_cd(t):
        mat, src_ap, dd, oc = dq_plan[t]
        i2 = t % 2
        E("ve", lambda g, i2=i2: g.tensor_scalar(
            dq_u[i2][:], dq_u[i2][:], CMAGIC, -CMAGIC, ALU.add, ALU.add),
          rd=[b_dqu[i2]], wr=[b_dqu[i2]])
        E("gp", lambda g, i2=i2: g.tensor_tensor(
            dq_nat[i2][:], dq_u[i2][:],
            bass.AP(dq_s[i2], 32, [[96, 128], [1, 32], [0, 32]]), ALU.mult),
          rd=[b_dqu[i2], b_dqs[i2]], wr=[b_dqn[i2]])
        for dc in range(NDC):
            E("sy", lambda g, dc=dc, i2=i2: g.dma_start(
                out=dq_tr[i2][:, dc * 128:(dc + 1) * 128],
                in_=dq_nat[i2][:, dc * 128:(dc + 1) * 128], transpose=True),
              rd=[b_dqn[i2]], wr=([b_dqt[i2]] if dc == 0 else []),
              wr2=([b_dqt[i2]] if dc > 0 else []), dma=f"dqt{i2}_{dc % 4}")
        tout = E("gp", lambda g, i2=i2, oc=oc, dst=dd: g.dma_start(
            out=dst.ap()[oc, :, :], in_=dq_tr[i2][:]),
          rd=[b_dqt[i2]], dma="dqo")
        dq_done[(mat, oc)] = (P.dma_ctr("dqo"), tout)

    dq_plan = []
    for oc in range(NIC):
        dq_plan.append(("up", w_up[oc * 128:(oc + 1) * 128, :], wupdq_sp, oc))
        dq_plan.append(("gate", w_gate[oc * 128:(oc + 1) * 128, :],
                        wgatedq_sp, oc))
    NDQ = len(dq_plan)
    dq_ld = [0]
    dq_abi = [0]
    dq_cdi = [0]

    def dq_target(c):
        return max(0, min(NDQ, (c + 1) * NDQ // NCH))

    def emit_dq_stages(c):
        if SKIP_DQ:
            return
        while dq_cdi[0] < dq_target(c - 2):
            dq_cd(dq_cdi[0]); dq_cdi[0] += 1
        while dq_abi[0] < dq_target(c - 1):
            dq_ab(dq_abi[0]); dq_abi[0] += 1
        while dq_ld[0] < dq_target(c):
            dq_load(dq_ld[0]); dq_ld[0] += 1

    def emit_proj(c, which):
        t0 = c * CH
        ps, b_ps, off = which
        for half in range(2):
            for dc in range(NDC):
                E("te", lambda g, ps=ps, half=half, dc=dc, t0=t0, off=off:
                  g.matmul(
                      ps[:, half * 512:(half + 1) * 512],
                      xnT[:, dc * L + t0: dc * L + t0 + CH],
                      wqkv_sb[:, dc * 3072 + off + half * 512:
                              dc * 3072 + off + (half + 1) * 512],
                      start=(dc == 0), stop=(dc == NDC - 1)),
                  rd=[b_xn[c], b_wqkv],
                  wr=([b_ps] if (half == 0 and dc == 0) else []),
                  wr2=([b_ps] if (half == 1 and dc == NDC - 1) else []),
                  noinc=(dc != NDC - 1))

    def emit_kln_apply(c):
        par = c % 2
        E("sc", lambda g: g.copy(q_tok[:], p_q[:]), rd=[b_pq], wr=[b_qtok])
        E("sc", lambda g, par=par: g.copy(v_ch[par][:], p_v[:]),
          rd=[b_pv], wr=[b_v[par]])
        E("ve", lambda g: g.tensor_tensor(
            kn_f32[:], k_raw[:],
            bass.AP(kst, 32, [[80, 128], [1, 16], [0, 64]]), ALU.subtract),
          rd=[b_kraw, b_kst], wr=[b_kn])
        E("ve", lambda g: g.tensor_tensor(
            kn_f32[:], kn_f32[:],
            bass.AP(kst, 48, [[80, 128], [1, 16], [0, 64]]), ALU.mult),
          rd=[b_kst], wr2=[b_kn])
        E("ve", lambda g, par=par: g.tensor_tensor(
            kh_ch[par][:], kn_f32[:], ck[:], ALU.mult),
          rd=[b_kn], wr=[b_kh[par]], waits=tab_waits)

    def emit_kln_pre(c):
        tks_ = E("ve", lambda g: g.tensor_reduce(
            kst[:, 0:16], p_k[:].rearrange("p (h e) -> p h e", e=DK),
            axis=AXX, op=ALU.add), rd=[b_pk], wr=[b_kst])
        E("sc", lambda g: g.copy(k_raw[:], p_k[:]), rd=[b_pk], wr=[b_kraw],
          waits=[(P.ctr["ve"], tks_)])
        E("sc", lambda g: g.activation(ksq_f32[:], k_raw[:], AF.Square),
          rd=[b_kraw], wr=[b_ksq])
        E("ve", lambda g: g.tensor_reduce(
            kst[:, 16:32], ksq_f32[:].rearrange("p (h e) -> p h e", e=DK),
            axis=AXX, op=ALU.add), rd=[b_ksq], wr2=[b_kst])
        E("ve", lambda g: g.tensor_scalar(
            kst[:, 32:48], kst[:, 0:16], 1.0 / DK, None, ALU.mult),
          rd=[b_kst], wr2=[b_kst])
        E("ve", lambda g: g.tensor_tensor(
            kst[:, 64:80], kst[:, 32:48], kst[:, 32:48], ALU.mult),
          rd=[b_kst], wr2=[b_kst])
        E("ve", lambda g: g.scalar_tensor_tensor(
            kst[:, 48:64], kst[:, 16:32], 1.0 / DK, kst[:, 64:80],
            ALU.mult, ALU.subtract), rd=[b_kst], wr2=[b_kst])
        E("ve", lambda g: g.tensor_scalar(
            kst[:, 48:64], kst[:, 48:64], LN_EPS, None, ALU.add),
          rd=[b_kst], wr2=[b_kst])
        E("sc", lambda g: g.activation(kst[:, 48:64], kst[:, 48:64], AF.Ln),
          rd=[b_kst], wr2=[b_kst])
        E("sc", lambda g: g.activation(
            kst[:, 48:64], kst[:, 48:64], AF.Exp, scale=-0.5),
          rd=[b_kst], wr2=[b_kst])

    def emit_transposes(c):
        par = c % 2
        for j in range(NDC):
            E("te", lambda g, j=j: g.transpose(
                p_v[:, j * 128:(j + 1) * 128], q_tok[:, j * 128:(j + 1) * 128],
                ident_sb[:]), rd=[b_qtok, b_const],
              wr=([b_pv] if j == 0 else []), wr2=([b_pv] if j > 0 else []))
        tqt_ = E("ve", lambda g, par=par: g.tensor_tensor(
            qt_ch[par][:], p_v[:], cq[:], ALU.mult),
          rd=[b_pv], wr=[b_qt[par]], waits=tab_waits)
        E("sc", lambda g, par=par: g.copy(q_ch[par][:], p_v[:]),
          rd=[b_pv], wr=[b_q[par]], waits=[(P.ctr["ve"], tqt_)])
        for j in range(NDC):
            E("te", lambda g, j=j: g.transpose(
                p_v[:, j * 128:(j + 1) * 128], kn_f32[:, j * 128:(j + 1) * 128],
                ident_sb[:]), rd=[b_kn, b_const],
              wr=([b_pv] if j == 0 else []), wr2=([b_pv] if j > 0 else []))
        E("sc", lambda g, par=par: g.copy(knT_ch[par][:], p_v[:]),
          rd=[b_pv], wr=[b_knT[par]])

    def emit_retention(c):
        par = c % 2
        gpar = (c // 4) % 2
        for p in range(8):
            rpar = p % 2
            for half in range(2):
                E("te", lambda g, p=p, half=half, par=par, rpar=rpar: g.matmul(
                    p_r[rpar][:, half * 128:(half + 1) * 128],
                    knT_ch[par][64 * half:64 * (half + 1),
                                p * 128:(p + 1) * 128],
                    q_ch[par][64 * half:64 * (half + 1),
                              p * 128:(p + 1) * 128],
                    start=True, stop=True, tile_position=(64 * half, 0)),
                  rd=[b_knT[par], b_q[par]],
                  wr=([b_pr[rpar]] if half == 0 else []),
                  wr2=([b_pr[rpar]] if half == 1 else []))
            E("ve", lambda g, p=p, rpar=rpar: g.tensor_tensor(
                a_scr[rpar][:], p_r[rpar][:, 0:256],
                gam[:, p * 256:(p + 1) * 256], ALU.mult),
              rd=[b_pr[rpar]], wr=[b_a[rpar]], waits=tab_waits)
            for half in range(2):
                h = 2 * p + half
                E("te", lambda g, h=h, half=half, par=par, rpar=rpar: g.matmul(
                    p_r[rpar][64 * half:64 * (half + 1), 256:384],
                    v_ch[par][:, h * 64:(h + 1) * 64],
                    a_scr[rpar][:, half * 128:(half + 1) * 128],
                    start=True, stop=False, tile_position=(0, 64 * half)),
                  rd=[b_v[par], b_a[rpar]], wr2=[b_pr[rpar]], noinc=True)
                E("te", lambda g, p=p, half=half, par=par, rpar=rpar: g.matmul(
                    p_r[rpar][64 * half:64 * (half + 1), 256:384],
                    st_bf[64 * half:64 * (half + 1), p * 64:(p + 1) * 64],
                    qt_ch[par][64 * half:64 * (half + 1),
                               p * 128:(p + 1) * 128],
                    start=False, stop=True,
                    tile_position=(64 * half, 64 * half)),
                  rd=[b_stb, b_qt[par]], wr2=[b_pr[rpar]])
            for half in range(2):
                h = 2 * p + half
                E("te", lambda g, h=h, half=half, par=par, rpar=rpar: g.matmul(
                    p_r[rpar][64 * half:64 * (half + 1), 384:448],
                    kh_ch[par][:, h * 64:(h + 1) * 64],
                    v_ch[par][:, h * 64:(h + 1) * 64],
                    start=True, stop=True, tile_position=(0, 64 * half)),
                  rd=[b_kh[par], b_v[par]], wr2=[b_pr[rpar]])
            tye_ = E("sc", lambda g, p=p, rpar=rpar, c=c, gpar=gpar: g.copy(
                y_stage[gpar][:, p * 512 + (c % 4) * 128:
                              p * 512 + (c % 4) * 128 + 128],
                p_r[rpar][:, 256:384]),
              rd=[b_pr[rpar]],
              wr=([b_yst[gpar]] if (p == 0 and c % 4 == 0) else []),
              wr2=([b_yst[gpar]] if not (p == 0 and c % 4 == 0) else []))
            E("ve", lambda g, p=p, rpar=rpar: g.scalar_tensor_tensor(
                st_bf[:, p * 64:(p + 1) * 64], st_bf[:, p * 64:(p + 1) * 64],
                gcp[:, p:p + 1], p_r[rpar][:, 384:448], ALU.mult, ALU.add),
              rd=[b_stb, b_pr[rpar]], wr=[b_stb],
              waits=tab_waits + [(P.ctr["sc"], tye_)])

    def emit_tail(c):
        par = c % 2
        gpar = (c // 4) % 2
        if c % 4 == 3:
            gidx = c // 4
            for pr in range(8):
                E("gp", lambda g, gidx=gidx, gpar=gpar, pr=pr: g.dma_start(
                    out=y_sp.ap()[pr, :, gidx * 4 * CH:(gidx + 1) * 4 * CH],
                    in_=y_stage[gpar][:, pr * 512:(pr + 1) * 512]),
                  rd=[b_yst[gpar]], wr2=[b_ysp],
                  dma=f"ysp{(gpar, pr % 2)}")
        emit_dq_stages(c)
        if c == 2 and debug not in ("p0", "p0nocc", "ar1"):
            ar1_issue()

    QKV = [(p_q, b_pq, 0), (p_k, b_pk, 1024), (p_v, b_pv, 2048)]
    emit_proj(0, QKV[0])
    emit_proj(0, QKV[1])
    emit_kln_pre(0)
    for c in range(P1_CHUNKS):
        emit_proj(c, QKV[2])
        emit_kln_apply(c)
        if c + 1 < P1_CHUNKS:
            emit_proj(c + 1, QKV[0])
            emit_proj(c + 1, QKV[1])
        emit_transposes(c)
        if c + 1 < P1_CHUNKS:
            emit_kln_pre(c + 1)
        emit_retention(c)
        emit_tail(c)
    for c in range(P1_CHUNKS, P1_CHUNKS + 3):
        emit_dq_stages(c)
    if debug not in ("p0", "p0nocc", "ar1"):
        ar1_consume()

    if XQ_BARRIER and not SKIP_XQ:
        P.barrier()
    for dc in range(NDC if not SKIP_XQ else 0):
        for half in range(4):
            base = dc * L + half * 1024
            grp = list(b_xn)
            E("ve", lambda g, base=base: g.tensor_scalar(
                ksq_f32[:], xnT[:, base:base + 1024], sc_(1), CMAGIC,
                ALU.mult, ALU.add),
              rd=[b_scl] + grp, wr=[b_ksq])
            E("ve", lambda g, base=base: g.tensor_scalar(
                xnT[:, base:base + 1024], ksq_f32[:], CMAGIC, sc_(0),
                ALU.subtract, ALU.mult),
              rd=[b_ksq, b_scl],
              wr=(grp if (dc == 0 and half == 0) else []),
              wr2=(grp if not (dc == 0 and half == 0) else []))

    if debug == "p1":
        P.wait_all_dma("sy")
        return nc, P, locals()

    P.barrier()
    es_a.close()
    es_b = contextlib.ExitStack()
    SBb = lambda n, s, d: es_b.enter_context(nc.sbuf_tensor(n, s, d))
    PSb = lambda n, s, d: es_b.enter_context(nc.psum_tensor(n, s, d))

    wst = [SBb(f"wst{i}", [128, 1024], BF16) for i in range(4)]
    b_wst = [Buf() for _ in range(4)]
    stg = [SBb(f"stg{i}", [128, 512], BF16) for i in range(4)]
    b_stg = [Buf() for _ in range(4)]
    dqdw_in = [SBb(f"dqdw_in{i}", [128, 3072], F32) for i in range(2)]
    b_dqdwin = [Buf(), Buf()]
    dqd_s = SBb("dqd_s", [128, 288], F32)
    b_dqds = Buf()
    dqd_u = SBb("dqd_u", [128, 3072], F32)
    b_dqdu = Buf()
    dqd_nat = SBb("dqd_nat", [128, 3072], BF16)
    b_dqdn = Buf()
    dqd_tr = [SBb(f"dqd_tr{i}", [128, 3072], BF16) for i in range(2)]
    b_dqdt = [Buf(), Buf()]
    p_f = [PSb(f"p_f{i}", [128, 512], F32) for i in range(4)]
    b_pf = [Buf() for _ in range(4)]

    dqd_done = {}

    def dqd_ab(t):
        par = t % 2
        E("sy", lambda g, t=t, par=par: g.dma_start(
            out=dqdw_in[par][:], in_=w_down[t * 128:(t + 1) * 128, :]),
          wr=[b_dqdwin[par]], dma="dqdi")
        E("ve", lambda g, par=par: g.tensor_reduce(
            dqd_s[:, 0:96], dqdw_in[par][:].rearrange("p (b k) -> p b k",
                                                      k=32),
            axis=AXX, op=ALU.max, apply_absolute_value=True),
          rd=[b_dqdwin[par]], wr=[b_dqds])
        E("ve", lambda g: g.tensor_scalar(
            dqd_s[:, 96:192], dqd_s[:, 0:96], 1.0 / 7.0, 1e-12, ALU.mult,
            ALU.max), rd=[b_dqds], wr2=[b_dqds])
        E("ve", lambda g: g.reciprocal(dqd_s[:, 192:288], dqd_s[:, 96:192]),
          rd=[b_dqds], wr2=[b_dqds])
        E("gp", lambda g, par=par: g.tensor_tensor(
            dqd_u[:], dqdw_in[par][:],
            bass.AP(dqd_s, 192, [[288, 128], [1, 96], [0, 32]]), ALU.mult),
          rd=[b_dqdwin[par], b_dqds], wr=[b_dqdu])

    def dqd_cd(t):
        par = t % 2
        E("ve", lambda g: g.tensor_scalar(
            dqd_u[:], dqd_u[:], CMAGIC, -CMAGIC, ALU.add, ALU.add),
          rd=[b_dqdu], wr=[b_dqdu])
        E("gp", lambda g: g.tensor_tensor(
            dqd_nat[:], dqd_u[:],
            bass.AP(dqd_s, 96, [[288, 128], [1, 96], [0, 32]]), ALU.mult),
          rd=[b_dqdu, b_dqds], wr=[b_dqdn])
        for ic in range(NIC):
            E("sy", lambda g, ic=ic, par=par: g.dma_start(
                out=dqd_tr[par][:, ic * 128:(ic + 1) * 128],
                in_=dqd_nat[:, ic * 128:(ic + 1) * 128], transpose=True),
              rd=[b_dqdn], wr=([b_dqdt[par]] if ic == 0 else []),
              wr2=([b_dqdt[par]] if ic > 0 else []), dma=f"dqdt{par}_{ic % 4}")
        tout = E("gp", lambda g, par=par, t=t: g.dma_start(
            out=wdndq_sp.ap()[t, :, :], in_=dqd_tr[par][:]),
          rd=[b_dqdt[par]], dma="dqdo")
        dqd_done[t] = (P.dma_ctr("dqdo"), tout)

    dqd_emitted = [0]
    mats2b = [("up", wupdq_sp, up_sp, b_upsp, 10),
              ("gate", wgatedq_sp, gate_sp, b_gatesp, 11)]
    ldplan = [(m[0], m[1], oc) for m in mats2b for oc in range(NIC)]

    def load_wst(k):
        mname, wsp, oc = ldplan[k]
        E("sy", lambda g, oc=oc, wsp=wsp, k=k: g.dma_start(
            out=wst[k % 4][:], in_=wsp.ap()[oc, :, :]),
          wr=[b_wst[k % 4]], waits=[dq_done[(mname, oc)]], dma=f"wst{k % 4}")

    load_wst(0)
    load_wst(1)
    kk = [2]
    for (mname, wsp, osp, b_osp_, mxcol) in mats2b:
        for oc in range(NIC):
            wb = b_wst[oc % 4]
            if kk[0] < len(ldplan):
                load_wst(kk[0])
                kk[0] += 1
            for slab in range(NSLAB):
                pf = (oc * NSLAB + slab) % 4
                for dc in range(NDC):
                    E("te", lambda g, oc=oc, slab=slab, dc=dc, pf=pf: g.matmul(
                        p_f[pf][:], wst[oc % 4][:, dc * 128:(dc + 1) * 128],
                        xnT[:, dc * L + slab * SLAB:
                            dc * L + (slab + 1) * SLAB],
                        start=(dc == 0), stop=(dc == NDC - 1)),
                      rd=[wb] + (list(b_xn) if (oc == 0 and slab == 0)
                                 else []),
                      wr=([b_pf[pf]] if dc == 0 else []),
                      wr2=([b_pf[pf]] if dc == NDC - 1 else []),
                      noinc=(dc != NDC - 1))
                E("sc", lambda g, pf=pf: g.copy(stg[pf][:], p_f[pf][:]),
                  rd=[b_pf[pf]], wr=[b_stg[pf]])
                E("ve", lambda g, pf=pf: g.tensor_reduce(
                    col(7), stg[pf][:], axis=AXX, op=ALU.max,
                    apply_absolute_value=True),
                  rd=[b_stg[pf]], wr=[b_stat])
                E("ve", lambda g, first=(oc == 0 and slab == 0), mxcol=mxcol:
                  g.tensor_tensor(
                      sc_(mxcol), col(7) if first else sc_(mxcol), col(7),
                      ALU.max), rd=[b_stat], wr2=[b_mx])
                E("gp", lambda g, oc=oc, slab=slab, pf=pf, osp=osp:
                  g.dma_start(
                      out=osp.ap()[oc, :, slab * SLAB:(slab + 1) * SLAB],
                      in_=stg[pf][:]),
                  rd=[b_stg[pf]], wr2=[b_osp_], dma=f"spl{pf}")
            if oc % 3 == 2:
                if 0 < dqd_emitted[0] <= NDC:
                    dqd_cd(dqd_emitted[0] - 1)
                if dqd_emitted[0] < NDC:
                    dqd_ab(dqd_emitted[0])
                dqd_emitted[0] += 1
    while dqd_emitted[0] <= NDC:
        if 0 < dqd_emitted[0] <= NDC:
            dqd_cd(dqd_emitted[0] - 1)
        if dqd_emitted[0] < NDC:
            dqd_ab(dqd_emitted[0])
        dqd_emitted[0] += 1

    if debug == "p2b":
        P.wait_all_dma("sy")
        return nc, P, locals()

    b_mx.note_w(P.ctr["ve"], P.ctr["ve"].c)
    allreduce_max([sc_(10), sc_(11)], [b_mx], p_f[0][:], b_pf[0])
    bcast_scale(0, 2, 3, p_f[0][:], b_pf[0])
    bcast_scale(1, 4, 5, p_f[0][:], b_pf[0])

    P.barrier()
    es_x.close()
    es_b.close()
    es_c = contextlib.ExitStack()
    es_cio = contextlib.ExitStack()
    SBc = lambda n, s, d: es_c.enter_context(nc.sbuf_tensor(n, s, d))
    SBio = lambda n, s, d: es_cio.enter_context(nc.sbuf_tensor(n, s, d))

    attn_sb = SBc("attn_sb", [128, NDC * L], BF16)
    b_attn = Buf()
    woT_sb = SBc("woT_sb", [128, NDC * D], BF16)
    b_wo = Buf()
    y_sl = [SBc(f"y_sl{i}", [128, 4096], BF16) for i in range(2)]
    b_ysl = [Buf(), Buf()]
    p_w = [es_c.enter_context(nc.psum_tensor(f"p_w{i}", [128, 512], F32))
           for i in range(2)]
    b_pw = [Buf(), Buf()]
    io_a = [SBio(f"io_a{i}", [128, 4096], BF16) for i in range(2)]
    b_ioa = [Buf(), Buf()]
    io_b = [SBio(f"io_b{i}", [128, 4096], BF16) for i in range(2)]
    b_iob = [Buf(), Buf()]
    io_c = [SBio(f"io_c{i}", [128, 4096], BF16) for i in range(2)]
    b_ioc = [Buf(), Buf()]
    fs1 = [SBio(f"fs1_{i}", [128, 2048], F32) for i in range(2)]
    b_fs1 = [Buf(), Buf()]
    fs2 = [SBio(f"fs2_{i}", [128, 2048], F32) for i in range(2)]
    b_fs2 = [Buf(), Buf()]
    fs3 = [SBio(f"fs3_{i}", [128, 2048], F32) for i in range(2)]
    b_fs3 = [Buf(), Buf()]

    for dc in range(NDC):
        E("gp", lambda g, dc=dc: g.dma_start(
            out=woT_sb[:, dc * D:(dc + 1) * D],
            in_=woT[dc * 128:(dc + 1) * 128, :]),
          wr=([b_wo] if dc == 0 else []), wr2=([b_wo] if dc > 0 else []),
          dma=f"wo{dc % 2}")

    wo_units = []
    for slab in range(NSLAB):
        wo_units.append(("load", slab))
        for oc in range(NDC):
            wo_units.append(("mm", slab, oc))
    wo_i = [0]

    def emit_wo_unit():
        if wo_i[0] >= len(wo_units):
            return
        u = wo_units[wo_i[0]]
        wo_i[0] += 1
        if u[0] == "load":
            slab = u[1]
            ypar = slab % 2
            for pr in range(8):
                E("sy", lambda g, slab=slab, ypar=ypar, pr=pr: g.dma_start(
                    out=y_sl[ypar][:, pr * 512:(pr + 1) * 512],
                    in_=y_sp.ap()[pr, :, slab * SLAB:(slab + 1) * SLAB]),
                  rd=[b_ysp], wr=([b_ysl[ypar]] if pr == 0 else []),
                  wr2=([b_ysl[ypar]] if pr > 0 else []),
                  dma=f"ysl{(ypar, pr % 2)}")
        else:
            slab, oc = u[1], u[2]
            ypar = slab % 2
            wpar = (slab * NDC + oc) % 2
            for dc in range(NDC):
                E("te", lambda g, oc=oc, dc=dc, ypar=ypar, wpar=wpar: g.matmul(
                    p_w[wpar][:],
                    woT_sb[:, dc * D + oc * 128: dc * D + (oc + 1) * 128],
                    y_sl[ypar][:, dc * 512:(dc + 1) * 512],
                    start=(dc == 0), stop=(dc == NDC - 1)),
                  rd=[b_wo, b_ysl[ypar]],
                  wr=([b_pw[wpar]] if dc == 0 else []),
                  wr2=([b_pw[wpar]] if dc == NDC - 1 else []),
                  noinc=(dc != NDC - 1))
            E("sc", lambda g, oc=oc, slab=slab, wpar=wpar: g.copy(
                attn_sb[:, oc * L + slab * SLAB: oc * L + (slab + 1) * SLAB],
                p_w[wpar][:]), rd=[b_pw[wpar]], wr2=[b_attn])

    for ic in range(NIC):
        par = ic % 2
        E("sy", lambda g, ic=ic, par=par: g.dma_start(
            out=io_a[par][:], in_=gate_sp.ap()[ic, :, :]),
          rd=[b_gatesp], wr=[b_ioa[par]], dma=f"ioa{par}")
        for hh in range(2):
            h0 = hh * 2048
            E("sc", lambda g, par=par, h0=h0: g.activation(
                fs1[par][:], io_a[par][:, h0:h0 + 2048], AF.Identity,
                bias=cmag[:], scale=sc_(5)),
              rd=[b_ioa[par], b_scl, b_cmag], wr=[b_fs1[par]])
            E("ve", lambda g, par=par: g.tensor_scalar(
                fs2[par][:], fs1[par][:], CMAGIC, sc_(4), ALU.subtract,
                ALU.mult),
              rd=[b_fs1[par], b_scl], wr=[b_fs2[par]])
            E("sc", lambda g, par=par: g.activation(fs1[par][:], fs2[par][:],
                                                    AF.Sigmoid),
              rd=[b_fs2[par]], wr=[b_fs1[par]])
            E("sc", lambda g, par=par: g.activation(
                fs3[par][:], fs1[par][:], AF.Identity, bias=cmag[:],
                scale=65536.0),
              rd=[b_fs1[par], b_cmag], wr=[b_fs3[par]])
            E("ve", lambda g, par=par: g.tensor_scalar(
                fs1[par][:], fs3[par][:], -CMAGIC, 65535.0, ALU.add, ALU.min),
              rd=[b_fs3[par]], wr=[b_fs1[par]])
            E("ve", lambda g, par=par: g.tensor_scalar(
                fs3[par][:], fs1[par][:], 0.0, 1.0 / 65536.0, ALU.max,
                ALU.mult),
              rd=[b_fs1[par]], wr=[b_fs3[par]])
            E("ve", lambda g, par=par, h0=h0: g.tensor_tensor(
                io_c[par][:, h0:h0 + 2048], fs2[par][:], fs3[par][:],
                ALU.mult),
              rd=[b_fs2[par], b_fs3[par]],
              wr=([b_ioc[par]] if hh == 0 else []),
              wr2=([b_ioc[par]] if hh else []))
        E("ve", lambda g, par=par: g.tensor_reduce(
            col(7), io_c[par][:], axis=AXX, op=ALU.max,
            apply_absolute_value=True), rd=[b_ioc[par]], wr=[b_stat])
        E("ve", lambda g, first=(ic == 0): g.tensor_tensor(
            sc_(12), col(7) if first else sc_(12), col(7), ALU.max),
          rd=[b_stat], wr2=[b_mx])
        E("gp", lambda g, ic=ic, par=par: g.dma_start(
            out=g2_sp.ap()[ic, :, :], in_=io_c[par][:]),
          rd=[b_ioc[par]], wr2=[b_g2sp], dma=f"ioc{par}")
        for _ in range(3):
            emit_wo_unit()
    while wo_i[0] < len(wo_units):
        emit_wo_unit()

    if debug == "p2d":
        P.wait_all_dma("sy")
        return nc, P, locals()

    b_mx.note_w(P.ctr["ve"], P.ctr["ve"].c)
    allreduce_max([sc_(12)], [b_mx], p_w[0][:], b_pw[0])
    bcast_scale(0, 6, 7, p_w[0][:], b_pw[0])

    for ic in range(NIC):
        par = ic % 2
        E("sy", lambda g, ic=ic, par=par: g.dma_start(
            out=io_a[par][:], in_=up_sp.ap()[ic, :, :]),
          rd=[b_upsp], wr=[b_ioa[par]], dma=f"ioa{par}")
        E("sy", lambda g, ic=ic, par=par: g.dma_start(
            out=io_b[par][:], in_=g2_sp.ap()[ic, :, :]),
          rd=[b_g2sp], wr=[b_iob[par]], dma=f"iob{par}")
        for hh in range(2):
            h0 = hh * 2048
            E("sc", lambda g, par=par, h0=h0: g.activation(
                fs1[par][:], io_a[par][:, h0:h0 + 2048], AF.Identity,
                bias=cmag[:], scale=sc_(3)),
              rd=[b_ioa[par], b_scl, b_cmag], wr=[b_fs1[par]])
            E("ve", lambda g, par=par: g.tensor_scalar(
                fs2[par][:], fs1[par][:], CMAGIC, sc_(2), ALU.subtract,
                ALU.mult),
              rd=[b_fs1[par], b_scl], wr=[b_fs2[par]])
            E("sc", lambda g, par=par, h0=h0: g.activation(
                fs3[par][:], io_b[par][:, h0:h0 + 2048], AF.Identity,
                bias=cmag[:], scale=sc_(7)),
              rd=[b_iob[par], b_scl, b_cmag], wr=[b_fs3[par]])
            E("ve", lambda g, par=par: g.tensor_scalar(
                fs1[par][:], fs3[par][:], CMAGIC, sc_(6), ALU.subtract,
                ALU.mult),
              rd=[b_fs3[par], b_scl], wr=[b_fs1[par]])
            E("ve", lambda g, par=par, h0=h0: g.tensor_tensor(
                io_c[par][:, h0:h0 + 2048], fs1[par][:], fs2[par][:],
                ALU.mult),
              rd=[b_fs1[par], b_fs2[par]],
              wr=([b_ioc[par]] if hh == 0 else []),
              wr2=([b_ioc[par]] if hh else []))
        E("ve", lambda g, par=par: g.tensor_reduce(
            col(7), io_c[par][:], axis=AXX, op=ALU.max,
            apply_absolute_value=True), rd=[b_ioc[par]], wr=[b_stat])
        E("ve", lambda g, first=(ic == 0): g.tensor_tensor(
            sc_(13), col(7) if first else sc_(13), col(7), ALU.max),
          rd=[b_stat], wr2=[b_mx])
        E("gp", lambda g, ic=ic, par=par: g.dma_start(
            out=o_sp.ap()[ic, :, :], in_=io_c[par][:]),
          rd=[b_ioc[par]], wr2=[b_osp], dma=f"ioc{par}")

    if debug == "p2e":
        P.wait_all_dma("sy")
        return nc, P, locals()

    b_mx.note_w(P.ctr["ve"], P.ctr["ve"].c)
    allreduce_max([sc_(13)], [b_mx], p_w[0][:], b_pw[0])
    bcast_scale(0, 8, 9, p_w[0][:], b_pw[0])

    P.barrier()
    es_cio.close()
    es_d = contextlib.ExitStack()
    SBd = lambda n, s, d: es_d.enter_context(nc.sbuf_tensor(n, s, d))
    oq_sl = [SBd(f"oq_sl{i}", [128, NIC * 512], BF16) for i in range(2)]
    b_oq = [Buf(), Buf()]
    wdn = [SBd(f"wdn{i}", [128, NIC * 128], BF16) for i in range(2)]
    b_wdn = [Buf(), Buf()]
    xTt = [SBd(f"xTt{i}", [128, 512], F32) for i in range(2)]
    b_xT = [Buf(), Buf()]
    outt = [SBd(f"outt{i}", [128, 512], F32) for i in range(2)]
    b_out = [Buf(), Buf()]
    ofs = [SBd(f"ofs{i}", [128, 2048], F32) for i in range(2)]
    b_ofs = [Buf(), Buf()]
    ofs2 = [SBd(f"ofs2{i}", [128, 2048], F32) for i in range(2)]
    b_ofs2 = [Buf(), Buf()]
    ep1 = SBd("ep1", [128, 512], F32)
    b_ep1 = Buf()
    p_d = [es_d.enter_context(nc.psum_tensor(f"p_d{i}", [128, 512], F32))
           for i in range(4)]
    b_pd = [Buf() for _ in range(4)]

    wdn_wait = (P.dma_ctr("dqdo"), max(t for (_c, t) in dqd_done.values()))
    last_out = 0

    def stage_o(slab):
        spar = slab % 2
        for ic in range(NIC):
            E("sy", lambda g, ic=ic, slab=slab, spar=spar: g.dma_start(
                out=oq_sl[spar][:, ic * 512:(ic + 1) * 512],
                in_=o_sp.ap()[ic, :, slab * SLAB:(slab + 1) * SLAB]),
              rd=[b_osp], wr=([b_oq[spar]] if ic == 0 else []),
              wr2=([b_oq[spar]] if ic > 0 else []), dma=f"oq{ic % 4}")
        for third in range(6):
            a0, a1 = third * 2048, (third + 1) * 2048
            tp = third % 2
            E("sc", lambda g, spar=spar, a0=a0, a1=a1, tp=tp: g.activation(
                ofs[tp][:], oq_sl[spar][:, a0:a1], AF.Identity, bias=cmag[:],
                scale=sc_(9)), rd=[b_oq[spar], b_scl, b_cmag], wr=[b_ofs[tp]])
            E("ve", lambda g, spar=spar, a0=a0, a1=a1, tp=tp: g.tensor_scalar(
                oq_sl[spar][:, a0:a1], ofs[tp][:], CMAGIC, sc_(8),
                ALU.subtract, ALU.mult),
              rd=[b_ofs[tp], b_scl], wr2=[b_oq[spar]])

    stage_o(0)
    for slab in range(NSLAB):
        spar = slab % 2
        if slab + 1 < NSLAB:
            stage_o(slab + 1)
        if slab == 0:
            E("sy", lambda g: g.dma_start(
                out=wdn[0][:], in_=wdndq_sp.ap()[0, :, :]),
              wr=[b_wdn[0]], waits=[wdn_wait], dma="wdn0")
        for oc in range(NDC):
            pd = oc % 4
            wpar2 = (slab * NDC + oc) % 2
            nslab, noc = (slab, oc + 1) if oc + 1 < NDC else (slab + 1, 0)
            if nslab < NSLAB:
                E("sy", lambda g, noc=noc, w2=1 - wpar2: g.dma_start(
                    out=wdn[w2][:], in_=wdndq_sp.ap()[noc, :, :]),
                  wr=[b_wdn[1 - wpar2]], waits=[wdn_wait],
                  dma=f"wdn{1 - wpar2}")
            for ic in range(NIC):
                E("te", lambda g, ic=ic, spar=spar, pd=pd, wpar2=wpar2:
                  g.matmul(
                      p_d[pd][:], wdn[wpar2][:, ic * 128:(ic + 1) * 128],
                      oq_sl[spar][:, ic * 512:(ic + 1) * 512],
                      start=(ic == 0), stop=(ic == NIC - 1)),
                  rd=[b_wdn[wpar2], b_oq[spar]],
                  wr=([b_pd[pd]] if ic == 0 else []),
                  wr2=([b_pd[pd]] if ic == NIC - 1 else []),
                  noinc=(ic != NIC - 1))
            opar = oc % 2
            E("sy", lambda g, oc=oc, slab=slab, opar=opar: g.dma_start(
                out=xTt[opar][:],
                in_=xT[oc * 128:(oc + 1) * 128,
                       slab * SLAB:(slab + 1) * SLAB]),
              wr=[b_xT[opar]], dma=f"xt{opar}")
            E("ve", lambda g, oc=oc, slab=slab, pd=pd: g.tensor_tensor(
                ep1[:], p_d[pd][:],
                attn_sb[:, oc * L + slab * SLAB: oc * L + (slab + 1) * SLAB],
                ALU.add), rd=[b_pd[pd], b_attn], wr=[b_ep1])
            E("ve", lambda g, opar=opar: g.tensor_tensor(
                outt[opar][:], ep1[:], xTt[opar][:], ALU.add),
              rd=[b_ep1, b_xT[opar]], wr=[b_out[opar]])
            last_out = E("gp", lambda g, oc=oc, slab=slab, opar=opar:
                         g.dma_start(
                             out=outT[oc * 128:(oc + 1) * 128,
                                      slab * SLAB:(slab + 1) * SLAB],
                             in_=outt[opar][:]),
                         rd=[b_out[opar]], dma=f"out{opar}")

    P.wait_all_dma("sy")
    return nc, P, locals()


def finalize(nc, P):
    with nc.Block() as blk:
        @blk.sync
        def _(g):
            for f in P.ops["sy"]:
                f(g)

        @blk.scalar
        def _(g):
            for f in P.ops["sc"]:
                f(g)

        @blk.vector
        def _(g):
            for f in P.ops["ve"]:
                f(g)

        @blk.tensor
        def _(g):
            for f in P.ops["te"]:
                f(g)

        @blk.gpsimd
        def _(g):
            for f in P.ops["gp"]:
                f(g)
    return nc


def host_consts():
    idx = np.arange(128, dtype=np.float32)
    emat = idx[None, :] - idx[:, None]
    emat = np.where(emat >= 0, emat, 1e9).astype(np.float32)
    utri = (idx[None, :] >= idx[:, None]).astype(np.float32)
    rampq = np.ascontiguousarray(
        np.broadcast_to(idx[None, :] + 1.0, (128, 128))).astype(np.float32)
    rampk = np.ascontiguousarray(
        np.broadcast_to((127.0 - idx)[:, None], (128, 64))).astype(np.float32)
    return dict(emat=np.ascontiguousarray(emat),
                utri=np.ascontiguousarray(utri), rampq=rampq, rampk=rampk,
                ident=np.eye(128, dtype=np.float32))


def host_inputs(L0_attn_x, L0_attn_W_qkv, L0_attn_W_o, L0_attn_log_gamma,
                L0_ffn_w_gate, L0_ffn_w_up, L0_ffn_w_down):
    x = np.asarray(L0_attn_x, dtype=np.float32)
    wqkvT = np.ascontiguousarray(np.asarray(L0_attn_W_qkv, np.float32).T)
    woT = np.ascontiguousarray(np.asarray(L0_attn_W_o, np.float32).T)
    lg = np.asarray(L0_attn_log_gamma, np.float32)
    lgp8 = np.empty((128, 8), np.float32)
    for j in range(8):
        lgp8[:64, j] = lg[2 * j]
        lgp8[64:, j] = lg[2 * j + 1]
    lg16 = np.ascontiguousarray(
        np.broadcast_to(lg[None, :], (128, 16))).astype(np.float32)
    shared = dict(wqkvT=wqkvT, woT=woT,
                  w_up=np.ascontiguousarray(np.asarray(L0_ffn_w_up,
                                                       np.float32)),
                  w_gate=np.ascontiguousarray(np.asarray(L0_ffn_w_gate,
                                                         np.float32)),
                  w_down=np.ascontiguousarray(np.asarray(L0_ffn_w_down,
                                                         np.float32)),
                  lgp8=lgp8, lg16=lg16, **host_consts())
    in_maps = []
    for c in range(NCORES):
        xc = np.ascontiguousarray(x[c])
        in_maps.append(dict(x_nat=xc, xT=np.ascontiguousarray(xc.T), **shared))
    return in_maps


def kernel(**inputs):
    nc, P, _ = build_program()
    finalize(nc, P)
    in_maps = host_inputs(**inputs)
    res = run_bass_kernel_spmd(nc, in_maps, core_ids=list(range(NCORES)))
    out = np.empty((NCORES, L, D), dtype=np.float32)
    for c in range(NCORES):
        out[c] = np.asarray(res.results[c]["outT"]).T
    return out


# version 15
# speedup vs baseline: 1.7149x; 1.0739x over previous
import contextlib
import numpy as np
import concourse.bass as bass
import concourse.mybir as mybir
import concourse.bass_isa as bass_isa
from concourse.bass_utils import run_bass_kernel_spmd

F32 = mybir.dt.float32
BF16 = mybir.dt.bfloat16
ALU = mybir.AluOpType
AF = mybir.ActivationFunctionType
AXX = mybir.AxisListType.X
RED = bass_isa.ReduceOp

NCORES = 8
L, D, H, DK, I3 = 4096, 1024, 16, 64, 3072
CH, SLAB = 128, 512
NCH, NSLAB, NDC, NIC = L // CH, L // SLAB, D // 128, I3 // 128
CMAGIC = 12582912.0
P0_TILES = NCH
SKIP_TABLES = False
SKIP_WQKV = False
SKIP_DQ = False
P1_CHUNKS = NCH
P1_CUT = 99
XQ_MODE = 0
SKIP_XQ = False
XQ_BARRIER = False
XQ_OPS = 3
SKIP_RET = False
SKIP_KLN = False
SKIP_TR = False
LN_EPS = 1e-05
QMAX = 32767.0


class Ctr:
    __slots__ = ("h", "c", "name")

    def __init__(self, h, name):
        self.h, self.c, self.name = h, 0, name

    def bump(self, n=1):
        self.c += n
        return self.c


class Buf:
    __slots__ = ("w", "r")

    def __init__(self):
        self.w, self.r = {}, {}

    def note_w(self, ctr, tok):
        self.w = {ctr: tok}
        self.r = {}

    def note_w2(self, ctr, tok):
        if self.w.get(ctr, 0) < tok:
            self.w[ctr] = tok

    def note_r(self, ctr, tok):
        if self.r.get(ctr, 0) < tok:
            self.r[ctr] = tok


class Prog:
    def __init__(self, nc):
        self.nc = nc
        self.ops = {e: [] for e in ("sy", "sc", "ve", "te", "gp")}
        self.es = contextlib.ExitStack()
        self._mk = lambda n: Ctr(self.es.enter_context(nc.semaphore(n)), n)
        self.ctr = {"sc": self._mk("c_sc"), "ve": self._mk("c_ve"),
                    "te": self._mk("c_te"), "gp": self._mk("c_gp"),
                    "cc": self._mk("c_cc")}
        self.dstr = {}
        self.plan = []
        self.waited = {e: {} for e in self.ops}

    def dma_ctr(self, name):
        if name not in self.dstr:
            self.dstr[name] = self._mk("dm_" + name)
        return self.dstr[name]

    def E(self, eng, build, rd=(), wr=(), wr2=(), waits=(), dma=None,
          cc=False, noinc=False):
        if cc:
            ctr, n = self.ctr["cc"], 1
        elif dma is not None:
            ctr, n = self.dma_ctr(dma), 16
        else:
            ctr, n = self.ctr[eng], 1
        need = {}
        if dma is not None and ctr.c:
            need[ctr] = ctr.c

        def add(c, v):
            if v and need.get(c, 0) < v:
                need[c] = v
        for (c, v) in waits:
            add(c, v)
        for b in rd:
            for c, v in b.w.items():
                add(c, v)
        for b in wr:
            for src in (b.w, b.r):
                for c, v in src.items():
                    add(c, v)
        wm = self.waited[eng]
        final = []
        for c, v in need.items():
            if wm.get(c, 0) >= v:
                continue
            final.append((c.h, v))
            wm[c] = v
        tok = ctr.c if noinc else ctr.bump(n)

        pad = (eng == "te" and not noinc and not cc)

        def f(g, _w=tuple(final), _b=build, _h=ctr.h, _n=n, _ni=noinc,
              _pad=pad):
            for (h, v) in _w:
                g.wait_ge(h, v)
            ins = _b(g)
            if not _ni:
                ins.then_inc(_h, _n)
            if _pad:
                g.nop()
                g.nop()
                g.nop()
                g.nop()
        self.ops[eng].append(f)
        self.plan.append(
            (eng, ctr, (None if noinc else tok), tuple(need.items()),
             tuple(id(b) for b in rd), tuple(id(b) for b in wr)
             + tuple(id(b) for b in wr2)))
        for b in rd:
            b.note_r(ctr, tok)
        for b in wr:
            b.note_w(ctr, tok)
        for b in wr2:
            b.note_w2(ctr, tok)
        return tok

    def W(self, eng, waits):
        wm = self.waited[eng]
        final = []
        final2 = []
        for c, v in waits:
            if v and wm.get(c, 0) < v:
                final.append((c.h, v))
                final2.append((c, v))
                wm[c] = v
        if final:
            def f(g, _w=tuple(final)):
                for (h, v) in _w:
                    g.wait_ge(h, v)
            self.ops[eng].append(f)
            self.plan.append((eng, None, None, tuple((c, v) for c, v in final2),
                              (), ()))

    def barrier(self):
        snap = [(c, c.c) for c in self.ctr.values()]
        snap += [(c, c.c) for c in self.dstr.values()]
        for e in self.ops:
            self.W(e, snap)

    def wait_all_dma(self, eng):
        self.W(eng, [(c, c.c) for c in self.dstr.values()])


def build_program(debug=None):
    nc = bass.Bass("TRN2", target_bir_lowering=False)
    P = Prog(nc)
    E, W = P.E, P.W

    dT = lambda n, s, k: nc.dram_tensor(n, s, F32, kind=k)
    x_nat = dT("x_nat", [L, D], "ExternalInput").ap()
    xT = dT("xT", [D, L], "ExternalInput").ap()
    wqkvT = dT("wqkvT", [D, 3 * D], "ExternalInput").ap()
    woT = dT("woT", [D, D], "ExternalInput").ap()
    w_up = dT("w_up", [I3, D], "ExternalInput").ap()
    w_gate = dT("w_gate", [I3, D], "ExternalInput").ap()
    w_down = dT("w_down", [D, I3], "ExternalInput").ap()
    lgp8 = dT("lgp8", [128, 8], "ExternalInput").ap()
    lg16 = dT("lg16", [128, 16], "ExternalInput").ap()
    emat = dT("emat", [128, 128], "ExternalInput").ap()
    utri = dT("utri", [128, 128], "ExternalInput").ap()
    rampq = dT("rampq", [128, 128], "ExternalInput").ap()
    rampk = dT("rampk", [128, 64], "ExternalInput").ap()
    ident = dT("ident", [128, 128], "ExternalInput").ap()
    outT = dT("outT", [D, L], "ExternalOutput").ap()

    def scratch(name, shape, dbgkey):
        kind = "ExternalOutput" if debug == dbgkey else "Internal"
        return nc.dram_tensor(name, shape, BF16, kind=kind)

    y_sp = scratch("y_sp", [8, 128, L], "p1")
    up_sp = scratch("up_sp", [NIC, 128, L], "p2b")
    gate_sp = scratch("gate_sp", [NIC, 128, L], "none")
    g2_sp = scratch("g2_sp", [NIC, 128, L], "p2d")
    o_sp = scratch("o_sp", [NIC, 128, L], "p2e")
    wupdq_sp = nc.dram_tensor(
        "wupdq_sp", [NIC, 128, NDC * 128], BF16,
        kind="ExternalOutput" if debug == "dq" else "Internal")
    wgatedq_sp = nc.dram_tensor("wgatedq_sp", [NIC, 128, NDC * 128], BF16)
    wdndq_sp = nc.dram_tensor("wdndq_sp", [NDC, 128, NIC * 128], BF16)
    pmax_sp = nc.dram_tensor("pmax_sp", [4, 128], F32)
    cc_in = nc.dram_tensor("cc_in", [1, 16], F32)
    cc_out = nc.dram_tensor("cc_out", [1, 16], F32)
    b_ccin, b_ccout = Buf(), Buf()
    b_ysp, b_upsp, b_gatesp, b_g2sp, b_osp = (Buf() for _ in range(5))
    dbg_xn = None
    if debug in ("p0", "p0nocc", "ar1"):
        dbg_xn = nc.dram_tensor("dbg_xn", [128, NDC * L], BF16,
                                kind="ExternalOutput")

    es_g, es_x, es_a = (contextlib.ExitStack() for _ in range(3))
    SBg = lambda n, s, d: es_g.enter_context(nc.sbuf_tensor(n, s, d))
    SBa = lambda n, s, d: es_a.enter_context(nc.sbuf_tensor(n, s, d))
    PSa = lambda n, s, d: es_a.enter_context(nc.psum_tensor(n, s, d))

    stat = SBg("stat", [128, 16], F32)
    col = lambda i: stat[:, i:i + 1]
    b_stat = Buf()
    kst = SBg("kst", [128, 80], F32)
    b_kst = Buf()
    ccsb = SBg("ccsb", [1, 16], F32)
    b_ccsb = Buf()
    pmrow = SBg("pmrow", [1, 128], F32)
    b_pmrow = Buf()
    b_pmax = Buf()
    scl = SBg("scl", [128, 16], F32)
    cmag = SBg("cmag", [128, 1], F32)
    b_cmag = Buf()
    utri_sb = SBg("utri_sb", [128, 128], F32)
    ident_sb = SBg("ident_sb", [128, 128], F32)
    sc_ = lambda i: scl[:, i:i + 1]
    b_scl = Buf()
    b_mx = Buf()

    xnT = es_x.enter_context(nc.sbuf_tensor("xnT", [128, NDC * L], BF16, side="right"))
    b_xn = [Buf() for _ in range(NCH)]

    wqkv_sb = SBa("wqkv_sb", [128, NDC * 3 * D], BF16)
    b_wqkv = Buf()
    gam = SBa("gam", [128, H * 128], BF16)
    cq = SBa("cq", [128, 1024], BF16)
    ck = SBa("ck", [128, 1024], BF16)
    gcp = SBa("gcp", [128, 8], F32)
    b_tab = Buf()
    lg16_sb = SBa("lg16_sb", [128, 16], F32)
    lgp8_sb = SBa("lgp8_sb", [128, 8], F32)
    emat_sb = SBa("emat_sb", [128, 128], F32)

    rampq_sb = SBa("rampq_sb", [128, 128], F32)
    rampk_sb = SBa("rampk_sb", [128, 64], F32)

    b_const = Buf()
    xin = [SBa(f"xin{i}", [128, 1024], F32) for i in range(2)]
    b_xin = [Buf(), Buf()]
    scr1 = SBa("scr1", [128, 1024], F32)
    b_scr1 = Buf()
    q_tok = SBa("q_tok", [128, 1024], F32)
    b_qtok = Buf()
    kn_f32 = SBa("kn_f32", [128, 1024], F32)
    b_kn = Buf()
    ksq_f32 = SBa("ksq_f32", [128, 1024], F32)
    b_ksq = Buf()
    k_raw = SBa("k_raw", [128, 1024], F32)
    b_kraw = Buf()
    q_ch = [SBa(f"q_ch{i}", [128, 1024], BF16) for i in range(2)]
    b_q = [Buf(), Buf()]
    qt_ch = [SBa(f"qt_ch{i}", [128, 1024], BF16) for i in range(2)]
    b_qt = [Buf(), Buf()]
    knT_ch = [SBa(f"knT_ch{i}", [128, 1024], BF16) for i in range(2)]
    b_knT = [Buf(), Buf()]
    kh_ch = [SBa(f"kh_ch{i}", [128, 1024], BF16) for i in range(2)]
    b_kh = [Buf(), Buf()]
    v_ch = [SBa(f"v_ch{i}", [128, 1024], BF16) for i in range(2)]
    b_v = [Buf(), Buf()]
    a_scr = [SBa(f"a_scr{i}", [128, 256], BF16) for i in range(2)]
    b_a = [Buf(), Buf()]
    st_bf = SBa("st_bf", [128, 512], BF16)
    b_stb = Buf()
    y_stage = [SBa(f"y_stage{i}", [128, 4096], BF16) for i in range(2)]
    b_yst = [Buf(), Buf()]

    dq_in = [xin[0], xin[1], SBa("dq_in2", [128, 1024], F32)]
    b_dqin = [b_xin[0], b_xin[1], Buf()]
    dq_s = [SBa(f"dq_s{i}", [128, 96], F32) for i in range(2)]
    b_dqs = [Buf(), Buf()]
    dq_u = [SBa("dq_u0", [128, 1024], F32), scr1]
    b_dqu = [Buf(), b_scr1]
    dq_nat = [SBa(f"dq_nat{i}", [128, 1024], BF16) for i in range(2)]
    b_dqn = [Buf(), Buf()]
    dq_tr = [SBa(f"dq_tr{i}", [128, 1024], BF16) for i in range(2)]
    b_dqt = [Buf(), Buf()]

    p_q = PSa("p_q", [128, 1024], F32)
    b_pq = Buf()
    p_k = PSa("p_k", [128, 1024], F32)
    b_pk = Buf()
    p_v = PSa("p_v", [128, 1024], F32)
    b_pv = Buf()
    p_r = [PSa(f"p_r{i}", [128, 512], F32) for i in range(2)]
    b_pr = [Buf(), Buf()]

    for (dst, src) in [(lgp8_sb, lgp8), (lg16_sb, lg16), (emat_sb, emat),
                       (utri_sb, utri), (rampq_sb, rampq), (rampk_sb, rampk),
                       (ident_sb, ident)]:
        E("sy", lambda g, d=dst, s=src: g.dma_start(out=d[:], in_=s[:]),
          wr2=[b_const], dma="cst")
    if not SKIP_WQKV:
        for dc in range(NDC):
            E("gp", lambda g, dc=dc: g.dma_start(
                out=wqkv_sb[:, dc * 3072:(dc + 1) * 3072],
                in_=wqkvT[dc * 128:(dc + 1) * 128, :]),
              wr=([b_wqkv] if dc == 0 else []),
              wr2=([b_wqkv] if dc > 0 else []), dma=f"wq{dc % 2}")
    E("gp", lambda g: g.memset(ccsb[:], 0.0), wr=[b_ccsb])
    E("gp", lambda g: g.memset(cmag[:], CMAGIC), wr=[b_cmag])
    E("gp", lambda g: g.dma_start(out=cc_in.ap(), in_=ccsb[:]),
      rd=[b_ccsb], wr=[b_ccin], dma="ccd")
    for fn in (AF.Sigmoid, AF.Ln):
        E("sc", lambda g, fn=fn: g.activation(lg16_sb[:], lg16_sb[:], fn),
          rd=[b_const], wr=[b_const])
        E("sc", lambda g, fn=fn: g.activation(lgp8_sb[:], lgp8_sb[:], fn),
          rd=[b_const], wr=[b_const])
    E("sc", lambda g: g.activation(gcp[:], lgp8_sb[:], AF.Exp, scale=128.0),
      rd=[b_const], wr2=[b_tab])
    for h in range(H if not SKIP_TABLES else 0):
        E("ve", lambda g, h=h: g.tensor_scalar(
            scr1[:, 0:128], emat_sb[:], lg16_sb[:, h:h + 1],
            None, ALU.mult), rd=[b_const], wr=[b_scr1])
        E("sc", lambda g, h=h: g.activation(
            gam[:, h * 128:(h + 1) * 128], scr1[:, 0:128],
            AF.Exp), rd=[b_scr1], wr2=[b_tab])
        E("ve", lambda g, h=h: g.tensor_tensor(
            gam[:, h * 128:(h + 1) * 128], gam[:, h * 128:(h + 1) * 128],
            utri_sb[:], ALU.mult), rd=[b_tab, b_const], wr2=[b_tab])
        E("ve", lambda g, h=h: g.tensor_scalar(
            scr1[:, 128:192], rampk_sb[:], lg16_sb[:, h:h + 1],
            None, ALU.mult), rd=[b_const], wr=[b_scr1])
        E("sc", lambda g, h=h: g.activation(
            ck[:, h * 64:(h + 1) * 64], scr1[:, 128:192], AF.Exp),
          rd=[b_scr1], wr2=[b_tab])
    for oc in range(8 if not SKIP_TABLES else 0):
        E("ve", lambda g, oc=oc: g.tensor_scalar(
            scr1[:, 0:128], rampq_sb[:], lgp8_sb[:, oc:oc + 1],
            None, ALU.mult), rd=[b_const], wr=[b_scr1])
        E("sc", lambda g, oc=oc: g.activation(
            cq[:, oc * 128:(oc + 1) * 128], scr1[:, 0:128],
            AF.Exp), rd=[b_scr1], wr2=[b_tab])
    E("gp", lambda g: g.memset(st_bf[:], 0.0), wr=[b_stb])

    tab_waits = [(P.ctr["sc"], P.ctr["sc"].c), (P.ctr["ve"], P.ctr["ve"].c)]

    for i in range(P0_TILES):
        par = i % 2
        t0 = i * CH
        E("sy", lambda g, i=i, par=par: g.dma_start(
            out=xin[par][:], in_=x_nat[i * CH:(i + 1) * CH, :]),
          wr=[b_xin[par]], dma=f"xin{par}")
        E("sc", lambda g, par=par: g.activation(scr1[:], xin[par][:],
                                                AF.Square),
          rd=[b_xin[par]], wr=[b_scr1])
        E("ve", lambda g, par=par: g.tensor_reduce(
            col(0), xin[par][:], axis=AXX, op=ALU.add),
          rd=[b_xin[par]], wr=[b_stat])
        E("ve", lambda g: g.tensor_reduce(col(1), scr1[:], axis=AXX,
                                          op=ALU.add),
          rd=[b_scr1], wr=[b_stat])
        E("ve", lambda g: g.tensor_scalar(col(2), col(0), 1.0 / 1024.0, None,
                                          ALU.mult), rd=[b_stat], wr=[b_stat])
        E("ve", lambda g: g.tensor_tensor(col(3), col(2), col(2), ALU.mult),
          rd=[b_stat], wr=[b_stat])
        E("ve", lambda g: g.scalar_tensor_tensor(
            col(4), col(1), 1.0 / 1024.0, col(3), ALU.mult, ALU.subtract),
          rd=[b_stat], wr=[b_stat])
        E("ve", lambda g: g.tensor_scalar(col(4), col(4), LN_EPS, None,
                                          ALU.add), rd=[b_stat], wr=[b_stat])
        E("sc", lambda g: g.activation(col(5), col(4), AF.Ln),
          rd=[b_stat], wr=[b_stat])
        E("sc", lambda g: g.activation(col(6), col(5), AF.Exp, scale=-0.5),
          rd=[b_stat], wr=[b_stat])
        E("ve", lambda g, par=par: g.tensor_scalar(
            scr1[:], xin[par][:], col(2), col(6), ALU.subtract, ALU.mult),
          rd=[b_xin[par], b_stat], wr=[b_scr1])
        E("ve", lambda g: g.tensor_reduce(
            col(7), scr1[:], axis=AXX, op=ALU.max, apply_absolute_value=True),
          rd=[b_scr1], wr=[b_stat])
        E("ve", lambda g, first=(i == 0): g.tensor_tensor(
            sc_(14), col(7) if first else sc_(14), col(7), ALU.max),
          rd=[b_stat], wr=[b_mx])
        for dc in range(NDC):
            E("te", lambda g, dc=dc: g.transpose(
                p_q[:, dc * 128:(dc + 1) * 128],
                scr1[:, dc * 128:(dc + 1) * 128], ident_sb[:]),
              rd=[b_scr1, b_const], wr=([b_pq] if dc == 0 else []),
              wr2=([b_pq] if dc > 0 else []))
        E("sc", lambda g, t0=t0: g.copy(
            bass.AP(xnT, t0, [[NDC * L, 128], [L, NDC], [1, CH]]),
            p_q[:].rearrange("p (dc t) -> p dc t", t=CH)),
          rd=[b_pq], wr2=[b_xn[i]])

    def allreduce_max(cols_in, in_bufs, psum, b_psum):
        for j, cin in enumerate(cols_in):
            E("gp", lambda g, j=j, cin=cin: g.dma_start(
                out=pmax_sp.ap()[j, :], in_=cin),
              rd=in_bufs, wr=[b_pmax], dma="ccd")
            E("gp", lambda g, j=j: g.dma_start(
                out=pmrow[:], in_=pmax_sp.ap()[j:j + 1, :]),
              rd=[b_pmax], wr=[b_pmrow], dma="ccd")
            E("ve", lambda g, j=j: g.tensor_reduce(
                ccsb[0:1, j:j + 1], pmrow[:], axis=AXX, op=ALU.max),
              rd=[b_pmrow], wr=([b_ccsb] if j == 0 else []),
              wr2=([b_ccsb] if j > 0 else []))
            E("gp", lambda g, j=j: g.dma_start(
                out=cc_in.ap()[0:1, j:j + 1], in_=ccsb[0:1, j:j + 1]),
              rd=[b_ccsb], wr=([b_ccin] if j == 0 else []),
              wr2=([b_ccin] if j > 0 else []), dma="ccd")
        E("gp", lambda g: g.collective_compute(
            "AllReduce", ALU.max, replica_groups=[list(range(NCORES))],
            ins=[cc_in.ap().opt()], outs=[cc_out.ap().opt()]),
          rd=[b_ccin], wr=[b_ccout], cc=True)
        E("gp", lambda g: g.dma_start(out=ccsb[:], in_=cc_out.ap()),
          rd=[b_ccout], wr=[b_ccsb], dma="ccd")

    def bcast_scale(j_src, c_s, c_sinv, psum, b_psum):
        E("te", lambda g, j=j_src, psum=psum: g.matmul(
            psum[:, 0:1], utri_sb[0:1, :], ccsb[0:1, j:j + 1],
            start=True, stop=True),
          rd=[b_ccsb, b_const], wr=[b_psum])
        E("ve", lambda g, c_s=c_s, psum=psum: g.tensor_scalar(
            sc_(c_s), psum[:, 0:1], 1.0 / QMAX, 1e-12, ALU.mult, ALU.max),
          rd=[b_psum], wr2=[b_scl])
        E("ve", lambda g, c_s=c_s, c_i=c_sinv: g.reciprocal(sc_(c_i),
                                                            sc_(c_s)),
          rd=[b_scl], wr2=[b_scl])

    def ar1_issue():
        j = 0
        E("gp", lambda g: g.dma_start(out=pmax_sp.ap()[j, :], in_=sc_(14)),
          rd=[b_mx], wr=[b_pmax], dma="ccd")
        E("gp", lambda g: g.dma_start(out=pmrow[:], in_=pmax_sp.ap()[j:j + 1, :]),
          rd=[b_pmax], wr=[b_pmrow], dma="ccd")
        E("ve", lambda g: g.tensor_reduce(
            ccsb[0:1, j:j + 1], pmrow[:], axis=AXX, op=ALU.max),
          rd=[b_pmrow], wr=[b_ccsb])
        E("gp", lambda g: g.dma_start(
            out=cc_in.ap()[0:1, j:j + 1], in_=ccsb[0:1, j:j + 1]),
          rd=[b_ccsb], wr=[b_ccin], dma="ccd")
        E("gp", lambda g: g.collective_compute(
            "AllReduce", ALU.max, replica_groups=[list(range(NCORES))],
            ins=[cc_in.ap().opt()], outs=[cc_out.ap().opt()]),
          rd=[b_ccin], wr=[b_ccout], cc=True)

    def ar1_consume():
        E("gp", lambda g: g.dma_start(
            out=sc_(0), in_=bass.AP(cc_out, 0, [[0, 128], [1, 1]])),
          rd=[b_ccout], wr2=[b_scl], dma="ccd")
        E("ve", lambda g: g.tensor_scalar(
            sc_(0), sc_(0), 1.0 / QMAX, 1e-12, ALU.mult, ALU.max),
          rd=[b_scl], wr2=[b_scl])
        E("ve", lambda g: g.reciprocal(sc_(1), sc_(0)),
          rd=[b_scl], wr2=[b_scl])

    if debug in ("p0", "p0nocc", "ar1"):
        if debug != "p0nocc":
            allreduce_max([sc_(14)], [b_mx], p_r[0][:, 256:384], b_pr[0])
            bcast_scale(0, 0, 1, p_r[0][:, 256:384], b_pr[0])

    if debug == "ar1":
        E("gp", lambda g: g.dma_start(out=dbg_xn.ap()[:, 0:16], in_=scl[:]),
          rd=[b_scl], dma="dbg")
        P.wait_all_dma("gp")
        P.wait_all_dma("sy")
        return nc, P, locals()

    if debug in ("p0", "p0nocc"):
        E("sy", lambda g: g.dma_start(out=dbg_xn.ap(), in_=xnT[:]),
          rd=list(b_xn), dma="dbg")
        P.wait_all_dma("sy")
        return nc, P, locals()

    dq_done = {}
    dq_cnt = [0]

    def dq_load(t):
        mat, src_ap, dd, oc = dq_plan[t]
        E("sy", lambda g, t=t, src_ap=src_ap: g.dma_start(
            out=dq_in[t % 3][:], in_=src_ap), wr=[b_dqin[t % 3]], dma="dqi")

    def dq_ab(t):
        i3, i2 = t % 3, t % 2
        E("ve", lambda g, i3=i3, i2=i2: g.tensor_reduce(
            dq_s[i2][:, 0:32],
            dq_in[i3][:].rearrange("p (b k) -> p b k", k=32),
            axis=AXX, op=ALU.max, apply_absolute_value=True),
          rd=[b_dqin[i3]], wr=[b_dqs[i2]])
        E("ve", lambda g, i2=i2: g.tensor_scalar(
            dq_s[i2][:, 32:64], dq_s[i2][:, 0:32], 1.0 / 7.0, 1e-12,
            ALU.mult, ALU.max), rd=[b_dqs[i2]], wr2=[b_dqs[i2]])
        E("ve", lambda g, i2=i2: g.reciprocal(
            dq_s[i2][:, 64:96], dq_s[i2][:, 32:64]),
          rd=[b_dqs[i2]], wr2=[b_dqs[i2]])
        E("gp", lambda g, i3=i3, i2=i2: g.tensor_tensor(
            dq_u[i2][:], dq_in[i3][:],
            bass.AP(dq_s[i2], 64, [[96, 128], [1, 32], [0, 32]]), ALU.mult),
          rd=[b_dqin[i3], b_dqs[i2]], wr=[b_dqu[i2]])

    def dq_cd(t):
        mat, src_ap, dd, oc = dq_plan[t]
        i2 = t % 2
        E("ve", lambda g, i2=i2: g.tensor_scalar(
            dq_u[i2][:], dq_u[i2][:], CMAGIC, -CMAGIC, ALU.add, ALU.add),
          rd=[b_dqu[i2]], wr=[b_dqu[i2]])
        E("gp", lambda g, i2=i2: g.tensor_tensor(
            dq_nat[i2][:], dq_u[i2][:],
            bass.AP(dq_s[i2], 32, [[96, 128], [1, 32], [0, 32]]), ALU.mult),
          rd=[b_dqu[i2], b_dqs[i2]], wr=[b_dqn[i2]])
        for dc in range(NDC):
            E("sy", lambda g, dc=dc, i2=i2: g.dma_start(
                out=dq_tr[i2][:, dc * 128:(dc + 1) * 128],
                in_=dq_nat[i2][:, dc * 128:(dc + 1) * 128], transpose=True),
              rd=[b_dqn[i2]], wr=([b_dqt[i2]] if dc == 0 else []),
              wr2=([b_dqt[i2]] if dc > 0 else []), dma=f"dqt{i2}_{dc % 4}")
        tout = E("gp", lambda g, i2=i2, oc=oc, dst=dd: g.dma_start(
            out=dst.ap()[oc, :, :], in_=dq_tr[i2][:]),
          rd=[b_dqt[i2]], dma="dqo")
        dq_done[(mat, oc)] = (P.dma_ctr("dqo"), tout)

    dq_plan = []
    for oc in range(NIC):
        dq_plan.append(("up", w_up[oc * 128:(oc + 1) * 128, :], wupdq_sp, oc))
        dq_plan.append(("gate", w_gate[oc * 128:(oc + 1) * 128, :],
                        wgatedq_sp, oc))
    NDQ = len(dq_plan)
    dq_ld = [0]
    dq_abi = [0]
    dq_cdi = [0]

    def dq_target(c):
        return max(0, min(NDQ, (c + 1) * NDQ // NCH))

    def emit_dq_stages(c):
        if SKIP_DQ:
            return
        while dq_cdi[0] < dq_target(c - 2):
            dq_cd(dq_cdi[0]); dq_cdi[0] += 1
        while dq_abi[0] < dq_target(c - 1):
            dq_ab(dq_abi[0]); dq_abi[0] += 1
        while dq_ld[0] < dq_target(c):
            dq_load(dq_ld[0]); dq_ld[0] += 1

    def emit_proj(c, which):
        t0 = c * CH
        ps, b_ps, off = which
        for half in range(2):
            for dc in range(NDC):
                E("te", lambda g, ps=ps, half=half, dc=dc, t0=t0, off=off:
                  g.matmul(
                      ps[:, half * 512:(half + 1) * 512],
                      xnT[:, dc * L + t0: dc * L + t0 + CH],
                      wqkv_sb[:, dc * 3072 + off + half * 512:
                              dc * 3072 + off + (half + 1) * 512],
                      start=(dc == 0), stop=(dc == NDC - 1)),
                  rd=[b_xn[c], b_wqkv],
                  wr=([b_ps] if (half == 0 and dc == 0) else []),
                  wr2=([b_ps] if (half == 1 and dc == NDC - 1) else []),
                  noinc=(dc != NDC - 1))

    def emit_kln_apply(c):
        par = c % 2
        E("sc", lambda g: g.copy(q_tok[:], p_q[:]), rd=[b_pq], wr=[b_qtok])
        E("sc", lambda g, par=par: g.copy(v_ch[par][:], p_v[:]),
          rd=[b_pv], wr=[b_v[par]])
        E("ve", lambda g: g.tensor_tensor(
            kn_f32[:], k_raw[:],
            bass.AP(kst, 32, [[80, 128], [1, 16], [0, 64]]), ALU.subtract),
          rd=[b_kraw, b_kst], wr=[b_kn])
        E("ve", lambda g: g.tensor_tensor(
            kn_f32[:], kn_f32[:],
            bass.AP(kst, 48, [[80, 128], [1, 16], [0, 64]]), ALU.mult),
          rd=[b_kst], wr2=[b_kn])
        E("ve", lambda g, par=par: g.tensor_tensor(
            kh_ch[par][:], kn_f32[:], ck[:], ALU.mult),
          rd=[b_kn], wr=[b_kh[par]], waits=tab_waits)

    def emit_kln_pre(c):
        tks_ = E("ve", lambda g: g.tensor_reduce(
            kst[:, 0:16], p_k[:].rearrange("p (h e) -> p h e", e=DK),
            axis=AXX, op=ALU.add), rd=[b_pk], wr=[b_kst])
        E("sc", lambda g: g.copy(k_raw[:], p_k[:]), rd=[b_pk], wr=[b_kraw],
          waits=[(P.ctr["ve"], tks_)])
        E("sc", lambda g: g.activation(ksq_f32[:], k_raw[:], AF.Square),
          rd=[b_kraw], wr=[b_ksq])
        E("ve", lambda g: g.tensor_reduce(
            kst[:, 16:32], ksq_f32[:].rearrange("p (h e) -> p h e", e=DK),
            axis=AXX, op=ALU.add), rd=[b_ksq], wr2=[b_kst])
        E("ve", lambda g: g.tensor_scalar(
            kst[:, 32:48], kst[:, 0:16], 1.0 / DK, None, ALU.mult),
          rd=[b_kst], wr2=[b_kst])
        E("ve", lambda g: g.tensor_tensor(
            kst[:, 64:80], kst[:, 32:48], kst[:, 32:48], ALU.mult),
          rd=[b_kst], wr2=[b_kst])
        E("ve", lambda g: g.scalar_tensor_tensor(
            kst[:, 48:64], kst[:, 16:32], 1.0 / DK, kst[:, 64:80],
            ALU.mult, ALU.subtract), rd=[b_kst], wr2=[b_kst])
        E("ve", lambda g: g.tensor_scalar(
            kst[:, 48:64], kst[:, 48:64], LN_EPS, None, ALU.add),
          rd=[b_kst], wr2=[b_kst])
        E("sc", lambda g: g.activation(kst[:, 48:64], kst[:, 48:64], AF.Ln),
          rd=[b_kst], wr2=[b_kst])
        E("sc", lambda g: g.activation(
            kst[:, 48:64], kst[:, 48:64], AF.Exp, scale=-0.5),
          rd=[b_kst], wr2=[b_kst])

    def emit_transposes(c):
        par = c % 2
        for j in range(NDC):
            E("te", lambda g, j=j: g.transpose(
                p_v[:, j * 128:(j + 1) * 128], q_tok[:, j * 128:(j + 1) * 128],
                ident_sb[:]), rd=[b_qtok, b_const],
              wr=([b_pv] if j == 0 else []), wr2=([b_pv] if j > 0 else []))
        tqt_ = E("ve", lambda g, par=par: g.tensor_tensor(
            qt_ch[par][:], p_v[:], cq[:], ALU.mult),
          rd=[b_pv], wr=[b_qt[par]], waits=tab_waits)
        E("sc", lambda g, par=par: g.copy(q_ch[par][:], p_v[:]),
          rd=[b_pv], wr=[b_q[par]], waits=[(P.ctr["ve"], tqt_)])
        for j in range(NDC):
            E("te", lambda g, j=j: g.transpose(
                p_v[:, j * 128:(j + 1) * 128], kn_f32[:, j * 128:(j + 1) * 128],
                ident_sb[:]), rd=[b_kn, b_const],
              wr=([b_pv] if j == 0 else []), wr2=([b_pv] if j > 0 else []))
        E("sc", lambda g, par=par: g.copy(knT_ch[par][:], p_v[:]),
          rd=[b_pv], wr=[b_knT[par]])

    def emit_retention(c):
        par = c % 2
        gpar = (c // 4) % 2
        for p in range(8):
            rpar = p % 2
            for half in range(2):
                E("te", lambda g, p=p, half=half, par=par, rpar=rpar: g.matmul(
                    p_r[rpar][:, half * 128:(half + 1) * 128],
                    knT_ch[par][64 * half:64 * (half + 1),
                                p * 128:(p + 1) * 128],
                    q_ch[par][64 * half:64 * (half + 1),
                              p * 128:(p + 1) * 128],
                    start=True, stop=True, tile_position=(64 * half, 0)),
                  rd=[b_knT[par], b_q[par]],
                  wr=([b_pr[rpar]] if half == 0 else []),
                  wr2=([b_pr[rpar]] if half == 1 else []))
            E("ve", lambda g, p=p, rpar=rpar: g.tensor_tensor(
                a_scr[rpar][:], p_r[rpar][:, 0:256],
                gam[:, p * 256:(p + 1) * 256], ALU.mult),
              rd=[b_pr[rpar]], wr=[b_a[rpar]], waits=tab_waits)
            for half in range(2):
                h = 2 * p + half
                E("te", lambda g, h=h, half=half, par=par, rpar=rpar: g.matmul(
                    p_r[rpar][64 * half:64 * (half + 1), 256:384],
                    v_ch[par][:, h * 64:(h + 1) * 64],
                    a_scr[rpar][:, half * 128:(half + 1) * 128],
                    start=True, stop=False, tile_position=(0, 64 * half)),
                  rd=[b_v[par], b_a[rpar]], wr2=[b_pr[rpar]], noinc=True)
                E("te", lambda g, p=p, half=half, par=par, rpar=rpar: g.matmul(
                    p_r[rpar][64 * half:64 * (half + 1), 256:384],
                    st_bf[64 * half:64 * (half + 1), p * 64:(p + 1) * 64],
                    qt_ch[par][64 * half:64 * (half + 1),
                               p * 128:(p + 1) * 128],
                    start=False, stop=True,
                    tile_position=(64 * half, 64 * half)),
                  rd=[b_stb, b_qt[par]], wr2=[b_pr[rpar]])
            for half in range(2):
                h = 2 * p + half
                E("te", lambda g, h=h, half=half, par=par, rpar=rpar: g.matmul(
                    p_r[rpar][64 * half:64 * (half + 1), 384:448],
                    kh_ch[par][:, h * 64:(h + 1) * 64],
                    v_ch[par][:, h * 64:(h + 1) * 64],
                    start=True, stop=True, tile_position=(0, 64 * half)),
                  rd=[b_kh[par], b_v[par]], wr2=[b_pr[rpar]])
            tye_ = E("sc", lambda g, p=p, rpar=rpar, c=c, gpar=gpar: g.copy(
                y_stage[gpar][:, p * 512 + (c % 4) * 128:
                              p * 512 + (c % 4) * 128 + 128],
                p_r[rpar][:, 256:384]),
              rd=[b_pr[rpar]],
              wr=([b_yst[gpar]] if (p == 0 and c % 4 == 0) else []),
              wr2=([b_yst[gpar]] if not (p == 0 and c % 4 == 0) else []))
            E("ve", lambda g, p=p, rpar=rpar: g.scalar_tensor_tensor(
                st_bf[:, p * 64:(p + 1) * 64], st_bf[:, p * 64:(p + 1) * 64],
                gcp[:, p:p + 1], p_r[rpar][:, 384:448], ALU.mult, ALU.add),
              rd=[b_stb, b_pr[rpar]], wr=[b_stb],
              waits=tab_waits + [(P.ctr["sc"], tye_)])

    def emit_tail(c):
        par = c % 2
        gpar = (c // 4) % 2
        if c % 4 == 3:
            gidx = c // 4
            for pr in range(8):
                E("gp", lambda g, gidx=gidx, gpar=gpar, pr=pr: g.dma_start(
                    out=y_sp.ap()[pr, :, gidx * 4 * CH:(gidx + 1) * 4 * CH],
                    in_=y_stage[gpar][:, pr * 512:(pr + 1) * 512]),
                  rd=[b_yst[gpar]], wr2=[b_ysp],
                  dma=f"ysp{(gpar, pr % 2)}")
        emit_dq_stages(c)
        if c == 2 and debug not in ("p0", "p0nocc", "ar1"):
            ar1_issue()

    QKV = [(p_q, b_pq, 0), (p_k, b_pk, 1024), (p_v, b_pv, 2048)]
    emit_proj(0, QKV[0])
    emit_proj(0, QKV[1])
    emit_kln_pre(0)
    for c in range(P1_CHUNKS):
        emit_proj(c, QKV[2])
        emit_kln_apply(c)
        if c + 1 < P1_CHUNKS:
            emit_proj(c + 1, QKV[0])
            emit_proj(c + 1, QKV[1])
        emit_transposes(c)
        if c + 1 < P1_CHUNKS:
            emit_kln_pre(c + 1)
        emit_retention(c)
        emit_tail(c)
    for c in range(P1_CHUNKS, P1_CHUNKS + 3):
        emit_dq_stages(c)
    if debug not in ("p0", "p0nocc", "ar1"):
        ar1_consume()

    if XQ_BARRIER and not SKIP_XQ:
        P.barrier()
    for dc in range(NDC if not SKIP_XQ else 0):
        for half in range(4):
            base = dc * L + half * 1024
            grp = list(b_xn)
            E("ve", lambda g, base=base: g.tensor_scalar(
                ksq_f32[:], xnT[:, base:base + 1024], sc_(1), CMAGIC,
                ALU.mult, ALU.add),
              rd=[b_scl] + grp, wr=[b_ksq])
            E("ve", lambda g, base=base: g.tensor_scalar(
                xnT[:, base:base + 1024], ksq_f32[:], CMAGIC, sc_(0),
                ALU.subtract, ALU.mult),
              rd=[b_ksq, b_scl],
              wr=(grp if (dc == 0 and half == 0) else []),
              wr2=(grp if not (dc == 0 and half == 0) else []))

    if debug == "p1":
        P.wait_all_dma("sy")
        return nc, P, locals()

    P.barrier()
    es_a.close()
    es_b = contextlib.ExitStack()
    SBb = lambda n, s, d: es_b.enter_context(nc.sbuf_tensor(n, s, d))
    PSb = lambda n, s, d: es_b.enter_context(nc.psum_tensor(n, s, d))

    wst = [SBb(f"wst{i}", [128, 1024], BF16) for i in range(4)]
    b_wst = [Buf() for _ in range(4)]
    stg = [SBb(f"stg{i}", [128, 512], BF16) for i in range(4)]
    b_stg = [Buf() for _ in range(4)]
    dqdw_in = [SBb(f"dqdw_in{i}", [128, 3072], F32) for i in range(2)]
    b_dqdwin = [Buf(), Buf()]
    dqd_s = SBb("dqd_s", [128, 288], F32)
    b_dqds = Buf()
    dqd_u = SBb("dqd_u", [128, 3072], F32)
    b_dqdu = Buf()
    dqd_nat = SBb("dqd_nat", [128, 3072], BF16)
    b_dqdn = Buf()
    dqd_tr = [SBb(f"dqd_tr{i}", [128, 3072], BF16) for i in range(2)]
    b_dqdt = [Buf(), Buf()]
    p_f = [PSb(f"p_f{i}", [128, 512], F32) for i in range(4)]
    b_pf = [Buf() for _ in range(4)]

    dqd_done = {}

    def dqd_load(t):
        E("sy", lambda g, t=t: g.dma_start(
            out=dqdw_in[t % 2][:], in_=w_down[t * 128:(t + 1) * 128, :]),
          wr=[b_dqdwin[t % 2]], dma="dqdi")

    def dqd_a(t):
        par = t % 2
        E("ve", lambda g, par=par: g.tensor_reduce(
            dqd_s[:, 0:96], dqdw_in[par][:].rearrange("p (b k) -> p b k",
                                                      k=32),
            axis=AXX, op=ALU.max, apply_absolute_value=True),
          rd=[b_dqdwin[par]], wr=[b_dqds])
        E("ve", lambda g: g.tensor_scalar(
            dqd_s[:, 96:192], dqd_s[:, 0:96], 1.0 / 7.0, 1e-12, ALU.mult,
            ALU.max), rd=[b_dqds], wr2=[b_dqds])
        E("ve", lambda g: g.reciprocal(dqd_s[:, 192:288], dqd_s[:, 96:192]),
          rd=[b_dqds], wr2=[b_dqds])
        E("gp", lambda g, par=par: g.tensor_tensor(
            dqd_u[:], dqdw_in[par][:],
            bass.AP(dqd_s, 192, [[288, 128], [1, 96], [0, 32]]), ALU.mult),
          rd=[b_dqdwin[par], b_dqds], wr=[b_dqdu])

    def dqd_c(t):
        E("ve", lambda g: g.tensor_scalar(
            dqd_u[:], dqd_u[:], CMAGIC, -CMAGIC, ALU.add, ALU.add),
          rd=[b_dqdu], wr=[b_dqdu])
        E("gp", lambda g: g.tensor_tensor(
            dqd_nat[:], dqd_u[:],
            bass.AP(dqd_s, 96, [[288, 128], [1, 96], [0, 32]]), ALU.mult),
          rd=[b_dqdu, b_dqds], wr=[b_dqdn])

    def dqd_t(t, third):
        par = t % 2
        for j in range(8):
            ic = third * 8 + j
            E("sy", lambda g, ic=ic, par=par: g.dma_start(
                out=dqd_tr[par][:, ic * 128:(ic + 1) * 128],
                in_=dqd_nat[:, ic * 128:(ic + 1) * 128], transpose=True),
              rd=[b_dqdn],
              wr=([b_dqdt[par]] if (third == 0 and j == 0) else []),
              wr2=([b_dqdt[par]] if not (third == 0 and j == 0) else []),
              dma=f"dqdt{par}_{j % 4}")

    def dqd_s_unit(t):
        par = t % 2
        tout = E("gp", lambda g, par=par, t=t: g.dma_start(
            out=wdndq_sp.ap()[t, :, :], in_=dqd_tr[par][:]),
          rd=[b_dqdt[par]], dma="dqdo")
        dqd_done[t] = (P.dma_ctr("dqdo"), tout)
        if t + 1 < NDC:
            dqd_load(t + 1)

    dqd_units = [lambda: dqd_load(0)]
    for t in range(NDC):
        dqd_units += [lambda t=t: dqd_a(t), lambda t=t: dqd_c(t),
                      lambda t=t: dqd_t(t, 0), lambda t=t: dqd_t(t, 1),
                      lambda t=t: dqd_t(t, 2), lambda t=t: dqd_s_unit(t)]

    dqd_emitted = [0]
    mats2b = [("up", wupdq_sp, up_sp, b_upsp, 10),
              ("gate", wgatedq_sp, gate_sp, b_gatesp, 11)]
    ldplan = [(m[0], m[1], oc) for m in mats2b for oc in range(NIC)]

    def load_wst(k):
        mname, wsp, oc = ldplan[k]
        E("sy", lambda g, oc=oc, wsp=wsp, k=k: g.dma_start(
            out=wst[k % 4][:], in_=wsp.ap()[oc, :, :]),
          wr=[b_wst[k % 4]], waits=[dq_done[(mname, oc)]], dma=f"wst{k % 4}")

    load_wst(0)
    load_wst(1)
    kk = [2]
    for (mname, wsp, osp, b_osp_, mxcol) in mats2b:
        for oc in range(NIC):
            wb = b_wst[oc % 4]
            if kk[0] < len(ldplan):
                load_wst(kk[0])
                kk[0] += 1
            for slab in range(NSLAB):
                pf = (oc * NSLAB + slab) % 4
                for dc in range(NDC):
                    E("te", lambda g, oc=oc, slab=slab, dc=dc, pf=pf: g.matmul(
                        p_f[pf][:], wst[oc % 4][:, dc * 128:(dc + 1) * 128],
                        xnT[:, dc * L + slab * SLAB:
                            dc * L + (slab + 1) * SLAB],
                        start=(dc == 0), stop=(dc == NDC - 1)),
                      rd=[wb] + (list(b_xn) if (oc == 0 and slab == 0)
                                 else []),
                      wr=([b_pf[pf]] if dc == 0 else []),
                      wr2=([b_pf[pf]] if dc == NDC - 1 else []),
                      noinc=(dc != NDC - 1))
                E("sc", lambda g, pf=pf: g.copy(stg[pf][:], p_f[pf][:]),
                  rd=[b_pf[pf]], wr=[b_stg[pf]])
                E("ve", lambda g, pf=pf: g.tensor_reduce(
                    col(7), stg[pf][:], axis=AXX, op=ALU.max,
                    apply_absolute_value=True),
                  rd=[b_stg[pf]], wr=[b_stat])
                E("ve", lambda g, first=(oc == 0 and slab == 0), mxcol=mxcol:
                  g.tensor_tensor(
                      sc_(mxcol), col(7) if first else sc_(mxcol), col(7),
                      ALU.max), rd=[b_stat], wr2=[b_mx])
                E("gp", lambda g, oc=oc, slab=slab, pf=pf, osp=osp:
                  g.dma_start(
                      out=osp.ap()[oc, :, slab * SLAB:(slab + 1) * SLAB],
                      in_=stg[pf][:]),
                  rd=[b_stg[pf]], wr2=[b_osp_], dma=f"spl{pf}")
            if dqd_emitted[0] < len(dqd_units):
                dqd_units[dqd_emitted[0]]()
                dqd_emitted[0] += 1
    while dqd_emitted[0] < len(dqd_units):
        dqd_units[dqd_emitted[0]]()
        dqd_emitted[0] += 1

    if debug == "p2b":
        P.wait_all_dma("sy")
        return nc, P, locals()

    b_mx.note_w(P.ctr["ve"], P.ctr["ve"].c)
    allreduce_max([sc_(10), sc_(11)], [b_mx], p_f[0][:], b_pf[0])
    bcast_scale(0, 2, 3, p_f[0][:], b_pf[0])
    bcast_scale(1, 4, 5, p_f[0][:], b_pf[0])

    P.barrier()
    es_x.close()
    es_b.close()
    es_c = contextlib.ExitStack()
    es_cio = contextlib.ExitStack()
    SBc = lambda n, s, d: es_c.enter_context(nc.sbuf_tensor(n, s, d))
    SBio = lambda n, s, d: es_cio.enter_context(nc.sbuf_tensor(n, s, d))

    attn_sb = SBc("attn_sb", [128, NDC * L], BF16)
    b_attn = Buf()
    woT_sb = SBc("woT_sb", [128, NDC * D], BF16)
    b_wo = Buf()
    y_sl = [SBc(f"y_sl{i}", [128, 4096], BF16) for i in range(2)]
    b_ysl = [Buf(), Buf()]
    p_w = [es_c.enter_context(nc.psum_tensor(f"p_w{i}", [128, 512], F32))
           for i in range(2)]
    b_pw = [Buf(), Buf()]
    io_a = [SBio(f"io_a{i}", [128, 4096], BF16) for i in range(2)]
    b_ioa = [Buf(), Buf()]
    io_b = [SBio(f"io_b{i}", [128, 4096], BF16) for i in range(2)]
    b_iob = [Buf(), Buf()]
    io_c = [SBio(f"io_c{i}", [128, 4096], BF16) for i in range(2)]
    b_ioc = [Buf(), Buf()]
    fs1 = [SBio(f"fs1_{i}", [128, 2048], F32) for i in range(2)]
    b_fs1 = [Buf(), Buf()]
    fs2 = [SBio(f"fs2_{i}", [128, 2048], F32) for i in range(2)]
    b_fs2 = [Buf(), Buf()]
    fs3 = [SBio(f"fs3_{i}", [128, 2048], F32) for i in range(2)]
    b_fs3 = [Buf(), Buf()]

    for dc in range(NDC):
        E("gp", lambda g, dc=dc: g.dma_start(
            out=woT_sb[:, dc * D:(dc + 1) * D],
            in_=woT[dc * 128:(dc + 1) * 128, :]),
          wr=([b_wo] if dc == 0 else []), wr2=([b_wo] if dc > 0 else []),
          dma=f"wo{dc % 2}")

    wo_units = []
    for slab in range(NSLAB):
        wo_units.append(("load", slab))
        for oc in range(NDC):
            wo_units.append(("mm", slab, oc))
    wo_i = [0]

    def emit_wo_unit():
        if wo_i[0] >= len(wo_units):
            return
        u = wo_units[wo_i[0]]
        wo_i[0] += 1
        if u[0] == "load":
            slab = u[1]
            ypar = slab % 2
            for pr in range(8):
                E("sy", lambda g, slab=slab, ypar=ypar, pr=pr: g.dma_start(
                    out=y_sl[ypar][:, pr * 512:(pr + 1) * 512],
                    in_=y_sp.ap()[pr, :, slab * SLAB:(slab + 1) * SLAB]),
                  rd=[b_ysp], wr=([b_ysl[ypar]] if pr == 0 else []),
                  wr2=([b_ysl[ypar]] if pr > 0 else []),
                  dma=f"ysl{(ypar, pr % 2)}")
        else:
            slab, oc = u[1], u[2]
            ypar = slab % 2
            wpar = (slab * NDC + oc) % 2
            for dc in range(NDC):
                E("te", lambda g, oc=oc, dc=dc, ypar=ypar, wpar=wpar: g.matmul(
                    p_w[wpar][:],
                    woT_sb[:, dc * D + oc * 128: dc * D + (oc + 1) * 128],
                    y_sl[ypar][:, dc * 512:(dc + 1) * 512],
                    start=(dc == 0), stop=(dc == NDC - 1)),
                  rd=[b_wo, b_ysl[ypar]],
                  wr=([b_pw[wpar]] if dc == 0 else []),
                  wr2=([b_pw[wpar]] if dc == NDC - 1 else []),
                  noinc=(dc != NDC - 1))
            E("sc", lambda g, oc=oc, slab=slab, wpar=wpar: g.copy(
                attn_sb[:, oc * L + slab * SLAB: oc * L + (slab + 1) * SLAB],
                p_w[wpar][:]), rd=[b_pw[wpar]], wr2=[b_attn])

    for ic in range(NIC):
        par = ic % 2
        E("sy", lambda g, ic=ic, par=par: g.dma_start(
            out=io_a[par][:], in_=gate_sp.ap()[ic, :, :]),
          rd=[b_gatesp], wr=[b_ioa[par]], dma=f"ioa{par}")
        for hh in range(2):
            h0 = hh * 2048
            E("sc", lambda g, par=par, h0=h0: g.activation(
                fs1[par][:], io_a[par][:, h0:h0 + 2048], AF.Identity,
                bias=cmag[:], scale=sc_(5)),
              rd=[b_ioa[par], b_scl, b_cmag], wr=[b_fs1[par]])
            E("ve", lambda g, par=par: g.tensor_scalar(
                fs2[par][:], fs1[par][:], CMAGIC, sc_(4), ALU.subtract,
                ALU.mult),
              rd=[b_fs1[par], b_scl], wr=[b_fs2[par]])
            E("sc", lambda g, par=par: g.activation(fs1[par][:], fs2[par][:],
                                                    AF.Sigmoid),
              rd=[b_fs2[par]], wr=[b_fs1[par]])
            E("sc", lambda g, par=par: g.activation(
                fs3[par][:], fs1[par][:], AF.Identity, bias=cmag[:],
                scale=65536.0),
              rd=[b_fs1[par], b_cmag], wr=[b_fs3[par]])
            E("ve", lambda g, par=par: g.tensor_scalar(
                fs1[par][:], fs3[par][:], -CMAGIC, 65535.0, ALU.add, ALU.min),
              rd=[b_fs3[par]], wr=[b_fs1[par]])
            E("ve", lambda g, par=par: g.tensor_scalar(
                fs3[par][:], fs1[par][:], 0.0, 1.0 / 65536.0, ALU.max,
                ALU.mult),
              rd=[b_fs1[par]], wr=[b_fs3[par]])
            E("ve", lambda g, par=par, h0=h0: g.tensor_tensor(
                io_c[par][:, h0:h0 + 2048], fs2[par][:], fs3[par][:],
                ALU.mult),
              rd=[b_fs2[par], b_fs3[par]],
              wr=([b_ioc[par]] if hh == 0 else []),
              wr2=([b_ioc[par]] if hh else []))
        E("ve", lambda g, par=par: g.tensor_reduce(
            col(7), io_c[par][:], axis=AXX, op=ALU.max,
            apply_absolute_value=True), rd=[b_ioc[par]], wr=[b_stat])
        E("ve", lambda g, first=(ic == 0): g.tensor_tensor(
            sc_(12), col(7) if first else sc_(12), col(7), ALU.max),
          rd=[b_stat], wr2=[b_mx])
        E("gp", lambda g, ic=ic, par=par: g.dma_start(
            out=g2_sp.ap()[ic, :, :], in_=io_c[par][:]),
          rd=[b_ioc[par]], wr2=[b_g2sp], dma=f"ioc{par}")
        for _ in range(3):
            emit_wo_unit()
    while wo_i[0] < len(wo_units):
        emit_wo_unit()

    if debug == "p2d":
        P.wait_all_dma("sy")
        return nc, P, locals()

    b_mx.note_w(P.ctr["ve"], P.ctr["ve"].c)
    allreduce_max([sc_(12)], [b_mx], p_w[0][:], b_pw[0])
    bcast_scale(0, 6, 7, p_w[0][:], b_pw[0])

    for ic in range(NIC):
        par = ic % 2
        E("sy", lambda g, ic=ic, par=par: g.dma_start(
            out=io_a[par][:], in_=up_sp.ap()[ic, :, :]),
          rd=[b_upsp], wr=[b_ioa[par]], dma=f"ioa{par}")
        E("sy", lambda g, ic=ic, par=par: g.dma_start(
            out=io_b[par][:], in_=g2_sp.ap()[ic, :, :]),
          rd=[b_g2sp], wr=[b_iob[par]], dma=f"iob{par}")
        for hh in range(2):
            h0 = hh * 2048
            E("sc", lambda g, par=par, h0=h0: g.activation(
                fs1[par][:], io_a[par][:, h0:h0 + 2048], AF.Identity,
                bias=cmag[:], scale=sc_(3)),
              rd=[b_ioa[par], b_scl, b_cmag], wr=[b_fs1[par]])
            E("ve", lambda g, par=par: g.tensor_scalar(
                fs2[par][:], fs1[par][:], CMAGIC, sc_(2), ALU.subtract,
                ALU.mult),
              rd=[b_fs1[par], b_scl], wr=[b_fs2[par]])
            E("sc", lambda g, par=par, h0=h0: g.activation(
                fs3[par][:], io_b[par][:, h0:h0 + 2048], AF.Identity,
                bias=cmag[:], scale=sc_(7)),
              rd=[b_iob[par], b_scl, b_cmag], wr=[b_fs3[par]])
            E("ve", lambda g, par=par: g.tensor_scalar(
                fs1[par][:], fs3[par][:], CMAGIC, sc_(6), ALU.subtract,
                ALU.mult),
              rd=[b_fs3[par], b_scl], wr=[b_fs1[par]])
            E("ve", lambda g, par=par, h0=h0: g.tensor_tensor(
                io_c[par][:, h0:h0 + 2048], fs1[par][:], fs2[par][:],
                ALU.mult),
              rd=[b_fs1[par], b_fs2[par]],
              wr=([b_ioc[par]] if hh == 0 else []),
              wr2=([b_ioc[par]] if hh else []))
        E("ve", lambda g, par=par: g.tensor_reduce(
            col(7), io_c[par][:], axis=AXX, op=ALU.max,
            apply_absolute_value=True), rd=[b_ioc[par]], wr=[b_stat])
        E("ve", lambda g, first=(ic == 0): g.tensor_tensor(
            sc_(13), col(7) if first else sc_(13), col(7), ALU.max),
          rd=[b_stat], wr2=[b_mx])
        E("gp", lambda g, ic=ic, par=par: g.dma_start(
            out=o_sp.ap()[ic, :, :], in_=io_c[par][:]),
          rd=[b_ioc[par]], wr2=[b_osp], dma=f"ioc{par}")

    if debug == "p2e":
        P.wait_all_dma("sy")
        return nc, P, locals()

    b_mx.note_w(P.ctr["ve"], P.ctr["ve"].c)
    allreduce_max([sc_(13)], [b_mx], p_w[0][:], b_pw[0])
    bcast_scale(0, 8, 9, p_w[0][:], b_pw[0])

    P.barrier()
    es_cio.close()
    es_d = contextlib.ExitStack()
    SBd = lambda n, s, d: es_d.enter_context(nc.sbuf_tensor(n, s, d))
    oq_sl = [SBd(f"oq_sl{i}", [128, NIC * 512], BF16) for i in range(2)]
    b_oq = [Buf(), Buf()]
    wdn = [SBd(f"wdn{i}", [128, NIC * 128], BF16) for i in range(2)]
    b_wdn = [Buf(), Buf()]
    xTt = [SBd(f"xTt{i}", [128, 512], F32) for i in range(2)]
    b_xT = [Buf(), Buf()]
    outt = [SBd(f"outt{i}", [128, 512], F32) for i in range(2)]
    b_out = [Buf(), Buf()]
    ofs = [SBd(f"ofs{i}", [128, 2048], F32) for i in range(2)]
    b_ofs = [Buf(), Buf()]
    ofs2 = [SBd(f"ofs2{i}", [128, 2048], F32) for i in range(2)]
    b_ofs2 = [Buf(), Buf()]
    ep1 = SBd("ep1", [128, 512], F32)
    b_ep1 = Buf()
    p_d = [es_d.enter_context(nc.psum_tensor(f"p_d{i}", [128, 512], F32))
           for i in range(4)]
    b_pd = [Buf() for _ in range(4)]

    wdn_wait = (P.dma_ctr("dqdo"), max(t for (_c, t) in dqd_done.values()))
    last_out = 0

    def stage_o(slab):
        spar = slab % 2
        for ic in range(NIC):
            E("sy", lambda g, ic=ic, slab=slab, spar=spar: g.dma_start(
                out=oq_sl[spar][:, ic * 512:(ic + 1) * 512],
                in_=o_sp.ap()[ic, :, slab * SLAB:(slab + 1) * SLAB]),
              rd=[b_osp], wr=([b_oq[spar]] if ic == 0 else []),
              wr2=([b_oq[spar]] if ic > 0 else []), dma=f"oq{ic % 4}")
        for third in range(6):
            a0, a1 = third * 2048, (third + 1) * 2048
            tp = third % 2
            E("sc", lambda g, spar=spar, a0=a0, a1=a1, tp=tp: g.activation(
                ofs[tp][:], oq_sl[spar][:, a0:a1], AF.Identity, bias=cmag[:],
                scale=sc_(9)), rd=[b_oq[spar], b_scl, b_cmag], wr=[b_ofs[tp]])
            E("ve", lambda g, spar=spar, a0=a0, a1=a1, tp=tp: g.tensor_scalar(
                oq_sl[spar][:, a0:a1], ofs[tp][:], CMAGIC, sc_(8),
                ALU.subtract, ALU.mult),
              rd=[b_ofs[tp], b_scl], wr2=[b_oq[spar]])

    stage_o(0)
    for slab in range(NSLAB):
        spar = slab % 2
        if slab + 1 < NSLAB:
            stage_o(slab + 1)
        if slab == 0:
            E("sy", lambda g: g.dma_start(
                out=wdn[0][:], in_=wdndq_sp.ap()[0, :, :]),
              wr=[b_wdn[0]], waits=[wdn_wait], dma="wdn0")
        for oc in range(NDC):
            pd = oc % 4
            wpar2 = (slab * NDC + oc) % 2
            nslab, noc = (slab, oc + 1) if oc + 1 < NDC else (slab + 1, 0)
            if nslab < NSLAB:
                E("sy", lambda g, noc=noc, w2=1 - wpar2: g.dma_start(
                    out=wdn[w2][:], in_=wdndq_sp.ap()[noc, :, :]),
                  wr=[b_wdn[1 - wpar2]], waits=[wdn_wait],
                  dma=f"wdn{1 - wpar2}")
            for ic in range(NIC):
                E("te", lambda g, ic=ic, spar=spar, pd=pd, wpar2=wpar2:
                  g.matmul(
                      p_d[pd][:], wdn[wpar2][:, ic * 128:(ic + 1) * 128],
                      oq_sl[spar][:, ic * 512:(ic + 1) * 512],
                      start=(ic == 0), stop=(ic == NIC - 1)),
                  rd=[b_wdn[wpar2], b_oq[spar]],
                  wr=([b_pd[pd]] if ic == 0 else []),
                  wr2=([b_pd[pd]] if ic == NIC - 1 else []),
                  noinc=(ic != NIC - 1))
            opar = oc % 2
            E("sy", lambda g, oc=oc, slab=slab, opar=opar: g.dma_start(
                out=xTt[opar][:],
                in_=xT[oc * 128:(oc + 1) * 128,
                       slab * SLAB:(slab + 1) * SLAB]),
              wr=[b_xT[opar]], dma=f"xt{opar}")
            E("ve", lambda g, oc=oc, slab=slab, pd=pd: g.tensor_tensor(
                ep1[:], p_d[pd][:],
                attn_sb[:, oc * L + slab * SLAB: oc * L + (slab + 1) * SLAB],
                ALU.add), rd=[b_pd[pd], b_attn], wr=[b_ep1])
            E("ve", lambda g, opar=opar: g.tensor_tensor(
                outt[opar][:], ep1[:], xTt[opar][:], ALU.add),
              rd=[b_ep1, b_xT[opar]], wr=[b_out[opar]])
            last_out = E("gp", lambda g, oc=oc, slab=slab, opar=opar:
                         g.dma_start(
                             out=outT[oc * 128:(oc + 1) * 128,
                                      slab * SLAB:(slab + 1) * SLAB],
                             in_=outt[opar][:]),
                         rd=[b_out[opar]], dma=f"out{opar}")

    P.wait_all_dma("sy")
    return nc, P, locals()


def finalize(nc, P):
    with nc.Block() as blk:
        @blk.sync
        def _(g):
            for f in P.ops["sy"]:
                f(g)

        @blk.scalar
        def _(g):
            for f in P.ops["sc"]:
                f(g)

        @blk.vector
        def _(g):
            for f in P.ops["ve"]:
                f(g)

        @blk.tensor
        def _(g):
            for f in P.ops["te"]:
                f(g)

        @blk.gpsimd
        def _(g):
            for f in P.ops["gp"]:
                f(g)
    return nc


def host_consts():
    idx = np.arange(128, dtype=np.float32)
    emat = idx[None, :] - idx[:, None]
    emat = np.where(emat >= 0, emat, 1e9).astype(np.float32)
    utri = (idx[None, :] >= idx[:, None]).astype(np.float32)
    rampq = np.ascontiguousarray(
        np.broadcast_to(idx[None, :] + 1.0, (128, 128))).astype(np.float32)
    rampk = np.ascontiguousarray(
        np.broadcast_to((127.0 - idx)[:, None], (128, 64))).astype(np.float32)
    return dict(emat=np.ascontiguousarray(emat),
                utri=np.ascontiguousarray(utri), rampq=rampq, rampk=rampk,
                ident=np.eye(128, dtype=np.float32))


def host_inputs(L0_attn_x, L0_attn_W_qkv, L0_attn_W_o, L0_attn_log_gamma,
                L0_ffn_w_gate, L0_ffn_w_up, L0_ffn_w_down):
    x = np.asarray(L0_attn_x, dtype=np.float32)
    wqkvT = np.ascontiguousarray(np.asarray(L0_attn_W_qkv, np.float32).T)
    woT = np.ascontiguousarray(np.asarray(L0_attn_W_o, np.float32).T)
    lg = np.asarray(L0_attn_log_gamma, np.float32)
    lgp8 = np.empty((128, 8), np.float32)
    for j in range(8):
        lgp8[:64, j] = lg[2 * j]
        lgp8[64:, j] = lg[2 * j + 1]
    lg16 = np.ascontiguousarray(
        np.broadcast_to(lg[None, :], (128, 16))).astype(np.float32)
    shared = dict(wqkvT=wqkvT, woT=woT,
                  w_up=np.ascontiguousarray(np.asarray(L0_ffn_w_up,
                                                       np.float32)),
                  w_gate=np.ascontiguousarray(np.asarray(L0_ffn_w_gate,
                                                         np.float32)),
                  w_down=np.ascontiguousarray(np.asarray(L0_ffn_w_down,
                                                         np.float32)),
                  lgp8=lgp8, lg16=lg16, **host_consts())
    in_maps = []
    for c in range(NCORES):
        xc = np.ascontiguousarray(x[c])
        in_maps.append(dict(x_nat=xc, xT=np.ascontiguousarray(xc.T), **shared))
    return in_maps


def kernel(**inputs):
    nc, P, _ = build_program()
    finalize(nc, P)
    in_maps = host_inputs(**inputs)
    res = run_bass_kernel_spmd(nc, in_maps, core_ids=list(range(NCORES)))
    out = np.empty((NCORES, L, D), dtype=np.float32)
    for c in range(NCORES):
        out[c] = np.asarray(res.results[c]["outT"]).T
    return out


# version 16
# speedup vs baseline: 1.7192x; 1.0025x over previous
import contextlib
import numpy as np
import concourse.bass as bass
import concourse.mybir as mybir
import concourse.bass_isa as bass_isa
from concourse.bass_utils import run_bass_kernel_spmd

F32 = mybir.dt.float32
BF16 = mybir.dt.bfloat16
ALU = mybir.AluOpType
AF = mybir.ActivationFunctionType
AXX = mybir.AxisListType.X
RED = bass_isa.ReduceOp

NCORES = 8
L, D, H, DK, I3 = 4096, 1024, 16, 64, 3072
CH, SLAB = 128, 512
NCH, NSLAB, NDC, NIC = L // CH, L // SLAB, D // 128, I3 // 128
CMAGIC = 12582912.0
P0_TILES = NCH
SKIP_TABLES = False
SKIP_WQKV = False
SKIP_DQ = False
P1_CHUNKS = NCH
P1_CUT = 99
XQ_MODE = 0
SKIP_XQ = False
XQ_BARRIER = False
XQ_OPS = 3
SKIP_RET = False
SKIP_KLN = False
SKIP_TR = False
LN_EPS = 1e-05
QMAX = 32767.0


class Ctr:
    __slots__ = ("h", "c", "name")

    def __init__(self, h, name):
        self.h, self.c, self.name = h, 0, name

    def bump(self, n=1):
        self.c += n
        return self.c


class Buf:
    __slots__ = ("w", "r")

    def __init__(self):
        self.w, self.r = {}, {}

    def note_w(self, ctr, tok):
        self.w = {ctr: tok}
        self.r = {}

    def note_w2(self, ctr, tok):
        if self.w.get(ctr, 0) < tok:
            self.w[ctr] = tok

    def note_r(self, ctr, tok):
        if self.r.get(ctr, 0) < tok:
            self.r[ctr] = tok


class Prog:
    def __init__(self, nc):
        self.nc = nc
        self.ops = {e: [] for e in ("sy", "sc", "ve", "te", "gp")}
        self.es = contextlib.ExitStack()
        self._mk = lambda n: Ctr(self.es.enter_context(nc.semaphore(n)), n)
        self.ctr = {"sc": self._mk("c_sc"), "ve": self._mk("c_ve"),
                    "te": self._mk("c_te"), "gp": self._mk("c_gp"),
                    "cc": self._mk("c_cc")}
        self.dstr = {}
        self.plan = []
        self.waited = {e: {} for e in self.ops}

    def dma_ctr(self, name):
        if name not in self.dstr:
            self.dstr[name] = self._mk("dm_" + name)
        return self.dstr[name]

    def E(self, eng, build, rd=(), wr=(), wr2=(), waits=(), dma=None,
          cc=False, noinc=False):
        if cc:
            ctr, n = self.ctr["cc"], 1
        elif dma is not None:
            ctr, n = self.dma_ctr(dma), 16
        else:
            ctr, n = self.ctr[eng], 1
        need = {}
        if dma is not None and ctr.c:
            need[ctr] = ctr.c

        def add(c, v):
            if v and need.get(c, 0) < v:
                need[c] = v
        for (c, v) in waits:
            add(c, v)
        for b in rd:
            for c, v in b.w.items():
                add(c, v)
        for b in wr:
            for src in (b.w, b.r):
                for c, v in src.items():
                    add(c, v)
        wm = self.waited[eng]
        final = []
        for c, v in need.items():
            if wm.get(c, 0) >= v:
                continue
            final.append((c.h, v))
            wm[c] = v
        tok = ctr.c if noinc else ctr.bump(n)

        pad = (eng == "te" and not noinc and not cc)

        def f(g, _w=tuple(final), _b=build, _h=ctr.h, _n=n, _ni=noinc,
              _pad=pad):
            for (h, v) in _w:
                g.wait_ge(h, v)
            ins = _b(g)
            if not _ni:
                ins.then_inc(_h, _n)
            if _pad:
                g.nop()
                g.nop()
                g.nop()
                g.nop()
        self.ops[eng].append(f)
        self.plan.append(
            (eng, ctr, (None if noinc else tok), tuple(need.items()),
             tuple(id(b) for b in rd), tuple(id(b) for b in wr)
             + tuple(id(b) for b in wr2)))
        for b in rd:
            b.note_r(ctr, tok)
        for b in wr:
            b.note_w(ctr, tok)
        for b in wr2:
            b.note_w2(ctr, tok)
        return tok

    def W(self, eng, waits):
        wm = self.waited[eng]
        final = []
        final2 = []
        for c, v in waits:
            if v and wm.get(c, 0) < v:
                final.append((c.h, v))
                final2.append((c, v))
                wm[c] = v
        if final:
            def f(g, _w=tuple(final)):
                for (h, v) in _w:
                    g.wait_ge(h, v)
            self.ops[eng].append(f)
            self.plan.append((eng, None, None, tuple((c, v) for c, v in final2),
                              (), ()))

    def barrier(self):
        snap = [(c, c.c) for c in self.ctr.values()]
        snap += [(c, c.c) for c in self.dstr.values()]
        for e in self.ops:
            self.W(e, snap)

    def wait_all_dma(self, eng):
        self.W(eng, [(c, c.c) for c in self.dstr.values()])


def build_program(debug=None):
    nc = bass.Bass("TRN2", target_bir_lowering=False)
    P = Prog(nc)
    E, W = P.E, P.W

    dT = lambda n, s, k: nc.dram_tensor(n, s, F32, kind=k)
    x_nat = dT("x_nat", [L, D], "ExternalInput").ap()
    xT = dT("xT", [D, L], "ExternalInput").ap()
    wqkvT = dT("wqkvT", [D, 3 * D], "ExternalInput").ap()
    woT = dT("woT", [D, D], "ExternalInput").ap()
    w_up = dT("w_up", [I3, D], "ExternalInput").ap()
    w_gate = dT("w_gate", [I3, D], "ExternalInput").ap()
    w_down = dT("w_down", [D, I3], "ExternalInput").ap()
    lgp8 = dT("lgp8", [128, 8], "ExternalInput").ap()
    lg16 = dT("lg16", [128, 16], "ExternalInput").ap()
    emat = dT("emat", [128, 128], "ExternalInput").ap()
    utri = dT("utri", [128, 128], "ExternalInput").ap()
    rampq = dT("rampq", [128, 128], "ExternalInput").ap()
    rampk = dT("rampk", [128, 64], "ExternalInput").ap()
    ident = dT("ident", [128, 128], "ExternalInput").ap()
    outT = dT("outT", [D, L], "ExternalOutput").ap()

    def scratch(name, shape, dbgkey):
        kind = "ExternalOutput" if debug == dbgkey else "Internal"
        return nc.dram_tensor(name, shape, BF16, kind=kind)

    y_sp = scratch("y_sp", [8, 128, L], "p1")
    up_sp = scratch("up_sp", [NIC, 128, L], "p2b")
    gate_sp = scratch("gate_sp", [NIC, 128, L], "none")
    g2_sp = scratch("g2_sp", [NIC, 128, L], "p2d")
    o_sp = scratch("o_sp", [NIC, 128, L], "p2e")
    wupdq_sp = nc.dram_tensor(
        "wupdq_sp", [NIC, 128, NDC * 128], BF16,
        kind="ExternalOutput" if debug == "dq" else "Internal")
    wgatedq_sp = nc.dram_tensor("wgatedq_sp", [NIC, 128, NDC * 128], BF16)
    wdndq_sp = nc.dram_tensor("wdndq_sp", [NDC, 128, NIC * 128], BF16)
    pmax_sp = nc.dram_tensor("pmax_sp", [4, 128], F32)
    cc_in = nc.dram_tensor("cc_in", [1, 16], F32)
    cc_out = nc.dram_tensor("cc_out", [1, 16], F32)
    b_ccin, b_ccout = Buf(), Buf()
    b_ysp, b_upsp, b_gatesp, b_g2sp, b_osp = (Buf() for _ in range(5))
    dbg_xn = None
    if debug in ("p0", "p0nocc", "ar1"):
        dbg_xn = nc.dram_tensor("dbg_xn", [128, NDC * L], BF16,
                                kind="ExternalOutput")

    es_g, es_x, es_a = (contextlib.ExitStack() for _ in range(3))
    SBg = lambda n, s, d: es_g.enter_context(nc.sbuf_tensor(n, s, d))
    SBa = lambda n, s, d: es_a.enter_context(nc.sbuf_tensor(n, s, d))
    PSa = lambda n, s, d: es_a.enter_context(nc.psum_tensor(n, s, d))

    stat = SBg("stat", [128, 32], F32)
    col = lambda i: stat[:, i:i + 1]
    b_stat = Buf()
    kst = SBg("kst", [128, 80], F32)
    b_kst = Buf()
    ccsb = SBg("ccsb", [1, 16], F32)
    b_ccsb = Buf()
    pmrow = SBg("pmrow", [1, 128], F32)
    b_pmrow = Buf()
    b_pmax = Buf()
    scl = SBg("scl", [128, 16], F32)
    cmag = SBg("cmag", [128, 1], F32)
    b_cmag = Buf()
    utri_sb = SBg("utri_sb", [128, 128], F32)
    ident_sb = SBg("ident_sb", [128, 128], F32)
    sc_ = lambda i: scl[:, i:i + 1]
    b_scl = Buf()
    b_mx = Buf()

    xnT = es_x.enter_context(nc.sbuf_tensor("xnT", [128, NDC * L], BF16, side="right"))
    b_xn = [Buf() for _ in range(NCH)]

    wqkv_sb = SBa("wqkv_sb", [128, NDC * 3 * D], BF16)
    b_wqkv = Buf()
    gam = SBa("gam", [128, H * 128], BF16)
    cq = SBa("cq", [128, 1024], BF16)
    ck = SBa("ck", [128, 1024], BF16)
    gcp = SBa("gcp", [128, 8], F32)
    b_tab = Buf()
    lg16_sb = SBa("lg16_sb", [128, 16], F32)
    lgp8_sb = SBa("lgp8_sb", [128, 8], F32)
    emat_sb = SBa("emat_sb", [128, 128], F32)

    rampq_sb = SBa("rampq_sb", [128, 128], F32)
    rampk_sb = SBa("rampk_sb", [128, 64], F32)

    b_const = Buf()
    xin = [SBa(f"xin{i}", [128, 1024], F32) for i in range(2)]
    b_xin = [Buf(), Buf()]
    scr1 = SBa("scr1", [128, 1024], F32)
    b_scr1 = Buf()
    q_tok = SBa("q_tok", [128, 1024], F32)
    b_qtok = Buf()
    kn_f32 = SBa("kn_f32", [128, 1024], F32)
    b_kn = Buf()
    ksq_f32 = SBa("ksq_f32", [128, 1024], F32)
    b_ksq = Buf()
    k_raw = SBa("k_raw", [128, 1024], F32)
    b_kraw = Buf()
    q_ch = [SBa(f"q_ch{i}", [128, 1024], BF16) for i in range(2)]
    b_q = [Buf(), Buf()]
    qt_ch = [SBa(f"qt_ch{i}", [128, 1024], BF16) for i in range(2)]
    b_qt = [Buf(), Buf()]
    knT_ch = [SBa(f"knT_ch{i}", [128, 1024], BF16) for i in range(2)]
    b_knT = [Buf(), Buf()]
    kh_ch = [SBa(f"kh_ch{i}", [128, 1024], BF16) for i in range(2)]
    b_kh = [Buf(), Buf()]
    v_ch = [SBa(f"v_ch{i}", [128, 1024], BF16) for i in range(2)]
    b_v = [Buf(), Buf()]
    a_scr = [SBa(f"a_scr{i}", [128, 256], BF16) for i in range(2)]
    b_a = [Buf(), Buf()]
    st_bf = SBa("st_bf", [128, 512], BF16)
    b_stb = Buf()
    y_stage = [SBa(f"y_stage{i}", [128, 4096], BF16) for i in range(2)]
    b_yst = [Buf(), Buf()]

    dq_in = [xin[0], xin[1], SBa("dq_in2", [128, 1024], F32)]
    b_dqin = [b_xin[0], b_xin[1], Buf()]
    dq_s = [SBa(f"dq_s{i}", [128, 96], F32) for i in range(2)]
    b_dqs = [Buf(), Buf()]
    dq_u = [SBa("dq_u0", [128, 1024], F32), scr1]
    b_dqu = [Buf(), b_scr1]
    dq_nat = [SBa(f"dq_nat{i}", [128, 1024], BF16) for i in range(2)]
    b_dqn = [Buf(), Buf()]
    dq_tr = [SBa(f"dq_tr{i}", [128, 1024], BF16) for i in range(2)]
    b_dqt = [Buf(), Buf()]

    p_q = PSa("p_q", [128, 1024], F32)
    b_pq = Buf()
    p_k = PSa("p_k", [128, 1024], F32)
    b_pk = Buf()
    p_v = PSa("p_v", [128, 1024], F32)
    b_pv = Buf()
    p_r = [PSa(f"p_r{i}", [128, 512], F32) for i in range(2)]
    b_pr = [Buf(), Buf()]

    for (dst, src) in [(lgp8_sb, lgp8), (lg16_sb, lg16), (emat_sb, emat),
                       (utri_sb, utri), (rampq_sb, rampq), (rampk_sb, rampk),
                       (ident_sb, ident)]:
        E("sy", lambda g, d=dst, s=src: g.dma_start(out=d[:], in_=s[:]),
          wr2=[b_const], dma="cst")
    if not SKIP_WQKV:
        for dc in range(NDC):
            E("gp", lambda g, dc=dc: g.dma_start(
                out=wqkv_sb[:, dc * 3072:(dc + 1) * 3072],
                in_=wqkvT[dc * 128:(dc + 1) * 128, :]),
              wr=([b_wqkv] if dc == 0 else []),
              wr2=([b_wqkv] if dc > 0 else []), dma=f"wq{dc % 2}")
    E("gp", lambda g: g.memset(ccsb[:], 0.0), wr=[b_ccsb])
    E("gp", lambda g: g.memset(cmag[:], CMAGIC), wr=[b_cmag])
    E("gp", lambda g: g.dma_start(out=cc_in.ap(), in_=ccsb[:]),
      rd=[b_ccsb], wr=[b_ccin], dma="ccd")
    for fn in (AF.Sigmoid, AF.Ln):
        E("sc", lambda g, fn=fn: g.activation(lg16_sb[:], lg16_sb[:], fn),
          rd=[b_const], wr=[b_const])
        E("sc", lambda g, fn=fn: g.activation(lgp8_sb[:], lgp8_sb[:], fn),
          rd=[b_const], wr=[b_const])
    E("sc", lambda g: g.activation(gcp[:], lgp8_sb[:], AF.Exp, scale=128.0),
      rd=[b_const], wr2=[b_tab])
    for h in range(H if not SKIP_TABLES else 0):
        E("ve", lambda g, h=h: g.tensor_scalar(
            scr1[:, 0:128], emat_sb[:], lg16_sb[:, h:h + 1],
            None, ALU.mult), rd=[b_const], wr=[b_scr1])
        E("sc", lambda g, h=h: g.activation(
            gam[:, h * 128:(h + 1) * 128], scr1[:, 0:128],
            AF.Exp), rd=[b_scr1], wr2=[b_tab])
        E("ve", lambda g, h=h: g.tensor_tensor(
            gam[:, h * 128:(h + 1) * 128], gam[:, h * 128:(h + 1) * 128],
            utri_sb[:], ALU.mult), rd=[b_tab, b_const], wr2=[b_tab])
        E("ve", lambda g, h=h: g.tensor_scalar(
            scr1[:, 128:192], rampk_sb[:], lg16_sb[:, h:h + 1],
            None, ALU.mult), rd=[b_const], wr=[b_scr1])
        E("sc", lambda g, h=h: g.activation(
            ck[:, h * 64:(h + 1) * 64], scr1[:, 128:192], AF.Exp),
          rd=[b_scr1], wr2=[b_tab])
    for oc in range(8 if not SKIP_TABLES else 0):
        E("ve", lambda g, oc=oc: g.tensor_scalar(
            scr1[:, 0:128], rampq_sb[:], lgp8_sb[:, oc:oc + 1],
            None, ALU.mult), rd=[b_const], wr=[b_scr1])
        E("sc", lambda g, oc=oc: g.activation(
            cq[:, oc * 128:(oc + 1) * 128], scr1[:, 0:128],
            AF.Exp), rd=[b_scr1], wr2=[b_tab])
    E("gp", lambda g: g.memset(st_bf[:], 0.0), wr=[b_stb])

    tab_waits = [(P.ctr["sc"], P.ctr["sc"].c), (P.ctr["ve"], P.ctr["ve"].c)]

    def p0_pre(i):
        par = i % 2
        off = 16 * par
        E("sy", lambda g, i=i, par=par: g.dma_start(
            out=xin[par][:], in_=x_nat[i * CH:(i + 1) * CH, :]),
          wr=[b_xin[par]], dma=f"xin{par}")
        E("sc", lambda g, par=par: g.activation(ksq_f32[:], xin[par][:],
                                                AF.Square),
          rd=[b_xin[par]], wr=[b_ksq])
        E("ve", lambda g, par=par, off=off: g.tensor_reduce(
            col(0 + off), xin[par][:], axis=AXX, op=ALU.add),
          rd=[b_xin[par]], wr=[b_stat])
        E("ve", lambda g, off=off: g.tensor_reduce(
            col(1 + off), ksq_f32[:], axis=AXX, op=ALU.add),
          rd=[b_ksq], wr2=[b_stat])
        E("ve", lambda g, off=off: g.tensor_scalar(
            col(2 + off), col(0 + off), 1.0 / 1024.0, None, ALU.mult),
          rd=[b_stat], wr2=[b_stat])
        E("ve", lambda g, off=off: g.tensor_tensor(
            col(3 + off), col(2 + off), col(2 + off), ALU.mult),
          rd=[b_stat], wr2=[b_stat])
        E("ve", lambda g, off=off: g.scalar_tensor_tensor(
            col(4 + off), col(1 + off), 1.0 / 1024.0, col(3 + off),
            ALU.mult, ALU.subtract), rd=[b_stat], wr2=[b_stat])
        E("ve", lambda g, off=off: g.tensor_scalar(
            col(4 + off), col(4 + off), LN_EPS, None, ALU.add),
          rd=[b_stat], wr2=[b_stat])
        E("sc", lambda g, off=off: g.activation(col(5 + off), col(4 + off),
                                                AF.Ln),
          rd=[b_stat], wr2=[b_stat])
        E("sc", lambda g, off=off: g.activation(
            col(6 + off), col(5 + off), AF.Exp, scale=-0.5),
          rd=[b_stat], wr2=[b_stat])

    def p0_apply(i):
        par = i % 2
        off = 16 * par
        t0 = i * CH
        E("ve", lambda g, par=par, off=off: g.tensor_scalar(
            scr1[:], xin[par][:], col(2 + off), col(6 + off),
            ALU.subtract, ALU.mult),
          rd=[b_xin[par], b_stat], wr=[b_scr1])
        E("ve", lambda g, off=off: g.tensor_reduce(
            col(15), scr1[:], axis=AXX, op=ALU.max,
            apply_absolute_value=True),
          rd=[b_scr1], wr2=[b_stat])
        E("ve", lambda g, first=(i == 0): g.tensor_tensor(
            sc_(14), col(15) if first else sc_(14), col(15), ALU.max),
          rd=[b_stat], wr=[b_mx])
        for dc in range(NDC):
            E("te", lambda g, dc=dc: g.transpose(
                p_q[:, dc * 128:(dc + 1) * 128],
                scr1[:, dc * 128:(dc + 1) * 128], ident_sb[:]),
              rd=[b_scr1, b_const], wr=([b_pq] if dc == 0 else []),
              wr2=([b_pq] if dc > 0 else []))
        E("sc", lambda g, t0=t0: g.copy(
            bass.AP(xnT, t0, [[NDC * L, 128], [L, NDC], [1, CH]]),
            p_q[:].rearrange("p (dc t) -> p dc t", t=CH)),
          rd=[b_pq], wr2=[b_xn[i]])

    p0_pre(0)
    for i in range(P0_TILES):
        if i + 1 < P0_TILES:
            p0_pre(i + 1)
        p0_apply(i)

    def allreduce_max(cols_in, in_bufs, psum, b_psum):
        for j, cin in enumerate(cols_in):
            E("gp", lambda g, j=j, cin=cin: g.dma_start(
                out=pmax_sp.ap()[j, :], in_=cin),
              rd=in_bufs, wr=[b_pmax], dma="ccd")
            E("gp", lambda g, j=j: g.dma_start(
                out=pmrow[:], in_=pmax_sp.ap()[j:j + 1, :]),
              rd=[b_pmax], wr=[b_pmrow], dma="ccd")
            E("ve", lambda g, j=j: g.tensor_reduce(
                ccsb[0:1, j:j + 1], pmrow[:], axis=AXX, op=ALU.max),
              rd=[b_pmrow], wr=([b_ccsb] if j == 0 else []),
              wr2=([b_ccsb] if j > 0 else []))
            E("gp", lambda g, j=j: g.dma_start(
                out=cc_in.ap()[0:1, j:j + 1], in_=ccsb[0:1, j:j + 1]),
              rd=[b_ccsb], wr=([b_ccin] if j == 0 else []),
              wr2=([b_ccin] if j > 0 else []), dma="ccd")
        E("gp", lambda g: g.collective_compute(
            "AllReduce", ALU.max, replica_groups=[list(range(NCORES))],
            ins=[cc_in.ap().opt()], outs=[cc_out.ap().opt()]),
          rd=[b_ccin], wr=[b_ccout], cc=True)
        E("gp", lambda g: g.dma_start(out=ccsb[:], in_=cc_out.ap()),
          rd=[b_ccout], wr=[b_ccsb], dma="ccd")

    def bcast_scale(j_src, c_s, c_sinv, psum, b_psum):
        E("te", lambda g, j=j_src, psum=psum: g.matmul(
            psum[:, 0:1], utri_sb[0:1, :], ccsb[0:1, j:j + 1],
            start=True, stop=True),
          rd=[b_ccsb, b_const], wr=[b_psum])
        E("ve", lambda g, c_s=c_s, psum=psum: g.tensor_scalar(
            sc_(c_s), psum[:, 0:1], 1.0 / QMAX, 1e-12, ALU.mult, ALU.max),
          rd=[b_psum], wr2=[b_scl])
        E("ve", lambda g, c_s=c_s, c_i=c_sinv: g.reciprocal(sc_(c_i),
                                                            sc_(c_s)),
          rd=[b_scl], wr2=[b_scl])

    def ar1_issue():
        j = 0
        E("gp", lambda g: g.dma_start(out=pmax_sp.ap()[j, :], in_=sc_(14)),
          rd=[b_mx], wr=[b_pmax], dma="ccd")
        E("gp", lambda g: g.dma_start(out=pmrow[:], in_=pmax_sp.ap()[j:j + 1, :]),
          rd=[b_pmax], wr=[b_pmrow], dma="ccd")
        E("ve", lambda g: g.tensor_reduce(
            ccsb[0:1, j:j + 1], pmrow[:], axis=AXX, op=ALU.max),
          rd=[b_pmrow], wr=[b_ccsb])
        E("gp", lambda g: g.dma_start(
            out=cc_in.ap()[0:1, j:j + 1], in_=ccsb[0:1, j:j + 1]),
          rd=[b_ccsb], wr=[b_ccin], dma="ccd")
        E("gp", lambda g: g.collective_compute(
            "AllReduce", ALU.max, replica_groups=[list(range(NCORES))],
            ins=[cc_in.ap().opt()], outs=[cc_out.ap().opt()]),
          rd=[b_ccin], wr=[b_ccout], cc=True)

    def ar1_consume():
        E("gp", lambda g: g.dma_start(
            out=sc_(0), in_=bass.AP(cc_out, 0, [[0, 128], [1, 1]])),
          rd=[b_ccout], wr2=[b_scl], dma="ccd")
        E("ve", lambda g: g.tensor_scalar(
            sc_(0), sc_(0), 1.0 / QMAX, 1e-12, ALU.mult, ALU.max),
          rd=[b_scl], wr2=[b_scl])
        E("ve", lambda g: g.reciprocal(sc_(1), sc_(0)),
          rd=[b_scl], wr2=[b_scl])

    if debug in ("p0", "p0nocc", "ar1"):
        if debug != "p0nocc":
            allreduce_max([sc_(14)], [b_mx], p_r[0][:, 256:384], b_pr[0])
            bcast_scale(0, 0, 1, p_r[0][:, 256:384], b_pr[0])

    if debug == "ar1":
        E("gp", lambda g: g.dma_start(out=dbg_xn.ap()[:, 0:16], in_=scl[:]),
          rd=[b_scl], dma="dbg")
        P.wait_all_dma("gp")
        P.wait_all_dma("sy")
        return nc, P, locals()

    if debug in ("p0", "p0nocc"):
        E("sy", lambda g: g.dma_start(out=dbg_xn.ap(), in_=xnT[:]),
          rd=list(b_xn), dma="dbg")
        P.wait_all_dma("sy")
        return nc, P, locals()

    dq_done = {}
    dq_cnt = [0]

    def dq_load(t):
        mat, src_ap, dd, oc = dq_plan[t]
        E("sy", lambda g, t=t, src_ap=src_ap: g.dma_start(
            out=dq_in[t % 3][:], in_=src_ap), wr=[b_dqin[t % 3]], dma="dqi")

    def dq_ab(t):
        i3, i2 = t % 3, t % 2
        E("ve", lambda g, i3=i3, i2=i2: g.tensor_reduce(
            dq_s[i2][:, 0:32],
            dq_in[i3][:].rearrange("p (b k) -> p b k", k=32),
            axis=AXX, op=ALU.max, apply_absolute_value=True),
          rd=[b_dqin[i3]], wr=[b_dqs[i2]])
        E("ve", lambda g, i2=i2: g.tensor_scalar(
            dq_s[i2][:, 32:64], dq_s[i2][:, 0:32], 1.0 / 7.0, 1e-12,
            ALU.mult, ALU.max), rd=[b_dqs[i2]], wr2=[b_dqs[i2]])
        E("ve", lambda g, i2=i2: g.reciprocal(
            dq_s[i2][:, 64:96], dq_s[i2][:, 32:64]),
          rd=[b_dqs[i2]], wr2=[b_dqs[i2]])
        E("gp", lambda g, i3=i3, i2=i2: g.tensor_tensor(
            dq_u[i2][:], dq_in[i3][:],
            bass.AP(dq_s[i2], 64, [[96, 128], [1, 32], [0, 32]]), ALU.mult),
          rd=[b_dqin[i3], b_dqs[i2]], wr=[b_dqu[i2]])

    def dq_cd(t):
        mat, src_ap, dd, oc = dq_plan[t]
        i2 = t % 2
        E("ve", lambda g, i2=i2: g.tensor_scalar(
            dq_u[i2][:], dq_u[i2][:], CMAGIC, -CMAGIC, ALU.add, ALU.add),
          rd=[b_dqu[i2]], wr=[b_dqu[i2]])
        E("gp", lambda g, i2=i2: g.tensor_tensor(
            dq_nat[i2][:], dq_u[i2][:],
            bass.AP(dq_s[i2], 32, [[96, 128], [1, 32], [0, 32]]), ALU.mult),
          rd=[b_dqu[i2], b_dqs[i2]], wr=[b_dqn[i2]])
        for dc in range(NDC):
            E("sy", lambda g, dc=dc, i2=i2: g.dma_start(
                out=dq_tr[i2][:, dc * 128:(dc + 1) * 128],
                in_=dq_nat[i2][:, dc * 128:(dc + 1) * 128], transpose=True),
              rd=[b_dqn[i2]], wr=([b_dqt[i2]] if dc == 0 else []),
              wr2=([b_dqt[i2]] if dc > 0 else []), dma=f"dqt{i2}_{dc % 4}")
        tout = E("gp", lambda g, i2=i2, oc=oc, dst=dd: g.dma_start(
            out=dst.ap()[oc, :, :], in_=dq_tr[i2][:]),
          rd=[b_dqt[i2]], dma="dqo")
        dq_done[(mat, oc)] = (P.dma_ctr("dqo"), tout)

    dq_plan = []
    for oc in range(NIC):
        dq_plan.append(("up", w_up[oc * 128:(oc + 1) * 128, :], wupdq_sp, oc))
        dq_plan.append(("gate", w_gate[oc * 128:(oc + 1) * 128, :],
                        wgatedq_sp, oc))
    NDQ = len(dq_plan)
    dq_ld = [0]
    dq_abi = [0]
    dq_cdi = [0]

    def dq_target(c):
        return max(0, min(NDQ, (c + 1) * NDQ // NCH))

    def emit_dq_stages(c):
        if SKIP_DQ:
            return
        while dq_cdi[0] < dq_target(c - 2):
            dq_cd(dq_cdi[0]); dq_cdi[0] += 1
        while dq_abi[0] < dq_target(c - 1):
            dq_ab(dq_abi[0]); dq_abi[0] += 1
        while dq_ld[0] < dq_target(c):
            dq_load(dq_ld[0]); dq_ld[0] += 1

    def emit_proj(c, which):
        t0 = c * CH
        ps, b_ps, off = which
        for half in range(2):
            for dc in range(NDC):
                E("te", lambda g, ps=ps, half=half, dc=dc, t0=t0, off=off:
                  g.matmul(
                      ps[:, half * 512:(half + 1) * 512],
                      xnT[:, dc * L + t0: dc * L + t0 + CH],
                      wqkv_sb[:, dc * 3072 + off + half * 512:
                              dc * 3072 + off + (half + 1) * 512],
                      start=(dc == 0), stop=(dc == NDC - 1)),
                  rd=[b_xn[c], b_wqkv],
                  wr=([b_ps] if (half == 0 and dc == 0) else []),
                  wr2=([b_ps] if (half == 1 and dc == NDC - 1) else []),
                  noinc=(dc != NDC - 1))

    def emit_kln_apply(c):
        par = c % 2
        E("sc", lambda g: g.copy(q_tok[:], p_q[:]), rd=[b_pq], wr=[b_qtok])
        E("sc", lambda g, par=par: g.copy(v_ch[par][:], p_v[:]),
          rd=[b_pv], wr=[b_v[par]])
        E("ve", lambda g: g.tensor_tensor(
            kn_f32[:], k_raw[:],
            bass.AP(kst, 32, [[80, 128], [1, 16], [0, 64]]), ALU.subtract),
          rd=[b_kraw, b_kst], wr=[b_kn])
        E("ve", lambda g: g.tensor_tensor(
            kn_f32[:], kn_f32[:],
            bass.AP(kst, 48, [[80, 128], [1, 16], [0, 64]]), ALU.mult),
          rd=[b_kst], wr2=[b_kn])
        E("ve", lambda g, par=par: g.tensor_tensor(
            kh_ch[par][:], kn_f32[:], ck[:], ALU.mult),
          rd=[b_kn], wr=[b_kh[par]], waits=tab_waits)

    def emit_kln_pre(c):
        tks_ = E("ve", lambda g: g.tensor_reduce(
            kst[:, 0:16], p_k[:].rearrange("p (h e) -> p h e", e=DK),
            axis=AXX, op=ALU.add), rd=[b_pk], wr=[b_kst])
        E("sc", lambda g: g.copy(k_raw[:], p_k[:]), rd=[b_pk], wr=[b_kraw],
          waits=[(P.ctr["ve"], tks_)])
        E("sc", lambda g: g.activation(ksq_f32[:], k_raw[:], AF.Square),
          rd=[b_kraw], wr=[b_ksq])
        E("ve", lambda g: g.tensor_reduce(
            kst[:, 16:32], ksq_f32[:].rearrange("p (h e) -> p h e", e=DK),
            axis=AXX, op=ALU.add), rd=[b_ksq], wr2=[b_kst])
        E("ve", lambda g: g.tensor_scalar(
            kst[:, 32:48], kst[:, 0:16], 1.0 / DK, None, ALU.mult),
          rd=[b_kst], wr2=[b_kst])
        E("ve", lambda g: g.tensor_tensor(
            kst[:, 64:80], kst[:, 32:48], kst[:, 32:48], ALU.mult),
          rd=[b_kst], wr2=[b_kst])
        E("ve", lambda g: g.scalar_tensor_tensor(
            kst[:, 48:64], kst[:, 16:32], 1.0 / DK, kst[:, 64:80],
            ALU.mult, ALU.subtract), rd=[b_kst], wr2=[b_kst])
        E("ve", lambda g: g.tensor_scalar(
            kst[:, 48:64], kst[:, 48:64], LN_EPS, None, ALU.add),
          rd=[b_kst], wr2=[b_kst])
        E("sc", lambda g: g.activation(kst[:, 48:64], kst[:, 48:64], AF.Ln),
          rd=[b_kst], wr2=[b_kst])
        E("sc", lambda g: g.activation(
            kst[:, 48:64], kst[:, 48:64], AF.Exp, scale=-0.5),
          rd=[b_kst], wr2=[b_kst])

    def emit_transposes(c):
        par = c % 2
        for j in range(NDC):
            E("te", lambda g, j=j: g.transpose(
                p_v[:, j * 128:(j + 1) * 128], q_tok[:, j * 128:(j + 1) * 128],
                ident_sb[:]), rd=[b_qtok, b_const],
              wr=([b_pv] if j == 0 else []), wr2=([b_pv] if j > 0 else []))
        tqt_ = E("ve", lambda g, par=par: g.tensor_tensor(
            qt_ch[par][:], p_v[:], cq[:], ALU.mult),
          rd=[b_pv], wr=[b_qt[par]], waits=tab_waits)
        E("sc", lambda g, par=par: g.copy(q_ch[par][:], p_v[:]),
          rd=[b_pv], wr=[b_q[par]], waits=[(P.ctr["ve"], tqt_)])
        for j in range(NDC):
            E("te", lambda g, j=j: g.transpose(
                p_v[:, j * 128:(j + 1) * 128], kn_f32[:, j * 128:(j + 1) * 128],
                ident_sb[:]), rd=[b_kn, b_const],
              wr=([b_pv] if j == 0 else []), wr2=([b_pv] if j > 0 else []))
        E("sc", lambda g, par=par: g.copy(knT_ch[par][:], p_v[:]),
          rd=[b_pv], wr=[b_knT[par]])

    def emit_retention(c):
        par = c % 2
        gpar = (c // 4) % 2
        for p in range(8):
            rpar = p % 2
            for half in range(2):
                E("te", lambda g, p=p, half=half, par=par, rpar=rpar: g.matmul(
                    p_r[rpar][:, half * 128:(half + 1) * 128],
                    knT_ch[par][64 * half:64 * (half + 1),
                                p * 128:(p + 1) * 128],
                    q_ch[par][64 * half:64 * (half + 1),
                              p * 128:(p + 1) * 128],
                    start=True, stop=True, tile_position=(64 * half, 0)),
                  rd=[b_knT[par], b_q[par]],
                  wr=([b_pr[rpar]] if half == 0 else []),
                  wr2=([b_pr[rpar]] if half == 1 else []))
            E("ve", lambda g, p=p, rpar=rpar: g.tensor_tensor(
                a_scr[rpar][:], p_r[rpar][:, 0:256],
                gam[:, p * 256:(p + 1) * 256], ALU.mult),
              rd=[b_pr[rpar]], wr=[b_a[rpar]], waits=tab_waits)
            for half in range(2):
                h = 2 * p + half
                E("te", lambda g, h=h, half=half, par=par, rpar=rpar: g.matmul(
                    p_r[rpar][64 * half:64 * (half + 1), 256:384],
                    v_ch[par][:, h * 64:(h + 1) * 64],
                    a_scr[rpar][:, half * 128:(half + 1) * 128],
                    start=True, stop=False, tile_position=(0, 64 * half)),
                  rd=[b_v[par], b_a[rpar]], wr2=[b_pr[rpar]], noinc=True)
                E("te", lambda g, p=p, half=half, par=par, rpar=rpar: g.matmul(
                    p_r[rpar][64 * half:64 * (half + 1), 256:384],
                    st_bf[64 * half:64 * (half + 1), p * 64:(p + 1) * 64],
                    qt_ch[par][64 * half:64 * (half + 1),
                               p * 128:(p + 1) * 128],
                    start=False, stop=True,
                    tile_position=(64 * half, 64 * half)),
                  rd=[b_stb, b_qt[par]], wr2=[b_pr[rpar]])
            for half in range(2):
                h = 2 * p + half
                E("te", lambda g, h=h, half=half, par=par, rpar=rpar: g.matmul(
                    p_r[rpar][64 * half:64 * (half + 1), 384:448],
                    kh_ch[par][:, h * 64:(h + 1) * 64],
                    v_ch[par][:, h * 64:(h + 1) * 64],
                    start=True, stop=True, tile_position=(0, 64 * half)),
                  rd=[b_kh[par], b_v[par]], wr2=[b_pr[rpar]])
            tye_ = E("sc", lambda g, p=p, rpar=rpar, c=c, gpar=gpar: g.copy(
                y_stage[gpar][:, p * 512 + (c % 4) * 128:
                              p * 512 + (c % 4) * 128 + 128],
                p_r[rpar][:, 256:384]),
              rd=[b_pr[rpar]],
              wr=([b_yst[gpar]] if (p == 0 and c % 4 == 0) else []),
              wr2=([b_yst[gpar]] if not (p == 0 and c % 4 == 0) else []))
            E("ve", lambda g, p=p, rpar=rpar: g.scalar_tensor_tensor(
                st_bf[:, p * 64:(p + 1) * 64], st_bf[:, p * 64:(p + 1) * 64],
                gcp[:, p:p + 1], p_r[rpar][:, 384:448], ALU.mult, ALU.add),
              rd=[b_stb, b_pr[rpar]], wr=[b_stb],
              waits=tab_waits + [(P.ctr["sc"], tye_)])

    def emit_tail(c):
        par = c % 2
        gpar = (c // 4) % 2
        if c % 4 == 3:
            gidx = c // 4
            for pr in range(8):
                E("gp", lambda g, gidx=gidx, gpar=gpar, pr=pr: g.dma_start(
                    out=y_sp.ap()[pr, :, gidx * 4 * CH:(gidx + 1) * 4 * CH],
                    in_=y_stage[gpar][:, pr * 512:(pr + 1) * 512]),
                  rd=[b_yst[gpar]], wr2=[b_ysp],
                  dma=f"ysp{(gpar, pr % 2)}")
        emit_dq_stages(c)
        if c == 2 and debug not in ("p0", "p0nocc", "ar1"):
            ar1_issue()

    QKV = [(p_q, b_pq, 0), (p_k, b_pk, 1024), (p_v, b_pv, 2048)]
    emit_proj(0, QKV[0])
    emit_proj(0, QKV[1])
    emit_kln_pre(0)
    for c in range(P1_CHUNKS):
        emit_proj(c, QKV[2])
        emit_kln_apply(c)
        if c + 1 < P1_CHUNKS:
            emit_proj(c + 1, QKV[0])
            emit_proj(c + 1, QKV[1])
        emit_transposes(c)
        if c + 1 < P1_CHUNKS:
            emit_kln_pre(c + 1)
        emit_retention(c)
        emit_tail(c)
    for c in range(P1_CHUNKS, P1_CHUNKS + 3):
        emit_dq_stages(c)
    if debug not in ("p0", "p0nocc", "ar1"):
        ar1_consume()

    if XQ_BARRIER and not SKIP_XQ:
        P.barrier()
    for dc in range(NDC if not SKIP_XQ else 0):
        for half in range(4):
            base = dc * L + half * 1024
            grp = list(b_xn)
            E("ve", lambda g, base=base: g.tensor_scalar(
                ksq_f32[:], xnT[:, base:base + 1024], sc_(1), CMAGIC,
                ALU.mult, ALU.add),
              rd=[b_scl] + grp, wr=[b_ksq])
            E("ve", lambda g, base=base: g.tensor_scalar(
                xnT[:, base:base + 1024], ksq_f32[:], CMAGIC, sc_(0),
                ALU.subtract, ALU.mult),
              rd=[b_ksq, b_scl],
              wr=(grp if (dc == 0 and half == 0) else []),
              wr2=(grp if not (dc == 0 and half == 0) else []))

    if debug == "p1":
        P.wait_all_dma("sy")
        return nc, P, locals()

    P.barrier()
    es_a.close()
    es_b = contextlib.ExitStack()
    SBb = lambda n, s, d: es_b.enter_context(nc.sbuf_tensor(n, s, d))
    PSb = lambda n, s, d: es_b.enter_context(nc.psum_tensor(n, s, d))

    wst = [SBb(f"wst{i}", [128, 1024], BF16) for i in range(4)]
    b_wst = [Buf() for _ in range(4)]
    stg = [SBb(f"stg{i}", [128, 512], BF16) for i in range(4)]
    b_stg = [Buf() for _ in range(4)]
    dqdw_in = [SBb(f"dqdw_in{i}", [128, 3072], F32) for i in range(2)]
    b_dqdwin = [Buf(), Buf()]
    dqd_s = SBb("dqd_s", [128, 288], F32)
    b_dqds = Buf()
    dqd_u = SBb("dqd_u", [128, 3072], F32)
    b_dqdu = Buf()
    dqd_nat = SBb("dqd_nat", [128, 3072], BF16)
    b_dqdn = Buf()
    dqd_tr = [SBb(f"dqd_tr{i}", [128, 3072], BF16) for i in range(2)]
    b_dqdt = [Buf(), Buf()]
    p_f = [PSb(f"p_f{i}", [128, 512], F32) for i in range(4)]
    b_pf = [Buf() for _ in range(4)]

    dqd_done = {}

    def dqd_load(t):
        E("sy", lambda g, t=t: g.dma_start(
            out=dqdw_in[t % 2][:], in_=w_down[t * 128:(t + 1) * 128, :]),
          wr=[b_dqdwin[t % 2]], dma="dqdi")

    def dqd_a(t):
        par = t % 2
        E("ve", lambda g, par=par: g.tensor_reduce(
            dqd_s[:, 0:96], dqdw_in[par][:].rearrange("p (b k) -> p b k",
                                                      k=32),
            axis=AXX, op=ALU.max, apply_absolute_value=True),
          rd=[b_dqdwin[par]], wr=[b_dqds])
        E("ve", lambda g: g.tensor_scalar(
            dqd_s[:, 96:192], dqd_s[:, 0:96], 1.0 / 7.0, 1e-12, ALU.mult,
            ALU.max), rd=[b_dqds], wr2=[b_dqds])
        E("ve", lambda g: g.reciprocal(dqd_s[:, 192:288], dqd_s[:, 96:192]),
          rd=[b_dqds], wr2=[b_dqds])
        E("gp", lambda g, par=par: g.tensor_tensor(
            dqd_u[:], dqdw_in[par][:],
            bass.AP(dqd_s, 192, [[288, 128], [1, 96], [0, 32]]), ALU.mult),
          rd=[b_dqdwin[par], b_dqds], wr=[b_dqdu])

    def dqd_c(t):
        E("ve", lambda g: g.tensor_scalar(
            dqd_u[:], dqd_u[:], CMAGIC, -CMAGIC, ALU.add, ALU.add),
          rd=[b_dqdu], wr=[b_dqdu])
        E("gp", lambda g: g.tensor_tensor(
            dqd_nat[:], dqd_u[:],
            bass.AP(dqd_s, 96, [[288, 128], [1, 96], [0, 32]]), ALU.mult),
          rd=[b_dqdu, b_dqds], wr=[b_dqdn])

    def dqd_t(t, third):
        par = t % 2
        for j in range(8):
            ic = third * 8 + j
            E("sy", lambda g, ic=ic, par=par: g.dma_start(
                out=dqd_tr[par][:, ic * 128:(ic + 1) * 128],
                in_=dqd_nat[:, ic * 128:(ic + 1) * 128], transpose=True),
              rd=[b_dqdn],
              wr=([b_dqdt[par]] if (third == 0 and j == 0) else []),
              wr2=([b_dqdt[par]] if not (third == 0 and j == 0) else []),
              dma=f"dqdt{par}_{j % 4}")

    def dqd_s_unit(t):
        par = t % 2
        tout = E("gp", lambda g, par=par, t=t: g.dma_start(
            out=wdndq_sp.ap()[t, :, :], in_=dqd_tr[par][:]),
          rd=[b_dqdt[par]], dma="dqdo")
        dqd_done[t] = (P.dma_ctr("dqdo"), tout)
        if t + 1 < NDC:
            dqd_load(t + 1)

    dqd_units = [lambda: dqd_load(0)]
    for t in range(NDC):
        dqd_units += [lambda t=t: dqd_a(t), lambda t=t: dqd_c(t),
                      lambda t=t: dqd_t(t, 0), lambda t=t: dqd_t(t, 1),
                      lambda t=t: dqd_t(t, 2), lambda t=t: dqd_s_unit(t)]

    dqd_emitted = [0]
    mats2b = [("up", wupdq_sp, up_sp, b_upsp, 10),
              ("gate", wgatedq_sp, gate_sp, b_gatesp, 11)]
    ldplan = [(m[0], m[1], oc) for m in mats2b for oc in range(NIC)]

    def load_wst(k):
        mname, wsp, oc = ldplan[k]
        E("sy", lambda g, oc=oc, wsp=wsp, k=k: g.dma_start(
            out=wst[k % 4][:], in_=wsp.ap()[oc, :, :]),
          wr=[b_wst[k % 4]], waits=[dq_done[(mname, oc)]], dma=f"wst{k % 4}")

    load_wst(0)
    load_wst(1)
    kk = [2]
    for (mname, wsp, osp, b_osp_, mxcol) in mats2b:
        for oc in range(NIC):
            wb = b_wst[oc % 4]
            if kk[0] < len(ldplan):
                load_wst(kk[0])
                kk[0] += 1
            for slab in range(NSLAB):
                pf = (oc * NSLAB + slab) % 4
                for dc in range(NDC):
                    E("te", lambda g, oc=oc, slab=slab, dc=dc, pf=pf: g.matmul(
                        p_f[pf][:], wst[oc % 4][:, dc * 128:(dc + 1) * 128],
                        xnT[:, dc * L + slab * SLAB:
                            dc * L + (slab + 1) * SLAB],
                        start=(dc == 0), stop=(dc == NDC - 1)),
                      rd=[wb] + (list(b_xn) if (oc == 0 and slab == 0)
                                 else []),
                      wr=([b_pf[pf]] if dc == 0 else []),
                      wr2=([b_pf[pf]] if dc == NDC - 1 else []),
                      noinc=(dc != NDC - 1))
                E("sc", lambda g, pf=pf: g.copy(stg[pf][:], p_f[pf][:]),
                  rd=[b_pf[pf]], wr=[b_stg[pf]])
                E("ve", lambda g, pf=pf: g.tensor_reduce(
                    col(7), stg[pf][:], axis=AXX, op=ALU.max,
                    apply_absolute_value=True),
                  rd=[b_stg[pf]], wr=[b_stat])
                E("ve", lambda g, first=(oc == 0 and slab == 0), mxcol=mxcol:
                  g.tensor_tensor(
                      sc_(mxcol), col(7) if first else sc_(mxcol), col(7),
                      ALU.max), rd=[b_stat], wr2=[b_mx])
                E("gp", lambda g, oc=oc, slab=slab, pf=pf, osp=osp:
                  g.dma_start(
                      out=osp.ap()[oc, :, slab * SLAB:(slab + 1) * SLAB],
                      in_=stg[pf][:]),
                  rd=[b_stg[pf]], wr2=[b_osp_], dma=f"spl{pf}")
            if dqd_emitted[0] < len(dqd_units):
                dqd_units[dqd_emitted[0]]()
                dqd_emitted[0] += 1
    while dqd_emitted[0] < len(dqd_units):
        dqd_units[dqd_emitted[0]]()
        dqd_emitted[0] += 1

    if debug == "p2b":
        P.wait_all_dma("sy")
        return nc, P, locals()

    b_mx.note_w(P.ctr["ve"], P.ctr["ve"].c)
    allreduce_max([sc_(10), sc_(11)], [b_mx], p_f[0][:], b_pf[0])
    bcast_scale(0, 2, 3, p_f[0][:], b_pf[0])
    bcast_scale(1, 4, 5, p_f[0][:], b_pf[0])

    P.barrier()
    es_x.close()
    es_b.close()
    es_c = contextlib.ExitStack()
    es_cio = contextlib.ExitStack()
    SBc = lambda n, s, d: es_c.enter_context(nc.sbuf_tensor(n, s, d))
    SBio = lambda n, s, d: es_cio.enter_context(nc.sbuf_tensor(n, s, d))

    attn_sb = SBc("attn_sb", [128, NDC * L], BF16)
    b_attn = Buf()
    woT_sb = SBc("woT_sb", [128, NDC * D], BF16)
    b_wo = Buf()
    y_sl = [SBc(f"y_sl{i}", [128, 4096], BF16) for i in range(2)]
    b_ysl = [Buf(), Buf()]
    p_w = [es_c.enter_context(nc.psum_tensor(f"p_w{i}", [128, 512], F32))
           for i in range(2)]
    b_pw = [Buf(), Buf()]
    io_a = [SBio(f"io_a{i}", [128, 4096], BF16) for i in range(2)]
    b_ioa = [Buf(), Buf()]
    io_b = [SBio(f"io_b{i}", [128, 4096], BF16) for i in range(2)]
    b_iob = [Buf(), Buf()]
    io_c = [SBio(f"io_c{i}", [128, 4096], BF16) for i in range(2)]
    b_ioc = [Buf(), Buf()]
    fs1 = [SBio(f"fs1_{i}", [128, 2048], F32) for i in range(2)]
    b_fs1 = [Buf(), Buf()]
    fs2 = [SBio(f"fs2_{i}", [128, 2048], F32) for i in range(2)]
    b_fs2 = [Buf(), Buf()]
    fs3 = [SBio(f"fs3_{i}", [128, 2048], F32) for i in range(2)]
    b_fs3 = [Buf(), Buf()]

    for dc in range(NDC):
        E("gp", lambda g, dc=dc: g.dma_start(
            out=woT_sb[:, dc * D:(dc + 1) * D],
            in_=woT[dc * 128:(dc + 1) * 128, :]),
          wr=([b_wo] if dc == 0 else []), wr2=([b_wo] if dc > 0 else []),
          dma=f"wo{dc % 2}")

    wo_units = []
    for slab in range(NSLAB):
        wo_units.append(("load", slab))
        for oc in range(NDC):
            wo_units.append(("mm", slab, oc))
    wo_i = [0]

    def emit_wo_unit():
        if wo_i[0] >= len(wo_units):
            return
        u = wo_units[wo_i[0]]
        wo_i[0] += 1
        if u[0] == "load":
            slab = u[1]
            ypar = slab % 2
            for pr in range(8):
                E("sy", lambda g, slab=slab, ypar=ypar, pr=pr: g.dma_start(
                    out=y_sl[ypar][:, pr * 512:(pr + 1) * 512],
                    in_=y_sp.ap()[pr, :, slab * SLAB:(slab + 1) * SLAB]),
                  rd=[b_ysp], wr=([b_ysl[ypar]] if pr == 0 else []),
                  wr2=([b_ysl[ypar]] if pr > 0 else []),
                  dma=f"ysl{(ypar, pr % 2)}")
        else:
            slab, oc = u[1], u[2]
            ypar = slab % 2
            wpar = (slab * NDC + oc) % 2
            for dc in range(NDC):
                E("te", lambda g, oc=oc, dc=dc, ypar=ypar, wpar=wpar: g.matmul(
                    p_w[wpar][:],
                    woT_sb[:, dc * D + oc * 128: dc * D + (oc + 1) * 128],
                    y_sl[ypar][:, dc * 512:(dc + 1) * 512],
                    start=(dc == 0), stop=(dc == NDC - 1)),
                  rd=[b_wo, b_ysl[ypar]],
                  wr=([b_pw[wpar]] if dc == 0 else []),
                  wr2=([b_pw[wpar]] if dc == NDC - 1 else []),
                  noinc=(dc != NDC - 1))
            E("sc", lambda g, oc=oc, slab=slab, wpar=wpar: g.copy(
                attn_sb[:, oc * L + slab * SLAB: oc * L + (slab + 1) * SLAB],
                p_w[wpar][:]), rd=[b_pw[wpar]], wr2=[b_attn])

    for ic in range(NIC):
        par = ic % 2
        E("sy", lambda g, ic=ic, par=par: g.dma_start(
            out=io_a[par][:], in_=gate_sp.ap()[ic, :, :]),
          rd=[b_gatesp], wr=[b_ioa[par]], dma=f"ioa{par}")
        for hh in range(2):
            h0 = hh * 2048
            E("sc", lambda g, par=par, h0=h0: g.activation(
                fs1[par][:], io_a[par][:, h0:h0 + 2048], AF.Identity,
                bias=cmag[:], scale=sc_(5)),
              rd=[b_ioa[par], b_scl, b_cmag], wr=[b_fs1[par]])
            E("ve", lambda g, par=par: g.tensor_scalar(
                fs2[par][:], fs1[par][:], CMAGIC, sc_(4), ALU.subtract,
                ALU.mult),
              rd=[b_fs1[par], b_scl], wr=[b_fs2[par]])
            E("sc", lambda g, par=par: g.activation(fs1[par][:], fs2[par][:],
                                                    AF.Sigmoid),
              rd=[b_fs2[par]], wr=[b_fs1[par]])
            E("sc", lambda g, par=par: g.activation(
                fs3[par][:], fs1[par][:], AF.Identity, bias=cmag[:],
                scale=65536.0),
              rd=[b_fs1[par], b_cmag], wr=[b_fs3[par]])
            E("ve", lambda g, par=par: g.tensor_scalar(
                fs1[par][:], fs3[par][:], -CMAGIC, 65535.0, ALU.add, ALU.min),
              rd=[b_fs3[par]], wr=[b_fs1[par]])
            E("ve", lambda g, par=par: g.tensor_scalar(
                fs3[par][:], fs1[par][:], 0.0, 1.0 / 65536.0, ALU.max,
                ALU.mult),
              rd=[b_fs1[par]], wr=[b_fs3[par]])
            E("ve", lambda g, par=par, h0=h0: g.tensor_tensor(
                io_c[par][:, h0:h0 + 2048], fs2[par][:], fs3[par][:],
                ALU.mult),
              rd=[b_fs2[par], b_fs3[par]],
              wr=([b_ioc[par]] if hh == 0 else []),
              wr2=([b_ioc[par]] if hh else []))
        E("ve", lambda g, par=par: g.tensor_reduce(
            col(7), io_c[par][:], axis=AXX, op=ALU.max,
            apply_absolute_value=True), rd=[b_ioc[par]], wr=[b_stat])
        E("ve", lambda g, first=(ic == 0): g.tensor_tensor(
            sc_(12), col(7) if first else sc_(12), col(7), ALU.max),
          rd=[b_stat], wr2=[b_mx])
        E("gp", lambda g, ic=ic, par=par: g.dma_start(
            out=g2_sp.ap()[ic, :, :], in_=io_c[par][:]),
          rd=[b_ioc[par]], wr2=[b_g2sp], dma=f"ioc{par}")
        for _ in range(3):
            emit_wo_unit()
    while wo_i[0] < len(wo_units):
        emit_wo_unit()

    if debug == "p2d":
        P.wait_all_dma("sy")
        return nc, P, locals()

    b_mx.note_w(P.ctr["ve"], P.ctr["ve"].c)
    allreduce_max([sc_(12)], [b_mx], p_w[0][:], b_pw[0])
    bcast_scale(0, 6, 7, p_w[0][:], b_pw[0])

    for ic in range(NIC):
        par = ic % 2
        E("sy", lambda g, ic=ic, par=par: g.dma_start(
            out=io_a[par][:], in_=up_sp.ap()[ic, :, :]),
          rd=[b_upsp], wr=[b_ioa[par]], dma=f"ioa{par}")
        E("sy", lambda g, ic=ic, par=par: g.dma_start(
            out=io_b[par][:], in_=g2_sp.ap()[ic, :, :]),
          rd=[b_g2sp], wr=[b_iob[par]], dma=f"iob{par}")
        for hh in range(2):
            h0 = hh * 2048
            E("sc", lambda g, par=par, h0=h0: g.activation(
                fs1[par][:], io_a[par][:, h0:h0 + 2048], AF.Identity,
                bias=cmag[:], scale=sc_(3)),
              rd=[b_ioa[par], b_scl, b_cmag], wr=[b_fs1[par]])
            E("ve", lambda g, par=par: g.tensor_scalar(
                fs2[par][:], fs1[par][:], CMAGIC, sc_(2), ALU.subtract,
                ALU.mult),
              rd=[b_fs1[par], b_scl], wr=[b_fs2[par]])
            E("sc", lambda g, par=par, h0=h0: g.activation(
                fs3[par][:], io_b[par][:, h0:h0 + 2048], AF.Identity,
                bias=cmag[:], scale=sc_(7)),
              rd=[b_iob[par], b_scl, b_cmag], wr=[b_fs3[par]])
            E("ve", lambda g, par=par: g.tensor_scalar(
                fs1[par][:], fs3[par][:], CMAGIC, sc_(6), ALU.subtract,
                ALU.mult),
              rd=[b_fs3[par], b_scl], wr=[b_fs1[par]])
            E("ve", lambda g, par=par, h0=h0: g.tensor_tensor(
                io_c[par][:, h0:h0 + 2048], fs1[par][:], fs2[par][:],
                ALU.mult),
              rd=[b_fs1[par], b_fs2[par]],
              wr=([b_ioc[par]] if hh == 0 else []),
              wr2=([b_ioc[par]] if hh else []))
        E("ve", lambda g, par=par: g.tensor_reduce(
            col(7), io_c[par][:], axis=AXX, op=ALU.max,
            apply_absolute_value=True), rd=[b_ioc[par]], wr=[b_stat])
        E("ve", lambda g, first=(ic == 0): g.tensor_tensor(
            sc_(13), col(7) if first else sc_(13), col(7), ALU.max),
          rd=[b_stat], wr2=[b_mx])
        E("gp", lambda g, ic=ic, par=par: g.dma_start(
            out=o_sp.ap()[ic, :, :], in_=io_c[par][:]),
          rd=[b_ioc[par]], wr2=[b_osp], dma=f"ioc{par}")

    if debug == "p2e":
        P.wait_all_dma("sy")
        return nc, P, locals()

    b_mx.note_w(P.ctr["ve"], P.ctr["ve"].c)
    allreduce_max([sc_(13)], [b_mx], p_w[0][:], b_pw[0])
    bcast_scale(0, 8, 9, p_w[0][:], b_pw[0])

    P.barrier()
    es_cio.close()
    es_d = contextlib.ExitStack()
    SBd = lambda n, s, d: es_d.enter_context(nc.sbuf_tensor(n, s, d))
    oq_sl = [SBd(f"oq_sl{i}", [128, NIC * 512], BF16) for i in range(2)]
    b_oq = [Buf(), Buf()]
    wdn = [SBd(f"wdn{i}", [128, NIC * 128], BF16) for i in range(2)]
    b_wdn = [Buf(), Buf()]
    xTt = [SBd(f"xTt{i}", [128, 512], F32) for i in range(2)]
    b_xT = [Buf(), Buf()]
    outt = [SBd(f"outt{i}", [128, 512], F32) for i in range(2)]
    b_out = [Buf(), Buf()]
    ofs = [SBd(f"ofs{i}", [128, 2048], F32) for i in range(2)]
    b_ofs = [Buf(), Buf()]
    ofs2 = [SBd(f"ofs2{i}", [128, 2048], F32) for i in range(2)]
    b_ofs2 = [Buf(), Buf()]
    ep1 = SBd("ep1", [128, 512], F32)
    b_ep1 = Buf()
    p_d = [es_d.enter_context(nc.psum_tensor(f"p_d{i}", [128, 512], F32))
           for i in range(4)]
    b_pd = [Buf() for _ in range(4)]

    wdn_wait = (P.dma_ctr("dqdo"), max(t for (_c, t) in dqd_done.values()))
    last_out = 0

    def stage_o(slab):
        spar = slab % 2
        for ic in range(NIC):
            E("sy", lambda g, ic=ic, slab=slab, spar=spar: g.dma_start(
                out=oq_sl[spar][:, ic * 512:(ic + 1) * 512],
                in_=o_sp.ap()[ic, :, slab * SLAB:(slab + 1) * SLAB]),
              rd=[b_osp], wr=([b_oq[spar]] if ic == 0 else []),
              wr2=([b_oq[spar]] if ic > 0 else []), dma=f"oq{ic % 4}")
        for third in range(6):
            a0, a1 = third * 2048, (third + 1) * 2048
            tp = third % 2
            E("sc", lambda g, spar=spar, a0=a0, a1=a1, tp=tp: g.activation(
                ofs[tp][:], oq_sl[spar][:, a0:a1], AF.Identity, bias=cmag[:],
                scale=sc_(9)), rd=[b_oq[spar], b_scl, b_cmag], wr=[b_ofs[tp]])
            E("ve", lambda g, spar=spar, a0=a0, a1=a1, tp=tp: g.tensor_scalar(
                oq_sl[spar][:, a0:a1], ofs[tp][:], CMAGIC, sc_(8),
                ALU.subtract, ALU.mult),
              rd=[b_ofs[tp], b_scl], wr2=[b_oq[spar]])

    stage_o(0)
    for slab in range(NSLAB):
        spar = slab % 2
        if slab + 1 < NSLAB:
            stage_o(slab + 1)
        if slab == 0:
            E("sy", lambda g: g.dma_start(
                out=wdn[0][:], in_=wdndq_sp.ap()[0, :, :]),
              wr=[b_wdn[0]], waits=[wdn_wait], dma="wdn0")
        for oc in range(NDC):
            pd = oc % 4
            wpar2 = (slab * NDC + oc) % 2
            nslab, noc = (slab, oc + 1) if oc + 1 < NDC else (slab + 1, 0)
            if nslab < NSLAB:
                E("sy", lambda g, noc=noc, w2=1 - wpar2: g.dma_start(
                    out=wdn[w2][:], in_=wdndq_sp.ap()[noc, :, :]),
                  wr=[b_wdn[1 - wpar2]], waits=[wdn_wait],
                  dma=f"wdn{1 - wpar2}")
            for ic in range(NIC):
                E("te", lambda g, ic=ic, spar=spar, pd=pd, wpar2=wpar2:
                  g.matmul(
                      p_d[pd][:], wdn[wpar2][:, ic * 128:(ic + 1) * 128],
                      oq_sl[spar][:, ic * 512:(ic + 1) * 512],
                      start=(ic == 0), stop=(ic == NIC - 1)),
                  rd=[b_wdn[wpar2], b_oq[spar]],
                  wr=([b_pd[pd]] if ic == 0 else []),
                  wr2=([b_pd[pd]] if ic == NIC - 1 else []),
                  noinc=(ic != NIC - 1))
            opar = oc % 2
            E("sy", lambda g, oc=oc, slab=slab, opar=opar: g.dma_start(
                out=xTt[opar][:],
                in_=xT[oc * 128:(oc + 1) * 128,
                       slab * SLAB:(slab + 1) * SLAB]),
              wr=[b_xT[opar]], dma=f"xt{opar}")
            E("ve", lambda g, oc=oc, slab=slab, pd=pd: g.tensor_tensor(
                ep1[:], p_d[pd][:],
                attn_sb[:, oc * L + slab * SLAB: oc * L + (slab + 1) * SLAB],
                ALU.add), rd=[b_pd[pd], b_attn], wr=[b_ep1])
            E("ve", lambda g, opar=opar: g.tensor_tensor(
                outt[opar][:], ep1[:], xTt[opar][:], ALU.add),
              rd=[b_ep1, b_xT[opar]], wr=[b_out[opar]])
            last_out = E("gp", lambda g, oc=oc, slab=slab, opar=opar:
                         g.dma_start(
                             out=outT[oc * 128:(oc + 1) * 128,
                                      slab * SLAB:(slab + 1) * SLAB],
                             in_=outt[opar][:]),
                         rd=[b_out[opar]], dma=f"out{opar}")

    P.wait_all_dma("sy")
    return nc, P, locals()


def finalize(nc, P):
    with nc.Block() as blk:
        @blk.sync
        def _(g):
            for f in P.ops["sy"]:
                f(g)

        @blk.scalar
        def _(g):
            for f in P.ops["sc"]:
                f(g)

        @blk.vector
        def _(g):
            for f in P.ops["ve"]:
                f(g)

        @blk.tensor
        def _(g):
            for f in P.ops["te"]:
                f(g)

        @blk.gpsimd
        def _(g):
            for f in P.ops["gp"]:
                f(g)
    return nc


def host_consts():
    idx = np.arange(128, dtype=np.float32)
    emat = idx[None, :] - idx[:, None]
    emat = np.where(emat >= 0, emat, 1e9).astype(np.float32)
    utri = (idx[None, :] >= idx[:, None]).astype(np.float32)
    rampq = np.ascontiguousarray(
        np.broadcast_to(idx[None, :] + 1.0, (128, 128))).astype(np.float32)
    rampk = np.ascontiguousarray(
        np.broadcast_to((127.0 - idx)[:, None], (128, 64))).astype(np.float32)
    return dict(emat=np.ascontiguousarray(emat),
                utri=np.ascontiguousarray(utri), rampq=rampq, rampk=rampk,
                ident=np.eye(128, dtype=np.float32))


def host_inputs(L0_attn_x, L0_attn_W_qkv, L0_attn_W_o, L0_attn_log_gamma,
                L0_ffn_w_gate, L0_ffn_w_up, L0_ffn_w_down):
    x = np.asarray(L0_attn_x, dtype=np.float32)
    wqkvT = np.ascontiguousarray(np.asarray(L0_attn_W_qkv, np.float32).T)
    woT = np.ascontiguousarray(np.asarray(L0_attn_W_o, np.float32).T)
    lg = np.asarray(L0_attn_log_gamma, np.float32)
    lgp8 = np.empty((128, 8), np.float32)
    for j in range(8):
        lgp8[:64, j] = lg[2 * j]
        lgp8[64:, j] = lg[2 * j + 1]
    lg16 = np.ascontiguousarray(
        np.broadcast_to(lg[None, :], (128, 16))).astype(np.float32)
    shared = dict(wqkvT=wqkvT, woT=woT,
                  w_up=np.ascontiguousarray(np.asarray(L0_ffn_w_up,
                                                       np.float32)),
                  w_gate=np.ascontiguousarray(np.asarray(L0_ffn_w_gate,
                                                         np.float32)),
                  w_down=np.ascontiguousarray(np.asarray(L0_ffn_w_down,
                                                         np.float32)),
                  lgp8=lgp8, lg16=lg16, **host_consts())
    in_maps = []
    for c in range(NCORES):
        xc = np.ascontiguousarray(x[c])
        in_maps.append(dict(x_nat=xc, xT=np.ascontiguousarray(xc.T), **shared))
    return in_maps


def kernel(**inputs):
    nc, P, _ = build_program()
    finalize(nc, P)
    in_maps = host_inputs(**inputs)
    res = run_bass_kernel_spmd(nc, in_maps, core_ids=list(range(NCORES)))
    out = np.empty((NCORES, L, D), dtype=np.float32)
    for c in range(NCORES):
        out[c] = np.asarray(res.results[c]["outT"]).T
    return out
